# version 17
# baseline (speedup 1.0000x reference)
import numpy as np
import ml_dtypes
import concourse.bass as bass
import concourse.mybir as mybir
from concourse.bass_utils import run_bass_kernel_spmd

F32 = mybir.dt.float32
BF16 = mybir.dt.bfloat16
F16 = mybir.dt.float16
I32 = mybir.dt.int32
U16 = mybir.dt.uint16
AF = mybir.ActivationFunctionType
ALU = mybir.AluOpType
AX = mybir.AxisListType

BF = ml_dtypes.bfloat16

B_, L_, DM = 4, 4096, 2048
NH, DK, DV = 16, 128, 128
SCALE = float(DK) ** -0.5
T = 2048
NBLK = 4
NCH = 16
NTT = 16
FFI = 5632
NPAIR = 44
NIC = 44
NTILES = 2112
TPC = 264
NGRP = NTILES // 8
GPC = TPC // 8

LN_EPS = 1e-5
RMS_EPS_FFN = 1e-6
INV_DM = 1.0 / 2048.0

CONST_VALS = [SCALE, -1.0, 1.0, 2.0, -0.5, INV_DM, LN_EPS, RMS_EPS_FFN]

ENGINES = ("sync", "gpsimd", "tensor", "vector", "scalar")
DMA_QUEUES = ("sync", "gpsimd", "scalar")
NDMASEM = 8
DMA_SEMS = tuple(f"{q}dma{i}" for q in DMA_QUEUES for i in range(NDMASEM))
ALL_SEMS = ENGINES + ("cc",) + DMA_SEMS


class Prog:

    def __init__(self):
        self.items = []
        self.count = {s: 0 for s in ALL_SEMS}
        self.hwm = {e: {s: 0 for s in ALL_SEMS} for e in ENGINES}
        self.last_w = {}
        self.readers = {}
        self.dma_idx = {q: 0 for q in DMA_QUEUES}

    def op(self, eng, fn, reads=(), writes=(), dma=False, deps=(), sem=None):
        dep_evs = []
        for k in reads:
            ev = self.last_w.get(k)
            if ev is not None:
                dep_evs.append(ev)
        for k in writes:
            ev = self.last_w.get(k)
            if ev is not None:
                dep_evs.append(ev)
            dep_evs.extend(self.readers.get(k, ()))
        dep_evs.extend(deps)
        if dma:
            idx = self.dma_idx[eng]
            self.dma_idx[eng] = idx + 1
            slot = idx % NDMASEM
            dmasem = f"{eng}dma{slot}"
            dep_evs.append((dmasem, 16 * (idx // NDMASEM), True))
        wmax = {}
        for (ds, dv, dasync) in dep_evs:
            if ds == eng and eng == "tensor" and not dasync:
                continue
            if self.hwm[eng][ds] >= dv:
                continue
            wmax[ds] = max(wmax.get(ds, 0), dv)
        for ds, dv in wmax.items():
            self.hwm[eng][ds] = dv
        if dma:
            semname = dmasem
        else:
            semname = sem if sem is not None else eng
        inc = 16 if dma else 1
        self.count[semname] += inc
        is_async = dma or (sem is not None)
        ev = (semname, self.count[semname], is_async)
        self.items.append((eng, fn, sorted(wmax.items()), semname, inc))
        for k in reads:
            self.readers.setdefault(k, []).append(ev)
        for k in writes:
            self.last_w[k] = ev
            self.readers[k] = []
        return ev

    def barrier(self):
        snap = {s: self.count[s] for s in ENGINES + DMA_SEMS}
        for e in ENGINES:
            waits = []
            for e2 in ENGINES + DMA_SEMS:
                if e2 == e:
                    continue
                if self.hwm[e][e2] < snap[e2]:
                    waits.append((e2, snap[e2]))
                    self.hwm[e][e2] = snap[e2]
            if waits:
                self.items.append((e, None, waits, e, 0))
        self.last_w = {}
        self.readers = {}

    def flush(self, nc, sems):
        items, self.items = self.items, []
        with nc.Block() as block:
            for eng_name in ENGINES:
                def make(eng_name=eng_name):
                    def body(eng):
                        for (en, fn, waits, semname, inc) in items:
                            if en != eng_name:
                                continue
                            for (se, sv) in waits:
                                eng.wait_ge(sems[se], sv)
                            if fn is not None:
                                fn(eng).then_inc(sems[semname], inc)
                    return body
                getattr(block, eng_name)(make())


def build_nc(debug_outputs=False, phases=("A", "B", "C")):
    nc = bass.Bass("TRN2", target_bir_lowering=False)

    def din(name, shape, dt):
        return nc.dram_tensor(name, shape, dt, kind="ExternalInput").ap()

    x_in = din("x", [T, DM], F32)
    wq_in = din("wq", [DM, DM], F32)
    wk_in = din("wk", [DM, DM], F32)
    wi_in = din("wi", [DM, DM], F32)
    wo_in = din("wo", [DM, DM], F32)
    gwc_in = din("gwc", [128, NH], F32)
    mask8_in = din("mask8", [128, 8], F32)
    trimask_in = din("trimask", [128, 128], BF16)
    ident_in = din("ident", [128, 128], BF16)
    consts_in = din("consts", [128, len(CONST_VALS)], F32)
    packed_in = din("packedsl", [TPC, 128, 16], I32)
    scales_in = din("scalessl", [128, TPC], F16)

    y_out = nc.dram_tensor("y", [T, DM], F32, kind="ExternalOutput").ap()

    dkind = "ExternalOutput" if debug_outputs else "Internal"

    def dscr(name, shape, dt, dbg=False):
        return nc.dram_tensor(name, shape, dt,
                              kind=(dkind if dbg else "Internal")).ap()

    wq16t = dscr("wq16t", [NH, 16, 128, 128], BF16)
    wk16t = dscr("wk16t", [NH, 16, 128, 128], BF16)
    wi16 = dscr("wi16", [DM, DM], BF16)
    wo16 = dscr("wo16", [DM, DM], BF16, dbg=True)
    ln_dram = dscr("ln_dram", [T, DM], BF16, dbg=True)
    ogg_dram = dscr("ogg_dram", [NCH, NH, 128, 128], BF16)
    o_dram = dscr("o_dram", [NCH, NH, 128, 128], BF16, dbg=True)
    x2_dram = dscr("x2_dram", [T, DM], F32, dbg=True)
    dbgf = dscr("dbgf", [128, 4096], F32, dbg=True)
    dbgb = dscr("dbgb", [128, 4096], BF16, dbg=True)
    cc_w_in = nc.dram_tensor("cc_w_in", [GPC * 128, 1024], BF16).ap()
    cc_w_out = nc.dram_tensor("cc_w_out", [NGRP * 128, 1024], BF16).ap()
    cc_s_in = nc.dram_tensor("cc_s_in", [NH * 128, 128], F32).ap()
    cc_s_out = nc.dram_tensor("cc_s_out", [8 * NH * 128, 128], F32).ap()

    P = Prog()

    sems = {}
    sem_ctx = []
    for s in ALL_SEMS:
        cm = nc.semaphore(f"sem_{s}")
        sem_ctx.append(cm)
        sems[s] = cm.__enter__()

    all_ctx = []

    def make_group():
        group = []

        def sb(name, shape, dt):
            cm = nc.sbuf_tensor("sb_" + name, shape, dt)
            group.append(cm)
            all_ctx.append(cm)
            return cm.__enter__()
        return group, sb

    def free_group(group):
        for cm in reversed(group):
            cm.__exit__(None, None, None)
            all_ctx.remove(cm)

    ps_ctx = []

    def psb(name):
        cm = nc.psum_tensor(name, [128, 512], F32)
        ps_ctx.append(cm)
        return cm.__enter__()

    gG, sbG = make_group()
    consts = sbG("consts", [128, len(CONST_VALS)], F32)
    gwc = sbG("gwc", [128, NH], F32)
    mask8 = sbG("mask8", [128, 8], F32)
    trimask = sbG("trimask", [128, 128], BF16)
    ident = sbG("ident", [128, 128], BF16)
    junk = sbG("junk", [128, 2048], BF16)
    junkf = sbG("junkf", [128, 512], F32)
    xln_tok = sbG("xln_tok", [128, 2, 2048], BF16)
    stat = sbG("stat", [128, 16], F32)

    for i, v in enumerate(CONST_VALS):
        nc.const_aps.register(F32, v, consts[:, i:i + 1])

    P0 = psb("P0")
    P1 = psb("P1")
    P2 = psb("P2")
    P3 = psb("P3")
    P4 = psb("P4")
    P5 = psb("P5")
    P6 = psb("P6")
    cm = nc.psum_tensor("P7b", [128, 1024], BF16)
    ps_ctx.append(cm)
    P7b = cm.__enter__()

    P.op("sync", lambda e: e.dma_start(out=consts[:], in_=consts_in),
         writes=["consts"], dma=True)
    P.op("sync", lambda e: e.dma_start(out=gwc[:], in_=gwc_in),
         writes=["gwc"], dma=True)
    P.op("sync", lambda e: e.dma_start(out=mask8[:], in_=mask8_in),
         writes=["mask8"], dma=True)
    P.op("sync", lambda e: e.dma_start(out=trimask[:], in_=trimask_in),
         writes=["trimask"], dma=True)
    P.op("sync", lambda e: e.dma_start(out=ident[:], in_=ident_in),
         writes=["ident"], dma=True)
    P.op("sync", lambda e: e.dma_start(out=scl16[:], in_=scales_in),
         writes=["scl16"], dma=True)
    P.op("vector", lambda e: e.tensor_copy(out=scl[:], in_=scl16[:]),
         reads=["scl16"], writes=["scl"])
    P.op("vector", lambda e: e.memset(S_sb[:], 0.0), writes=["S"])
    P.op("vector", lambda e: e.memset(S_bf[:], 0.0), writes=["Sbf"])
    P.op("vector", lambda e: e.memset(R_run[:], 1.0), writes=["Rrun"])

    wi_r = wi_in.rearrange("(hc p) (dc f) -> dc hc p f", p=128, f=512)
    wi16_r = wi16.rearrange("(hc p) (dc f) -> dc hc p f", p=128, f=512)
    wq_r = wq_in.rearrange("(hc p) (mc f) -> mc hc p f", p=128, f=128)
    wk_r = wk_in.rearrange("(hc p) (mc f) -> mc hc p f", p=128, f=128)

    def cast_wi(dvc):
        P.op("gpsimd", lambda e, dvc=dvc: e.dma_start(
            out=wi16_r[dvc], in_=wi_r[dvc]),
            writes=[f"wi16d{dvc}"], dma=True)

    def cast_wq(mc):
        P.op("gpsimd", lambda e, mc=mc: e.dma_start(
            out=wq16t[mc], in_=wq_r[mc]), writes=[f"wq16d{mc}"], dma=True)

    def cast_wk(mc):
        P.op("gpsimd", lambda e, mc=mc: e.dma_start(
            out=wk16t[mc], in_=wk_r[mc]), writes=[f"wk16d{mc}"], dma=True)

    cast_wi(0)
    cast_wq(0)
    cast_wk(0)
    for dvc in range(1, 4):
        cast_wi(dvc)
    for mc in range(1, 16):
        cast_wq(mc)
        cast_wk(mc)

    gA, sbA = make_group()
    scl16 = sbA("scl16", [128, TPC], F16)
    scl = sbA("scl", [128, TPC], F32)
    S_sb = sbA("S_sb", [128, NH, 128], F32)
    S_bf = sbA("S_bf", [128, NH, 128], BF16)
    R_run = sbA("R_run", [128, NH], F32)
    xt = sbA("xt", [128, 2, 2048], F32)
    xln_feat = sbA("xln_feat", [128, 16, 512], BF16)
    v_tok = sbA("v_tok", [128, 4, 2048], BF16)
    wi_sb = sbA("wi_sb", [128, 16, 512], BF16)
    wq_sb = sbA("wq_sb", [128, 2, 16, 128], BF16)
    wk_sb = sbA("wk_sb", [128, 2, 16, 128], BF16)
    q_h = sbA("q_h", [128, 3, 512], BF16)
    kraw_h = sbA("kraw_h", [128, 3, 512], F32)
    sp_h = sbA("sp_h", [128, 3, 512], F32)
    gh_h = sbA("gh_h", [128, 3, 512], F32)
    bb_h = sbA("bb_h", [128, 3, 512], F32)
    eg_h = sbA("eg_h", [128, 3, 512], F32)
    qg_h = sbA("qg_h", [128, 3, 512], BF16)
    qb_h = sbA("qb_h", [128, 3, 4, 64], BF16)
    ka_h = sbA("ka_h", [128, 3, 4, 64], BF16)
    kb_h = sbA("kb_h", [128, 3, 4, 64], BF16)
    kba_h = sbA("kba_h", [128, 3, 4, 64], BF16)
    kh_h = sbA("kh_h", [128, 3, 512], BF16)
    kh_tok = sbA("kh_tok", [128, 3, 2, 128], BF16)
    ng_h = sbA("ng_h", [128, 3, 8], F32)
    asb = sbA("asb", [128, 2, 128], BF16)
    zchunk = sbA("zchunk", [128, 128], F32)
    qgg_slab = sbA("qgg_slab", [128, 4, 16, 128], BF16)
    o_slab = sbA("o_slab", [128, 4, 16, 128], BF16)
    dq_pk = sbA("dq_pk", [128, 2, 4, 16], I32)
    dq_nib = sbA("dq_nib", [128, 128, 4], U16)
    dq_bits = sbA("dq_bits", [128, 512], U16)
    dq_sgn = sbA("dq_sgn", [128, 512], U16)
    dq_sgnf = sbA("dq_sgnf", [128, 512], BF16)
    dq_u16 = sbA("dq_u16", [128, 512], U16)
    dq_relu = sbA("dq_relu", [128, 512], BF16)
    dq_min = sbA("dq_min", [128, 512], BF16)
    dq_out = sbA("dq_out", [128, 2, 512], BF16)
    dq_tr = sbA("dq_tr", [128, 2, 128], BF16)

    P.op("vector", lambda e: e.memset(zchunk[:], 0.0), writes=["zchunk"])
    P.op("vector", lambda e: e.memset(P2[64:128, 0:64], 0.0), writes=["PA0"])
    P.op("vector", lambda e: e.memset(P3[64:128, 0:64], 0.0), writes=["PA1"])

    def emit_dequant_dve(bi):
        pk_par = bi % 2
        op_par = bi % 2
        lt0 = 4 * bi
        pk = dq_pk[:, pk_par, :, :]
        P.op("gpsimd",
             lambda e, pk=pk, lt0=lt0: e.dma_start(
                 out=pk, in_=packed_in[lt0:lt0 + 4].rearrange(
                     "t p w -> p t w")),
             writes=[f"dqpk{pk_par}"], dma=True)
        pku = pk.rearrange("p t w -> p (t w)").bitcast(U16)
        for k in range(4):
            P.op("vector",
                 lambda e, k=k, pku=pku: e.tensor_scalar(
                     out=dq_nib[:, :, k], in0=pku, scalar1=4 * k, scalar2=15,
                     op0=ALU.logical_shift_right, op1=ALU.bitwise_and),
                 reads=[f"dqpk{pk_par}"], writes=["dqnib"])
        nib3 = dq_nib[:, :, :]
        P.op("vector", lambda e, nib3=nib3: e.tensor_scalar(
            out=dq_bits[:], in0=nib3, scalar1=7, scalar2=6,
            op0=ALU.bitwise_and, op1=ALU.logical_shift_left),
            reads=["dqnib"], writes=["dqbits"])
        P.op("vector", lambda e: e.tensor_scalar(
            out=dq_u16[:], in0=dq_bits[:], scalar1=0x3F00,
            scalar2=None, op0=ALU.add), reads=["dqbits"], writes=["dqu16"])
        P.op("vector", lambda e, nib3=nib3: e.tensor_scalar(
            out=dq_sgn[:], in0=nib3, scalar1=3, scalar2=1,
            op0=ALU.logical_shift_right, op1=ALU.bitwise_and),
            reads=["dqnib"], writes=["dqsgn"])
        P.op("vector", lambda e: e.tensor_scalar(
            out=dq_sgnf[:], in0=dq_sgn[:], scalar1=-2,
            scalar2=1, op0=ALU.mult, op1=ALU.add),
            reads=["dqsgn"], writes=["dqsgnf"])
        fmv = dq_u16[:].bitcast(BF16)
        P.op("scalar", lambda e, fmv=fmv: e.activation(
            out=dq_relu[:], in_=fmv, func=AF.Relu, bias=-1.0,
            scale=2.0),
            reads=["dqu16", "consts"], writes=["dqrelu"])
        P.op("vector", lambda e, fmv=fmv: e.tensor_tensor(
            out=dq_min[:], in0=dq_relu[:], in1=fmv,
            op=ALU.min),
            reads=["dqrelu", "dqu16"], writes=["dqmin"])
        for j in range(4):
            lt = lt0 + j
            P.op("vector",
                 lambda e, j=j, lt=lt, op_par=op_par: e.scalar_tensor_tensor(
                     out=dq_out[:, op_par, j * 128:(j + 1) * 128],
                     in0=dq_min[:, j * 128:(j + 1) * 128],
                     scalar=scl[:, lt:lt + 1],
                     in1=dq_sgnf[:, j * 128:(j + 1) * 128],
                     op0=ALU.mult, op1=ALU.mult),
                 reads=["dqmin", "dqsgnf", "scl"], writes=[f"dqout{op_par}"])

    def emit_dequant_tr(bi):
        op_par = bi % 2
        lt0 = 4 * bi
        for j in range(4):
            lt = lt0 + j
            tp = j % 2
            P.op("tensor", lambda e, j=j, op_par=op_par: e.transpose(
                P7b[:, 0:128], dq_out[:, op_par, j * 128:(j + 1) * 128],
                ident[:]),
                reads=[f"dqout{op_par}", "ident"], writes=["P7b"])
            P.op("scalar", lambda e, tp=tp: e.copy(
                out=dq_tr[:, tp, :], in_=P7b[:, 0:128]),
                reads=["P7b"], writes=[f"dqtr{tp}"])
            g, cb = lt // 8, lt % 8
            P.op("gpsimd", lambda e, g=g, cb=cb, tp=tp: e.dma_start(
                out=cc_w_in[g * 128:(g + 1) * 128, cb * 128:(cb + 1) * 128],
                in_=dq_tr[:, tp, :]),
                reads=[f"dqtr{tp}"], dma=True)

    dq_iter = iter(range(TPC // 4))

    dq_pending = []

    def emit_some_dequant(n):
        for _ in range(n):
            bi = next(dq_iter, None)
            if bi is None:
                while dq_pending:
                    emit_dequant_tr(dq_pending.pop(0))
                return
            emit_dequant_dve(bi)
            dq_pending.append(bi)
            if len(dq_pending) > 1:
                emit_dequant_tr(dq_pending.pop(0))

    def emit_ln(xin_ap, xkey, out_ap, okey, c0):
        m_c = stat[:, c0:c0 + 1]
        v_c = stat[:, c0 + 1:c0 + 2]
        km, kv = f"stat{c0}", f"stat{c0 + 1}"
        P.op("vector", lambda e: e.reduce_sum(m_c, xin_ap, axis=AX.X),
             reads=[xkey], writes=[km])
        P.op("scalar", lambda e: e.activation(
            out=junk[:, 0:2048], in_=xin_ap, func=AF.Square, accum_out=v_c),
            reads=[xkey], writes=[kv, "junk"])
        P.op("vector", lambda e: e.tensor_scalar(
            out=m_c, in0=m_c, scalar1=INV_DM, scalar2=None, op0=ALU.mult),
            reads=[km], writes=[km])
        P.op("vector", lambda e: e.tensor_scalar(
            out=v_c, in0=v_c, scalar1=INV_DM, scalar2=None, op0=ALU.mult),
            reads=[kv], writes=[kv])
        P.op("vector", lambda e: e.scalar_tensor_tensor(
            out=v_c, in0=m_c, scalar=m_c, in1=v_c,
            op0=ALU.mult, op1=ALU.subtract),
            reads=[km, kv], writes=[kv])
        P.op("vector", lambda e: e.tensor_scalar(
            out=v_c, in0=v_c, scalar1=-1.0, scalar2=None, op0=ALU.mult),
            reads=[kv], writes=[kv])
        P.op("scalar", lambda e: e.activation(
            out=v_c, in_=v_c, func=AF.Ln, bias=LN_EPS),
            reads=[kv, "consts"], writes=[kv])
        P.op("scalar", lambda e: e.activation(
            out=v_c, in_=v_c, func=AF.Exp, scale=-0.5),
            reads=[kv, "consts"], writes=[kv])
        P.op("vector", lambda e: e.tensor_scalar(
            out=out_ap, in0=xin_ap, scalar1=m_c, scalar2=v_c,
            op0=ALU.subtract, op1=ALU.mult),
            reads=[xkey, km, kv], writes=[okey])

    ccw_ev = None
    for tb in range(NBLK):
        for tt in range(4):
            gt = tb * 4 + tt
            par = gt % 2
            P.op("sync", lambda e, gt=gt, par=par: e.dma_start(
                out=xt[:, par, :], in_=x_in[gt * 128:(gt + 1) * 128, :]),
                writes=[f"xt{par}"], dma=True)
            emit_ln(xt[:, par, :], f"xt{par}", xln_tok[:, par, :],
                    f"xlntok{par}", 0)
            P.op("sync", lambda e, gt=gt, par=par: e.dma_start(
                out=ln_dram[gt * 128:(gt + 1) * 128, :],
                in_=xln_tok[:, par, :]),
                reads=[f"xlntok{par}"], writes=[f"lnd{gt}"], dma=True)
        for hc in range(16):
            P.op("sync", lambda e, hc=hc, tb=tb: e.dma_start(
                out=xln_feat[:, hc, :],
                in_=ln_dram[tb * 512:(tb + 1) * 512,
                            hc * 128:(hc + 1) * 128],
                transpose=True),
                reads=[f"lnd{tb * 4 + i}" for i in range(4)],
                writes=["xlf"], dma=True)

        for dvc in range(4):
            P.op("gpsimd", lambda e, dvc=dvc: e.dma_start(
                out=wi_sb[:, :, :],
                in_=wi16_r[dvc].rearrange("hc p f -> p hc f")),
                reads=[f"wi16d{dvc}"], writes=["wi_sb"], dma=True)
            for c in range(4):
                pp = [P0, P1][c % 2]
                pkey = f"P{c % 2}"
                for hc in range(16):
                    P.op("tensor", lambda e, pp=pp, hc=hc, c=c:
                         e.matmul(pp[:, :],
                                  xln_feat[:, hc, c * 128:(c + 1) * 128],
                                  wi_sb[:, hc, :],
                                  start=(hc == 0), stop=(hc == 15)),
                         reads=["xlf", "wi_sb"], writes=[pkey])
                P.op("scalar", lambda e, pp=pp, c=c, dvc=dvc: e.copy(
                    out=v_tok[:, c, dvc * 512:(dvc + 1) * 512], in_=pp[:, :]),
                    reads=[pkey], writes=[f"vtok{c}"])

        def emit_wload(h):
            hp = h % 2
            P.op("sync", lambda e, h=h, hp=hp: e.dma_start(
                out=wq_sb[:, hp, :, :],
                in_=wq16t[h].rearrange("hc p f -> p hc f")),
                reads=[f"wq16d{h}"], writes=[f"wq_sb{hp}"], dma=True)
            P.op("sync", lambda e, h=h, hp=hp: e.dma_start(
                out=wk_sb[:, hp, :, :],
                in_=wk16t[h].rearrange("hc p f -> p hc f")),
                reads=[f"wk16d{h}"], writes=[f"wk_sb{hp}"], dma=True)

        def emit_projprep(h):
            hp = h % 3
            wp2 = h % 2
            for hc in range(16):
                P.op("tensor", lambda e, hc=hc, wp2=wp2: e.matmul(
                    P0[:, :], wq_sb[:, wp2, hc, :], xln_feat[:, hc, :],
                    start=(hc == 0), stop=(hc == 15)),
                    reads=[f"wq_sb{wp2}", "xlf"], writes=["P0"])
            P.op("scalar", lambda e, hp=hp: e.activation(
                out=q_h[:, hp, :], in_=P0[:, :], func=AF.Relu, scale=SCALE),
                reads=["P0", "consts"], writes=[f"qh{hp}"])
            for hc in range(16):
                P.op("tensor", lambda e, hc=hc, wp2=wp2: e.matmul(
                    P1[:, :], wk_sb[:, wp2, hc, :], xln_feat[:, hc, :],
                    start=(hc == 0), stop=(hc == 15)),
                    reads=[f"wk_sb{wp2}", "xlf"], writes=["P1"])
            kr = kraw_h[:, hp, :]
            spv = sp_h[:, hp, :]
            ghv = gh_h[:, hp, :]
            bbv = bb_h[:, hp, :]
            egv = eg_h[:, hp, :]
            P.op("scalar", lambda e, spv=spv: e.activation(
                out=spv, in_=P1[:, :], func=AF.Abs),
                reads=["P1"], writes=[f"sp{hp}"])
            P.op("scalar", lambda e, hp=hp: e.copy(
                out=kraw_h[:, hp, :], in_=P1[:, :]),
                reads=["P1"], writes=[f"krawh{hp}"])
            P.op("scalar", lambda e, spv=spv: e.activation(
                out=spv, in_=spv, func=AF.Exp, scale=-1.0),
                reads=[f"sp{hp}", "consts"], writes=[f"sp{hp}"])
            P.op("scalar", lambda e, spv=spv: e.activation(
                out=spv, in_=spv, func=AF.Ln, bias=1.0),
                reads=[f"sp{hp}", "consts"], writes=[f"sp{hp}"])
            P.op("vector", lambda e, kr=kr, bbv=bbv: e.tensor_scalar(
                out=bbv, in0=kr, scalar1=0.0, scalar2=None, op0=ALU.max),
                reads=[f"krawh{hp}"], writes=[f"bb{hp}"])
            P.op("vector", lambda e, spv=spv, bbv=bbv: e.tensor_add(
                out=spv, in0=spv, in1=bbv),
                reads=[f"sp{hp}", f"bb{hp}"], writes=[f"sp{hp}"])
            for c in range(4):
                P.op("vector", lambda e, c=c, spv=spv, ghv=ghv:
                     e.tensor_tensor_scan(
                         out=ghv[:, c * 128:(c + 1) * 128],
                         data0=spv[:, c * 128:(c + 1) * 128],
                         data1=zchunk[:, :],
                         initial=0.0, op0=ALU.add, op1=ALU.add),
                     reads=[f"sp{hp}", "zchunk"], writes=[f"gh{hp}"])
            P.op("vector", lambda e, kr=kr, spv=spv, bbv=bbv: e.tensor_sub(
                out=bbv, in0=kr, in1=spv),
                reads=[f"krawh{hp}", f"sp{hp}"], writes=[f"bb{hp}"])
            P.op("vector", lambda e, bbv=bbv, ghv=ghv: e.tensor_add(
                out=bbv, in0=bbv, in1=ghv),
                reads=[f"bb{hp}", f"gh{hp}"], writes=[f"bb{hp}"])
            P.op("scalar", lambda e, ghv=ghv, egv=egv: e.activation(
                out=egv, in_=ghv, func=AF.Exp, scale=-1.0),
                reads=[f"gh{hp}", "consts"], writes=[f"eg{hp}"])
            P.op("vector", lambda e, hp=hp, egv=egv: e.tensor_mul(
                out=qg_h[:, hp, :], in0=q_h[:, hp, :], in1=egv),
                reads=[f"qh{hp}", f"eg{hp}"], writes=[f"qg{hp}"])
            P.op("vector", lambda e, hp=hp, ghv=ghv: e.tensor_scalar(
                out=ng_h[:, hp, 0:8:2],
                in0=ghv[:, 63:512:128],
                scalar1=-1.0, scalar2=None, op0=ALU.mult),
                reads=[f"gh{hp}"], writes=[f"ng{hp}"])
            P.op("vector", lambda e, hp=hp, ghv=ghv: e.tensor_scalar(
                out=ng_h[:, hp, 1:8:2],
                in0=ghv[:, 127:512:128],
                scalar1=-1.0, scalar2=None, op0=ALU.mult),
                reads=[f"gh{hp}"], writes=[f"ng{hp}"])
            for c in range(4):
                P.op("scalar", lambda e, c=c, hp=hp, bbv=bbv: e.activation(
                    out=ka_h[:, hp, c, :],
                    in_=bbv[:, c * 128:c * 128 + 64],
                    func=AF.Exp),
                    reads=[f"bb{hp}"], writes=[f"ka{hp}"])
            for c in range(4):
                ng63 = ng_h[:, hp, 2 * c:2 * c + 1]
                ng127 = ng_h[:, hp, 2 * c + 1:2 * c + 2]
                g63 = gh_h[:, hp, c * 128 + 63:c * 128 + 64]
                P.op("scalar", lambda e, c=c, hp=hp, bbv=bbv, ng63=ng63:
                     e.activation(
                         out=kb_h[:, hp, c, :],
                         in_=bbv[:, c * 128 + 64:(c + 1) * 128],
                         func=AF.Exp, bias=ng63),
                     reads=[f"bb{hp}", f"ng{hp}"], writes=[f"kb{hp}"])
                P.op("scalar", lambda e, c=c, hp=hp, bbv=bbv, ng63=ng63:
                     e.activation(
                         out=kba_h[:, hp, c, :],
                         in_=bbv[:, c * 128:c * 128 + 64],
                         func=AF.Exp, bias=ng63),
                     reads=[f"bb{hp}", f"ng{hp}"], writes=[f"kba{hp}"])
                P.op("scalar", lambda e, c=c, hp=hp, bbv=bbv, ng127=ng127:
                     e.activation(
                         out=kh_h[:, hp, c * 128:(c + 1) * 128],
                         in_=bbv[:, c * 128:(c + 1) * 128],
                         func=AF.Exp, bias=ng127),
                     reads=[f"bb{hp}", f"ng{hp}"], writes=[f"kh{hp}"])
                P.op("scalar", lambda e, c=c, hp=hp, ghv=ghv, g63=g63:
                     e.activation(
                         out=junkf[:, c * 64:(c + 1) * 64],
                         in_=ghv[:, c * 128 + 64:(c + 1) * 128],
                         func=AF.Exp, scale=-1.0, bias=g63),
                     reads=[f"gh{hp}", "consts"], writes=["jf"])
                P.op("vector", lambda e, c=c, hp=hp: e.tensor_mul(
                    out=qb_h[:, hp, c, :],
                    in0=q_h[:, hp, c * 128 + 64:(c + 1) * 128],
                    in1=junkf[:, c * 64:(c + 1) * 64]),
                    reads=[f"qh{hp}", "jf"], writes=[f"qb{hp}"])
            if tb == 0:
                for c in range(4):
                    P.op("vector", lambda e, c=c, hp=hp, h=h: e.tensor_scalar(
                        out=qgg_slab[:, c, h, :],
                        in0=qg_h[:, hp, c * 128:(c + 1) * 128],
                        scalar1=R_run[:, h:h + 1], scalar2=None, op0=ALU.mult),
                        reads=[f"qg{hp}", "Rrun"], writes=[f"qgg{c}"])
                    P.op("vector", lambda e, c=c, hp=hp, h=h, egv=egv:
                         e.tensor_scalar(
                             out=R_run[:, h:h + 1], in0=R_run[:, h:h + 1],
                             scalar1=egv[:, c * 128 + 127:c * 128 + 128],
                             scalar2=None, op0=ALU.mult),
                         reads=["Rrun", f"eg{hp}"], writes=["Rrun"])

            if debug_outputs and tb == 0 and h == 0:
                for i, (buf, keyr) in enumerate([
                        (kraw_h[:, hp, :], f"krawh{hp}"),
                        (sp_h[:, hp, :], f"sp{hp}"),
                        (gh_h[:, hp, :], f"gh{hp}"),
                        (bb_h[:, hp, :], f"bb{hp}"),
                        (eg_h[:, hp, :], f"eg{hp}")]):
                    P.op("sync", lambda e, buf=buf, i=i: e.dma_start(
                        out=dbgf[:, i * 512:(i + 1) * 512], in_=buf),
                        reads=[keyr], writes=["dbgf"], dma=True)
                col = 0
                for (buf, keyr, w) in [
                        (q_h[:, hp, :], f"qh{hp}", 512),
                        (qg_h[:, hp, :], f"qg{hp}", 512),
                        (kh_h[:, hp, :], f"kh{hp}", 512),
                        (v_tok[:, 0, 0 * 128:4 * 128], "vtok0", 512),
                        (ka_h[:, hp, :, :].rearrange("p c z -> p (c z)"),
                         f"ka{hp}", 256),
                        (kb_h[:, hp, :, :].rearrange("p c z -> p (c z)"),
                         f"kb{hp}", 256)]:
                    P.op("sync", lambda e, buf=buf, col=col, w=w: e.dma_start(
                        out=dbgb[:, col:col + w], in_=buf),
                        reads=[keyr], writes=["dbgb"], dma=True)
                    col += w

        def emit_apart(h, c):
            hp = h % 3
            cp = c % 2
            PA = [P2, P3][cp]
            pak = f"PA{cp}"
            P.op("tensor", lambda e, c=c, hp=hp, cp=cp: e.transpose(
                P7b[:, 0:128], kh_h[:, hp, c * 128:(c + 1) * 128],
                ident[:]),
                reads=[f"kh{hp}", "ident"], writes=["P7b"])
            P.op("vector", lambda e, hp=hp, cp=cp: e.tensor_copy(
                out=kh_tok[:, hp, cp, :], in_=P7b[:, 0:128]),
                reads=["P7b"], writes=[f"khtok{hp}{cp}"])
            P.op("tensor", lambda e, c=c, hp=hp, PA=PA: e.matmul(
                PA[0:64, 0:64], ka_h[:, hp, c, :],
                qg_h[:, hp, c * 128:c * 128 + 64],
                start=True, stop=True),
                reads=[f"ka{hp}", f"qg{hp}"], writes=[pak])
            P.op("tensor", lambda e, c=c, hp=hp, PA=PA: e.matmul(
                PA[0:64, 64:128], kba_h[:, hp, c, :], qb_h[:, hp, c, :],
                start=True, stop=True),
                reads=[f"kba{hp}", f"qb{hp}"], writes=[pak])
            P.op("tensor", lambda e, c=c, hp=hp, PA=PA: e.matmul(
                PA[64:128, 64:128], kb_h[:, hp, c, :], qb_h[:, hp, c, :],
                start=True, stop=True),
                reads=[f"kb{hp}", f"qb{hp}"], writes=[pak])
            P.op("vector", lambda e, cp=cp, PA=PA: e.tensor_tensor(
                out=asb[:, cp, :], in0=PA[:, 0:128],
                in1=trimask[:], op=ALU.mult),
                reads=[pak, "trimask"], writes=[f"asb{cp}"])

        def emit_opart(h, c):
            hp = h % 3
            cp = c % 2
            PO = [P4, P5][cp]
            pok = f"PO{cp}"
            egv = eg_h[:, hp, :]
            P.op("tensor", lambda e, c=c, h=h, cp=cp, PO=PO: e.matmul(
                PO[:, 0:64], v_tok[:, c, h * 128:(h + 1) * 128],
                asb[:, cp, 0:64], start=True, stop=False),
                reads=[f"vtok{c}", f"asb{cp}"], writes=[pok])
            P.op("tensor", lambda e, c=c, h=h, hp=hp, PO=PO: e.matmul(
                PO[:, 0:64], S_bf[:, h, :],
                qg_h[:, hp, c * 128:c * 128 + 64],
                start=False, stop=True),
                reads=["Sbf", f"qg{hp}"], writes=[pok])
            P.op("tensor", lambda e, c=c, h=h, cp=cp, PO=PO: e.matmul(
                PO[:, 64:128], v_tok[:, c, h * 128:(h + 1) * 128],
                asb[:, cp, 64:128], start=True, stop=False),
                reads=[f"vtok{c}", f"asb{cp}"], writes=[pok])
            P.op("tensor", lambda e, c=c, h=h, hp=hp, PO=PO: e.matmul(
                PO[:, 64:128], S_bf[:, h, :],
                qg_h[:, hp, c * 128 + 64:(c + 1) * 128],
                start=False, stop=True),
                reads=["Sbf", f"qg{hp}"], writes=[pok])
            P.op("tensor", lambda e, c=c, h=h, hp=hp, cp=cp: e.matmul(
                P6[:, 0:128], kh_tok[:, hp, cp, :],
                v_tok[:, c, h * 128:(h + 1) * 128],
                start=True, stop=True),
                reads=[f"khtok{hp}{cp}", f"vtok{c}"], writes=["P6"])
            P.op("scalar", lambda e, c=c, h=h, PO=PO: e.copy(
                out=o_slab[:, c, h, :], in_=PO[:, 0:128]),
                reads=[pok], writes=[f"oslab{c}"])
            P.op("vector", lambda e, c=c, h=h, hp=hp, egv=egv:
                 e.scalar_tensor_tensor(
                     out=S_sb[:, h, :], in0=S_sb[:, h, :],
                     scalar=egv[:, c * 128 + 127:c * 128 + 128],
                     in1=P6[:, 0:128], op0=ALU.mult, op1=ALU.add),
                 reads=["S", f"eg{hp}", "P6"], writes=["S"])
            P.op("scalar", lambda e, h=h: e.copy(
                out=S_bf[:, h, :], in_=S_sb[:, h, :]),
                reads=["S"], writes=["Sbf"])

        def emit_scans(h):
            emit_apart(h, 0)
            for c in range(4):
                if c < 3:
                    emit_apart(h, c + 1)
                emit_opart(h, c)

        emit_wload(0)
        for h in range(NH):
            if h > 1:
                emit_scans(h - 2)
            if h + 1 < NH:
                emit_wload(h + 1)
            emit_projprep(h)
            if tb < 2:
                emit_some_dequant(3 if h % 8 == 0 else 2)
        emit_scans(NH - 2)
        emit_scans(NH - 1)

        for c in range(4):
            gc = tb * 4 + c
            P.op("sync", lambda e, gc=gc, c=c: e.dma_start(
                out=o_dram[gc].rearrange("h p f -> p h f"),
                in_=o_slab[:, c, :, :]),
                reads=[f"oslab{c}"], writes=[f"od{gc}"], dma=True)
            if tb == 0:
                P.op("sync", lambda e, gc=gc, c=c: e.dma_start(
                    out=ogg_dram[gc].rearrange("h p f -> p h f"),
                    in_=qgg_slab[:, c, :, :]),
                    reads=[f"qgg{c}"], writes=[f"oggd{gc}"], dma=True)

        if tb == 1:
            emit_some_dequant(999)
            ccw_ev = P.op(
                "gpsimd", lambda e: e.collective_compute(
                    "AllGather", ALU.bypass,
                    replica_groups=[list(range(8))],
                    ins=[cc_w_in.opt()], outs=[cc_w_out.opt()]),
                deps=[(f"gpsimddma{i}", P.count[f"gpsimddma{i}"], True)
                      for i in range(NDMASEM)],
                sem="cc")

    P.op("sync", lambda e: e.dma_start(
        out=cc_s_in.rearrange("(h p) f -> p h f", p=128), in_=S_sb[:, :, :]),
        reads=["S"], writes=["ccsin"], dma=True)
    ccs_ev = P.op("gpsimd", lambda e: e.collective_compute(
        "AllGather", ALU.bypass,
        replica_groups=[list(range(8))],
        ins=[cc_s_in.opt()], outs=[cc_s_out.opt()]),
        reads=["ccsin"], sem="cc")

    P.barrier()
    P.flush(nc, sems)
    free_group(gA)

    if "B" not in phases:
        for cmx in reversed(ps_ctx):
            cmx.__exit__(None, None, None)
        for cmx in sem_ctx:
            cmx.__exit__(None, None, None)
        return nc

    gB, sbB = make_group()
    wo_sb = sbB("wo_sb", [128, 16, 2048], BF16)
    sgat = sbB("sgat", [128, NH, 128], F32)
    sini = sbB("sini", [128, NH, 128], F32)
    sini_bf = sbB("sini_bf", [128, NH, 128], BF16)
    off_sb = sbB("off_sb", [128, 2, 16, 128], BF16)
    qgg_sb = sbB("qgg_sb", [128, 2, 16, 128], BF16)
    ofull = sbB("ofull", [128, 16, 128], BF16)
    ogw = sbB("ogw", [128, 16, 128], BF16)
    sqsl = sbB("sqsl", [128, 16, 128], BF16)
    sqpart = sbB("sqpart", [128, 128], F32)
    sqpartb = sbB("sqpartb", [128, 128], BF16)
    rstd_sb = sbB("rstd_sb", [128, 2], F32)
    xb_sb = sbB("xb_sb", [128, 2, 2048], F32)
    x2_sb = sbB("x2_sb", [128, 2, 2048], F32)

    P.op("gpsimd", lambda e: e.dma_start(out=wo16, in_=wo_in),
         writes=["wo16d"], dma=True)
    P.op("gpsimd", lambda e: e.dma_start(
        out=wo_sb[:, :, :],
        in_=wo16.rearrange("(hc p) f -> p hc f", p=128)),
        reads=["wo16d"], writes=["wo_sb"], dma=True)

    def emit_sini():
        P.op("vector", lambda e: e.memset(sini[:, :, :], 0.0),
             writes=["sini"])
        for r in range(8):
            P.op("sync", lambda e, r=r: e.dma_start(
                out=sgat[:, :, :],
                in_=cc_s_out[r * 2048:(r + 1) * 2048, :]
                .rearrange("(h p) f -> p h f", p=128)),
                deps=[ccs_ev], writes=["sgat"], dma=True)
            P.op("vector", lambda e, r=r: e.scalar_tensor_tensor(
                out=sini[:, :, :], in0=sgat[:, :, :],
                scalar=mask8[:, r:r + 1],
                in1=sini[:, :, :], op0=ALU.mult, op1=ALU.add),
                reads=["sgat", "mask8", "sini"], writes=["sini"])
        P.op("vector", lambda e: e.tensor_copy(out=sini_bf[:, :, :],
                                               in_=sini[:, :, :]),
             reads=["sini"], writes=["sinibf"])

    for gt in list(range(4, NTT)) + list(range(4)):
        par = gt % 2
        if gt == 0:
            emit_sini()
        P.op("sync", lambda e, gt=gt, par=par: e.dma_start(
            out=off_sb[:, par, :, :],
            in_=o_dram[gt].rearrange("h p f -> p h f")),
            writes=[f"off{par}"], dma=True)
        if gt < 4:
            P.op("sync", lambda e, gt=gt, par=par: e.dma_start(
                out=qgg_sb[:, par, :, :],
                in_=ogg_dram[gt].rearrange("h p f -> p h f")),
                writes=[f"qggsb{par}"], dma=True)
        P.op("sync", lambda e, gt=gt, par=par: e.dma_start(
            out=xb_sb[:, par, :], in_=x_in[gt * 128:(gt + 1) * 128, :]),
            writes=[f"xb{par}"], dma=True)
        if gt < 4:
            for h in range(NH):
                pc = h % 2
                PC = [P4, P5][pc]
                pck = f"PB{pc}"
                P.op("tensor", lambda e, h=h, par=par, PC=PC: e.matmul(
                    PC[:, 0:128], sini_bf[:, h, :], qgg_sb[:, par, h, :],
                    start=True, stop=True),
                    reads=["sinibf", f"qggsb{par}"], writes=[pck])
                P.op("vector", lambda e, h=h, par=par, PC=PC: e.tensor_add(
                    out=ofull[:, h, :], in0=off_sb[:, par, h, :],
                    in1=PC[:, 0:128]),
                    reads=[f"off{par}", pck], writes=["ofull"])
                P.op("vector", lambda e, h=h: e.tensor_scalar(
                    out=ogw[:, h, :], in0=ofull[:, h, :],
                    scalar1=gwc[:, h:h + 1], scalar2=None, op0=ALU.mult),
                    reads=["ofull", "gwc"], writes=["ogw"])
            P.op("scalar", lambda e: e.activation(
                out=sqsl[:, :, :], in_=ofull[:, :, :], func=AF.Square),
                reads=["ofull"], writes=["sqsl"])
        else:
            for h in range(NH):
                P.op("vector", lambda e, h=h, par=par: e.tensor_scalar(
                    out=ogw[:, h, :], in0=off_sb[:, par, h, :],
                    scalar1=gwc[:, h:h + 1], scalar2=None, op0=ALU.mult),
                    reads=[f"off{par}", "gwc"], writes=["ogw"])
            P.op("scalar", lambda e, par=par: e.activation(
                out=sqsl[:, :, :], in_=off_sb[:, par, :, :], func=AF.Square),
                reads=[f"off{par}"], writes=["sqsl"])
        P.op("vector", lambda e: e.tensor_reduce(
            out=sqpart[:, :],
            in_=sqsl.rearrange("p h f -> p f h"),
            axis=AX.X, op=ALU.add),
            reads=["sqsl"], writes=["sqpart"])
        P.op("vector", lambda e: e.tensor_copy(out=sqpartb[:, :],
                                               in_=sqpart[:, :]),
             reads=["sqpart"], writes=["sqpartb"])
        P.op("tensor", lambda e: e.matmul(
            P6[:, 0:1], sqpartb[:, :], trimask[:, 127:128],
            start=True, stop=True),
            reads=["sqpartb", "trimask"], writes=["P6B"])
        rc = rstd_sb[:, par:par + 1]
        P.op("scalar", lambda e, rc=rc: e.activation(
            out=rc, in_=P6[:, 0:1], func=AF.Ln, scale=INV_DM, bias=LN_EPS),
            reads=["P6B", "consts"], writes=[f"rstd{par}"])
        P.op("scalar", lambda e, rc=rc: e.activation(
            out=rc, in_=rc, func=AF.Exp, scale=-0.5),
            reads=[f"rstd{par}", "consts"], writes=[f"rstd{par}"])
        if debug_outputs and gt == 0:
            P.op("sync", lambda e: e.dma_start(
                out=dbgb[:, 2048:4096],
                in_=ofull[:, :, :]),
                reads=["ofull"], writes=["dbgb"], dma=True)
            P.op("sync", lambda e: e.dma_start(
                out=dbgf[:, 3072:3200], in_=sqpart[:, :]),
                reads=["sqpart"], writes=["dbgf"], dma=True)
            P.op("sync", lambda e, rc=rc: e.dma_start(
                out=dbgf[:, 3200:3201], in_=rc,
                allow_slow_non_contiguous=True),
                reads=[f"rstd{par}"], writes=["dbgf"], dma=True)
            P.op("sync", lambda e: e.dma_start(
                out=dbgb[:, 0:2048],
                in_=ogw[:, :, :]),
                reads=["ogw"], writes=["dbgb"], dma=True)
        for mcq in range(4):
            pw = mcq % 2
            PW = [P0, P1][pw]
            pwk = f"PW{pw}"
            for hc in range(16):
                P.op("tensor", lambda e, hc=hc, mcq=mcq, PW=PW: e.matmul(
                    PW[:, :], ogw[:, hc, :],
                    wo_sb[:, hc, mcq * 512:(mcq + 1) * 512],
                    start=(hc == 0), stop=(hc == 15)),
                    reads=["ogw", "wo_sb"], writes=[pwk])
            P.op("vector", lambda e, mcq=mcq, par=par, PW=PW, rc=rc:
                 e.scalar_tensor_tensor(
                     out=x2_sb[:, par, mcq * 512:(mcq + 1) * 512],
                     in0=PW[:, :], scalar=rc,
                     in1=xb_sb[:, par, mcq * 512:(mcq + 1) * 512],
                     op0=ALU.mult, op1=ALU.add),
                 reads=[pwk, f"rstd{par}", f"xb{par}"], writes=[f"x2{par}"])
        P.op("sync", lambda e, gt=gt, par=par: e.dma_start(
            out=x2_dram[gt * 128:(gt + 1) * 128, :], in_=x2_sb[:, par, :]),
            reads=[f"x2{par}"], writes=[f"x2d{gt}"], dma=True)
        emit_ln(x2_sb[:, par, :], f"x2{par}", xln_tok[:, par, :],
                f"xlntok{par}", 2)
        P.op("sync", lambda e, gt=gt, par=par: e.dma_start(
            out=ln_dram[gt * 128:(gt + 1) * 128, :],
            in_=xln_tok[:, par, :]),
            reads=[f"xlntok{par}"], writes=[f"ln2d{gt}"], dma=True)

    P.barrier()
    P.flush(nc, sems)
    free_group(gB)

    if "C" not in phases:
        for cmx in reversed(ps_ctx):
            cmx.__exit__(None, None, None)
        for cmx in sem_ctx:
            cmx.__exit__(None, None, None)
        return nc

    gC, sbC = make_group()
    xln2_feat = sbC("xln2_feat", [128, 16, 512], BF16)
    hact = sbC("hact", [128, NIC, 512], BF16)
    wgu_sb = sbC("wgu_sb", [128, 2, 4, 1024], BF16)
    wdn_sb = sbC("wdn_sb", [128, 22, 1024], BF16)
    sg_sb = sbC("sg_sb", [128, 2, 512], BF16)
    y_sb = sbC("y_sb", [128, 4, 2048], F32)
    x2e_sb = sbC("x2e_sb", [128, 2, 2048], F32)
    out_sb = sbC("out_sb", [128, 2048], F32)

    for tb in range(4):
        for hc in range(16):
            P.op("sync", lambda e, hc=hc, tb=tb: e.dma_start(
                out=xln2_feat[:, hc, :],
                in_=ln_dram[tb * 512:(tb + 1) * 512,
                            hc * 128:(hc + 1) * 128],
                transpose=True),
                reads=[f"ln2d{tb * 4 + i}" for i in range(4)],
                writes=["x2f"], dma=True)

        for p in range(NPAIR):
            wp = p % 2
            P.op("sync", lambda e, p=p, wp=wp: e.dma_start(
                out=wgu_sb[:, wp, :, :],
                in_=cc_w_out[p * 4 * 128:(p + 1) * 4 * 128, :]
                .rearrange("(g p2) f -> p2 g f", p2=128)),
                deps=[ccw_ev], writes=[f"wgu{wp}"], dma=True)
            pg = p % 2
            PG, PU = [P0, P2][pg], [P1, P3][pg]
            pgk, puk = f"PG{pg}", f"PU{pg}"
            for gu in range(2):
                PP = PG if gu == 0 else PU
                ppk = pgk if gu == 0 else puk
                for hc in range(16):
                    lidx = gu * 16 + hc
                    g, cb = lidx // 8, lidx % 8
                    P.op("tensor",
                         lambda e, hc=hc, g=g, cb=cb, wp=wp, PP=PP: e.matmul(
                             PP[:, :],
                             wgu_sb[:, wp, g, cb * 128:(cb + 1) * 128],
                             xln2_feat[:, hc, :],
                             start=(hc == 0), stop=(hc == 15)),
                         reads=[f"wgu{wp}", "x2f"], writes=[ppk])
            P.op("scalar", lambda e, pg=pg, PG=PG: e.activation(
                out=sg_sb[:, pg, :], in_=PG[:, :], func=AF.Silu),
                reads=[pgk], writes=[f"sg{pg}"])
            P.op("vector", lambda e, p=p, pg=pg, PU=PU: e.tensor_tensor(
                out=hact[:, p, :], in0=sg_sb[:, pg, :], in1=PU[:, :],
                op=ALU.mult),
                reads=[f"sg{pg}", puk], writes=["hact"])

        for mcg in range(4):
            for half in range(2):
                g0 = 176 + 22 * mcg + 11 * half
                P.op("sync", lambda e, g0=g0, half=half: e.dma_start(
                    out=wdn_sb[:, half * 11:(half + 1) * 11, :],
                    in_=cc_w_out[g0 * 128:(g0 + 11) * 128, :]
                    .rearrange("(g p2) f -> p2 g f", p2=128)),
                    deps=[ccw_ev], writes=[f"wdn{half}"], dma=True)
            for tt in range(4):
                pd = tt % 2
                PD = [P4, P5][pd]
                pdk = f"PD{pd}"
                for ic in range(NIC):
                    half = 0 if ic < 22 else 1
                    gl = ic // 2
                    co = (ic % 2) * 512
                    P.op("tensor",
                         lambda e, ic=ic, tt=tt, gl=gl, co=co, PD=PD:
                         e.matmul(
                             PD[:, :], hact[:, ic, tt * 128:(tt + 1) * 128],
                             wdn_sb[:, gl, co:co + 512],
                             start=(ic == 0), stop=(ic == NIC - 1)),
                         reads=["hact", f"wdn{half}"], writes=[pdk])
                P.op("scalar", lambda e, tt=tt, mcg=mcg, PD=PD: e.copy(
                    out=y_sb[:, tt, mcg * 512:(mcg + 1) * 512], in_=PD[:, :]),
                    reads=[pdk], writes=[f"y{tt}"])

        for tt in range(4):
            gt = tb * 4 + tt
            pe = tt % 2
            P.op("sync", lambda e, gt=gt, pe=pe: e.dma_start(
                out=x2e_sb[:, pe, :],
                in_=x2_dram[gt * 128:(gt + 1) * 128, :]),
                writes=[f"x2e{pe}"], dma=True)
            P.op("scalar", lambda e, tt=tt: e.activation(
                out=junk[:, 0:2048], in_=y_sb[:, tt, :], func=AF.Square,
                accum_out=stat[:, 4:5]),
                reads=[f"y{tt}"], writes=["statc", "junk"])
            P.op("scalar", lambda e: e.activation(
                out=stat[:, 4:5], in_=stat[:, 4:5], func=AF.Ln,
                scale=INV_DM, bias=RMS_EPS_FFN),
                reads=["statc", "consts"], writes=["statc"])
            P.op("scalar", lambda e: e.activation(
                out=stat[:, 4:5], in_=stat[:, 4:5], func=AF.Exp, scale=-0.5),
                reads=["statc", "consts"], writes=["statc"])
            P.op("vector", lambda e, tt=tt, pe=pe: e.scalar_tensor_tensor(
                out=out_sb[:, :], in0=y_sb[:, tt, :],
                scalar=stat[:, 4:5], in1=x2e_sb[:, pe, :],
                op0=ALU.mult, op1=ALU.add),
                reads=[f"y{tt}", "statc", f"x2e{pe}"], writes=["outsb"])
            P.op("sync", lambda e, gt=gt: e.dma_start(
                out=y_out[gt * 128:(gt + 1) * 128, :], in_=out_sb[:, :]),
                reads=["outsb"], writes=["yd"], dma=True)

    P.barrier()
    P.flush(nc, sems)
    free_group(gC)
    free_group(gG)

    for cmx in reversed(ps_ctx):
        cmx.__exit__(None, None, None)
    for cmx in sem_ctx:
        cmx.__exit__(None, None, None)
    return nc


def _host_inputs(L0_attn_hidden_states, L0_attn_Wq, L0_attn_Wk, L0_attn_Wi,
                 L0_attn_Wo, L0_attn_g_weight, L0_ffn_gate_up_packed,
                 L0_ffn_gate_up_scales, L0_ffn_down_packed,
                 L0_ffn_down_scales):
    x = np.asarray(L0_attn_hidden_states, dtype=np.float32)
    wq = np.ascontiguousarray(np.asarray(L0_attn_Wq, np.float32))
    wk = np.ascontiguousarray(np.asarray(L0_attn_Wk, np.float32))
    wi = np.ascontiguousarray(np.asarray(L0_attn_Wi, np.float32))
    wo = np.ascontiguousarray(np.asarray(L0_attn_Wo, np.float32))
    gw = np.asarray(L0_attn_g_weight, np.float32)
    gup = np.asarray(L0_ffn_gate_up_packed, np.int32)
    gus = np.asarray(L0_ffn_gate_up_scales, np.float16)
    dnp = np.asarray(L0_ffn_down_packed, np.int32)
    dns = np.asarray(L0_ffn_down_scales, np.float16)

    gwc = np.ascontiguousarray(gw.reshape(NH, 128).T)
    trimask = np.zeros((128, 128), dtype=BF)
    su = np.tril(np.ones((64, 64), np.float32)).T.astype(BF)
    trimask[0:64, 0:64] = su
    trimask[0:64, 64:128] = 1.0
    trimask[64:128, 64:128] = su
    ident = np.eye(128, dtype=BF)
    consts = np.broadcast_to(np.array(CONST_VALS, np.float32),
                             (128, len(CONST_VALS))).copy()

    def tile_slices(gidx):
        if gidx < 1408:
            pair, rem = divmod(gidx, 32)
            gu, hc = divmod(rem, 16)
            r0 = (gu * NPAIR + pair) * 128
            pk = gup[r0:r0 + 128, hc * 16:(hc + 1) * 16]
            sc = gus[hc, r0:r0 + 128]
        else:
            g2 = gidx - 1408
            mcg, rem = divmod(g2, 176)
            ic, mi = divmod(rem, 4)
            mt = mcg * 4 + mi
            pk = dnp[mt * 128:(mt + 1) * 128, ic * 16:(ic + 1) * 16]
            sc = dns[ic, mt * 128:(mt + 1) * 128]
        return pk, sc

    per_core = []
    for c in range(8):
        b, half = c // 2, c % 2
        xs = np.ascontiguousarray(x[b, half * T:(half + 1) * T, :])
        pks = np.empty((TPC, 128, 16), np.int32)
        scs = np.empty((TPC, 128), np.float16)
        for i in range(TPC):
            pk, sc = tile_slices(c * TPC + i)
            pks[i] = pk
            scs[i] = sc
        mask8 = np.zeros((128, 8), np.float32)
        if half == 1:
            mask8[:, c - 1] = 1.0
        per_core.append(dict(
            x=xs, wq=wq, wk=wk, wi=wi, wo=wo,
            gwc=gwc, mask8=mask8, trimask=trimask, ident=ident,
            consts=consts, packedsl=pks,
            scalessl=np.ascontiguousarray(scs.T)))
    return per_core


def kernel(**inputs):
    per_core = _host_inputs(**inputs)
    nc = build_nc()
    res = run_bass_kernel_spmd(nc, per_core, core_ids=list(range(8)))
    out = np.empty((B_, L_, DM), np.float32)
    for c in range(8):
        b, half = c // 2, c % 2
        out[b, half * T:(half + 1) * T, :] = res.results[c]["y"]
    return out


if __name__ == "__main__":
    nc = build_nc()
    print("built ok; instructions:", nc.n_instructions())


# version 18
# speedup vs baseline: 1.1106x; 1.1106x over previous
import numpy as np
import ml_dtypes
import concourse.bass as bass
import concourse.mybir as mybir
from concourse.bass_utils import run_bass_kernel_spmd

F32 = mybir.dt.float32
BF16 = mybir.dt.bfloat16
F16 = mybir.dt.float16
I32 = mybir.dt.int32
U16 = mybir.dt.uint16
AF = mybir.ActivationFunctionType
ALU = mybir.AluOpType
AX = mybir.AxisListType

BF = ml_dtypes.bfloat16

B_, L_, DM = 4, 4096, 2048
NH, DK, DV = 16, 128, 128
SCALE = float(DK) ** -0.5
T = 2048
NBLK = 4
NCH = 16
NTT = 16
FFI = 5632
NPAIR = 44
NIC = 44
NTILES = 2112
TPC = 264
NGRP = NTILES // 8
GPC = TPC // 8

LN_EPS = 1e-5
RMS_EPS_FFN = 1e-6
INV_DM = 1.0 / 2048.0

CONST_VALS = [SCALE, -1.0, 1.0, 2.0, -0.5, INV_DM, LN_EPS, RMS_EPS_FFN]

ENGINES = ("sync", "gpsimd", "tensor", "vector", "scalar")
DMA_QUEUES = ("sync", "gpsimd", "scalar")
NDMASEM = 8
DMA_SEMS = tuple(f"{q}dma{i}" for q in DMA_QUEUES for i in range(NDMASEM))
ALL_SEMS = ENGINES + ("cc",) + DMA_SEMS


class Prog:

    def __init__(self):
        self.items = []
        self.count = {s: 0 for s in ALL_SEMS}
        self.hwm = {e: {s: 0 for s in ALL_SEMS} for e in ENGINES}
        self.last_w = {}
        self.readers = {}
        self.dma_idx = {q: 0 for q in DMA_QUEUES}

    def op(self, eng, fn, reads=(), writes=(), dma=False, deps=(), sem=None):
        dep_evs = []
        for k in reads:
            ev = self.last_w.get(k)
            if ev is not None:
                dep_evs.append(ev)
        for k in writes:
            ev = self.last_w.get(k)
            if ev is not None:
                dep_evs.append(ev)
            dep_evs.extend(self.readers.get(k, ()))
        dep_evs.extend(deps)
        if dma:
            idx = self.dma_idx[eng]
            self.dma_idx[eng] = idx + 1
            slot = idx % NDMASEM
            dmasem = f"{eng}dma{slot}"
            dep_evs.append((dmasem, 16 * (idx // NDMASEM), True))
        wmax = {}
        for (ds, dv, dasync) in dep_evs:
            if ds == eng and eng == "tensor" and not dasync:
                continue
            if self.hwm[eng][ds] >= dv:
                continue
            wmax[ds] = max(wmax.get(ds, 0), dv)
        for ds, dv in wmax.items():
            self.hwm[eng][ds] = dv
        if dma:
            semname = dmasem
        else:
            semname = sem if sem is not None else eng
        inc = 16 if dma else 1
        self.count[semname] += inc
        is_async = dma or (sem is not None)
        ev = (semname, self.count[semname], is_async)
        self.items.append((eng, fn, sorted(wmax.items()), semname, inc))
        for k in reads:
            self.readers.setdefault(k, []).append(ev)
        for k in writes:
            self.last_w[k] = ev
            self.readers[k] = []
        return ev

    def barrier(self):
        snap = {s: self.count[s] for s in ENGINES + DMA_SEMS}
        for e in ENGINES:
            waits = []
            for e2 in ENGINES + DMA_SEMS:
                if e2 == e:
                    continue
                if self.hwm[e][e2] < snap[e2]:
                    waits.append((e2, snap[e2]))
                    self.hwm[e][e2] = snap[e2]
            if waits:
                self.items.append((e, None, waits, e, 0))
        self.last_w = {}
        self.readers = {}

    def flush(self, nc, sems):
        items, self.items = self.items, []
        with nc.Block() as block:
            for eng_name in ENGINES:
                def make(eng_name=eng_name):
                    def body(eng):
                        for (en, fn, waits, semname, inc) in items:
                            if en != eng_name:
                                continue
                            for (se, sv) in waits:
                                eng.wait_ge(sems[se], sv)
                            if fn is not None:
                                fn(eng).then_inc(sems[semname], inc)
                    return body
                getattr(block, eng_name)(make())


def build_nc(debug_outputs=False, phases=("A", "B", "C")):
    nc = bass.Bass("TRN2", target_bir_lowering=False)

    def din(name, shape, dt):
        return nc.dram_tensor(name, shape, dt, kind="ExternalInput").ap()

    x_in = din("x", [T, DM], F32)
    wq_in = din("wq", [DM, DM], F32)
    wk_in = din("wk", [DM, DM], F32)
    wi_in = din("wi", [DM, DM], F32)
    wo_in = din("wo", [DM, DM], F32)
    gwc_in = din("gwc", [128, NH], F32)
    mask8_in = din("mask8", [128, 8], F32)
    trimask_in = din("trimask", [128, 128], BF16)
    ident_in = din("ident", [128, 128], BF16)
    consts_in = din("consts", [128, len(CONST_VALS)], F32)
    packed_in = din("packedsl", [TPC, 128, 16], I32)
    scales_in = din("scalessl", [128, TPC], F16)

    y_out = nc.dram_tensor("y", [T, DM], F32, kind="ExternalOutput").ap()

    dkind = "ExternalOutput" if debug_outputs else "Internal"

    def dscr(name, shape, dt, dbg=False):
        return nc.dram_tensor(name, shape, dt,
                              kind=(dkind if dbg else "Internal")).ap()

    wq16t = dscr("wq16t", [NH, 16, 128, 128], BF16)
    wk16t = dscr("wk16t", [NH, 16, 128, 128], BF16)
    wi16 = dscr("wi16", [DM, DM], BF16)
    wo16 = dscr("wo16", [DM, DM], BF16, dbg=True)
    ln_dram = dscr("ln_dram", [T, DM], BF16, dbg=True)
    ogg_dram = dscr("ogg_dram", [NCH, NH, 128, 128], BF16)
    o_dram = dscr("o_dram", [NCH, NH, 128, 128], BF16, dbg=True)
    x2_dram = dscr("x2_dram", [T, DM], F32, dbg=True)
    dbgf = dscr("dbgf", [128, 4096], F32, dbg=True)
    dbgb = dscr("dbgb", [128, 4096], BF16, dbg=True)
    cc_w_in = nc.dram_tensor("cc_w_in", [GPC * 128, 1024], BF16).ap()
    cc_w_out = nc.dram_tensor("cc_w_out", [NGRP * 128, 1024], BF16,
                              addr_space="Shared").ap()
    cc_s_in = nc.dram_tensor("cc_s_in", [NH * 128, 128], F32).ap()
    cc_s_out = nc.dram_tensor("cc_s_out", [8 * NH * 128, 128], F32,
                              addr_space="Shared").ap()

    P = Prog()

    sems = {}
    sem_ctx = []
    for s in ALL_SEMS:
        cm = nc.semaphore(f"sem_{s}")
        sem_ctx.append(cm)
        sems[s] = cm.__enter__()

    all_ctx = []

    def make_group():
        group = []

        def sb(name, shape, dt):
            cm = nc.sbuf_tensor("sb_" + name, shape, dt)
            group.append(cm)
            all_ctx.append(cm)
            return cm.__enter__()
        return group, sb

    def free_group(group):
        for cm in reversed(group):
            cm.__exit__(None, None, None)
            all_ctx.remove(cm)

    ps_ctx = []

    def psb(name):
        cm = nc.psum_tensor(name, [128, 512], F32)
        ps_ctx.append(cm)
        return cm.__enter__()

    gG, sbG = make_group()
    consts = sbG("consts", [128, len(CONST_VALS)], F32)
    gwc = sbG("gwc", [128, NH], F32)
    mask8 = sbG("mask8", [128, 8], F32)
    trimask = sbG("trimask", [128, 128], BF16)
    ident = sbG("ident", [128, 128], BF16)
    junk = sbG("junk", [128, 2048], BF16)
    junkf = sbG("junkf", [128, 512], F32)
    xln_tok = sbG("xln_tok", [128, 2, 2048], BF16)
    stat = sbG("stat", [128, 16], F32)

    for i, v in enumerate(CONST_VALS):
        nc.const_aps.register(F32, v, consts[:, i:i + 1])

    P0 = psb("P0")
    P1 = psb("P1")
    P2 = psb("P2")
    P3 = psb("P3")
    P4 = psb("P4")
    P5 = psb("P5")
    P6 = psb("P6")
    cm = nc.psum_tensor("P7b", [128, 1024], BF16)
    ps_ctx.append(cm)
    P7b = cm.__enter__()

    P.op("sync", lambda e: e.dma_start(out=consts[:], in_=consts_in),
         writes=["consts"], dma=True)
    P.op("sync", lambda e: e.dma_start(out=gwc[:], in_=gwc_in),
         writes=["gwc"], dma=True)
    P.op("sync", lambda e: e.dma_start(out=mask8[:], in_=mask8_in),
         writes=["mask8"], dma=True)
    P.op("sync", lambda e: e.dma_start(out=trimask[:], in_=trimask_in),
         writes=["trimask"], dma=True)
    P.op("sync", lambda e: e.dma_start(out=ident[:], in_=ident_in),
         writes=["ident"], dma=True)
    P.op("sync", lambda e: e.dma_start(out=scl16[:], in_=scales_in),
         writes=["scl16"], dma=True)
    P.op("vector", lambda e: e.tensor_copy(out=scl[:], in_=scl16[:]),
         reads=["scl16"], writes=["scl"])
    P.op("vector", lambda e: e.memset(S_sb[:], 0.0), writes=["S"])
    P.op("vector", lambda e: e.memset(S_bf[:], 0.0), writes=["Sbf"])
    P.op("vector", lambda e: e.memset(R_run[:], 1.0), writes=["Rrun"])

    wi_r = wi_in.rearrange("(hc p) (dc f) -> dc hc p f", p=128, f=512)
    wi16_r = wi16.rearrange("(hc p) (dc f) -> dc hc p f", p=128, f=512)
    wq_r = wq_in.rearrange("(hc p) (mc f) -> mc hc p f", p=128, f=128)
    wk_r = wk_in.rearrange("(hc p) (mc f) -> mc hc p f", p=128, f=128)

    def cast_wi(dvc):
        P.op("gpsimd", lambda e, dvc=dvc: e.dma_start(
            out=wi16_r[dvc], in_=wi_r[dvc]),
            writes=[f"wi16d{dvc}"], dma=True)

    def cast_wq(mc):
        P.op("gpsimd", lambda e, mc=mc: e.dma_start(
            out=wq16t[mc], in_=wq_r[mc]), writes=[f"wq16d{mc}"], dma=True)

    def cast_wk(mc):
        P.op("gpsimd", lambda e, mc=mc: e.dma_start(
            out=wk16t[mc], in_=wk_r[mc]), writes=[f"wk16d{mc}"], dma=True)

    cast_wi(0)
    cast_wq(0)
    cast_wk(0)
    for dvc in range(1, 4):
        cast_wi(dvc)
    for mc in range(1, 16):
        cast_wq(mc)
        cast_wk(mc)

    gA, sbA = make_group()
    scl16 = sbA("scl16", [128, TPC], F16)
    scl = sbA("scl", [128, TPC], F32)
    S_sb = sbA("S_sb", [128, NH, 128], F32)
    S_bf = sbA("S_bf", [128, NH, 128], BF16)
    R_run = sbA("R_run", [128, NH], F32)
    xt = sbA("xt", [128, 2, 2048], F32)
    xln_feat = sbA("xln_feat", [128, 16, 512], BF16)
    v_tok = sbA("v_tok", [128, 4, 2048], BF16)
    wi_sb = sbA("wi_sb", [128, 16, 512], BF16)
    wq_sb = sbA("wq_sb", [128, 2, 16, 128], BF16)
    wk_sb = sbA("wk_sb", [128, 2, 16, 128], BF16)
    q_h = sbA("q_h", [128, 3, 512], BF16)
    kraw_h = sbA("kraw_h", [128, 3, 512], F32)
    sp_h = sbA("sp_h", [128, 3, 512], F32)
    gh_h = sbA("gh_h", [128, 3, 512], F32)
    bb_h = sbA("bb_h", [128, 3, 512], F32)
    eg_h = sbA("eg_h", [128, 3, 512], F32)
    qg_h = sbA("qg_h", [128, 3, 512], BF16)
    qb_h = sbA("qb_h", [128, 3, 4, 64], BF16)
    ka_h = sbA("ka_h", [128, 3, 4, 64], BF16)
    kb_h = sbA("kb_h", [128, 3, 4, 64], BF16)
    kba_h = sbA("kba_h", [128, 3, 4, 64], BF16)
    kh_h = sbA("kh_h", [128, 3, 512], BF16)
    kh_tok = sbA("kh_tok", [128, 3, 2, 128], BF16)
    ng_h = sbA("ng_h", [128, 3, 8], F32)
    asb = sbA("asb", [128, 2, 128], BF16)
    zchunk = sbA("zchunk", [128, 128], F32)
    qgg_slab = sbA("qgg_slab", [128, 4, 16, 128], BF16)
    o_slab = sbA("o_slab", [128, 4, 16, 128], BF16)
    dq_pk = sbA("dq_pk", [128, 2, 4, 16], I32)
    dq_nib = sbA("dq_nib", [128, 128, 4], U16)
    dq_bits = sbA("dq_bits", [128, 512], U16)
    dq_sgn = sbA("dq_sgn", [128, 512], U16)
    dq_sgnf = sbA("dq_sgnf", [128, 512], BF16)
    dq_u16 = sbA("dq_u16", [128, 512], U16)
    dq_relu = sbA("dq_relu", [128, 512], BF16)
    dq_min = sbA("dq_min", [128, 512], BF16)
    dq_out = sbA("dq_out", [128, 2, 512], BF16)
    dq_tr = sbA("dq_tr", [128, 2, 128], BF16)

    P.op("vector", lambda e: e.memset(zchunk[:], 0.0), writes=["zchunk"])
    P.op("vector", lambda e: e.memset(P2[64:128, 0:64], 0.0), writes=["PA0"])
    P.op("vector", lambda e: e.memset(P3[64:128, 0:64], 0.0), writes=["PA1"])

    def emit_dequant_dve(bi):
        pk_par = bi % 2
        op_par = bi % 2
        lt0 = 4 * bi
        pk = dq_pk[:, pk_par, :, :]
        P.op("gpsimd",
             lambda e, pk=pk, lt0=lt0: e.dma_start(
                 out=pk, in_=packed_in[lt0:lt0 + 4].rearrange(
                     "t p w -> p t w")),
             writes=[f"dqpk{pk_par}"], dma=True)
        pku = pk.rearrange("p t w -> p (t w)").bitcast(U16)
        for k in range(4):
            P.op("vector",
                 lambda e, k=k, pku=pku: e.tensor_scalar(
                     out=dq_nib[:, :, k], in0=pku, scalar1=4 * k, scalar2=15,
                     op0=ALU.logical_shift_right, op1=ALU.bitwise_and),
                 reads=[f"dqpk{pk_par}"], writes=["dqnib"])
        nib3 = dq_nib[:, :, :]
        P.op("vector", lambda e, nib3=nib3: e.tensor_scalar(
            out=dq_bits[:], in0=nib3, scalar1=7, scalar2=6,
            op0=ALU.bitwise_and, op1=ALU.logical_shift_left),
            reads=["dqnib"], writes=["dqbits"])
        P.op("vector", lambda e: e.tensor_scalar(
            out=dq_u16[:], in0=dq_bits[:], scalar1=0x3F00,
            scalar2=None, op0=ALU.add), reads=["dqbits"], writes=["dqu16"])
        P.op("vector", lambda e, nib3=nib3: e.tensor_scalar(
            out=dq_sgn[:], in0=nib3, scalar1=3, scalar2=1,
            op0=ALU.logical_shift_right, op1=ALU.bitwise_and),
            reads=["dqnib"], writes=["dqsgn"])
        P.op("vector", lambda e: e.tensor_scalar(
            out=dq_sgnf[:], in0=dq_sgn[:], scalar1=-2,
            scalar2=1, op0=ALU.mult, op1=ALU.add),
            reads=["dqsgn"], writes=["dqsgnf"])
        fmv = dq_u16[:].bitcast(BF16)
        P.op("scalar", lambda e, fmv=fmv: e.activation(
            out=dq_relu[:], in_=fmv, func=AF.Relu, bias=-1.0,
            scale=2.0),
            reads=["dqu16", "consts"], writes=["dqrelu"])
        P.op("vector", lambda e, fmv=fmv: e.tensor_tensor(
            out=dq_min[:], in0=dq_relu[:], in1=fmv,
            op=ALU.min),
            reads=["dqrelu", "dqu16"], writes=["dqmin"])
        for j in range(4):
            lt = lt0 + j
            P.op("vector",
                 lambda e, j=j, lt=lt, op_par=op_par: e.scalar_tensor_tensor(
                     out=dq_out[:, op_par, j * 128:(j + 1) * 128],
                     in0=dq_min[:, j * 128:(j + 1) * 128],
                     scalar=scl[:, lt:lt + 1],
                     in1=dq_sgnf[:, j * 128:(j + 1) * 128],
                     op0=ALU.mult, op1=ALU.mult),
                 reads=["dqmin", "dqsgnf", "scl"], writes=[f"dqout{op_par}"])

    def emit_dequant_tr(bi):
        op_par = bi % 2
        lt0 = 4 * bi
        for j in range(4):
            lt = lt0 + j
            tp = j % 2
            P.op("tensor", lambda e, j=j, op_par=op_par: e.transpose(
                P7b[:, 0:128], dq_out[:, op_par, j * 128:(j + 1) * 128],
                ident[:]),
                reads=[f"dqout{op_par}", "ident"], writes=["P7b"])
            P.op("scalar", lambda e, tp=tp: e.copy(
                out=dq_tr[:, tp, :], in_=P7b[:, 0:128]),
                reads=["P7b"], writes=[f"dqtr{tp}"])
            g, cb = lt // 8, lt % 8
            P.op("gpsimd", lambda e, g=g, cb=cb, tp=tp: e.dma_start(
                out=cc_w_in[g * 128:(g + 1) * 128, cb * 128:(cb + 1) * 128],
                in_=dq_tr[:, tp, :]),
                reads=[f"dqtr{tp}"], dma=True)

    dq_iter = iter(range(TPC // 4))

    dq_pending = []

    def emit_some_dequant(n):
        for _ in range(n):
            bi = next(dq_iter, None)
            if bi is None:
                while dq_pending:
                    emit_dequant_tr(dq_pending.pop(0))
                return
            emit_dequant_dve(bi)
            dq_pending.append(bi)
            if len(dq_pending) > 1:
                emit_dequant_tr(dq_pending.pop(0))

    def emit_ln(xin_ap, xkey, out_ap, okey, c0):
        m_c = stat[:, c0:c0 + 1]
        v_c = stat[:, c0 + 1:c0 + 2]
        km, kv = f"stat{c0}", f"stat{c0 + 1}"
        P.op("vector", lambda e: e.reduce_sum(m_c, xin_ap, axis=AX.X),
             reads=[xkey], writes=[km])
        P.op("scalar", lambda e: e.activation(
            out=junk[:, 0:2048], in_=xin_ap, func=AF.Square, accum_out=v_c),
            reads=[xkey], writes=[kv, "junk"])
        P.op("vector", lambda e: e.tensor_scalar(
            out=m_c, in0=m_c, scalar1=INV_DM, scalar2=None, op0=ALU.mult),
            reads=[km], writes=[km])
        P.op("vector", lambda e: e.tensor_scalar(
            out=v_c, in0=v_c, scalar1=INV_DM, scalar2=None, op0=ALU.mult),
            reads=[kv], writes=[kv])
        P.op("vector", lambda e: e.scalar_tensor_tensor(
            out=v_c, in0=m_c, scalar=m_c, in1=v_c,
            op0=ALU.mult, op1=ALU.subtract),
            reads=[km, kv], writes=[kv])
        P.op("vector", lambda e: e.tensor_scalar(
            out=v_c, in0=v_c, scalar1=-1.0, scalar2=None, op0=ALU.mult),
            reads=[kv], writes=[kv])
        P.op("scalar", lambda e: e.activation(
            out=v_c, in_=v_c, func=AF.Ln, bias=LN_EPS),
            reads=[kv, "consts"], writes=[kv])
        P.op("scalar", lambda e: e.activation(
            out=v_c, in_=v_c, func=AF.Exp, scale=-0.5),
            reads=[kv, "consts"], writes=[kv])
        P.op("vector", lambda e: e.tensor_scalar(
            out=out_ap, in0=xin_ap, scalar1=m_c, scalar2=v_c,
            op0=ALU.subtract, op1=ALU.mult),
            reads=[xkey, km, kv], writes=[okey])

    ccw_ev = None
    for tb in range(NBLK):
        for tt in range(4):
            gt = tb * 4 + tt
            par = gt % 2
            P.op("sync", lambda e, gt=gt, par=par: e.dma_start(
                out=xt[:, par, :], in_=x_in[gt * 128:(gt + 1) * 128, :]),
                writes=[f"xt{par}"], dma=True)
            emit_ln(xt[:, par, :], f"xt{par}", xln_tok[:, par, :],
                    f"xlntok{par}", 0)
            P.op("sync", lambda e, gt=gt, par=par: e.dma_start(
                out=ln_dram[gt * 128:(gt + 1) * 128, :],
                in_=xln_tok[:, par, :]),
                reads=[f"xlntok{par}"], writes=[f"lnd{gt}"], dma=True)
        for hc in range(16):
            P.op("sync", lambda e, hc=hc, tb=tb: e.dma_start(
                out=xln_feat[:, hc, :],
                in_=ln_dram[tb * 512:(tb + 1) * 512,
                            hc * 128:(hc + 1) * 128],
                transpose=True),
                reads=[f"lnd{tb * 4 + i}" for i in range(4)],
                writes=["xlf"], dma=True)

        for dvc in range(4):
            P.op("gpsimd", lambda e, dvc=dvc: e.dma_start(
                out=wi_sb[:, :, :],
                in_=wi16_r[dvc].rearrange("hc p f -> p hc f")),
                reads=[f"wi16d{dvc}"], writes=["wi_sb"], dma=True)
            for c in range(4):
                pp = [P0, P1][c % 2]
                pkey = f"P{c % 2}"
                for hc in range(16):
                    P.op("tensor", lambda e, pp=pp, hc=hc, c=c:
                         e.matmul(pp[:, :],
                                  xln_feat[:, hc, c * 128:(c + 1) * 128],
                                  wi_sb[:, hc, :],
                                  start=(hc == 0), stop=(hc == 15)),
                         reads=["xlf", "wi_sb"], writes=[pkey])
                P.op("scalar", lambda e, pp=pp, c=c, dvc=dvc: e.copy(
                    out=v_tok[:, c, dvc * 512:(dvc + 1) * 512], in_=pp[:, :]),
                    reads=[pkey], writes=[f"vtok{c}"])

        def emit_wload(h):
            hp = h % 2
            P.op("sync", lambda e, h=h, hp=hp: e.dma_start(
                out=wq_sb[:, hp, :, :],
                in_=wq16t[h].rearrange("hc p f -> p hc f")),
                reads=[f"wq16d{h}"], writes=[f"wq_sb{hp}"], dma=True)
            P.op("sync", lambda e, h=h, hp=hp: e.dma_start(
                out=wk_sb[:, hp, :, :],
                in_=wk16t[h].rearrange("hc p f -> p hc f")),
                reads=[f"wk16d{h}"], writes=[f"wk_sb{hp}"], dma=True)

        def emit_projprep(h):
            hp = h % 3
            wp2 = h % 2
            for hc in range(16):
                P.op("tensor", lambda e, hc=hc, wp2=wp2: e.matmul(
                    P0[:, :], wq_sb[:, wp2, hc, :], xln_feat[:, hc, :],
                    start=(hc == 0), stop=(hc == 15)),
                    reads=[f"wq_sb{wp2}", "xlf"], writes=["P0"])
            P.op("scalar", lambda e, hp=hp: e.activation(
                out=q_h[:, hp, :], in_=P0[:, :], func=AF.Relu, scale=SCALE),
                reads=["P0", "consts"], writes=[f"qh{hp}"])
            for hc in range(16):
                P.op("tensor", lambda e, hc=hc, wp2=wp2: e.matmul(
                    P1[:, :], wk_sb[:, wp2, hc, :], xln_feat[:, hc, :],
                    start=(hc == 0), stop=(hc == 15)),
                    reads=[f"wk_sb{wp2}", "xlf"], writes=["P1"])
            kr = kraw_h[:, hp, :]
            spv = sp_h[:, hp, :]
            ghv = gh_h[:, hp, :]
            bbv = bb_h[:, hp, :]
            egv = eg_h[:, hp, :]
            P.op("scalar", lambda e, spv=spv: e.activation(
                out=spv, in_=P1[:, :], func=AF.Abs),
                reads=["P1"], writes=[f"sp{hp}"])
            P.op("scalar", lambda e, hp=hp: e.copy(
                out=kraw_h[:, hp, :], in_=P1[:, :]),
                reads=["P1"], writes=[f"krawh{hp}"])
            P.op("scalar", lambda e, spv=spv: e.activation(
                out=spv, in_=spv, func=AF.Exp, scale=-1.0),
                reads=[f"sp{hp}", "consts"], writes=[f"sp{hp}"])
            P.op("scalar", lambda e, spv=spv: e.activation(
                out=spv, in_=spv, func=AF.Ln, bias=1.0),
                reads=[f"sp{hp}", "consts"], writes=[f"sp{hp}"])
            P.op("vector", lambda e, kr=kr, bbv=bbv: e.tensor_scalar(
                out=bbv, in0=kr, scalar1=0.0, scalar2=None, op0=ALU.max),
                reads=[f"krawh{hp}"], writes=[f"bb{hp}"])
            P.op("vector", lambda e, spv=spv, bbv=bbv: e.tensor_add(
                out=spv, in0=spv, in1=bbv),
                reads=[f"sp{hp}", f"bb{hp}"], writes=[f"sp{hp}"])
            for c in range(4):
                P.op("vector", lambda e, c=c, spv=spv, ghv=ghv:
                     e.tensor_tensor_scan(
                         out=ghv[:, c * 128:(c + 1) * 128],
                         data0=spv[:, c * 128:(c + 1) * 128],
                         data1=zchunk[:, :],
                         initial=0.0, op0=ALU.add, op1=ALU.add),
                     reads=[f"sp{hp}", "zchunk"], writes=[f"gh{hp}"])
            P.op("vector", lambda e, kr=kr, spv=spv, bbv=bbv: e.tensor_sub(
                out=bbv, in0=kr, in1=spv),
                reads=[f"krawh{hp}", f"sp{hp}"], writes=[f"bb{hp}"])
            P.op("vector", lambda e, bbv=bbv, ghv=ghv: e.tensor_add(
                out=bbv, in0=bbv, in1=ghv),
                reads=[f"bb{hp}", f"gh{hp}"], writes=[f"bb{hp}"])
            P.op("scalar", lambda e, ghv=ghv, egv=egv: e.activation(
                out=egv, in_=ghv, func=AF.Exp, scale=-1.0),
                reads=[f"gh{hp}", "consts"], writes=[f"eg{hp}"])
            P.op("vector", lambda e, hp=hp, egv=egv: e.tensor_mul(
                out=qg_h[:, hp, :], in0=q_h[:, hp, :], in1=egv),
                reads=[f"qh{hp}", f"eg{hp}"], writes=[f"qg{hp}"])
            P.op("vector", lambda e, hp=hp, ghv=ghv: e.tensor_scalar(
                out=ng_h[:, hp, 0:8:2],
                in0=ghv[:, 63:512:128],
                scalar1=-1.0, scalar2=None, op0=ALU.mult),
                reads=[f"gh{hp}"], writes=[f"ng{hp}"])
            P.op("vector", lambda e, hp=hp, ghv=ghv: e.tensor_scalar(
                out=ng_h[:, hp, 1:8:2],
                in0=ghv[:, 127:512:128],
                scalar1=-1.0, scalar2=None, op0=ALU.mult),
                reads=[f"gh{hp}"], writes=[f"ng{hp}"])
            for c in range(4):
                P.op("scalar", lambda e, c=c, hp=hp, bbv=bbv: e.activation(
                    out=ka_h[:, hp, c, :],
                    in_=bbv[:, c * 128:c * 128 + 64],
                    func=AF.Exp),
                    reads=[f"bb{hp}"], writes=[f"ka{hp}"])
            for c in range(4):
                ng63 = ng_h[:, hp, 2 * c:2 * c + 1]
                ng127 = ng_h[:, hp, 2 * c + 1:2 * c + 2]
                g63 = gh_h[:, hp, c * 128 + 63:c * 128 + 64]
                P.op("scalar", lambda e, c=c, hp=hp, bbv=bbv, ng63=ng63:
                     e.activation(
                         out=kb_h[:, hp, c, :],
                         in_=bbv[:, c * 128 + 64:(c + 1) * 128],
                         func=AF.Exp, bias=ng63),
                     reads=[f"bb{hp}", f"ng{hp}"], writes=[f"kb{hp}"])
                P.op("scalar", lambda e, c=c, hp=hp, bbv=bbv, ng63=ng63:
                     e.activation(
                         out=kba_h[:, hp, c, :],
                         in_=bbv[:, c * 128:c * 128 + 64],
                         func=AF.Exp, bias=ng63),
                     reads=[f"bb{hp}", f"ng{hp}"], writes=[f"kba{hp}"])
                P.op("scalar", lambda e, c=c, hp=hp, bbv=bbv, ng127=ng127:
                     e.activation(
                         out=kh_h[:, hp, c * 128:(c + 1) * 128],
                         in_=bbv[:, c * 128:(c + 1) * 128],
                         func=AF.Exp, bias=ng127),
                     reads=[f"bb{hp}", f"ng{hp}"], writes=[f"kh{hp}"])
                P.op("scalar", lambda e, c=c, hp=hp, ghv=ghv, g63=g63:
                     e.activation(
                         out=junkf[:, c * 64:(c + 1) * 64],
                         in_=ghv[:, c * 128 + 64:(c + 1) * 128],
                         func=AF.Exp, scale=-1.0, bias=g63),
                     reads=[f"gh{hp}", "consts"], writes=["jf"])
                P.op("vector", lambda e, c=c, hp=hp: e.tensor_mul(
                    out=qb_h[:, hp, c, :],
                    in0=q_h[:, hp, c * 128 + 64:(c + 1) * 128],
                    in1=junkf[:, c * 64:(c + 1) * 64]),
                    reads=[f"qh{hp}", "jf"], writes=[f"qb{hp}"])
            if tb == 0:
                for c in range(4):
                    P.op("vector", lambda e, c=c, hp=hp, h=h: e.tensor_scalar(
                        out=qgg_slab[:, c, h, :],
                        in0=qg_h[:, hp, c * 128:(c + 1) * 128],
                        scalar1=R_run[:, h:h + 1], scalar2=None, op0=ALU.mult),
                        reads=[f"qg{hp}", "Rrun"], writes=[f"qgg{c}"])
                    P.op("vector", lambda e, c=c, hp=hp, h=h, egv=egv:
                         e.tensor_scalar(
                             out=R_run[:, h:h + 1], in0=R_run[:, h:h + 1],
                             scalar1=egv[:, c * 128 + 127:c * 128 + 128],
                             scalar2=None, op0=ALU.mult),
                         reads=["Rrun", f"eg{hp}"], writes=["Rrun"])

            if debug_outputs and tb == 0 and h == 0:
                for i, (buf, keyr) in enumerate([
                        (kraw_h[:, hp, :], f"krawh{hp}"),
                        (sp_h[:, hp, :], f"sp{hp}"),
                        (gh_h[:, hp, :], f"gh{hp}"),
                        (bb_h[:, hp, :], f"bb{hp}"),
                        (eg_h[:, hp, :], f"eg{hp}")]):
                    P.op("sync", lambda e, buf=buf, i=i: e.dma_start(
                        out=dbgf[:, i * 512:(i + 1) * 512], in_=buf),
                        reads=[keyr], writes=["dbgf"], dma=True)
                col = 0
                for (buf, keyr, w) in [
                        (q_h[:, hp, :], f"qh{hp}", 512),
                        (qg_h[:, hp, :], f"qg{hp}", 512),
                        (kh_h[:, hp, :], f"kh{hp}", 512),
                        (v_tok[:, 0, 0 * 128:4 * 128], "vtok0", 512),
                        (ka_h[:, hp, :, :].rearrange("p c z -> p (c z)"),
                         f"ka{hp}", 256),
                        (kb_h[:, hp, :, :].rearrange("p c z -> p (c z)"),
                         f"kb{hp}", 256)]:
                    P.op("sync", lambda e, buf=buf, col=col, w=w: e.dma_start(
                        out=dbgb[:, col:col + w], in_=buf),
                        reads=[keyr], writes=["dbgb"], dma=True)
                    col += w

        def emit_apart(h, c):
            hp = h % 3
            cp = c % 2
            PA = [P2, P3][cp]
            pak = f"PA{cp}"
            P.op("tensor", lambda e, c=c, hp=hp, cp=cp: e.transpose(
                P7b[:, 0:128], kh_h[:, hp, c * 128:(c + 1) * 128],
                ident[:]),
                reads=[f"kh{hp}", "ident"], writes=["P7b"])
            P.op("vector", lambda e, hp=hp, cp=cp: e.tensor_copy(
                out=kh_tok[:, hp, cp, :], in_=P7b[:, 0:128]),
                reads=["P7b"], writes=[f"khtok{hp}{cp}"])
            P.op("tensor", lambda e, c=c, hp=hp, PA=PA: e.matmul(
                PA[0:64, 0:64], ka_h[:, hp, c, :],
                qg_h[:, hp, c * 128:c * 128 + 64],
                start=True, stop=True),
                reads=[f"ka{hp}", f"qg{hp}"], writes=[pak])
            P.op("tensor", lambda e, c=c, hp=hp, PA=PA: e.matmul(
                PA[0:64, 64:128], kba_h[:, hp, c, :], qb_h[:, hp, c, :],
                start=True, stop=True),
                reads=[f"kba{hp}", f"qb{hp}"], writes=[pak])
            P.op("tensor", lambda e, c=c, hp=hp, PA=PA: e.matmul(
                PA[64:128, 64:128], kb_h[:, hp, c, :], qb_h[:, hp, c, :],
                start=True, stop=True),
                reads=[f"kb{hp}", f"qb{hp}"], writes=[pak])
            P.op("vector", lambda e, cp=cp, PA=PA: e.tensor_tensor(
                out=asb[:, cp, :], in0=PA[:, 0:128],
                in1=trimask[:], op=ALU.mult),
                reads=[pak, "trimask"], writes=[f"asb{cp}"])

        def emit_opart(h, c):
            hp = h % 3
            cp = c % 2
            PO = [P4, P5][cp]
            pok = f"PO{cp}"
            egv = eg_h[:, hp, :]
            P.op("tensor", lambda e, c=c, h=h, cp=cp, PO=PO: e.matmul(
                PO[:, 0:64], v_tok[:, c, h * 128:(h + 1) * 128],
                asb[:, cp, 0:64], start=True, stop=False),
                reads=[f"vtok{c}", f"asb{cp}"], writes=[pok])
            P.op("tensor", lambda e, c=c, h=h, hp=hp, PO=PO: e.matmul(
                PO[:, 0:64], S_bf[:, h, :],
                qg_h[:, hp, c * 128:c * 128 + 64],
                start=False, stop=True),
                reads=["Sbf", f"qg{hp}"], writes=[pok])
            P.op("tensor", lambda e, c=c, h=h, cp=cp, PO=PO: e.matmul(
                PO[:, 64:128], v_tok[:, c, h * 128:(h + 1) * 128],
                asb[:, cp, 64:128], start=True, stop=False),
                reads=[f"vtok{c}", f"asb{cp}"], writes=[pok])
            P.op("tensor", lambda e, c=c, h=h, hp=hp, PO=PO: e.matmul(
                PO[:, 64:128], S_bf[:, h, :],
                qg_h[:, hp, c * 128 + 64:(c + 1) * 128],
                start=False, stop=True),
                reads=["Sbf", f"qg{hp}"], writes=[pok])
            P.op("tensor", lambda e, c=c, h=h, hp=hp, cp=cp: e.matmul(
                P6[:, 0:128], kh_tok[:, hp, cp, :],
                v_tok[:, c, h * 128:(h + 1) * 128],
                start=True, stop=True),
                reads=[f"khtok{hp}{cp}", f"vtok{c}"], writes=["P6"])
            P.op("scalar", lambda e, c=c, h=h, PO=PO: e.copy(
                out=o_slab[:, c, h, :], in_=PO[:, 0:128]),
                reads=[pok], writes=[f"oslab{c}"])
            P.op("vector", lambda e, c=c, h=h, hp=hp, egv=egv:
                 e.scalar_tensor_tensor(
                     out=S_sb[:, h, :], in0=S_sb[:, h, :],
                     scalar=egv[:, c * 128 + 127:c * 128 + 128],
                     in1=P6[:, 0:128], op0=ALU.mult, op1=ALU.add),
                 reads=["S", f"eg{hp}", "P6"], writes=["S"])
            P.op("scalar", lambda e, h=h: e.copy(
                out=S_bf[:, h, :], in_=S_sb[:, h, :]),
                reads=["S"], writes=["Sbf"])

        def emit_scans(h):
            emit_apart(h, 0)
            for c in range(4):
                if c < 3:
                    emit_apart(h, c + 1)
                emit_opart(h, c)

        emit_wload(0)
        for h in range(NH):
            if h > 1:
                emit_scans(h - 2)
            if h + 1 < NH:
                emit_wload(h + 1)
            emit_projprep(h)
            if tb < 2:
                emit_some_dequant(3 if h % 8 == 0 else 2)
        emit_scans(NH - 2)
        emit_scans(NH - 1)

        for c in range(4):
            gc = tb * 4 + c
            P.op("sync", lambda e, gc=gc, c=c: e.dma_start(
                out=o_dram[gc].rearrange("h p f -> p h f"),
                in_=o_slab[:, c, :, :]),
                reads=[f"oslab{c}"], writes=[f"od{gc}"], dma=True)
            if tb == 0:
                P.op("sync", lambda e, gc=gc, c=c: e.dma_start(
                    out=ogg_dram[gc].rearrange("h p f -> p h f"),
                    in_=qgg_slab[:, c, :, :]),
                    reads=[f"qgg{c}"], writes=[f"oggd{gc}"], dma=True)

        if tb == 1:
            emit_some_dequant(999)
            ccw_ev = P.op(
                "gpsimd", lambda e: e.collective_compute(
                    "AllGather", ALU.bypass,
                    replica_groups=[list(range(8))],
                    ins=[cc_w_in.opt()], outs=[cc_w_out.opt()]),
                deps=[(f"gpsimddma{i}", P.count[f"gpsimddma{i}"], True)
                      for i in range(NDMASEM)],
                sem="cc")

    P.op("sync", lambda e: e.dma_start(
        out=cc_s_in.rearrange("(h p) f -> p h f", p=128), in_=S_sb[:, :, :]),
        reads=["S"], writes=["ccsin"], dma=True)
    ccs_ev = P.op("gpsimd", lambda e: e.collective_compute(
        "AllGather", ALU.bypass,
        replica_groups=[list(range(8))],
        ins=[cc_s_in.opt()], outs=[cc_s_out.opt()]),
        reads=["ccsin"], sem="cc")

    P.barrier()
    P.flush(nc, sems)
    free_group(gA)

    if "B" not in phases:
        for cmx in reversed(ps_ctx):
            cmx.__exit__(None, None, None)
        for cmx in sem_ctx:
            cmx.__exit__(None, None, None)
        return nc

    gB, sbB = make_group()
    wo_sb = sbB("wo_sb", [128, 16, 2048], BF16)
    sgat = sbB("sgat", [128, NH, 128], F32)
    sini = sbB("sini", [128, NH, 128], F32)
    sini_bf = sbB("sini_bf", [128, NH, 128], BF16)
    off_sb = sbB("off_sb", [128, 2, 16, 128], BF16)
    qgg_sb = sbB("qgg_sb", [128, 2, 16, 128], BF16)
    ofull = sbB("ofull", [128, 16, 128], BF16)
    ogw = sbB("ogw", [128, 16, 128], BF16)
    sqsl = sbB("sqsl", [128, 16, 128], BF16)
    sqpart = sbB("sqpart", [128, 128], F32)
    sqpartb = sbB("sqpartb", [128, 128], BF16)
    rstd_sb = sbB("rstd_sb", [128, 2], F32)
    xb_sb = sbB("xb_sb", [128, 2, 2048], F32)
    x2_sb = sbB("x2_sb", [128, 2, 2048], F32)

    P.op("gpsimd", lambda e: e.dma_start(out=wo16, in_=wo_in),
         writes=["wo16d"], dma=True)
    P.op("gpsimd", lambda e: e.dma_start(
        out=wo_sb[:, :, :],
        in_=wo16.rearrange("(hc p) f -> p hc f", p=128)),
        reads=["wo16d"], writes=["wo_sb"], dma=True)

    def emit_sini():
        P.op("vector", lambda e: e.memset(sini[:, :, :], 0.0),
             writes=["sini"])
        for r in range(8):
            P.op("sync", lambda e, r=r: e.dma_start(
                out=sgat[:, :, :],
                in_=cc_s_out[r * 2048:(r + 1) * 2048, :]
                .rearrange("(h p) f -> p h f", p=128)),
                deps=[ccs_ev], writes=["sgat"], dma=True)
            P.op("vector", lambda e, r=r: e.scalar_tensor_tensor(
                out=sini[:, :, :], in0=sgat[:, :, :],
                scalar=mask8[:, r:r + 1],
                in1=sini[:, :, :], op0=ALU.mult, op1=ALU.add),
                reads=["sgat", "mask8", "sini"], writes=["sini"])
        P.op("vector", lambda e: e.tensor_copy(out=sini_bf[:, :, :],
                                               in_=sini[:, :, :]),
             reads=["sini"], writes=["sinibf"])

    for gt in list(range(4, NTT)) + list(range(4)):
        par = gt % 2
        if gt == 0:
            emit_sini()
        P.op("sync", lambda e, gt=gt, par=par: e.dma_start(
            out=off_sb[:, par, :, :],
            in_=o_dram[gt].rearrange("h p f -> p h f")),
            writes=[f"off{par}"], dma=True)
        if gt < 4:
            P.op("sync", lambda e, gt=gt, par=par: e.dma_start(
                out=qgg_sb[:, par, :, :],
                in_=ogg_dram[gt].rearrange("h p f -> p h f")),
                writes=[f"qggsb{par}"], dma=True)
        P.op("sync", lambda e, gt=gt, par=par: e.dma_start(
            out=xb_sb[:, par, :], in_=x_in[gt * 128:(gt + 1) * 128, :]),
            writes=[f"xb{par}"], dma=True)
        if gt < 4:
            for h in range(NH):
                pc = h % 2
                PC = [P4, P5][pc]
                pck = f"PB{pc}"
                P.op("tensor", lambda e, h=h, par=par, PC=PC: e.matmul(
                    PC[:, 0:128], sini_bf[:, h, :], qgg_sb[:, par, h, :],
                    start=True, stop=True),
                    reads=["sinibf", f"qggsb{par}"], writes=[pck])
                P.op("vector", lambda e, h=h, par=par, PC=PC: e.tensor_add(
                    out=ofull[:, h, :], in0=off_sb[:, par, h, :],
                    in1=PC[:, 0:128]),
                    reads=[f"off{par}", pck], writes=["ofull"])
                P.op("vector", lambda e, h=h: e.tensor_scalar(
                    out=ogw[:, h, :], in0=ofull[:, h, :],
                    scalar1=gwc[:, h:h + 1], scalar2=None, op0=ALU.mult),
                    reads=["ofull", "gwc"], writes=["ogw"])
            P.op("scalar", lambda e: e.activation(
                out=sqsl[:, :, :], in_=ofull[:, :, :], func=AF.Square),
                reads=["ofull"], writes=["sqsl"])
        else:
            for h in range(NH):
                P.op("vector", lambda e, h=h, par=par: e.tensor_scalar(
                    out=ogw[:, h, :], in0=off_sb[:, par, h, :],
                    scalar1=gwc[:, h:h + 1], scalar2=None, op0=ALU.mult),
                    reads=[f"off{par}", "gwc"], writes=["ogw"])
            P.op("scalar", lambda e, par=par: e.activation(
                out=sqsl[:, :, :], in_=off_sb[:, par, :, :], func=AF.Square),
                reads=[f"off{par}"], writes=["sqsl"])
        P.op("vector", lambda e: e.tensor_reduce(
            out=sqpart[:, :],
            in_=sqsl.rearrange("p h f -> p f h"),
            axis=AX.X, op=ALU.add),
            reads=["sqsl"], writes=["sqpart"])
        P.op("vector", lambda e: e.tensor_copy(out=sqpartb[:, :],
                                               in_=sqpart[:, :]),
             reads=["sqpart"], writes=["sqpartb"])
        P.op("tensor", lambda e: e.matmul(
            P6[:, 0:1], sqpartb[:, :], trimask[:, 127:128],
            start=True, stop=True),
            reads=["sqpartb", "trimask"], writes=["P6B"])
        rc = rstd_sb[:, par:par + 1]
        P.op("scalar", lambda e, rc=rc: e.activation(
            out=rc, in_=P6[:, 0:1], func=AF.Ln, scale=INV_DM, bias=LN_EPS),
            reads=["P6B", "consts"], writes=[f"rstd{par}"])
        P.op("scalar", lambda e, rc=rc: e.activation(
            out=rc, in_=rc, func=AF.Exp, scale=-0.5),
            reads=[f"rstd{par}", "consts"], writes=[f"rstd{par}"])
        if debug_outputs and gt == 0:
            P.op("sync", lambda e: e.dma_start(
                out=dbgb[:, 2048:4096],
                in_=ofull[:, :, :]),
                reads=["ofull"], writes=["dbgb"], dma=True)
            P.op("sync", lambda e: e.dma_start(
                out=dbgf[:, 3072:3200], in_=sqpart[:, :]),
                reads=["sqpart"], writes=["dbgf"], dma=True)
            P.op("sync", lambda e, rc=rc: e.dma_start(
                out=dbgf[:, 3200:3201], in_=rc,
                allow_slow_non_contiguous=True),
                reads=[f"rstd{par}"], writes=["dbgf"], dma=True)
            P.op("sync", lambda e: e.dma_start(
                out=dbgb[:, 0:2048],
                in_=ogw[:, :, :]),
                reads=["ogw"], writes=["dbgb"], dma=True)
        for mcq in range(4):
            pw = mcq % 2
            PW = [P0, P1][pw]
            pwk = f"PW{pw}"
            for hc in range(16):
                P.op("tensor", lambda e, hc=hc, mcq=mcq, PW=PW: e.matmul(
                    PW[:, :], ogw[:, hc, :],
                    wo_sb[:, hc, mcq * 512:(mcq + 1) * 512],
                    start=(hc == 0), stop=(hc == 15)),
                    reads=["ogw", "wo_sb"], writes=[pwk])
            P.op("vector", lambda e, mcq=mcq, par=par, PW=PW, rc=rc:
                 e.scalar_tensor_tensor(
                     out=x2_sb[:, par, mcq * 512:(mcq + 1) * 512],
                     in0=PW[:, :], scalar=rc,
                     in1=xb_sb[:, par, mcq * 512:(mcq + 1) * 512],
                     op0=ALU.mult, op1=ALU.add),
                 reads=[pwk, f"rstd{par}", f"xb{par}"], writes=[f"x2{par}"])
        P.op("sync", lambda e, gt=gt, par=par: e.dma_start(
            out=x2_dram[gt * 128:(gt + 1) * 128, :], in_=x2_sb[:, par, :]),
            reads=[f"x2{par}"], writes=[f"x2d{gt}"], dma=True)
        emit_ln(x2_sb[:, par, :], f"x2{par}", xln_tok[:, par, :],
                f"xlntok{par}", 2)
        P.op("sync", lambda e, gt=gt, par=par: e.dma_start(
            out=ln_dram[gt * 128:(gt + 1) * 128, :],
            in_=xln_tok[:, par, :]),
            reads=[f"xlntok{par}"], writes=[f"ln2d{gt}"], dma=True)

    P.barrier()
    P.flush(nc, sems)
    free_group(gB)

    if "C" not in phases:
        for cmx in reversed(ps_ctx):
            cmx.__exit__(None, None, None)
        for cmx in sem_ctx:
            cmx.__exit__(None, None, None)
        return nc

    gC, sbC = make_group()
    xln2_feat = sbC("xln2_feat", [128, 16, 512], BF16)
    hact = sbC("hact", [128, NIC, 512], BF16)
    wgu_sb = sbC("wgu_sb", [128, 2, 4, 1024], BF16)
    wdn_sb = sbC("wdn_sb", [128, 22, 1024], BF16)
    sg_sb = sbC("sg_sb", [128, 2, 512], BF16)
    y_sb = sbC("y_sb", [128, 4, 2048], F32)
    x2e_sb = sbC("x2e_sb", [128, 2, 2048], F32)
    out_sb = sbC("out_sb", [128, 2048], F32)

    for tb in range(4):
        for hc in range(16):
            P.op("sync", lambda e, hc=hc, tb=tb: e.dma_start(
                out=xln2_feat[:, hc, :],
                in_=ln_dram[tb * 512:(tb + 1) * 512,
                            hc * 128:(hc + 1) * 128],
                transpose=True),
                reads=[f"ln2d{tb * 4 + i}" for i in range(4)],
                writes=["x2f"], dma=True)

        for p in range(NPAIR):
            wp = p % 2
            P.op("sync", lambda e, p=p, wp=wp: e.dma_start(
                out=wgu_sb[:, wp, :, :],
                in_=cc_w_out[p * 4 * 128:(p + 1) * 4 * 128, :]
                .rearrange("(g p2) f -> p2 g f", p2=128)),
                deps=[ccw_ev], writes=[f"wgu{wp}"], dma=True)
            pg = p % 2
            PG, PU = [P0, P2][pg], [P1, P3][pg]
            pgk, puk = f"PG{pg}", f"PU{pg}"
            for gu in range(2):
                PP = PG if gu == 0 else PU
                ppk = pgk if gu == 0 else puk
                for hc in range(16):
                    lidx = gu * 16 + hc
                    g, cb = lidx // 8, lidx % 8
                    P.op("tensor",
                         lambda e, hc=hc, g=g, cb=cb, wp=wp, PP=PP: e.matmul(
                             PP[:, :],
                             wgu_sb[:, wp, g, cb * 128:(cb + 1) * 128],
                             xln2_feat[:, hc, :],
                             start=(hc == 0), stop=(hc == 15)),
                         reads=[f"wgu{wp}", "x2f"], writes=[ppk])
            P.op("scalar", lambda e, pg=pg, PG=PG: e.activation(
                out=sg_sb[:, pg, :], in_=PG[:, :], func=AF.Silu),
                reads=[pgk], writes=[f"sg{pg}"])
            P.op("vector", lambda e, p=p, pg=pg, PU=PU: e.tensor_tensor(
                out=hact[:, p, :], in0=sg_sb[:, pg, :], in1=PU[:, :],
                op=ALU.mult),
                reads=[f"sg{pg}", puk], writes=["hact"])

        for mcg in range(4):
            for half in range(2):
                g0 = 176 + 22 * mcg + 11 * half
                P.op("sync", lambda e, g0=g0, half=half: e.dma_start(
                    out=wdn_sb[:, half * 11:(half + 1) * 11, :],
                    in_=cc_w_out[g0 * 128:(g0 + 11) * 128, :]
                    .rearrange("(g p2) f -> p2 g f", p2=128)),
                    deps=[ccw_ev], writes=[f"wdn{half}"], dma=True)
            for tt in range(4):
                pd = tt % 2
                PD = [P4, P5][pd]
                pdk = f"PD{pd}"
                for ic in range(NIC):
                    half = 0 if ic < 22 else 1
                    gl = ic // 2
                    co = (ic % 2) * 512
                    P.op("tensor",
                         lambda e, ic=ic, tt=tt, gl=gl, co=co, PD=PD:
                         e.matmul(
                             PD[:, :], hact[:, ic, tt * 128:(tt + 1) * 128],
                             wdn_sb[:, gl, co:co + 512],
                             start=(ic == 0), stop=(ic == NIC - 1)),
                         reads=["hact", f"wdn{half}"], writes=[pdk])
                P.op("scalar", lambda e, tt=tt, mcg=mcg, PD=PD: e.copy(
                    out=y_sb[:, tt, mcg * 512:(mcg + 1) * 512], in_=PD[:, :]),
                    reads=[pdk], writes=[f"y{tt}"])

        for tt in range(4):
            gt = tb * 4 + tt
            pe = tt % 2
            P.op("sync", lambda e, gt=gt, pe=pe: e.dma_start(
                out=x2e_sb[:, pe, :],
                in_=x2_dram[gt * 128:(gt + 1) * 128, :]),
                writes=[f"x2e{pe}"], dma=True)
            P.op("scalar", lambda e, tt=tt: e.activation(
                out=junk[:, 0:2048], in_=y_sb[:, tt, :], func=AF.Square,
                accum_out=stat[:, 4:5]),
                reads=[f"y{tt}"], writes=["statc", "junk"])
            P.op("scalar", lambda e: e.activation(
                out=stat[:, 4:5], in_=stat[:, 4:5], func=AF.Ln,
                scale=INV_DM, bias=RMS_EPS_FFN),
                reads=["statc", "consts"], writes=["statc"])
            P.op("scalar", lambda e: e.activation(
                out=stat[:, 4:5], in_=stat[:, 4:5], func=AF.Exp, scale=-0.5),
                reads=["statc", "consts"], writes=["statc"])
            P.op("vector", lambda e, tt=tt, pe=pe: e.scalar_tensor_tensor(
                out=out_sb[:, :], in0=y_sb[:, tt, :],
                scalar=stat[:, 4:5], in1=x2e_sb[:, pe, :],
                op0=ALU.mult, op1=ALU.add),
                reads=[f"y{tt}", "statc", f"x2e{pe}"], writes=["outsb"])
            P.op("sync", lambda e, gt=gt: e.dma_start(
                out=y_out[gt * 128:(gt + 1) * 128, :], in_=out_sb[:, :]),
                reads=["outsb"], writes=["yd"], dma=True)

    P.barrier()
    P.flush(nc, sems)
    free_group(gC)
    free_group(gG)

    for cmx in reversed(ps_ctx):
        cmx.__exit__(None, None, None)
    for cmx in sem_ctx:
        cmx.__exit__(None, None, None)
    return nc


def _host_inputs(L0_attn_hidden_states, L0_attn_Wq, L0_attn_Wk, L0_attn_Wi,
                 L0_attn_Wo, L0_attn_g_weight, L0_ffn_gate_up_packed,
                 L0_ffn_gate_up_scales, L0_ffn_down_packed,
                 L0_ffn_down_scales):
    x = np.asarray(L0_attn_hidden_states, dtype=np.float32)
    wq = np.ascontiguousarray(np.asarray(L0_attn_Wq, np.float32))
    wk = np.ascontiguousarray(np.asarray(L0_attn_Wk, np.float32))
    wi = np.ascontiguousarray(np.asarray(L0_attn_Wi, np.float32))
    wo = np.ascontiguousarray(np.asarray(L0_attn_Wo, np.float32))
    gw = np.asarray(L0_attn_g_weight, np.float32)
    gup = np.asarray(L0_ffn_gate_up_packed, np.int32)
    gus = np.asarray(L0_ffn_gate_up_scales, np.float16)
    dnp = np.asarray(L0_ffn_down_packed, np.int32)
    dns = np.asarray(L0_ffn_down_scales, np.float16)

    gwc = np.ascontiguousarray(gw.reshape(NH, 128).T)
    trimask = np.zeros((128, 128), dtype=BF)
    su = np.tril(np.ones((64, 64), np.float32)).T.astype(BF)
    trimask[0:64, 0:64] = su
    trimask[0:64, 64:128] = 1.0
    trimask[64:128, 64:128] = su
    ident = np.eye(128, dtype=BF)
    consts = np.broadcast_to(np.array(CONST_VALS, np.float32),
                             (128, len(CONST_VALS))).copy()

    def tile_slices(gidx):
        if gidx < 1408:
            pair, rem = divmod(gidx, 32)
            gu, hc = divmod(rem, 16)
            r0 = (gu * NPAIR + pair) * 128
            pk = gup[r0:r0 + 128, hc * 16:(hc + 1) * 16]
            sc = gus[hc, r0:r0 + 128]
        else:
            g2 = gidx - 1408
            mcg, rem = divmod(g2, 176)
            ic, mi = divmod(rem, 4)
            mt = mcg * 4 + mi
            pk = dnp[mt * 128:(mt + 1) * 128, ic * 16:(ic + 1) * 16]
            sc = dns[ic, mt * 128:(mt + 1) * 128]
        return pk, sc

    per_core = []
    for c in range(8):
        b, half = c // 2, c % 2
        xs = np.ascontiguousarray(x[b, half * T:(half + 1) * T, :])
        pks = np.empty((TPC, 128, 16), np.int32)
        scs = np.empty((TPC, 128), np.float16)
        for i in range(TPC):
            pk, sc = tile_slices(c * TPC + i)
            pks[i] = pk
            scs[i] = sc
        mask8 = np.zeros((128, 8), np.float32)
        if half == 1:
            mask8[:, c - 1] = 1.0
        per_core.append(dict(
            x=xs, wq=wq, wk=wk, wi=wi, wo=wo,
            gwc=gwc, mask8=mask8, trimask=trimask, ident=ident,
            consts=consts, packedsl=pks,
            scalessl=np.ascontiguousarray(scs.T)))
    return per_core


def kernel(**inputs):
    per_core = _host_inputs(**inputs)
    nc = build_nc()
    res = run_bass_kernel_spmd(nc, per_core, core_ids=list(range(8)))
    out = np.empty((B_, L_, DM), np.float32)
    for c in range(8):
        b, half = c // 2, c % 2
        out[b, half * T:(half + 1) * T, :] = res.results[c]["y"]
    return out


if __name__ == "__main__":
    nc = build_nc()
    print("built ok; instructions:", nc.n_instructions())


# version 19
# speedup vs baseline: 1.1440x; 1.0300x over previous
import numpy as np
import ml_dtypes
import concourse.bass as bass
import concourse.mybir as mybir
from concourse.bass_utils import run_bass_kernel_spmd

F32 = mybir.dt.float32
BF16 = mybir.dt.bfloat16
F16 = mybir.dt.float16
I32 = mybir.dt.int32
U16 = mybir.dt.uint16
AF = mybir.ActivationFunctionType
ALU = mybir.AluOpType
AX = mybir.AxisListType

BF = ml_dtypes.bfloat16

B_, L_, DM = 4, 4096, 2048
NH, DK, DV = 16, 128, 128
SCALE = float(DK) ** -0.5
T = 2048
NBLK = 4
NCH = 16
NTT = 16
FFI = 5632
NPAIR = 44
NIC = 44
NTILES = 2112
TPC = 264
NGRP = NTILES // 8
GPC = TPC // 8

LN_EPS = 1e-5
RMS_EPS_FFN = 1e-6
INV_DM = 1.0 / 2048.0

CONST_VALS = [SCALE, -1.0, 1.0, 2.0, -0.5, INV_DM, LN_EPS, RMS_EPS_FFN]

ENGINES = ("sync", "gpsimd", "tensor", "vector", "scalar")
DMA_QUEUES = ("sync", "gpsimd", "scalar")
NDMASEM = 8
DMA_SEMS = tuple(f"{q}dma{i}" for q in DMA_QUEUES for i in range(NDMASEM))
ALL_SEMS = ENGINES + ("cc",) + DMA_SEMS


class Prog:

    def __init__(self):
        self.items = []
        self.count = {s: 0 for s in ALL_SEMS}
        self.hwm = {e: {s: 0 for s in ALL_SEMS} for e in ENGINES}
        self.last_w = {}
        self.readers = {}
        self.dma_idx = {q: 0 for q in DMA_QUEUES}

    def op(self, eng, fn, reads=(), writes=(), dma=False, deps=(), sem=None):
        dep_evs = []
        for k in reads:
            ev = self.last_w.get(k)
            if ev is not None:
                dep_evs.append(ev)
        for k in writes:
            ev = self.last_w.get(k)
            if ev is not None:
                dep_evs.append(ev)
            dep_evs.extend(self.readers.get(k, ()))
        dep_evs.extend(deps)
        if dma:
            idx = self.dma_idx[eng]
            self.dma_idx[eng] = idx + 1
            slot = idx % NDMASEM
            dmasem = f"{eng}dma{slot}"
            dep_evs.append((dmasem, 16 * (idx // NDMASEM), True))
        wmax = {}
        for (ds, dv, dasync) in dep_evs:
            if ds == eng and eng == "tensor" and not dasync:
                continue
            if self.hwm[eng][ds] >= dv:
                continue
            wmax[ds] = max(wmax.get(ds, 0), dv)
        for ds, dv in wmax.items():
            self.hwm[eng][ds] = dv
        if dma:
            semname = dmasem
        else:
            semname = sem if sem is not None else eng
        inc = 16 if dma else 1
        self.count[semname] += inc
        is_async = dma or (sem is not None)
        ev = (semname, self.count[semname], is_async)
        self.items.append((eng, fn, sorted(wmax.items()), semname, inc))
        for k in reads:
            self.readers.setdefault(k, []).append(ev)
        for k in writes:
            self.last_w[k] = ev
            self.readers[k] = []
        return ev

    def barrier(self):
        snap = {s: self.count[s] for s in ENGINES + DMA_SEMS}
        for e in ENGINES:
            waits = []
            for e2 in ENGINES + DMA_SEMS:
                if e2 == e:
                    continue
                if self.hwm[e][e2] < snap[e2]:
                    waits.append((e2, snap[e2]))
                    self.hwm[e][e2] = snap[e2]
            if waits:
                self.items.append((e, None, waits, e, 0))
        self.last_w = {}
        self.readers = {}

    def flush(self, nc, sems):
        items, self.items = self.items, []
        with nc.Block() as block:
            for eng_name in ENGINES:
                def make(eng_name=eng_name):
                    def body(eng):
                        for (en, fn, waits, semname, inc) in items:
                            if en != eng_name:
                                continue
                            for (se, sv) in waits:
                                eng.wait_ge(sems[se], sv)
                            if fn is not None:
                                fn(eng).then_inc(sems[semname], inc)
                    return body
                getattr(block, eng_name)(make())


def build_nc(debug_outputs=False, phases=("A", "B", "C")):
    nc = bass.Bass("TRN2", target_bir_lowering=False)

    def din(name, shape, dt):
        return nc.dram_tensor(name, shape, dt, kind="ExternalInput").ap()

    x_in = din("x", [T, DM], F32)
    wq_in = din("wq", [DM, DM], F32)
    wk_in = din("wk", [DM, DM], F32)
    wi_in = din("wi", [DM, DM], F32)
    wo_in = din("wo", [DM, DM], F32)
    gwc_in = din("gwc", [128, NH], F32)
    mask8_in = din("mask8", [128, 8], F32)
    trimask_in = din("trimask", [128, 128], BF16)
    ident_in = din("ident", [128, 128], BF16)
    consts_in = din("consts", [128, len(CONST_VALS)], F32)
    packed_in = din("packedsl", [TPC, 128, 16], I32)
    scales_in = din("scalessl", [128, TPC], F16)

    y_out = nc.dram_tensor("y", [T, DM], F32, kind="ExternalOutput").ap()

    dkind = "ExternalOutput" if debug_outputs else "Internal"

    def dscr(name, shape, dt, dbg=False):
        return nc.dram_tensor(name, shape, dt,
                              kind=(dkind if dbg else "Internal")).ap()

    wq16t = dscr("wq16t", [NH, 16, 128, 128], BF16)
    wk16t = dscr("wk16t", [NH, 16, 128, 128], BF16)
    wi16 = dscr("wi16", [DM, DM], BF16)
    wo16 = dscr("wo16", [DM, DM], BF16, dbg=True)
    ln_dram = dscr("ln_dram", [T, DM], BF16, dbg=True)
    ogg_dram = dscr("ogg_dram", [NCH, NH, 128, 128], BF16)
    o_dram = dscr("o_dram", [NCH, NH, 128, 128], BF16, dbg=True)
    x2_dram = dscr("x2_dram", [T, DM], F32, dbg=True)
    dbgf = dscr("dbgf", [128, 4096], F32, dbg=True)
    dbgb = dscr("dbgb", [128, 4096], BF16, dbg=True)
    cc_w_in = nc.dram_tensor("cc_w_in", [GPC * 128, 1024], BF16).ap()
    cc_w_out = nc.dram_tensor("cc_w_out", [NGRP * 128, 1024], BF16,
                              addr_space="Shared").ap()
    cc_s_in = nc.dram_tensor("cc_s_in", [NH * 128, 128], F32).ap()
    cc_s_out = nc.dram_tensor("cc_s_out", [8 * NH * 128, 128], F32,
                              addr_space="Shared").ap()

    P = Prog()

    sems = {}
    sem_ctx = []
    for s in ALL_SEMS:
        cm = nc.semaphore(f"sem_{s}")
        sem_ctx.append(cm)
        sems[s] = cm.__enter__()

    all_ctx = []

    def make_group():
        group = []

        def sb(name, shape, dt):
            cm = nc.sbuf_tensor("sb_" + name, shape, dt)
            group.append(cm)
            all_ctx.append(cm)
            return cm.__enter__()
        return group, sb

    def free_group(group):
        for cm in reversed(group):
            cm.__exit__(None, None, None)
            all_ctx.remove(cm)

    ps_ctx = []

    def psb(name):
        cm = nc.psum_tensor(name, [128, 512], F32)
        ps_ctx.append(cm)
        return cm.__enter__()

    gG, sbG = make_group()
    consts = sbG("consts", [128, len(CONST_VALS)], F32)
    gwc = sbG("gwc", [128, NH], F32)
    mask8 = sbG("mask8", [128, 8], F32)
    trimask = sbG("trimask", [128, 128], BF16)
    ident = sbG("ident", [128, 128], BF16)
    junk = sbG("junk", [128, 2048], BF16)
    junkf = sbG("junkf", [128, 512], F32)
    xln_tok = sbG("xln_tok", [128, 2, 2048], BF16)
    stat = sbG("stat", [128, 16], F32)

    for i, v in enumerate(CONST_VALS):
        nc.const_aps.register(F32, v, consts[:, i:i + 1])

    P0 = psb("P0")
    P1 = psb("P1")
    P2 = psb("P2")
    P3 = psb("P3")
    P4 = psb("P4")
    P5 = psb("P5")
    P6 = psb("P6")
    cm = nc.psum_tensor("P7b", [128, 1024], BF16)
    ps_ctx.append(cm)
    P7b = cm.__enter__()

    P.op("sync", lambda e: e.dma_start(out=consts[:], in_=consts_in),
         writes=["consts"], dma=True)
    P.op("sync", lambda e: e.dma_start(out=gwc[:], in_=gwc_in),
         writes=["gwc"], dma=True)
    P.op("sync", lambda e: e.dma_start(out=mask8[:], in_=mask8_in),
         writes=["mask8"], dma=True)
    P.op("sync", lambda e: e.dma_start(out=trimask[:], in_=trimask_in),
         writes=["trimask"], dma=True)
    P.op("sync", lambda e: e.dma_start(out=ident[:], in_=ident_in),
         writes=["ident"], dma=True)
    P.op("sync", lambda e: e.dma_start(out=scl16[:], in_=scales_in),
         writes=["scl16"], dma=True)
    P.op("vector", lambda e: e.tensor_copy(out=scl[:], in_=scl16[:]),
         reads=["scl16"], writes=["scl"])
    P.op("vector", lambda e: e.memset(S_sb[:], 0.0), writes=["S"])
    P.op("vector", lambda e: e.memset(S_bf[:], 0.0), writes=["Sbf"])
    P.op("vector", lambda e: e.memset(R_run[:], 1.0), writes=["Rrun"])

    wi_r = wi_in.rearrange("(hc p) (dc f) -> dc hc p f", p=128, f=512)
    wi16_r = wi16.rearrange("(hc p) (dc f) -> dc hc p f", p=128, f=512)
    wq_r = wq_in.rearrange("(hc p) (mc f) -> mc hc p f", p=128, f=128)
    wk_r = wk_in.rearrange("(hc p) (mc f) -> mc hc p f", p=128, f=128)

    def cast_wi(dvc):
        P.op("gpsimd", lambda e, dvc=dvc: e.dma_start(
            out=wi16_r[dvc], in_=wi_r[dvc]),
            writes=[f"wi16d{dvc}"], dma=True)

    for dvc in range(4):
        cast_wi(dvc)

    gA, sbA = make_group()
    scl16 = sbA("scl16", [128, TPC], F16)
    scl = sbA("scl", [128, TPC], F32)
    S_sb = sbA("S_sb", [128, NH, 128], F32)
    S_bf = sbA("S_bf", [128, NH, 128], BF16)
    R_run = sbA("R_run", [128, NH], F32)
    xt = sbA("xt", [128, 2, 2048], F32)
    xln_feat = sbA("xln_feat", [128, 16, 512], BF16)
    v_tok = sbA("v_tok", [128, 4, 2048], BF16)
    wi_sb = sbA("wi_sb", [128, 16, 512], BF16)
    wq_sb = sbA("wq_sb", [128, 2, 16, 128], BF16)
    wk_sb = sbA("wk_sb", [128, 2, 16, 128], BF16)
    q_h = sbA("q_h", [128, 3, 512], BF16)
    kraw_h = sbA("kraw_h", [128, 3, 512], F32)
    sp_h = sbA("sp_h", [128, 3, 512], F32)
    gh_h = sbA("gh_h", [128, 3, 512], F32)
    bb_h = sbA("bb_h", [128, 3, 512], F32)
    eg_h = sbA("eg_h", [128, 3, 512], F32)
    qg_h = sbA("qg_h", [128, 3, 512], BF16)
    qb_h = sbA("qb_h", [128, 3, 4, 64], BF16)
    ka_h = sbA("ka_h", [128, 3, 4, 64], BF16)
    kb_h = sbA("kb_h", [128, 3, 4, 64], BF16)
    kba_h = sbA("kba_h", [128, 3, 4, 64], BF16)
    kh_h = sbA("kh_h", [128, 3, 512], BF16)
    kh_tok = sbA("kh_tok", [128, 3, 2, 128], BF16)
    ng_h = sbA("ng_h", [128, 3, 8], F32)
    asb = sbA("asb", [128, 2, 128], BF16)
    zchunk = sbA("zchunk", [128, 128], F32)
    qgg_slab = sbA("qgg_slab", [128, 4, 16, 128], BF16)
    o_slab = sbA("o_slab", [128, 4, 16, 128], BF16)
    dq_pk = sbA("dq_pk", [128, 2, 4, 16], I32)
    dq_nib = sbA("dq_nib", [128, 128, 4], U16)
    dq_bits = sbA("dq_bits", [128, 512], U16)
    dq_sgn = sbA("dq_sgn", [128, 512], U16)
    dq_sgnf = sbA("dq_sgnf", [128, 512], BF16)
    dq_u16 = sbA("dq_u16", [128, 512], U16)
    dq_relu = sbA("dq_relu", [128, 512], BF16)
    dq_min = sbA("dq_min", [128, 512], BF16)
    dq_out = sbA("dq_out", [128, 2, 512], BF16)
    dq_tr = sbA("dq_tr", [128, 2, 128], BF16)

    P.op("vector", lambda e: e.memset(zchunk[:], 0.0), writes=["zchunk"])
    P.op("vector", lambda e: e.memset(P2[64:128, 0:64], 0.0), writes=["PA0"])
    P.op("vector", lambda e: e.memset(P3[64:128, 0:64], 0.0), writes=["PA1"])

    def emit_dequant_dve(bi):
        pk_par = bi % 2
        op_par = bi % 2
        lt0 = 4 * bi
        pk = dq_pk[:, pk_par, :, :]
        P.op("gpsimd",
             lambda e, pk=pk, lt0=lt0: e.dma_start(
                 out=pk, in_=packed_in[lt0:lt0 + 4].rearrange(
                     "t p w -> p t w")),
             writes=[f"dqpk{pk_par}"], dma=True)
        pku = pk.rearrange("p t w -> p (t w)").bitcast(U16)
        for k in range(4):
            P.op("vector",
                 lambda e, k=k, pku=pku: e.tensor_scalar(
                     out=dq_nib[:, :, k], in0=pku, scalar1=4 * k, scalar2=15,
                     op0=ALU.logical_shift_right, op1=ALU.bitwise_and),
                 reads=[f"dqpk{pk_par}"], writes=["dqnib"])
        nib3 = dq_nib[:, :, :]
        P.op("vector", lambda e, nib3=nib3: e.tensor_scalar(
            out=dq_bits[:], in0=nib3, scalar1=7, scalar2=6,
            op0=ALU.bitwise_and, op1=ALU.logical_shift_left),
            reads=["dqnib"], writes=["dqbits"])
        P.op("vector", lambda e: e.tensor_scalar(
            out=dq_u16[:], in0=dq_bits[:], scalar1=0x3F00,
            scalar2=None, op0=ALU.add), reads=["dqbits"], writes=["dqu16"])
        P.op("vector", lambda e, nib3=nib3: e.tensor_scalar(
            out=dq_sgn[:], in0=nib3, scalar1=3, scalar2=1,
            op0=ALU.logical_shift_right, op1=ALU.bitwise_and),
            reads=["dqnib"], writes=["dqsgn"])
        P.op("vector", lambda e: e.tensor_scalar(
            out=dq_sgnf[:], in0=dq_sgn[:], scalar1=-2,
            scalar2=1, op0=ALU.mult, op1=ALU.add),
            reads=["dqsgn"], writes=["dqsgnf"])
        fmv = dq_u16[:].bitcast(BF16)
        P.op("scalar", lambda e, fmv=fmv: e.activation(
            out=dq_relu[:], in_=fmv, func=AF.Relu, bias=-1.0,
            scale=2.0),
            reads=["dqu16", "consts"], writes=["dqrelu"])
        P.op("vector", lambda e, fmv=fmv: e.tensor_tensor(
            out=dq_min[:], in0=dq_relu[:], in1=fmv,
            op=ALU.min),
            reads=["dqrelu", "dqu16"], writes=["dqmin"])
        for j in range(4):
            lt = lt0 + j
            P.op("vector",
                 lambda e, j=j, lt=lt, op_par=op_par: e.scalar_tensor_tensor(
                     out=dq_out[:, op_par, j * 128:(j + 1) * 128],
                     in0=dq_min[:, j * 128:(j + 1) * 128],
                     scalar=scl[:, lt:lt + 1],
                     in1=dq_sgnf[:, j * 128:(j + 1) * 128],
                     op0=ALU.mult, op1=ALU.mult),
                 reads=["dqmin", "dqsgnf", "scl"], writes=[f"dqout{op_par}"])

    def emit_dequant_tr(bi):
        op_par = bi % 2
        lt0 = 4 * bi
        for j in range(4):
            lt = lt0 + j
            tp = j % 2
            P.op("tensor", lambda e, j=j, op_par=op_par: e.transpose(
                P7b[:, 0:128], dq_out[:, op_par, j * 128:(j + 1) * 128],
                ident[:]),
                reads=[f"dqout{op_par}", "ident"], writes=["P7b"])
            P.op("scalar", lambda e, tp=tp: e.copy(
                out=dq_tr[:, tp, :], in_=P7b[:, 0:128]),
                reads=["P7b"], writes=[f"dqtr{tp}"])
            g, cb = lt // 8, lt % 8
            P.op("gpsimd", lambda e, g=g, cb=cb, tp=tp: e.dma_start(
                out=cc_w_in[g * 128:(g + 1) * 128, cb * 128:(cb + 1) * 128],
                in_=dq_tr[:, tp, :]),
                reads=[f"dqtr{tp}"], dma=True)

    dq_iter = iter(range(TPC // 4))

    dq_pending = []

    def emit_some_dequant(n):
        for _ in range(n):
            bi = next(dq_iter, None)
            if bi is None:
                while dq_pending:
                    emit_dequant_tr(dq_pending.pop(0))
                return
            emit_dequant_dve(bi)
            dq_pending.append(bi)
            if len(dq_pending) > 1:
                emit_dequant_tr(dq_pending.pop(0))

    def emit_ln(xin_ap, xkey, out_ap, okey, c0):
        m_c = stat[:, c0:c0 + 1]
        v_c = stat[:, c0 + 1:c0 + 2]
        km, kv = f"stat{c0}", f"stat{c0 + 1}"
        P.op("vector", lambda e: e.reduce_sum(m_c, xin_ap, axis=AX.X),
             reads=[xkey], writes=[km])
        P.op("scalar", lambda e: e.activation(
            out=junk[:, 0:2048], in_=xin_ap, func=AF.Square, accum_out=v_c),
            reads=[xkey], writes=[kv, "junk"])
        P.op("vector", lambda e: e.tensor_scalar(
            out=m_c, in0=m_c, scalar1=INV_DM, scalar2=None, op0=ALU.mult),
            reads=[km], writes=[km])
        P.op("vector", lambda e: e.tensor_scalar(
            out=v_c, in0=v_c, scalar1=INV_DM, scalar2=None, op0=ALU.mult),
            reads=[kv], writes=[kv])
        P.op("vector", lambda e: e.scalar_tensor_tensor(
            out=v_c, in0=m_c, scalar=m_c, in1=v_c,
            op0=ALU.mult, op1=ALU.subtract),
            reads=[km, kv], writes=[kv])
        P.op("vector", lambda e: e.tensor_scalar(
            out=v_c, in0=v_c, scalar1=-1.0, scalar2=None, op0=ALU.mult),
            reads=[kv], writes=[kv])
        P.op("scalar", lambda e: e.activation(
            out=v_c, in_=v_c, func=AF.Ln, bias=LN_EPS),
            reads=[kv, "consts"], writes=[kv])
        P.op("scalar", lambda e: e.activation(
            out=v_c, in_=v_c, func=AF.Exp, scale=-0.5),
            reads=[kv, "consts"], writes=[kv])
        P.op("vector", lambda e: e.tensor_scalar(
            out=out_ap, in0=xin_ap, scalar1=m_c, scalar2=v_c,
            op0=ALU.subtract, op1=ALU.mult),
            reads=[xkey, km, kv], writes=[okey])

    ccw_ev = None
    for tb in range(NBLK):
        if tb == 3:
            P.op("gpsimd", lambda e: e.dma_start(out=wo16, in_=wo_in),
                 writes=["wo16d"], dma=True)
        for tt in range(4):
            gt = tb * 4 + tt
            par = gt % 2
            P.op("sync", lambda e, gt=gt, par=par: e.dma_start(
                out=xt[:, par, :], in_=x_in[gt * 128:(gt + 1) * 128, :]),
                writes=[f"xt{par}"], dma=True)
            emit_ln(xt[:, par, :], f"xt{par}", xln_tok[:, par, :],
                    f"xlntok{par}", 0)
            P.op("sync", lambda e, gt=gt, par=par: e.dma_start(
                out=ln_dram[gt * 128:(gt + 1) * 128, :],
                in_=xln_tok[:, par, :]),
                reads=[f"xlntok{par}"], writes=[f"lnd{gt}"], dma=True)
        for hc in range(16):
            P.op("sync", lambda e, hc=hc, tb=tb: e.dma_start(
                out=xln_feat[:, hc, :],
                in_=ln_dram[tb * 512:(tb + 1) * 512,
                            hc * 128:(hc + 1) * 128],
                transpose=True),
                reads=[f"lnd{tb * 4 + i}" for i in range(4)],
                writes=["xlf"], dma=True)

        for dvc in range(4):
            P.op("gpsimd", lambda e, dvc=dvc: e.dma_start(
                out=wi_sb[:, :, :],
                in_=wi16_r[dvc].rearrange("hc p f -> p hc f")),
                reads=[f"wi16d{dvc}"], writes=["wi_sb"], dma=True)
            for c in range(4):
                pp = [P0, P1][c % 2]
                pkey = f"P{c % 2}"
                for hc in range(16):
                    P.op("tensor", lambda e, pp=pp, hc=hc, c=c:
                         e.matmul(pp[:, :],
                                  xln_feat[:, hc, c * 128:(c + 1) * 128],
                                  wi_sb[:, hc, :],
                                  start=(hc == 0), stop=(hc == 15)),
                         reads=["xlf", "wi_sb"], writes=[pkey])
                P.op("scalar", lambda e, pp=pp, c=c, dvc=dvc: e.copy(
                    out=v_tok[:, c, dvc * 512:(dvc + 1) * 512], in_=pp[:, :]),
                    reads=[pkey], writes=[f"vtok{c}"])

        def emit_wload(h):
            hp = h % 2
            if tb == 0:
                P.op("gpsimd", lambda e, h=h, hp=hp: e.dma_start(
                    out=wq_sb[:, hp, :, :],
                    in_=wq_r[h].rearrange("hc p f -> p hc f")),
                    writes=[f"wq_sb{hp}"], dma=True)
                P.op("sync", lambda e, h=h, hp=hp: e.dma_start(
                    out=wq16t[h].rearrange("hc p f -> p hc f"),
                    in_=wq_sb[:, hp, :, :]),
                    reads=[f"wq_sb{hp}"], writes=[f"wq16d{h}"], dma=True)
                P.op("gpsimd", lambda e, h=h, hp=hp: e.dma_start(
                    out=wk_sb[:, hp, :, :],
                    in_=wk_r[h].rearrange("hc p f -> p hc f")),
                    writes=[f"wk_sb{hp}"], dma=True)
                P.op("sync", lambda e, h=h, hp=hp: e.dma_start(
                    out=wk16t[h].rearrange("hc p f -> p hc f"),
                    in_=wk_sb[:, hp, :, :]),
                    reads=[f"wk_sb{hp}"], writes=[f"wk16d{h}"], dma=True)
            else:
                P.op("sync", lambda e, h=h, hp=hp: e.dma_start(
                    out=wq_sb[:, hp, :, :],
                    in_=wq16t[h].rearrange("hc p f -> p hc f")),
                    reads=[f"wq16d{h}"], writes=[f"wq_sb{hp}"], dma=True)
                P.op("sync", lambda e, h=h, hp=hp: e.dma_start(
                    out=wk_sb[:, hp, :, :],
                    in_=wk16t[h].rearrange("hc p f -> p hc f")),
                    reads=[f"wk16d{h}"], writes=[f"wk_sb{hp}"], dma=True)

        def emit_projprep(h):
            hp = h % 3
            wp2 = h % 2
            for hc in range(16):
                P.op("tensor", lambda e, hc=hc, wp2=wp2: e.matmul(
                    P0[:, :], wq_sb[:, wp2, hc, :], xln_feat[:, hc, :],
                    start=(hc == 0), stop=(hc == 15)),
                    reads=[f"wq_sb{wp2}", "xlf"], writes=["P0"])
            P.op("scalar", lambda e, hp=hp: e.activation(
                out=q_h[:, hp, :], in_=P0[:, :], func=AF.Relu, scale=SCALE),
                reads=["P0", "consts"], writes=[f"qh{hp}"])
            for hc in range(16):
                P.op("tensor", lambda e, hc=hc, wp2=wp2: e.matmul(
                    P1[:, :], wk_sb[:, wp2, hc, :], xln_feat[:, hc, :],
                    start=(hc == 0), stop=(hc == 15)),
                    reads=[f"wk_sb{wp2}", "xlf"], writes=["P1"])
            kr = kraw_h[:, hp, :]
            spv = sp_h[:, hp, :]
            ghv = gh_h[:, hp, :]
            bbv = bb_h[:, hp, :]
            egv = eg_h[:, hp, :]
            P.op("scalar", lambda e, spv=spv: e.activation(
                out=spv, in_=P1[:, :], func=AF.Abs),
                reads=["P1"], writes=[f"sp{hp}"])
            P.op("scalar", lambda e, hp=hp: e.copy(
                out=kraw_h[:, hp, :], in_=P1[:, :]),
                reads=["P1"], writes=[f"krawh{hp}"])
            P.op("scalar", lambda e, spv=spv: e.activation(
                out=spv, in_=spv, func=AF.Exp, scale=-1.0),
                reads=[f"sp{hp}", "consts"], writes=[f"sp{hp}"])
            P.op("scalar", lambda e, spv=spv: e.activation(
                out=spv, in_=spv, func=AF.Ln, bias=1.0),
                reads=[f"sp{hp}", "consts"], writes=[f"sp{hp}"])
            P.op("vector", lambda e, kr=kr, bbv=bbv: e.tensor_scalar(
                out=bbv, in0=kr, scalar1=0.0, scalar2=None, op0=ALU.max),
                reads=[f"krawh{hp}"], writes=[f"bb{hp}"])
            P.op("vector", lambda e, spv=spv, bbv=bbv: e.tensor_add(
                out=spv, in0=spv, in1=bbv),
                reads=[f"sp{hp}", f"bb{hp}"], writes=[f"sp{hp}"])
            for c in range(4):
                P.op("vector", lambda e, c=c, spv=spv, ghv=ghv:
                     e.tensor_tensor_scan(
                         out=ghv[:, c * 128:(c + 1) * 128],
                         data0=spv[:, c * 128:(c + 1) * 128],
                         data1=zchunk[:, :],
                         initial=0.0, op0=ALU.add, op1=ALU.add),
                     reads=[f"sp{hp}", "zchunk"], writes=[f"gh{hp}"])
            P.op("vector", lambda e, kr=kr, spv=spv, bbv=bbv: e.tensor_sub(
                out=bbv, in0=kr, in1=spv),
                reads=[f"krawh{hp}", f"sp{hp}"], writes=[f"bb{hp}"])
            P.op("vector", lambda e, bbv=bbv, ghv=ghv: e.tensor_add(
                out=bbv, in0=bbv, in1=ghv),
                reads=[f"bb{hp}", f"gh{hp}"], writes=[f"bb{hp}"])
            P.op("scalar", lambda e, ghv=ghv, egv=egv: e.activation(
                out=egv, in_=ghv, func=AF.Exp, scale=-1.0),
                reads=[f"gh{hp}", "consts"], writes=[f"eg{hp}"])
            P.op("vector", lambda e, hp=hp, egv=egv: e.tensor_mul(
                out=qg_h[:, hp, :], in0=q_h[:, hp, :], in1=egv),
                reads=[f"qh{hp}", f"eg{hp}"], writes=[f"qg{hp}"])
            P.op("vector", lambda e, hp=hp, ghv=ghv: e.tensor_scalar(
                out=ng_h[:, hp, 0:8:2],
                in0=ghv[:, 63:512:128],
                scalar1=-1.0, scalar2=None, op0=ALU.mult),
                reads=[f"gh{hp}"], writes=[f"ng{hp}"])
            P.op("vector", lambda e, hp=hp, ghv=ghv: e.tensor_scalar(
                out=ng_h[:, hp, 1:8:2],
                in0=ghv[:, 127:512:128],
                scalar1=-1.0, scalar2=None, op0=ALU.mult),
                reads=[f"gh{hp}"], writes=[f"ng{hp}"])
            for c in range(4):
                P.op("scalar", lambda e, c=c, hp=hp, bbv=bbv: e.activation(
                    out=ka_h[:, hp, c, :],
                    in_=bbv[:, c * 128:c * 128 + 64],
                    func=AF.Exp),
                    reads=[f"bb{hp}"], writes=[f"ka{hp}"])
            for c in range(4):
                ng63 = ng_h[:, hp, 2 * c:2 * c + 1]
                ng127 = ng_h[:, hp, 2 * c + 1:2 * c + 2]
                g63 = gh_h[:, hp, c * 128 + 63:c * 128 + 64]
                P.op("scalar", lambda e, c=c, hp=hp, bbv=bbv, ng63=ng63:
                     e.activation(
                         out=kb_h[:, hp, c, :],
                         in_=bbv[:, c * 128 + 64:(c + 1) * 128],
                         func=AF.Exp, bias=ng63),
                     reads=[f"bb{hp}", f"ng{hp}"], writes=[f"kb{hp}"])
                P.op("scalar", lambda e, c=c, hp=hp, bbv=bbv, ng63=ng63:
                     e.activation(
                         out=kba_h[:, hp, c, :],
                         in_=bbv[:, c * 128:c * 128 + 64],
                         func=AF.Exp, bias=ng63),
                     reads=[f"bb{hp}", f"ng{hp}"], writes=[f"kba{hp}"])
                P.op("scalar", lambda e, c=c, hp=hp, bbv=bbv, ng127=ng127:
                     e.activation(
                         out=kh_h[:, hp, c * 128:(c + 1) * 128],
                         in_=bbv[:, c * 128:(c + 1) * 128],
                         func=AF.Exp, bias=ng127),
                     reads=[f"bb{hp}", f"ng{hp}"], writes=[f"kh{hp}"])
                P.op("scalar", lambda e, c=c, hp=hp, ghv=ghv, g63=g63:
                     e.activation(
                         out=junkf[:, c * 64:(c + 1) * 64],
                         in_=ghv[:, c * 128 + 64:(c + 1) * 128],
                         func=AF.Exp, scale=-1.0, bias=g63),
                     reads=[f"gh{hp}", "consts"], writes=["jf"])
                P.op("vector", lambda e, c=c, hp=hp: e.tensor_mul(
                    out=qb_h[:, hp, c, :],
                    in0=q_h[:, hp, c * 128 + 64:(c + 1) * 128],
                    in1=junkf[:, c * 64:(c + 1) * 64]),
                    reads=[f"qh{hp}", "jf"], writes=[f"qb{hp}"])
            if tb == 0:
                for c in range(4):
                    P.op("vector", lambda e, c=c, hp=hp, h=h: e.tensor_scalar(
                        out=qgg_slab[:, c, h, :],
                        in0=qg_h[:, hp, c * 128:(c + 1) * 128],
                        scalar1=R_run[:, h:h + 1], scalar2=None, op0=ALU.mult),
                        reads=[f"qg{hp}", "Rrun"], writes=[f"qgg{c}"])
                    P.op("vector", lambda e, c=c, hp=hp, h=h, egv=egv:
                         e.tensor_scalar(
                             out=R_run[:, h:h + 1], in0=R_run[:, h:h + 1],
                             scalar1=egv[:, c * 128 + 127:c * 128 + 128],
                             scalar2=None, op0=ALU.mult),
                         reads=["Rrun", f"eg{hp}"], writes=["Rrun"])

            if debug_outputs and tb == 0 and h == 0:
                for i, (buf, keyr) in enumerate([
                        (kraw_h[:, hp, :], f"krawh{hp}"),
                        (sp_h[:, hp, :], f"sp{hp}"),
                        (gh_h[:, hp, :], f"gh{hp}"),
                        (bb_h[:, hp, :], f"bb{hp}"),
                        (eg_h[:, hp, :], f"eg{hp}")]):
                    P.op("sync", lambda e, buf=buf, i=i: e.dma_start(
                        out=dbgf[:, i * 512:(i + 1) * 512], in_=buf),
                        reads=[keyr], writes=["dbgf"], dma=True)
                col = 0
                for (buf, keyr, w) in [
                        (q_h[:, hp, :], f"qh{hp}", 512),
                        (qg_h[:, hp, :], f"qg{hp}", 512),
                        (kh_h[:, hp, :], f"kh{hp}", 512),
                        (v_tok[:, 0, 0 * 128:4 * 128], "vtok0", 512),
                        (ka_h[:, hp, :, :].rearrange("p c z -> p (c z)"),
                         f"ka{hp}", 256),
                        (kb_h[:, hp, :, :].rearrange("p c z -> p (c z)"),
                         f"kb{hp}", 256)]:
                    P.op("sync", lambda e, buf=buf, col=col, w=w: e.dma_start(
                        out=dbgb[:, col:col + w], in_=buf),
                        reads=[keyr], writes=["dbgb"], dma=True)
                    col += w

        def emit_apart(h, c):
            hp = h % 3
            cp = c % 2
            PA = [P2, P3][cp]
            pak = f"PA{cp}"
            P.op("tensor", lambda e, c=c, hp=hp, cp=cp: e.transpose(
                P7b[:, 0:128], kh_h[:, hp, c * 128:(c + 1) * 128],
                ident[:]),
                reads=[f"kh{hp}", "ident"], writes=["P7b"])
            P.op("vector", lambda e, hp=hp, cp=cp: e.tensor_copy(
                out=kh_tok[:, hp, cp, :], in_=P7b[:, 0:128]),
                reads=["P7b"], writes=[f"khtok{hp}{cp}"])
            P.op("tensor", lambda e, c=c, hp=hp, PA=PA: e.matmul(
                PA[0:64, 0:64], ka_h[:, hp, c, :],
                qg_h[:, hp, c * 128:c * 128 + 64],
                start=True, stop=True),
                reads=[f"ka{hp}", f"qg{hp}"], writes=[pak])
            P.op("tensor", lambda e, c=c, hp=hp, PA=PA: e.matmul(
                PA[0:64, 64:128], kba_h[:, hp, c, :], qb_h[:, hp, c, :],
                start=True, stop=True),
                reads=[f"kba{hp}", f"qb{hp}"], writes=[pak])
            P.op("tensor", lambda e, c=c, hp=hp, PA=PA: e.matmul(
                PA[64:128, 64:128], kb_h[:, hp, c, :], qb_h[:, hp, c, :],
                start=True, stop=True),
                reads=[f"kb{hp}", f"qb{hp}"], writes=[pak])
            P.op("vector", lambda e, cp=cp, PA=PA: e.tensor_tensor(
                out=asb[:, cp, :], in0=PA[:, 0:128],
                in1=trimask[:], op=ALU.mult),
                reads=[pak, "trimask"], writes=[f"asb{cp}"])

        def emit_opart(h, c):
            hp = h % 3
            cp = c % 2
            PO = [P4, P5][cp]
            pok = f"PO{cp}"
            egv = eg_h[:, hp, :]
            P.op("tensor", lambda e, c=c, h=h, cp=cp, PO=PO: e.matmul(
                PO[:, 0:64], v_tok[:, c, h * 128:(h + 1) * 128],
                asb[:, cp, 0:64], start=True, stop=False),
                reads=[f"vtok{c}", f"asb{cp}"], writes=[pok])
            P.op("tensor", lambda e, c=c, h=h, hp=hp, PO=PO: e.matmul(
                PO[:, 0:64], S_bf[:, h, :],
                qg_h[:, hp, c * 128:c * 128 + 64],
                start=False, stop=True),
                reads=["Sbf", f"qg{hp}"], writes=[pok])
            P.op("tensor", lambda e, c=c, h=h, cp=cp, PO=PO: e.matmul(
                PO[:, 64:128], v_tok[:, c, h * 128:(h + 1) * 128],
                asb[:, cp, 64:128], start=True, stop=False),
                reads=[f"vtok{c}", f"asb{cp}"], writes=[pok])
            P.op("tensor", lambda e, c=c, h=h, hp=hp, PO=PO: e.matmul(
                PO[:, 64:128], S_bf[:, h, :],
                qg_h[:, hp, c * 128 + 64:(c + 1) * 128],
                start=False, stop=True),
                reads=["Sbf", f"qg{hp}"], writes=[pok])
            P.op("tensor", lambda e, c=c, h=h, hp=hp, cp=cp: e.matmul(
                P6[:, 0:128], kh_tok[:, hp, cp, :],
                v_tok[:, c, h * 128:(h + 1) * 128],
                start=True, stop=True),
                reads=[f"khtok{hp}{cp}", f"vtok{c}"], writes=["P6"])
            P.op("scalar", lambda e, c=c, h=h, PO=PO: e.copy(
                out=o_slab[:, c, h, :], in_=PO[:, 0:128]),
                reads=[pok], writes=[f"oslab{c}"])
            P.op("vector", lambda e, c=c, h=h, hp=hp, egv=egv:
                 e.scalar_tensor_tensor(
                     out=S_sb[:, h, :], in0=S_sb[:, h, :],
                     scalar=egv[:, c * 128 + 127:c * 128 + 128],
                     in1=P6[:, 0:128], op0=ALU.mult, op1=ALU.add),
                 reads=["S", f"eg{hp}", "P6"], writes=["S"])
            P.op("scalar", lambda e, h=h: e.copy(
                out=S_bf[:, h, :], in_=S_sb[:, h, :]),
                reads=["S"], writes=["Sbf"])

        def emit_scans(h):
            emit_apart(h, 0)
            for c in range(4):
                if c < 3:
                    emit_apart(h, c + 1)
                emit_opart(h, c)

        emit_wload(0)
        for h in range(NH):
            if h > 1:
                emit_scans(h - 2)
            if h + 1 < NH:
                emit_wload(h + 1)
            emit_projprep(h)
            if tb < 2:
                emit_some_dequant(3 if h % 8 == 0 else 2)
        emit_scans(NH - 2)
        emit_scans(NH - 1)

        for c in range(4):
            gc = tb * 4 + c
            P.op("sync", lambda e, gc=gc, c=c: e.dma_start(
                out=o_dram[gc].rearrange("h p f -> p h f"),
                in_=o_slab[:, c, :, :]),
                reads=[f"oslab{c}"], writes=[f"od{gc}"], dma=True)
            if tb == 0:
                P.op("sync", lambda e, gc=gc, c=c: e.dma_start(
                    out=ogg_dram[gc].rearrange("h p f -> p h f"),
                    in_=qgg_slab[:, c, :, :]),
                    reads=[f"qgg{c}"], writes=[f"oggd{gc}"], dma=True)

        if tb == 1:
            emit_some_dequant(999)
            ccw_ev = P.op(
                "gpsimd", lambda e: e.collective_compute(
                    "AllGather", ALU.bypass,
                    replica_groups=[list(range(8))],
                    ins=[cc_w_in.opt()], outs=[cc_w_out.opt()]),
                deps=[(f"gpsimddma{i}", P.count[f"gpsimddma{i}"], True)
                      for i in range(NDMASEM)],
                sem="cc")

    P.op("sync", lambda e: e.dma_start(
        out=cc_s_in.rearrange("(h p) f -> p h f", p=128), in_=S_sb[:, :, :]),
        reads=["S"], writes=["ccsin"], dma=True)
    ccs_ev = P.op("gpsimd", lambda e: e.collective_compute(
        "AllGather", ALU.bypass,
        replica_groups=[list(range(8))],
        ins=[cc_s_in.opt()], outs=[cc_s_out.opt()]),
        reads=["ccsin"], sem="cc")

    P.barrier()
    P.flush(nc, sems)
    free_group(gA)

    if "B" not in phases:
        for cmx in reversed(ps_ctx):
            cmx.__exit__(None, None, None)
        for cmx in sem_ctx:
            cmx.__exit__(None, None, None)
        return nc

    gB, sbB = make_group()
    wo_sb = sbB("wo_sb", [128, 16, 2048], BF16)
    sgat = sbB("sgat", [128, NH, 128], F32)
    sini = sbB("sini", [128, NH, 128], F32)
    sini_bf = sbB("sini_bf", [128, NH, 128], BF16)
    off_sb = sbB("off_sb", [128, 2, 16, 128], BF16)
    qgg_sb = sbB("qgg_sb", [128, 2, 16, 128], BF16)
    ofull = sbB("ofull", [128, 16, 128], BF16)
    ogw = sbB("ogw", [128, 16, 128], BF16)
    sqsl = sbB("sqsl", [128, 16, 128], BF16)
    sqpart = sbB("sqpart", [128, 128], F32)
    sqpartb = sbB("sqpartb", [128, 128], BF16)
    rstd_sb = sbB("rstd_sb", [128, 2], F32)
    xb_sb = sbB("xb_sb", [128, 2, 2048], F32)
    x2_sb = sbB("x2_sb", [128, 2, 2048], F32)

    P.op("gpsimd", lambda e: e.dma_start(
        out=wo_sb[:, :, :],
        in_=wo16.rearrange("(hc p) f -> p hc f", p=128)),
        reads=["wo16d"], writes=["wo_sb"], dma=True)

    def emit_sini():
        P.op("vector", lambda e: e.memset(sini[:, :, :], 0.0),
             writes=["sini"])
        for r in range(8):
            P.op("sync", lambda e, r=r: e.dma_start(
                out=sgat[:, :, :],
                in_=cc_s_out[r * 2048:(r + 1) * 2048, :]
                .rearrange("(h p) f -> p h f", p=128)),
                deps=[ccs_ev], writes=["sgat"], dma=True)
            P.op("vector", lambda e, r=r: e.scalar_tensor_tensor(
                out=sini[:, :, :], in0=sgat[:, :, :],
                scalar=mask8[:, r:r + 1],
                in1=sini[:, :, :], op0=ALU.mult, op1=ALU.add),
                reads=["sgat", "mask8", "sini"], writes=["sini"])
        P.op("vector", lambda e: e.tensor_copy(out=sini_bf[:, :, :],
                                               in_=sini[:, :, :]),
             reads=["sini"], writes=["sinibf"])

    for gt in list(range(4, NTT)) + list(range(4)):
        par = gt % 2
        if gt == 0:
            emit_sini()
        P.op("sync", lambda e, gt=gt, par=par: e.dma_start(
            out=off_sb[:, par, :, :],
            in_=o_dram[gt].rearrange("h p f -> p h f")),
            writes=[f"off{par}"], dma=True)
        if gt < 4:
            P.op("sync", lambda e, gt=gt, par=par: e.dma_start(
                out=qgg_sb[:, par, :, :],
                in_=ogg_dram[gt].rearrange("h p f -> p h f")),
                writes=[f"qggsb{par}"], dma=True)
        P.op("sync", lambda e, gt=gt, par=par: e.dma_start(
            out=xb_sb[:, par, :], in_=x_in[gt * 128:(gt + 1) * 128, :]),
            writes=[f"xb{par}"], dma=True)
        if gt < 4:
            for h in range(NH):
                pc = h % 2
                PC = [P4, P5][pc]
                pck = f"PB{pc}"
                P.op("tensor", lambda e, h=h, par=par, PC=PC: e.matmul(
                    PC[:, 0:128], sini_bf[:, h, :], qgg_sb[:, par, h, :],
                    start=True, stop=True),
                    reads=["sinibf", f"qggsb{par}"], writes=[pck])
                P.op("vector", lambda e, h=h, par=par, PC=PC: e.tensor_add(
                    out=ofull[:, h, :], in0=off_sb[:, par, h, :],
                    in1=PC[:, 0:128]),
                    reads=[f"off{par}", pck], writes=["ofull"])
                P.op("vector", lambda e, h=h: e.tensor_scalar(
                    out=ogw[:, h, :], in0=ofull[:, h, :],
                    scalar1=gwc[:, h:h + 1], scalar2=None, op0=ALU.mult),
                    reads=["ofull", "gwc"], writes=["ogw"])
            P.op("scalar", lambda e: e.activation(
                out=sqsl[:, :, :], in_=ofull[:, :, :], func=AF.Square),
                reads=["ofull"], writes=["sqsl"])
        else:
            for h in range(NH):
                P.op("vector", lambda e, h=h, par=par: e.tensor_scalar(
                    out=ogw[:, h, :], in0=off_sb[:, par, h, :],
                    scalar1=gwc[:, h:h + 1], scalar2=None, op0=ALU.mult),
                    reads=[f"off{par}", "gwc"], writes=["ogw"])
            P.op("scalar", lambda e, par=par: e.activation(
                out=sqsl[:, :, :], in_=off_sb[:, par, :, :], func=AF.Square),
                reads=[f"off{par}"], writes=["sqsl"])
        P.op("vector", lambda e: e.tensor_reduce(
            out=sqpart[:, :],
            in_=sqsl.rearrange("p h f -> p f h"),
            axis=AX.X, op=ALU.add),
            reads=["sqsl"], writes=["sqpart"])
        P.op("vector", lambda e: e.tensor_copy(out=sqpartb[:, :],
                                               in_=sqpart[:, :]),
             reads=["sqpart"], writes=["sqpartb"])
        P.op("tensor", lambda e: e.matmul(
            P6[:, 0:1], sqpartb[:, :], trimask[:, 127:128],
            start=True, stop=True),
            reads=["sqpartb", "trimask"], writes=["P6B"])
        rc = rstd_sb[:, par:par + 1]
        P.op("scalar", lambda e, rc=rc: e.activation(
            out=rc, in_=P6[:, 0:1], func=AF.Ln, scale=INV_DM, bias=LN_EPS),
            reads=["P6B", "consts"], writes=[f"rstd{par}"])
        P.op("scalar", lambda e, rc=rc: e.activation(
            out=rc, in_=rc, func=AF.Exp, scale=-0.5),
            reads=[f"rstd{par}", "consts"], writes=[f"rstd{par}"])
        if debug_outputs and gt == 0:
            P.op("sync", lambda e: e.dma_start(
                out=dbgb[:, 2048:4096],
                in_=ofull[:, :, :]),
                reads=["ofull"], writes=["dbgb"], dma=True)
            P.op("sync", lambda e: e.dma_start(
                out=dbgf[:, 3072:3200], in_=sqpart[:, :]),
                reads=["sqpart"], writes=["dbgf"], dma=True)
            P.op("sync", lambda e, rc=rc: e.dma_start(
                out=dbgf[:, 3200:3201], in_=rc,
                allow_slow_non_contiguous=True),
                reads=[f"rstd{par}"], writes=["dbgf"], dma=True)
            P.op("sync", lambda e: e.dma_start(
                out=dbgb[:, 0:2048],
                in_=ogw[:, :, :]),
                reads=["ogw"], writes=["dbgb"], dma=True)
        for mcq in range(4):
            pw = mcq % 2
            PW = [P0, P1][pw]
            pwk = f"PW{pw}"
            for hc in range(16):
                P.op("tensor", lambda e, hc=hc, mcq=mcq, PW=PW: e.matmul(
                    PW[:, :], ogw[:, hc, :],
                    wo_sb[:, hc, mcq * 512:(mcq + 1) * 512],
                    start=(hc == 0), stop=(hc == 15)),
                    reads=["ogw", "wo_sb"], writes=[pwk])
            P.op("vector", lambda e, mcq=mcq, par=par, PW=PW, rc=rc:
                 e.scalar_tensor_tensor(
                     out=x2_sb[:, par, mcq * 512:(mcq + 1) * 512],
                     in0=PW[:, :], scalar=rc,
                     in1=xb_sb[:, par, mcq * 512:(mcq + 1) * 512],
                     op0=ALU.mult, op1=ALU.add),
                 reads=[pwk, f"rstd{par}", f"xb{par}"], writes=[f"x2{par}"])
        P.op("sync", lambda e, gt=gt, par=par: e.dma_start(
            out=x2_dram[gt * 128:(gt + 1) * 128, :], in_=x2_sb[:, par, :]),
            reads=[f"x2{par}"], writes=[f"x2d{gt}"], dma=True)
        emit_ln(x2_sb[:, par, :], f"x2{par}", xln_tok[:, par, :],
                f"xlntok{par}", 2)
        P.op("sync", lambda e, gt=gt, par=par: e.dma_start(
            out=ln_dram[gt * 128:(gt + 1) * 128, :],
            in_=xln_tok[:, par, :]),
            reads=[f"xlntok{par}"], writes=[f"ln2d{gt}"], dma=True)

    P.barrier()
    P.flush(nc, sems)
    free_group(gB)

    if "C" not in phases:
        for cmx in reversed(ps_ctx):
            cmx.__exit__(None, None, None)
        for cmx in sem_ctx:
            cmx.__exit__(None, None, None)
        return nc

    gC, sbC = make_group()
    xln2_feat = sbC("xln2_feat", [128, 16, 512], BF16)
    hact = sbC("hact", [128, NIC, 512], BF16)
    wgu_sb = sbC("wgu_sb", [128, 2, 4, 1024], BF16)
    wdn_sb = sbC("wdn_sb", [128, 22, 1024], BF16)
    sg_sb = sbC("sg_sb", [128, 2, 512], BF16)
    y_sb = sbC("y_sb", [128, 4, 2048], F32)
    x2e_sb = sbC("x2e_sb", [128, 2, 2048], F32)
    out_sb = sbC("out_sb", [128, 2048], F32)

    for tb in range(4):
        for hc in range(16):
            P.op("sync", lambda e, hc=hc, tb=tb: e.dma_start(
                out=xln2_feat[:, hc, :],
                in_=ln_dram[tb * 512:(tb + 1) * 512,
                            hc * 128:(hc + 1) * 128],
                transpose=True),
                reads=[f"ln2d{tb * 4 + i}" for i in range(4)],
                writes=["x2f"], dma=True)

        for p in range(NPAIR):
            wp = p % 2
            P.op("sync", lambda e, p=p, wp=wp: e.dma_start(
                out=wgu_sb[:, wp, :, :],
                in_=cc_w_out[p * 4 * 128:(p + 1) * 4 * 128, :]
                .rearrange("(g p2) f -> p2 g f", p2=128)),
                deps=[ccw_ev], writes=[f"wgu{wp}"], dma=True)
            pg = p % 2
            PG, PU = [P0, P2][pg], [P1, P3][pg]
            pgk, puk = f"PG{pg}", f"PU{pg}"
            for gu in range(2):
                PP = PG if gu == 0 else PU
                ppk = pgk if gu == 0 else puk
                for hc in range(16):
                    lidx = gu * 16 + hc
                    g, cb = lidx // 8, lidx % 8
                    P.op("tensor",
                         lambda e, hc=hc, g=g, cb=cb, wp=wp, PP=PP: e.matmul(
                             PP[:, :],
                             wgu_sb[:, wp, g, cb * 128:(cb + 1) * 128],
                             xln2_feat[:, hc, :],
                             start=(hc == 0), stop=(hc == 15)),
                         reads=[f"wgu{wp}", "x2f"], writes=[ppk])
            P.op("scalar", lambda e, pg=pg, PG=PG: e.activation(
                out=sg_sb[:, pg, :], in_=PG[:, :], func=AF.Silu),
                reads=[pgk], writes=[f"sg{pg}"])
            P.op("vector", lambda e, p=p, pg=pg, PU=PU: e.tensor_tensor(
                out=hact[:, p, :], in0=sg_sb[:, pg, :], in1=PU[:, :],
                op=ALU.mult),
                reads=[f"sg{pg}", puk], writes=["hact"])

        for mcg in range(4):
            for half in range(2):
                g0 = 176 + 22 * mcg + 11 * half
                P.op("sync", lambda e, g0=g0, half=half: e.dma_start(
                    out=wdn_sb[:, half * 11:(half + 1) * 11, :],
                    in_=cc_w_out[g0 * 128:(g0 + 11) * 128, :]
                    .rearrange("(g p2) f -> p2 g f", p2=128)),
                    deps=[ccw_ev], writes=[f"wdn{half}"], dma=True)
            for tt in range(4):
                pd = tt % 2
                PD = [P4, P5][pd]
                pdk = f"PD{pd}"
                for ic in range(NIC):
                    half = 0 if ic < 22 else 1
                    gl = ic // 2
                    co = (ic % 2) * 512
                    P.op("tensor",
                         lambda e, ic=ic, tt=tt, gl=gl, co=co, PD=PD:
                         e.matmul(
                             PD[:, :], hact[:, ic, tt * 128:(tt + 1) * 128],
                             wdn_sb[:, gl, co:co + 512],
                             start=(ic == 0), stop=(ic == NIC - 1)),
                         reads=["hact", f"wdn{half}"], writes=[pdk])
                P.op("scalar", lambda e, tt=tt, mcg=mcg, PD=PD: e.copy(
                    out=y_sb[:, tt, mcg * 512:(mcg + 1) * 512], in_=PD[:, :]),
                    reads=[pdk], writes=[f"y{tt}"])

        for tt in range(4):
            gt = tb * 4 + tt
            pe = tt % 2
            P.op("sync", lambda e, gt=gt, pe=pe: e.dma_start(
                out=x2e_sb[:, pe, :],
                in_=x2_dram[gt * 128:(gt + 1) * 128, :]),
                writes=[f"x2e{pe}"], dma=True)
            P.op("scalar", lambda e, tt=tt: e.activation(
                out=junk[:, 0:2048], in_=y_sb[:, tt, :], func=AF.Square,
                accum_out=stat[:, 4:5]),
                reads=[f"y{tt}"], writes=["statc", "junk"])
            P.op("scalar", lambda e: e.activation(
                out=stat[:, 4:5], in_=stat[:, 4:5], func=AF.Ln,
                scale=INV_DM, bias=RMS_EPS_FFN),
                reads=["statc", "consts"], writes=["statc"])
            P.op("scalar", lambda e: e.activation(
                out=stat[:, 4:5], in_=stat[:, 4:5], func=AF.Exp, scale=-0.5),
                reads=["statc", "consts"], writes=["statc"])
            P.op("vector", lambda e, tt=tt, pe=pe: e.scalar_tensor_tensor(
                out=out_sb[:, :], in0=y_sb[:, tt, :],
                scalar=stat[:, 4:5], in1=x2e_sb[:, pe, :],
                op0=ALU.mult, op1=ALU.add),
                reads=[f"y{tt}", "statc", f"x2e{pe}"], writes=["outsb"])
            P.op("sync", lambda e, gt=gt: e.dma_start(
                out=y_out[gt * 128:(gt + 1) * 128, :], in_=out_sb[:, :]),
                reads=["outsb"], writes=["yd"], dma=True)

    P.barrier()
    P.flush(nc, sems)
    free_group(gC)
    free_group(gG)

    for cmx in reversed(ps_ctx):
        cmx.__exit__(None, None, None)
    for cmx in sem_ctx:
        cmx.__exit__(None, None, None)
    return nc


def _host_inputs(L0_attn_hidden_states, L0_attn_Wq, L0_attn_Wk, L0_attn_Wi,
                 L0_attn_Wo, L0_attn_g_weight, L0_ffn_gate_up_packed,
                 L0_ffn_gate_up_scales, L0_ffn_down_packed,
                 L0_ffn_down_scales):
    x = np.asarray(L0_attn_hidden_states, dtype=np.float32)
    wq = np.ascontiguousarray(np.asarray(L0_attn_Wq, np.float32))
    wk = np.ascontiguousarray(np.asarray(L0_attn_Wk, np.float32))
    wi = np.ascontiguousarray(np.asarray(L0_attn_Wi, np.float32))
    wo = np.ascontiguousarray(np.asarray(L0_attn_Wo, np.float32))
    gw = np.asarray(L0_attn_g_weight, np.float32)
    gup = np.asarray(L0_ffn_gate_up_packed, np.int32)
    gus = np.asarray(L0_ffn_gate_up_scales, np.float16)
    dnp = np.asarray(L0_ffn_down_packed, np.int32)
    dns = np.asarray(L0_ffn_down_scales, np.float16)

    gwc = np.ascontiguousarray(gw.reshape(NH, 128).T)
    trimask = np.zeros((128, 128), dtype=BF)
    su = np.tril(np.ones((64, 64), np.float32)).T.astype(BF)
    trimask[0:64, 0:64] = su
    trimask[0:64, 64:128] = 1.0
    trimask[64:128, 64:128] = su
    ident = np.eye(128, dtype=BF)
    consts = np.broadcast_to(np.array(CONST_VALS, np.float32),
                             (128, len(CONST_VALS))).copy()

    def tile_slices(gidx):
        if gidx < 1408:
            pair, rem = divmod(gidx, 32)
            gu, hc = divmod(rem, 16)
            r0 = (gu * NPAIR + pair) * 128
            pk = gup[r0:r0 + 128, hc * 16:(hc + 1) * 16]
            sc = gus[hc, r0:r0 + 128]
        else:
            g2 = gidx - 1408
            mcg, rem = divmod(g2, 176)
            ic, mi = divmod(rem, 4)
            mt = mcg * 4 + mi
            pk = dnp[mt * 128:(mt + 1) * 128, ic * 16:(ic + 1) * 16]
            sc = dns[ic, mt * 128:(mt + 1) * 128]
        return pk, sc

    per_core = []
    for c in range(8):
        b, half = c // 2, c % 2
        xs = np.ascontiguousarray(x[b, half * T:(half + 1) * T, :])
        pks = np.empty((TPC, 128, 16), np.int32)
        scs = np.empty((TPC, 128), np.float16)
        for i in range(TPC):
            pk, sc = tile_slices(c * TPC + i)
            pks[i] = pk
            scs[i] = sc
        mask8 = np.zeros((128, 8), np.float32)
        if half == 1:
            mask8[:, c - 1] = 1.0
        per_core.append(dict(
            x=xs, wq=wq, wk=wk, wi=wi, wo=wo,
            gwc=gwc, mask8=mask8, trimask=trimask, ident=ident,
            consts=consts, packedsl=pks,
            scalessl=np.ascontiguousarray(scs.T)))
    return per_core


def kernel(**inputs):
    per_core = _host_inputs(**inputs)
    nc = build_nc()
    res = run_bass_kernel_spmd(nc, per_core, core_ids=list(range(8)))
    out = np.empty((B_, L_, DM), np.float32)
    for c in range(8):
        b, half = c // 2, c % 2
        out[b, half * T:(half + 1) * T, :] = res.results[c]["y"]
    return out


if __name__ == "__main__":
    nc = build_nc()
    print("built ok; instructions:", nc.n_instructions())


# version 21
# speedup vs baseline: 1.2357x; 1.0802x over previous
import numpy as np
import ml_dtypes
import concourse.bass as bass
import concourse.mybir as mybir
from concourse.bass_utils import run_bass_kernel_spmd

F32 = mybir.dt.float32
BF16 = mybir.dt.bfloat16
F16 = mybir.dt.float16
I32 = mybir.dt.int32
U16 = mybir.dt.uint16
AF = mybir.ActivationFunctionType
ALU = mybir.AluOpType
AX = mybir.AxisListType

BF = ml_dtypes.bfloat16

B_, L_, DM = 4, 4096, 2048
NH, DK, DV = 16, 128, 128
SCALE = float(DK) ** -0.5
T = 2048
NBLK = 4
NCH = 16
NTT = 16
FFI = 5632
NPAIR = 44
NIC = 44
NTILES = 2112
TPC = 264
NGRP = NTILES // 8
GPC = TPC // 8

LN_EPS = 1e-5
RMS_EPS_FFN = 1e-6
INV_DM = 1.0 / 2048.0

CONST_VALS = [SCALE, -1.0, 1.0, 2.0, -0.5, INV_DM, LN_EPS, RMS_EPS_FFN]

ENGINES = ("sync", "gpsimd", "tensor", "vector", "scalar")
DMA_QUEUES = ("sync", "gpsimd", "scalar")
NDMASEM = 8
DMA_SEMS = tuple(f"{q}dma{i}" for q in DMA_QUEUES for i in range(NDMASEM))
ALL_SEMS = ENGINES + ("cc",) + DMA_SEMS


class Prog:

    def __init__(self):
        self.items = []
        self.count = {s: 0 for s in ALL_SEMS}
        self.hwm = {e: {s: 0 for s in ALL_SEMS} for e in ENGINES}
        self.last_w = {}
        self.readers = {}
        self.dma_idx = {q: 0 for q in DMA_QUEUES}

    def op(self, eng, fn, reads=(), writes=(), dma=False, deps=(), sem=None):
        dep_evs = []
        for k in reads:
            ev = self.last_w.get(k)
            if ev is not None:
                dep_evs.append(ev)
        for k in writes:
            ev = self.last_w.get(k)
            if ev is not None:
                dep_evs.append(ev)
            dep_evs.extend(self.readers.get(k, ()))
        dep_evs.extend(deps)
        if dma:
            idx = self.dma_idx[eng]
            self.dma_idx[eng] = idx + 1
            slot = idx % NDMASEM
            dmasem = f"{eng}dma{slot}"
            dep_evs.append((dmasem, 16 * (idx // NDMASEM), True))
        wmax = {}
        for (ds, dv, dasync) in dep_evs:
            if ds == eng and eng == "tensor" and not dasync:
                continue
            if self.hwm[eng][ds] >= dv:
                continue
            wmax[ds] = max(wmax.get(ds, 0), dv)
        for ds, dv in wmax.items():
            self.hwm[eng][ds] = dv
        if dma:
            semname = dmasem
        else:
            semname = sem if sem is not None else eng
        inc = 16 if dma else 1
        self.count[semname] += inc
        is_async = dma or (sem is not None)
        ev = (semname, self.count[semname], is_async)
        self.items.append((eng, fn, sorted(wmax.items()), semname, inc))
        for k in reads:
            self.readers.setdefault(k, []).append(ev)
        for k in writes:
            self.last_w[k] = ev
            self.readers[k] = []
        return ev

    def barrier(self):
        snap = {s: self.count[s] for s in ENGINES + DMA_SEMS}
        for e in ENGINES:
            waits = []
            for e2 in ENGINES + DMA_SEMS:
                if e2 == e:
                    continue
                if self.hwm[e][e2] < snap[e2]:
                    waits.append((e2, snap[e2]))
                    self.hwm[e][e2] = snap[e2]
            if waits:
                self.items.append((e, None, waits, e, 0))
        self.last_w = {}
        self.readers = {}

    def flush(self, nc, sems):
        items, self.items = self.items, []
        with nc.Block() as block:
            for eng_name in ENGINES:
                def make(eng_name=eng_name):
                    def body(eng):
                        for (en, fn, waits, semname, inc) in items:
                            if en != eng_name:
                                continue
                            for (se, sv) in waits:
                                eng.wait_ge(sems[se], sv)
                            if fn is not None:
                                fn(eng).then_inc(sems[semname], inc)
                    return body
                getattr(block, eng_name)(make())


def build_nc(debug_outputs=False, phases=("A", "B", "C")):
    nc = bass.Bass("TRN2", target_bir_lowering=False)

    def din(name, shape, dt):
        return nc.dram_tensor(name, shape, dt, kind="ExternalInput").ap()

    x_in = din("x", [T, DM], F32)
    wq_in = din("wq", [DM, DM], F32)
    wk_in = din("wk", [DM, DM], F32)
    wi_in = din("wi", [DM, DM], F32)
    wo_in = din("wo", [DM, DM], F32)
    gwc_in = din("gwc", [128, NH], F32)
    mask8_in = din("mask8", [128, 8], F32)
    trimask_in = din("trimask", [128, 128], BF16)
    ident_in = din("ident", [128, 128], BF16)
    consts_in = din("consts", [128, len(CONST_VALS)], F32)
    packed_in = din("packedsl", [TPC, 128, 16], I32)
    scales_in = din("scalessl", [128, TPC], F16)

    y_out = nc.dram_tensor("y", [T, DM], F32, kind="ExternalOutput").ap()

    dkind = "ExternalOutput" if debug_outputs else "Internal"

    def dscr(name, shape, dt, dbg=False):
        return nc.dram_tensor(name, shape, dt,
                              kind=(dkind if dbg else "Internal")).ap()

    wq16t = dscr("wq16t", [NH, 16, 128, 128], BF16)
    wk16t = dscr("wk16t", [NH, 16, 128, 128], BF16)
    wi16 = dscr("wi16", [DM, DM], BF16)
    wo16 = dscr("wo16", [DM, DM], BF16, dbg=True)
    ln_dram = dscr("ln_dram", [T, DM], BF16, dbg=True)
    ogg_dram = dscr("ogg_dram", [NCH, NH, 128, 128], BF16)
    o_dram = dscr("o_dram", [NCH, NH, 128, 128], BF16, dbg=True)
    x2_dram = dscr("x2_dram", [T, DM], F32, dbg=True)
    dbgf = dscr("dbgf", [128, 4096], F32, dbg=True)
    dbgb = dscr("dbgb", [128, 4096], BF16, dbg=True)
    cc_w_in = nc.dram_tensor("cc_w_in", [GPC * 128, 1024], BF16).ap()
    cc_w_out = nc.dram_tensor("cc_w_out", [NGRP * 128, 1024], BF16,
                              addr_space="Shared").ap()
    cc_s_in = nc.dram_tensor("cc_s_in", [NH * 128, 128], F32).ap()
    cc_s_out = nc.dram_tensor("cc_s_out", [8 * NH * 128, 128], F32,
                              addr_space="Shared").ap()

    P = Prog()

    sems = {}
    sem_ctx = []
    for s in ALL_SEMS:
        cm = nc.semaphore(f"sem_{s}")
        sem_ctx.append(cm)
        sems[s] = cm.__enter__()

    all_ctx = []

    def make_group():
        group = []

        def sb(name, shape, dt):
            cm = nc.sbuf_tensor("sb_" + name, shape, dt)
            group.append(cm)
            all_ctx.append(cm)
            return cm.__enter__()
        return group, sb

    def free_group(group):
        for cm in reversed(group):
            cm.__exit__(None, None, None)
            all_ctx.remove(cm)

    ps_ctx = []

    def psb(name):
        cm = nc.psum_tensor(name, [128, 512], F32)
        ps_ctx.append(cm)
        return cm.__enter__()

    gG, sbG = make_group()
    consts = sbG("consts", [128, len(CONST_VALS)], F32)
    gwc = sbG("gwc", [128, NH], F32)
    mask8 = sbG("mask8", [128, 8], F32)
    trimask = sbG("trimask", [128, 128], BF16)
    ident = sbG("ident", [128, 128], BF16)
    junk = sbG("junk", [128, 2048], BF16)
    junkf = sbG("junkf", [128, 256], F32)
    xln_tok = sbG("xln_tok", [128, 1, 2048], BF16)
    stat = sbG("stat", [128, 16], F32)

    for i, v in enumerate(CONST_VALS):
        nc.const_aps.register(F32, v, consts[:, i:i + 1])

    P0 = psb("P0")
    P1 = psb("P1")
    P2 = psb("P2")
    P3 = psb("P3")
    P4 = psb("P4")
    P5 = psb("P5")
    P6 = psb("P6")
    cm = nc.psum_tensor("P7b", [128, 1024], BF16)
    ps_ctx.append(cm)
    P7b = cm.__enter__()

    P.op("sync", lambda e: e.dma_start(out=consts[:], in_=consts_in),
         writes=["consts"], dma=True)
    P.op("sync", lambda e: e.dma_start(out=gwc[:], in_=gwc_in),
         writes=["gwc"], dma=True)
    P.op("sync", lambda e: e.dma_start(out=mask8[:], in_=mask8_in),
         writes=["mask8"], dma=True)
    P.op("sync", lambda e: e.dma_start(out=trimask[:], in_=trimask_in),
         writes=["trimask"], dma=True)
    P.op("sync", lambda e: e.dma_start(out=ident[:], in_=ident_in),
         writes=["ident"], dma=True)
    P.op("sync", lambda e: e.dma_start(out=scl16[:], in_=scales_in),
         writes=["scl16"], dma=True)
    P.op("vector", lambda e: e.tensor_copy(out=scl[:], in_=scl16[:]),
         reads=["scl16"], writes=["scl"])
    P.op("vector", lambda e: e.memset(S_sb[:], 0.0), writes=["S"])
    P.op("vector", lambda e: e.memset(S_bf[:], 0.0), writes=["Sbf"])
    P.op("vector", lambda e: e.memset(R_run[:], 1.0), writes=["Rrun"])

    wi_r = wi_in.rearrange("(hc p) (dc f) -> dc hc p f", p=128, f=512)
    wi16_r = wi16.rearrange("(hc p) (dc f) -> dc hc p f", p=128, f=512)
    wq_r = wq_in.rearrange("(hc p) (mc f) -> mc hc p f", p=128, f=128)
    wk_r = wk_in.rearrange("(hc p) (mc f) -> mc hc p f", p=128, f=128)

    def cast_wi(dvc):
        P.op("gpsimd", lambda e, dvc=dvc: e.dma_start(
            out=wi16_r[dvc], in_=wi_r[dvc]),
            writes=[f"wi16d{dvc}"], dma=True)

    for dvc in range(4):
        cast_wi(dvc)

    gA, sbA = make_group()
    scl16 = sbA("scl16", [128, TPC], F16)
    scl = sbA("scl", [128, TPC], F32)
    S_sb = sbA("S_sb", [128, NH, 128], F32)
    S_bf = sbA("S_bf", [128, NH, 128], BF16)
    R_run = sbA("R_run", [128, NH], F32)
    xt = sbA("xt", [128, 2, 2048], F32)
    xln_feat = sbA("xln_feat", [128, 16, 512], BF16)
    v_tok = sbA("v_tok", [128, 4, 2048], BF16)
    wi_sb = sbA("wi_sb", [128, 16, 512], BF16)
    wq_sb = sbA("wq_sb", [128, 2, 16, 128], BF16)
    wk_sb = sbA("wk_sb", [128, 2, 16, 128], BF16)
    q_h = sbA("q_h", [128, 4, 512], BF16)
    kraw_h = sbA("kraw_h", [128, 4, 512], F32)
    sp_h = sbA("sp_h", [128, 4, 512], F32)
    gh_h = sbA("gh_h", [128, 4, 512], F32)
    bb_h = sbA("bb_h", [128, 4, 512], F32)
    eg_h = sbA("eg_h", [128, 4, 512], F32)
    qg_h = sbA("qg_h", [128, 4, 512], BF16)
    qb_h = sbA("qb_h", [128, 4, 4, 64], BF16)
    ka_h = sbA("ka_h", [128, 4, 4, 64], BF16)
    kb_h = sbA("kb_h", [128, 4, 4, 64], BF16)
    kba_h = sbA("kba_h", [128, 4, 4, 64], BF16)
    kh_h = sbA("kh_h", [128, 4, 512], BF16)
    kh_tok = sbA("kh_tok", [128, 2, 2, 128], BF16)
    ng_h = sbA("ng_h", [128, 4, 8], F32)
    asb = sbA("asb", [128, 2, 128], BF16)
    zchunk = sbA("zchunk", [128, 128], F32)
    qgg_slab = sbA("qgg_slab", [128, 4, 16, 128], BF16)
    o_slab = sbA("o_slab", [128, 4, 16, 128], BF16)
    dq_pk = sbA("dq_pk", [128, 2, 4, 16], I32)
    dq_nib = sbA("dq_nib", [128, 128, 4], U16)
    dq_bits = sbA("dq_bits", [128, 512], U16)
    dq_sgn = sbA("dq_sgn", [128, 512], U16)
    dq_sgnf = sbA("dq_sgnf", [128, 512], BF16)
    dq_u16 = sbA("dq_u16", [128, 512], U16)
    dq_relu = sbA("dq_relu", [128, 512], BF16)
    dq_min = sbA("dq_min", [128, 512], BF16)
    dq_out = sbA("dq_out", [128, 2, 512], BF16)
    dq_tr = sbA("dq_tr", [128, 2, 128], BF16)

    P.op("vector", lambda e: e.memset(zchunk[:], 0.0), writes=["zchunk"])
    P.op("vector", lambda e: e.memset(P2[64:128, 0:64], 0.0), writes=["PA0"])
    P.op("vector", lambda e: e.memset(P3[64:128, 0:64], 0.0), writes=["PA1"])

    def emit_dequant_dve(bi):
        pk_par = bi % 2
        op_par = bi % 2
        lt0 = 4 * bi
        pk = dq_pk[:, pk_par, :, :]
        P.op("gpsimd",
             lambda e, pk=pk, lt0=lt0: e.dma_start(
                 out=pk, in_=packed_in[lt0:lt0 + 4].rearrange(
                     "t p w -> p t w")),
             writes=[f"dqpk{pk_par}"], dma=True)
        pku = pk.rearrange("p t w -> p (t w)").bitcast(U16)
        for k in range(4):
            P.op("vector",
                 lambda e, k=k, pku=pku: e.tensor_scalar(
                     out=dq_nib[:, :, k], in0=pku, scalar1=4 * k, scalar2=15,
                     op0=ALU.logical_shift_right, op1=ALU.bitwise_and),
                 reads=[f"dqpk{pk_par}"], writes=["dqnib"])
        nib3 = dq_nib[:, :, :]
        P.op("vector", lambda e, nib3=nib3: e.tensor_scalar(
            out=dq_bits[:], in0=nib3, scalar1=7, scalar2=6,
            op0=ALU.bitwise_and, op1=ALU.logical_shift_left),
            reads=["dqnib"], writes=["dqbits"])
        P.op("vector", lambda e: e.tensor_scalar(
            out=dq_u16[:], in0=dq_bits[:], scalar1=0x3F00,
            scalar2=None, op0=ALU.add), reads=["dqbits"], writes=["dqu16"])
        P.op("vector", lambda e, nib3=nib3: e.tensor_scalar(
            out=dq_sgn[:], in0=nib3, scalar1=3, scalar2=1,
            op0=ALU.logical_shift_right, op1=ALU.bitwise_and),
            reads=["dqnib"], writes=["dqsgn"])
        P.op("vector", lambda e: e.tensor_scalar(
            out=dq_sgnf[:], in0=dq_sgn[:], scalar1=-2,
            scalar2=1, op0=ALU.mult, op1=ALU.add),
            reads=["dqsgn"], writes=["dqsgnf"])
        fmv = dq_u16[:].bitcast(BF16)
        P.op("scalar", lambda e, fmv=fmv: e.activation(
            out=dq_relu[:], in_=fmv, func=AF.Relu, bias=-1.0,
            scale=2.0),
            reads=["dqu16", "consts"], writes=["dqrelu"])
        P.op("vector", lambda e, fmv=fmv: e.tensor_tensor(
            out=dq_min[:], in0=dq_relu[:], in1=fmv,
            op=ALU.min),
            reads=["dqrelu", "dqu16"], writes=["dqmin"])
        for j in range(4):
            lt = lt0 + j
            P.op("vector",
                 lambda e, j=j, lt=lt, op_par=op_par: e.scalar_tensor_tensor(
                     out=dq_out[:, op_par, j * 128:(j + 1) * 128],
                     in0=dq_min[:, j * 128:(j + 1) * 128],
                     scalar=scl[:, lt:lt + 1],
                     in1=dq_sgnf[:, j * 128:(j + 1) * 128],
                     op0=ALU.mult, op1=ALU.mult),
                 reads=["dqmin", "dqsgnf", "scl"], writes=[f"dqout{op_par}"])

    def emit_dequant_tr(bi):
        op_par = bi % 2
        lt0 = 4 * bi
        for j in range(4):
            lt = lt0 + j
            tp = j % 2
            P.op("tensor", lambda e, j=j, op_par=op_par: e.transpose(
                P7b[:, 0:128], dq_out[:, op_par, j * 128:(j + 1) * 128],
                ident[:]),
                reads=[f"dqout{op_par}", "ident"], writes=["P7b"])
            P.op("scalar", lambda e, tp=tp: e.copy(
                out=dq_tr[:, tp, :], in_=P7b[:, 0:128]),
                reads=["P7b"], writes=[f"dqtr{tp}"])
            g, cb = lt // 8, lt % 8
            P.op("gpsimd", lambda e, g=g, cb=cb, tp=tp: e.dma_start(
                out=cc_w_in[g * 128:(g + 1) * 128, cb * 128:(cb + 1) * 128],
                in_=dq_tr[:, tp, :]),
                reads=[f"dqtr{tp}"], dma=True)

    dq_iter = iter(range(TPC // 4))

    dq_pending = []

    def emit_some_dequant(n):
        for _ in range(n):
            bi = next(dq_iter, None)
            if bi is None:
                while dq_pending:
                    emit_dequant_tr(dq_pending.pop(0))
                return
            emit_dequant_dve(bi)
            dq_pending.append(bi)
            if len(dq_pending) > 1:
                emit_dequant_tr(dq_pending.pop(0))

    def emit_ln(xin_ap, xkey, out_ap, okey, c0):
        m_c = stat[:, c0:c0 + 1]
        v_c = stat[:, c0 + 1:c0 + 2]
        km, kv = f"stat{c0}", f"stat{c0 + 1}"
        P.op("vector", lambda e: e.reduce_sum(m_c, xin_ap, axis=AX.X),
             reads=[xkey], writes=[km])
        P.op("scalar", lambda e: e.activation(
            out=junk[:, 0:2048], in_=xin_ap, func=AF.Square, accum_out=v_c),
            reads=[xkey], writes=[kv, "junk"])
        P.op("vector", lambda e: e.tensor_scalar(
            out=m_c, in0=m_c, scalar1=INV_DM, scalar2=None, op0=ALU.mult),
            reads=[km], writes=[km])
        P.op("vector", lambda e: e.tensor_scalar(
            out=v_c, in0=v_c, scalar1=INV_DM, scalar2=None, op0=ALU.mult),
            reads=[kv], writes=[kv])
        P.op("vector", lambda e: e.scalar_tensor_tensor(
            out=v_c, in0=m_c, scalar=m_c, in1=v_c,
            op0=ALU.mult, op1=ALU.subtract),
            reads=[km, kv], writes=[kv])
        P.op("vector", lambda e: e.tensor_scalar(
            out=v_c, in0=v_c, scalar1=-1.0, scalar2=None, op0=ALU.mult),
            reads=[kv], writes=[kv])
        P.op("scalar", lambda e: e.activation(
            out=v_c, in_=v_c, func=AF.Ln, bias=LN_EPS),
            reads=[kv, "consts"], writes=[kv])
        P.op("scalar", lambda e: e.activation(
            out=v_c, in_=v_c, func=AF.Exp, scale=-0.5),
            reads=[kv, "consts"], writes=[kv])
        P.op("vector", lambda e: e.tensor_scalar(
            out=out_ap, in0=xin_ap, scalar1=m_c, scalar2=v_c,
            op0=ALU.subtract, op1=ALU.mult),
            reads=[xkey, km, kv], writes=[okey])

    ccw_ev = None
    for tb in range(NBLK):
        if tb == 3:
            P.op("gpsimd", lambda e: e.dma_start(out=wo16, in_=wo_in),
                 writes=["wo16d"], dma=True)
        for tt in range(4):
            gt = tb * 4 + tt
            par = gt % 2
            P.op("sync", lambda e, gt=gt, par=par: e.dma_start(
                out=xt[:, par, :], in_=x_in[gt * 128:(gt + 1) * 128, :]),
                writes=[f"xt{par}"], dma=True)
            emit_ln(xt[:, par, :], f"xt{par}", xln_tok[:, 0, :],
                    "xlntok", 0)
            P.op("sync", lambda e, gt=gt, par=par: e.dma_start(
                out=ln_dram[gt * 128:(gt + 1) * 128, :],
                in_=xln_tok[:, 0, :]),
                reads=["xlntok"], writes=[f"lnd{gt}"], dma=True)
        for hc in range(16):
            P.op("sync", lambda e, hc=hc, tb=tb: e.dma_start(
                out=xln_feat[:, hc, :],
                in_=ln_dram[tb * 512:(tb + 1) * 512,
                            hc * 128:(hc + 1) * 128],
                transpose=True),
                reads=[f"lnd{tb * 4 + i}" for i in range(4)],
                writes=["xlf"], dma=True)

        for dvc in range(4):
            P.op("gpsimd", lambda e, dvc=dvc: e.dma_start(
                out=wi_sb[:, :, :],
                in_=wi16_r[dvc].rearrange("hc p f -> p hc f")),
                reads=[f"wi16d{dvc}"], writes=["wi_sb"], dma=True)
            for c in range(4):
                pp = [P0, P1][c % 2]
                pkey = f"P{c % 2}"
                for hc in range(16):
                    P.op("tensor", lambda e, pp=pp, hc=hc, c=c:
                         e.matmul(pp[:, :],
                                  xln_feat[:, hc, c * 128:(c + 1) * 128],
                                  wi_sb[:, hc, :],
                                  start=(hc == 0), stop=(hc == 15)),
                         reads=["xlf", "wi_sb"], writes=[pkey])
                P.op("scalar", lambda e, pp=pp, c=c, dvc=dvc: e.copy(
                    out=v_tok[:, c, dvc * 512:(dvc + 1) * 512], in_=pp[:, :]),
                    reads=[pkey], writes=[f"vtok{c}"])

        def emit_wload(h):
            wp2 = h % 2
            if tb == 0:
                P.op("gpsimd", lambda e, h=h, wp2=wp2: e.dma_start(
                    out=wq_sb[:, wp2, :, :],
                    in_=wq_r[h].rearrange("hc p f -> p hc f")),
                    writes=[f"wq_sb{wp2}"], dma=True)
                P.op("sync", lambda e, h=h, wp2=wp2: e.dma_start(
                    out=wq16t[h].rearrange("hc p f -> p hc f"),
                    in_=wq_sb[:, wp2, :, :]),
                    reads=[f"wq_sb{wp2}"], writes=[f"wq16d{h}"], dma=True)
                P.op("gpsimd", lambda e, h=h, wp2=wp2: e.dma_start(
                    out=wk_sb[:, wp2, :, :],
                    in_=wk_r[h].rearrange("hc p f -> p hc f")),
                    writes=[f"wk_sb{wp2}"], dma=True)
                P.op("sync", lambda e, h=h, wp2=wp2: e.dma_start(
                    out=wk16t[h].rearrange("hc p f -> p hc f"),
                    in_=wk_sb[:, wp2, :, :]),
                    reads=[f"wk_sb{wp2}"], writes=[f"wk16d{h}"], dma=True)
            else:
                P.op("sync", lambda e, h=h, wp2=wp2: e.dma_start(
                    out=wq_sb[:, wp2, :, :],
                    in_=wq16t[h].rearrange("hc p f -> p hc f")),
                    reads=[f"wq16d{h}"], writes=[f"wq_sb{wp2}"], dma=True)
                P.op("sync", lambda e, h=h, wp2=wp2: e.dma_start(
                    out=wk_sb[:, wp2, :, :],
                    in_=wk16t[h].rearrange("hc p f -> p hc f")),
                    reads=[f"wk16d{h}"], writes=[f"wk_sb{wp2}"], dma=True)

        def emit_proj(h):
            hp = h % 4
            wp2 = h % 2
            for hc in range(16):
                P.op("tensor", lambda e, hc=hc, wp2=wp2: e.matmul(
                    P0[:, :], wq_sb[:, wp2, hc, :], xln_feat[:, hc, :],
                    start=(hc == 0), stop=(hc == 15)),
                    reads=[f"wq_sb{wp2}", "xlf"], writes=["P0"])
            P.op("scalar", lambda e, hp=hp: e.activation(
                out=q_h[:, hp, :], in_=P0[:, :], func=AF.Relu, scale=SCALE),
                reads=["P0", "consts"], writes=[f"qh{hp}"])
            for hc in range(16):
                P.op("tensor", lambda e, hc=hc, wp2=wp2: e.matmul(
                    P1[:, :], wk_sb[:, wp2, hc, :], xln_feat[:, hc, :],
                    start=(hc == 0), stop=(hc == 15)),
                    reads=[f"wk_sb{wp2}", "xlf"], writes=["P1"])

        def emit_prep1(h):
            hp = h % 4
            spv = sp_h[:, hp, :]
            P.op("scalar", lambda e, spv=spv: e.activation(
                out=spv, in_=P1[:, :], func=AF.Abs),
                reads=["P1"], writes=[f"sp{hp}"])
            P.op("scalar", lambda e, hp=hp: e.copy(
                out=kraw_h[:, hp, :], in_=P1[:, :]),
                reads=["P1"], writes=[f"krawh{hp}"])
            P.op("scalar", lambda e, spv=spv: e.activation(
                out=spv, in_=spv, func=AF.Exp, scale=-1.0),
                reads=[f"sp{hp}", "consts"], writes=[f"sp{hp}"])
            P.op("scalar", lambda e, spv=spv: e.activation(
                out=spv, in_=spv, func=AF.Ln, bias=1.0),
                reads=[f"sp{hp}", "consts"], writes=[f"sp{hp}"])

        def emit_prep2(h):
            hp = h % 4
            kr = kraw_h[:, hp, :]
            spv = sp_h[:, hp, :]
            ghv = gh_h[:, hp, :]
            bbv = bb_h[:, hp, :]
            P.op("vector", lambda e, kr=kr, bbv=bbv: e.tensor_scalar(
                out=bbv, in0=kr, scalar1=0.0, scalar2=None, op0=ALU.max),
                reads=[f"krawh{hp}"], writes=[f"bb{hp}"])
            P.op("vector", lambda e, spv=spv, bbv=bbv: e.tensor_add(
                out=spv, in0=spv, in1=bbv),
                reads=[f"sp{hp}", f"bb{hp}"], writes=[f"sp{hp}"])
            for c in range(4):
                P.op("vector", lambda e, c=c, spv=spv, ghv=ghv:
                     e.tensor_tensor_scan(
                         out=ghv[:, c * 128:(c + 1) * 128],
                         data0=spv[:, c * 128:(c + 1) * 128],
                         data1=zchunk[:, :],
                         initial=0.0, op0=ALU.add, op1=ALU.add),
                     reads=[f"sp{hp}", "zchunk"], writes=[f"gh{hp}"])
            P.op("vector", lambda e, kr=kr, spv=spv, bbv=bbv: e.tensor_sub(
                out=bbv, in0=kr, in1=spv),
                reads=[f"krawh{hp}", f"sp{hp}"], writes=[f"bb{hp}"])
            P.op("vector", lambda e, bbv=bbv, ghv=ghv: e.tensor_add(
                out=bbv, in0=bbv, in1=ghv),
                reads=[f"bb{hp}", f"gh{hp}"], writes=[f"bb{hp}"])
            P.op("vector", lambda e, hp=hp, ghv=ghv: e.tensor_scalar(
                out=ng_h[:, hp, 0:8:2],
                in0=ghv[:, 63:512:128],
                scalar1=-1.0, scalar2=None, op0=ALU.mult),
                reads=[f"gh{hp}"], writes=[f"ng{hp}"])
            P.op("vector", lambda e, hp=hp, ghv=ghv: e.tensor_scalar(
                out=ng_h[:, hp, 1:8:2],
                in0=ghv[:, 127:512:128],
                scalar1=-1.0, scalar2=None, op0=ALU.mult),
                reads=[f"gh{hp}"], writes=[f"ng{hp}"])

        def emit_prep3(h):
            hp = h % 4
            ghv = gh_h[:, hp, :]
            bbv = bb_h[:, hp, :]
            egv = eg_h[:, hp, :]
            P.op("scalar", lambda e, ghv=ghv, egv=egv: e.activation(
                out=egv, in_=ghv, func=AF.Exp, scale=-1.0),
                reads=[f"gh{hp}", "consts"], writes=[f"eg{hp}"])
            P.op("vector", lambda e, hp=hp, egv=egv: e.tensor_mul(
                out=qg_h[:, hp, :], in0=q_h[:, hp, :], in1=egv),
                reads=[f"qh{hp}", f"eg{hp}"], writes=[f"qg{hp}"])
            for c in range(4):
                P.op("scalar", lambda e, c=c, hp=hp, bbv=bbv: e.activation(
                    out=ka_h[:, hp, c, :],
                    in_=bbv[:, c * 128:c * 128 + 64],
                    func=AF.Exp),
                    reads=[f"bb{hp}"], writes=[f"ka{hp}"])
            for c in range(4):
                ng63 = ng_h[:, hp, 2 * c:2 * c + 1]
                ng127 = ng_h[:, hp, 2 * c + 1:2 * c + 2]
                g63 = gh_h[:, hp, c * 128 + 63:c * 128 + 64]
                P.op("scalar", lambda e, c=c, hp=hp, bbv=bbv, ng63=ng63:
                     e.activation(
                         out=kb_h[:, hp, c, :],
                         in_=bbv[:, c * 128 + 64:(c + 1) * 128],
                         func=AF.Exp, bias=ng63),
                     reads=[f"bb{hp}", f"ng{hp}"], writes=[f"kb{hp}"])
                P.op("scalar", lambda e, c=c, hp=hp, bbv=bbv, ng63=ng63:
                     e.activation(
                         out=kba_h[:, hp, c, :],
                         in_=bbv[:, c * 128:c * 128 + 64],
                         func=AF.Exp, bias=ng63),
                     reads=[f"bb{hp}", f"ng{hp}"], writes=[f"kba{hp}"])
                P.op("scalar", lambda e, c=c, hp=hp, bbv=bbv, ng127=ng127:
                     e.activation(
                         out=kh_h[:, hp, c * 128:(c + 1) * 128],
                         in_=bbv[:, c * 128:(c + 1) * 128],
                         func=AF.Exp, bias=ng127),
                     reads=[f"bb{hp}", f"ng{hp}"], writes=[f"kh{hp}"])
                P.op("scalar", lambda e, c=c, hp=hp, ghv=ghv, g63=g63:
                     e.activation(
                         out=junkf[:, c * 64:(c + 1) * 64],
                         in_=ghv[:, c * 128 + 64:(c + 1) * 128],
                         func=AF.Exp, scale=-1.0, bias=g63),
                     reads=[f"gh{hp}", "consts"], writes=["jf"])
                P.op("vector", lambda e, c=c, hp=hp: e.tensor_mul(
                    out=qb_h[:, hp, c, :],
                    in0=q_h[:, hp, c * 128 + 64:(c + 1) * 128],
                    in1=junkf[:, c * 64:(c + 1) * 64]),
                    reads=[f"qh{hp}", "jf"], writes=[f"qb{hp}"])
            if tb == 0:
                for c in range(4):
                    P.op("vector", lambda e, c=c, hp=hp, h=h: e.tensor_scalar(
                        out=qgg_slab[:, c, h, :],
                        in0=qg_h[:, hp, c * 128:(c + 1) * 128],
                        scalar1=R_run[:, h:h + 1], scalar2=None,
                        op0=ALU.mult),
                        reads=[f"qg{hp}", "Rrun"], writes=[f"qgg{c}"])
                    P.op("vector", lambda e, c=c, hp=hp, h=h, egv=egv:
                         e.tensor_scalar(
                             out=R_run[:, h:h + 1], in0=R_run[:, h:h + 1],
                             scalar1=egv[:, c * 128 + 127:c * 128 + 128],
                             scalar2=None, op0=ALU.mult),
                         reads=["Rrun", f"eg{hp}"], writes=["Rrun"])

        def emit_apart(h, c):
            hp = h % 4
            cp = c % 2
            PA = [P2, P3][cp]
            pak = f"PA{cp}"
            P.op("tensor", lambda e, c=c, hp=hp, cp=cp: e.transpose(
                P7b[:, 0:128], kh_h[:, hp, c * 128:(c + 1) * 128],
                ident[:]),
                reads=[f"kh{hp}", "ident"], writes=["P7b"])
            P.op("vector", lambda e, hp=hp, cp=cp, h=h: e.tensor_copy(
                out=kh_tok[:, h % 2, cp, :], in_=P7b[:, 0:128]),
                reads=["P7b"], writes=[f"khtok{h % 2}{cp}"])
            P.op("tensor", lambda e, c=c, hp=hp, PA=PA: e.matmul(
                PA[0:64, 0:64], ka_h[:, hp, c, :],
                qg_h[:, hp, c * 128:c * 128 + 64],
                start=True, stop=True),
                reads=[f"ka{hp}", f"qg{hp}"], writes=[pak])
            P.op("tensor", lambda e, c=c, hp=hp, PA=PA: e.matmul(
                PA[0:64, 64:128], kba_h[:, hp, c, :], qb_h[:, hp, c, :],
                start=True, stop=True),
                reads=[f"kba{hp}", f"qb{hp}"], writes=[pak])
            P.op("tensor", lambda e, c=c, hp=hp, PA=PA: e.matmul(
                PA[64:128, 64:128], kb_h[:, hp, c, :], qb_h[:, hp, c, :],
                start=True, stop=True),
                reads=[f"kb{hp}", f"qb{hp}"], writes=[pak])
            P.op("vector", lambda e, cp=cp, PA=PA: e.tensor_tensor(
                out=asb[:, cp, :], in0=PA[:, 0:128],
                in1=trimask[:], op=ALU.mult),
                reads=[pak, "trimask"], writes=[f"asb{cp}"])

        def emit_opart(h, c):
            hp = h % 4
            cp = c % 2
            PO = [P4, P5][cp]
            pok = f"PO{cp}"
            egv = eg_h[:, hp, :]
            P.op("tensor", lambda e, c=c, h=h, cp=cp, PO=PO: e.matmul(
                PO[:, 0:64], v_tok[:, c, h * 128:(h + 1) * 128],
                asb[:, cp, 0:64], start=True, stop=False),
                reads=[f"vtok{c}", f"asb{cp}"], writes=[pok])
            P.op("tensor", lambda e, c=c, h=h, hp=hp, PO=PO: e.matmul(
                PO[:, 0:64], S_bf[:, h, :],
                qg_h[:, hp, c * 128:c * 128 + 64],
                start=False, stop=True),
                reads=["Sbf", f"qg{hp}"], writes=[pok])
            P.op("tensor", lambda e, c=c, h=h, cp=cp, PO=PO: e.matmul(
                PO[:, 64:128], v_tok[:, c, h * 128:(h + 1) * 128],
                asb[:, cp, 64:128], start=True, stop=False),
                reads=[f"vtok{c}", f"asb{cp}"], writes=[pok])
            P.op("tensor", lambda e, c=c, h=h, hp=hp, PO=PO: e.matmul(
                PO[:, 64:128], S_bf[:, h, :],
                qg_h[:, hp, c * 128 + 64:(c + 1) * 128],
                start=False, stop=True),
                reads=["Sbf", f"qg{hp}"], writes=[pok])
            P.op("tensor", lambda e, c=c, h=h, hp=hp: e.matmul(
                P6[:, 0:128], kh_tok[:, h % 2, cp, :],
                v_tok[:, c, h * 128:(h + 1) * 128],
                start=True, stop=True),
                reads=[f"khtok{h % 2}{cp}", f"vtok{c}"], writes=["P6"])
            P.op("scalar", lambda e, c=c, h=h, PO=PO: e.copy(
                out=o_slab[:, c, h, :], in_=PO[:, 0:128]),
                reads=[pok], writes=[f"oslab{c}"])
            P.op("vector", lambda e, c=c, h=h, hp=hp, egv=egv:
                 e.scalar_tensor_tensor(
                     out=S_sb[:, h, :], in0=S_sb[:, h, :],
                     scalar=egv[:, c * 128 + 127:c * 128 + 128],
                     in1=P6[:, 0:128], op0=ALU.mult, op1=ALU.add),
                 reads=["S", f"eg{hp}", "P6"], writes=["S"])
            P.op("vector", lambda e, h=h: e.tensor_copy(
                out=S_bf[:, h, :], in_=S_sb[:, h, :]),
                reads=["S"], writes=["Sbf"])

        def emit_scans(h):
            emit_apart(h, 0)
            for c in range(4):
                if c < 3:
                    emit_apart(h, c + 1)
                emit_opart(h, c)

        for i in range(NH + 3):
            if 0 <= i - 3 < NH:
                emit_scans(i - 3)
            if 0 <= i - 2 < NH:
                emit_prep3(i - 2)
            if 0 <= i - 1 < NH:
                emit_prep1(i - 1)
                emit_prep2(i - 1)
            if i + 1 < NH:
                emit_wload(i + 1)
            if i == 0:
                emit_wload(0)
            if i < NH:
                emit_proj(i)
            if tb < 2 and i < NH:
                emit_some_dequant(3 if i % 8 == 0 else 2)
        if tb == 1:
            emit_some_dequant(999)

        for c in range(4):
            gc = tb * 4 + c
            P.op("sync", lambda e, gc=gc, c=c: e.dma_start(
                out=o_dram[gc].rearrange("h p f -> p h f"),
                in_=o_slab[:, c, :, :]),
                reads=[f"oslab{c}"], writes=[f"od{gc}"], dma=True)
            if tb == 0:
                P.op("sync", lambda e, gc=gc, c=c: e.dma_start(
                    out=ogg_dram[gc].rearrange("h p f -> p h f"),
                    in_=qgg_slab[:, c, :, :]),
                    reads=[f"qgg{c}"], writes=[f"oggd{gc}"], dma=True)

        if tb == 1:
            ccw_ev = P.op(
                "gpsimd", lambda e: e.collective_compute(
                    "AllGather", ALU.bypass,
                    replica_groups=[list(range(8))],
                    ins=[cc_w_in.opt()], outs=[cc_w_out.opt()]),
                deps=[(f"gpsimddma{i}", P.count[f"gpsimddma{i}"], True)
                      for i in range(NDMASEM)],
                sem="cc")

    P.op("sync", lambda e: e.dma_start(
        out=cc_s_in.rearrange("(h p) f -> p h f", p=128), in_=S_sb[:, :, :]),
        reads=["S"], writes=["ccsin"], dma=True)
    ccs_ev = P.op("gpsimd", lambda e: e.collective_compute(
        "AllGather", ALU.bypass,
        replica_groups=[list(range(8))],
        ins=[cc_s_in.opt()], outs=[cc_s_out.opt()]),
        reads=["ccsin"], sem="cc")

    P.barrier()
    P.flush(nc, sems)
    free_group(gA)

    if "B" not in phases:
        for cmx in reversed(ps_ctx):
            cmx.__exit__(None, None, None)
        for cmx in sem_ctx:
            cmx.__exit__(None, None, None)
        return nc

    gB, sbB = make_group()
    wo_sb = sbB("wo_sb", [128, 16, 2048], BF16)
    sgat = sbB("sgat", [128, NH, 128], F32)
    sini = sbB("sini", [128, NH, 128], F32)
    sini_bf = sbB("sini_bf", [128, NH, 128], BF16)
    off_sb = sbB("off_sb", [128, 2, 16, 128], BF16)
    qgg_sb = sbB("qgg_sb", [128, 2, 16, 128], BF16)
    ofull = sbB("ofull", [128, 16, 128], BF16)
    ogw = sbB("ogw", [128, 16, 128], BF16)
    sqsl = sbB("sqsl", [128, 16, 128], BF16)
    sqpart = sbB("sqpart", [128, 128], F32)
    sqpartb = sbB("sqpartb", [128, 128], BF16)
    rstd_sb = sbB("rstd_sb", [128, 2], F32)
    xb_sb = sbB("xb_sb", [128, 2, 2048], F32)
    x2_sb = sbB("x2_sb", [128, 2, 2048], F32)

    P.op("gpsimd", lambda e: e.dma_start(
        out=wo_sb[:, :, :],
        in_=wo16.rearrange("(hc p) f -> p hc f", p=128)),
        reads=["wo16d"], writes=["wo_sb"], dma=True)

    def emit_sini():
        P.op("vector", lambda e: e.memset(sini[:, :, :], 0.0),
             writes=["sini"])
        for r in range(8):
            P.op("sync", lambda e, r=r: e.dma_start(
                out=sgat[:, :, :],
                in_=cc_s_out[r * 2048:(r + 1) * 2048, :]
                .rearrange("(h p) f -> p h f", p=128)),
                deps=[ccs_ev], writes=["sgat"], dma=True)
            P.op("vector", lambda e, r=r: e.scalar_tensor_tensor(
                out=sini[:, :, :], in0=sgat[:, :, :],
                scalar=mask8[:, r:r + 1],
                in1=sini[:, :, :], op0=ALU.mult, op1=ALU.add),
                reads=["sgat", "mask8", "sini"], writes=["sini"])
        P.op("vector", lambda e: e.tensor_copy(out=sini_bf[:, :, :],
                                               in_=sini[:, :, :]),
             reads=["sini"], writes=["sinibf"])

    for gt in list(range(4, NTT)) + list(range(4)):
        par = gt % 2
        if gt == 0:
            emit_sini()
        P.op("sync", lambda e, gt=gt, par=par: e.dma_start(
            out=off_sb[:, par, :, :],
            in_=o_dram[gt].rearrange("h p f -> p h f")),
            writes=[f"off{par}"], dma=True)
        if gt < 4:
            P.op("sync", lambda e, gt=gt, par=par: e.dma_start(
                out=qgg_sb[:, par, :, :],
                in_=ogg_dram[gt].rearrange("h p f -> p h f")),
                writes=[f"qggsb{par}"], dma=True)
        P.op("sync", lambda e, gt=gt, par=par: e.dma_start(
            out=xb_sb[:, par, :], in_=x_in[gt * 128:(gt + 1) * 128, :]),
            writes=[f"xb{par}"], dma=True)
        if gt < 4:
            for h in range(NH):
                pc = h % 2
                PC = [P4, P5][pc]
                pck = f"PB{pc}"
                P.op("tensor", lambda e, h=h, par=par, PC=PC: e.matmul(
                    PC[:, 0:128], sini_bf[:, h, :], qgg_sb[:, par, h, :],
                    start=True, stop=True),
                    reads=["sinibf", f"qggsb{par}"], writes=[pck])
                P.op("vector", lambda e, h=h, par=par, PC=PC: e.tensor_add(
                    out=ofull[:, h, :], in0=off_sb[:, par, h, :],
                    in1=PC[:, 0:128]),
                    reads=[f"off{par}", pck], writes=["ofull"])
                P.op("vector", lambda e, h=h: e.tensor_scalar(
                    out=ogw[:, h, :], in0=ofull[:, h, :],
                    scalar1=gwc[:, h:h + 1], scalar2=None, op0=ALU.mult),
                    reads=["ofull", "gwc"], writes=["ogw"])
            P.op("scalar", lambda e: e.activation(
                out=sqsl[:, :, :], in_=ofull[:, :, :], func=AF.Square),
                reads=["ofull"], writes=["sqsl"])
        else:
            for h in range(NH):
                P.op("vector", lambda e, h=h, par=par: e.tensor_scalar(
                    out=ogw[:, h, :], in0=off_sb[:, par, h, :],
                    scalar1=gwc[:, h:h + 1], scalar2=None, op0=ALU.mult),
                    reads=[f"off{par}", "gwc"], writes=["ogw"])
            P.op("scalar", lambda e, par=par: e.activation(
                out=sqsl[:, :, :], in_=off_sb[:, par, :, :], func=AF.Square),
                reads=[f"off{par}"], writes=["sqsl"])
        P.op("vector", lambda e: e.tensor_reduce(
            out=sqpart[:, :],
            in_=sqsl.rearrange("p h f -> p f h"),
            axis=AX.X, op=ALU.add),
            reads=["sqsl"], writes=["sqpart"])
        P.op("vector", lambda e: e.tensor_copy(out=sqpartb[:, :],
                                               in_=sqpart[:, :]),
             reads=["sqpart"], writes=["sqpartb"])
        P.op("tensor", lambda e: e.matmul(
            P6[:, 0:1], sqpartb[:, :], trimask[:, 127:128],
            start=True, stop=True),
            reads=["sqpartb", "trimask"], writes=["P6B"])
        rc = rstd_sb[:, par:par + 1]
        P.op("scalar", lambda e, rc=rc: e.activation(
            out=rc, in_=P6[:, 0:1], func=AF.Ln, scale=INV_DM, bias=LN_EPS),
            reads=["P6B", "consts"], writes=[f"rstd{par}"])
        P.op("scalar", lambda e, rc=rc: e.activation(
            out=rc, in_=rc, func=AF.Exp, scale=-0.5),
            reads=[f"rstd{par}", "consts"], writes=[f"rstd{par}"])
        if debug_outputs and gt == 0:
            P.op("sync", lambda e: e.dma_start(
                out=dbgb[:, 2048:4096],
                in_=ofull[:, :, :]),
                reads=["ofull"], writes=["dbgb"], dma=True)
            P.op("sync", lambda e: e.dma_start(
                out=dbgf[:, 3072:3200], in_=sqpart[:, :]),
                reads=["sqpart"], writes=["dbgf"], dma=True)
            P.op("sync", lambda e, rc=rc: e.dma_start(
                out=dbgf[:, 3200:3201], in_=rc,
                allow_slow_non_contiguous=True),
                reads=[f"rstd{par}"], writes=["dbgf"], dma=True)
            P.op("sync", lambda e: e.dma_start(
                out=dbgb[:, 0:2048],
                in_=ogw[:, :, :]),
                reads=["ogw"], writes=["dbgb"], dma=True)
        for mcq in range(4):
            pw = mcq % 2
            PW = [P0, P1][pw]
            pwk = f"PW{pw}"
            for hc in range(16):
                P.op("tensor", lambda e, hc=hc, mcq=mcq, PW=PW: e.matmul(
                    PW[:, :], ogw[:, hc, :],
                    wo_sb[:, hc, mcq * 512:(mcq + 1) * 512],
                    start=(hc == 0), stop=(hc == 15)),
                    reads=["ogw", "wo_sb"], writes=[pwk])
            P.op("vector", lambda e, mcq=mcq, par=par, PW=PW, rc=rc:
                 e.scalar_tensor_tensor(
                     out=x2_sb[:, par, mcq * 512:(mcq + 1) * 512],
                     in0=PW[:, :], scalar=rc,
                     in1=xb_sb[:, par, mcq * 512:(mcq + 1) * 512],
                     op0=ALU.mult, op1=ALU.add),
                 reads=[pwk, f"rstd{par}", f"xb{par}"], writes=[f"x2{par}"])
        P.op("sync", lambda e, gt=gt, par=par: e.dma_start(
            out=x2_dram[gt * 128:(gt + 1) * 128, :], in_=x2_sb[:, par, :]),
            reads=[f"x2{par}"], writes=[f"x2d{gt}"], dma=True)
        emit_ln(x2_sb[:, par, :], f"x2{par}", xln_tok[:, 0, :],
                "xlntok", 2)
        P.op("sync", lambda e, gt=gt, par=par: e.dma_start(
            out=ln_dram[gt * 128:(gt + 1) * 128, :],
            in_=xln_tok[:, 0, :]),
            reads=["xlntok"], writes=[f"ln2d{gt}"], dma=True)

    P.barrier()
    P.flush(nc, sems)
    free_group(gB)

    if "C" not in phases:
        for cmx in reversed(ps_ctx):
            cmx.__exit__(None, None, None)
        for cmx in sem_ctx:
            cmx.__exit__(None, None, None)
        return nc

    gC, sbC = make_group()
    xln2_feat = sbC("xln2_feat", [128, 16, 512], BF16)
    hact = sbC("hact", [128, NIC, 512], BF16)
    wgu_sb = sbC("wgu_sb", [128, 2, 4, 1024], BF16)
    wdn_sb = sbC("wdn_sb", [128, 22, 1024], BF16)
    sg_sb = sbC("sg_sb", [128, 2, 512], BF16)
    y_sb = sbC("y_sb", [128, 4, 2048], F32)
    x2e_sb = sbC("x2e_sb", [128, 2, 2048], F32)
    out_sb = sbC("out_sb", [128, 2048], F32)

    for tb in range(4):
        for hc in range(16):
            P.op("sync", lambda e, hc=hc, tb=tb: e.dma_start(
                out=xln2_feat[:, hc, :],
                in_=ln_dram[tb * 512:(tb + 1) * 512,
                            hc * 128:(hc + 1) * 128],
                transpose=True),
                reads=[f"ln2d{tb * 4 + i}" for i in range(4)],
                writes=["x2f"], dma=True)

        for p in range(NPAIR):
            wp = p % 2
            P.op("sync", lambda e, p=p, wp=wp: e.dma_start(
                out=wgu_sb[:, wp, :, :],
                in_=cc_w_out[p * 4 * 128:(p + 1) * 4 * 128, :]
                .rearrange("(g p2) f -> p2 g f", p2=128)),
                deps=[ccw_ev], writes=[f"wgu{wp}"], dma=True)
            pg = p % 2
            PG, PU = [P0, P2][pg], [P1, P3][pg]
            pgk, puk = f"PG{pg}", f"PU{pg}"
            for gu in range(2):
                PP = PG if gu == 0 else PU
                ppk = pgk if gu == 0 else puk
                for hc in range(16):
                    lidx = gu * 16 + hc
                    g, cb = lidx // 8, lidx % 8
                    P.op("tensor",
                         lambda e, hc=hc, g=g, cb=cb, wp=wp, PP=PP: e.matmul(
                             PP[:, :],
                             wgu_sb[:, wp, g, cb * 128:(cb + 1) * 128],
                             xln2_feat[:, hc, :],
                             start=(hc == 0), stop=(hc == 15)),
                         reads=[f"wgu{wp}", "x2f"], writes=[ppk])
            P.op("scalar", lambda e, pg=pg, PG=PG: e.activation(
                out=sg_sb[:, pg, :], in_=PG[:, :], func=AF.Silu),
                reads=[pgk], writes=[f"sg{pg}"])
            P.op("vector", lambda e, p=p, pg=pg, PU=PU: e.tensor_tensor(
                out=hact[:, p, :], in0=sg_sb[:, pg, :], in1=PU[:, :],
                op=ALU.mult),
                reads=[f"sg{pg}", puk], writes=["hact"])

        for mcg in range(4):
            for half in range(2):
                g0 = 176 + 22 * mcg + 11 * half
                P.op("sync", lambda e, g0=g0, half=half: e.dma_start(
                    out=wdn_sb[:, half * 11:(half + 1) * 11, :],
                    in_=cc_w_out[g0 * 128:(g0 + 11) * 128, :]
                    .rearrange("(g p2) f -> p2 g f", p2=128)),
                    deps=[ccw_ev], writes=[f"wdn{half}"], dma=True)
            for tt in range(4):
                pd = tt % 2
                PD = [P4, P5][pd]
                pdk = f"PD{pd}"
                for ic in range(NIC):
                    half = 0 if ic < 22 else 1
                    gl = ic // 2
                    co = (ic % 2) * 512
                    P.op("tensor",
                         lambda e, ic=ic, tt=tt, gl=gl, co=co, PD=PD:
                         e.matmul(
                             PD[:, :], hact[:, ic, tt * 128:(tt + 1) * 128],
                             wdn_sb[:, gl, co:co + 512],
                             start=(ic == 0), stop=(ic == NIC - 1)),
                         reads=["hact", f"wdn{half}"], writes=[pdk])
                P.op("scalar", lambda e, tt=tt, mcg=mcg, PD=PD: e.copy(
                    out=y_sb[:, tt, mcg * 512:(mcg + 1) * 512], in_=PD[:, :]),
                    reads=[pdk], writes=[f"y{tt}"])

        for tt in range(4):
            gt = tb * 4 + tt
            pe = tt % 2
            P.op("sync", lambda e, gt=gt, pe=pe: e.dma_start(
                out=x2e_sb[:, pe, :],
                in_=x2_dram[gt * 128:(gt + 1) * 128, :]),
                writes=[f"x2e{pe}"], dma=True)
            P.op("scalar", lambda e, tt=tt: e.activation(
                out=junk[:, 0:2048], in_=y_sb[:, tt, :], func=AF.Square,
                accum_out=stat[:, 4:5]),
                reads=[f"y{tt}"], writes=["statc", "junk"])
            P.op("scalar", lambda e: e.activation(
                out=stat[:, 4:5], in_=stat[:, 4:5], func=AF.Ln,
                scale=INV_DM, bias=RMS_EPS_FFN),
                reads=["statc", "consts"], writes=["statc"])
            P.op("scalar", lambda e: e.activation(
                out=stat[:, 4:5], in_=stat[:, 4:5], func=AF.Exp, scale=-0.5),
                reads=["statc", "consts"], writes=["statc"])
            P.op("vector", lambda e, tt=tt, pe=pe: e.scalar_tensor_tensor(
                out=out_sb[:, :], in0=y_sb[:, tt, :],
                scalar=stat[:, 4:5], in1=x2e_sb[:, pe, :],
                op0=ALU.mult, op1=ALU.add),
                reads=[f"y{tt}", "statc", f"x2e{pe}"], writes=["outsb"])
            P.op("sync", lambda e, gt=gt: e.dma_start(
                out=y_out[gt * 128:(gt + 1) * 128, :], in_=out_sb[:, :]),
                reads=["outsb"], writes=["yd"], dma=True)

    P.barrier()
    P.flush(nc, sems)
    free_group(gC)
    free_group(gG)

    for cmx in reversed(ps_ctx):
        cmx.__exit__(None, None, None)
    for cmx in sem_ctx:
        cmx.__exit__(None, None, None)
    return nc


def _host_inputs(L0_attn_hidden_states, L0_attn_Wq, L0_attn_Wk, L0_attn_Wi,
                 L0_attn_Wo, L0_attn_g_weight, L0_ffn_gate_up_packed,
                 L0_ffn_gate_up_scales, L0_ffn_down_packed,
                 L0_ffn_down_scales):
    x = np.asarray(L0_attn_hidden_states, dtype=np.float32)
    wq = np.ascontiguousarray(np.asarray(L0_attn_Wq, np.float32))
    wk = np.ascontiguousarray(np.asarray(L0_attn_Wk, np.float32))
    wi = np.ascontiguousarray(np.asarray(L0_attn_Wi, np.float32))
    wo = np.ascontiguousarray(np.asarray(L0_attn_Wo, np.float32))
    gw = np.asarray(L0_attn_g_weight, np.float32)
    gup = np.asarray(L0_ffn_gate_up_packed, np.int32)
    gus = np.asarray(L0_ffn_gate_up_scales, np.float16)
    dnp = np.asarray(L0_ffn_down_packed, np.int32)
    dns = np.asarray(L0_ffn_down_scales, np.float16)

    gwc = np.ascontiguousarray(gw.reshape(NH, 128).T)
    trimask = np.zeros((128, 128), dtype=BF)
    su = np.tril(np.ones((64, 64), np.float32)).T.astype(BF)
    trimask[0:64, 0:64] = su
    trimask[0:64, 64:128] = 1.0
    trimask[64:128, 64:128] = su
    ident = np.eye(128, dtype=BF)
    consts = np.broadcast_to(np.array(CONST_VALS, np.float32),
                             (128, len(CONST_VALS))).copy()

    def tile_slices(gidx):
        if gidx < 1408:
            pair, rem = divmod(gidx, 32)
            gu, hc = divmod(rem, 16)
            r0 = (gu * NPAIR + pair) * 128
            pk = gup[r0:r0 + 128, hc * 16:(hc + 1) * 16]
            sc = gus[hc, r0:r0 + 128]
        else:
            g2 = gidx - 1408
            mcg, rem = divmod(g2, 176)
            ic, mi = divmod(rem, 4)
            mt = mcg * 4 + mi
            pk = dnp[mt * 128:(mt + 1) * 128, ic * 16:(ic + 1) * 16]
            sc = dns[ic, mt * 128:(mt + 1) * 128]
        return pk, sc

    per_core = []
    for c in range(8):
        b, half = c // 2, c % 2
        xs = np.ascontiguousarray(x[b, half * T:(half + 1) * T, :])
        pks = np.empty((TPC, 128, 16), np.int32)
        scs = np.empty((TPC, 128), np.float16)
        for i in range(TPC):
            pk, sc = tile_slices(c * TPC + i)
            pks[i] = pk
            scs[i] = sc
        mask8 = np.zeros((128, 8), np.float32)
        if half == 1:
            mask8[:, c - 1] = 1.0
        per_core.append(dict(
            x=xs, wq=wq, wk=wk, wi=wi, wo=wo,
            gwc=gwc, mask8=mask8, trimask=trimask, ident=ident,
            consts=consts, packedsl=pks,
            scalessl=np.ascontiguousarray(scs.T)))
    return per_core


def kernel(**inputs):
    per_core = _host_inputs(**inputs)
    nc = build_nc()
    res = run_bass_kernel_spmd(nc, per_core, core_ids=list(range(8)))
    out = np.empty((B_, L_, DM), np.float32)
    for c in range(8):
        b, half = c // 2, c % 2
        out[b, half * T:(half + 1) * T, :] = res.results[c]["y"]
    return out


if __name__ == "__main__":
    nc = build_nc()
    print("built ok; instructions:", nc.n_instructions())


# version 22
# speedup vs baseline: 1.2397x; 1.0032x over previous
import numpy as np
import ml_dtypes
import concourse.bass as bass
import concourse.mybir as mybir
from concourse.bass_utils import run_bass_kernel_spmd

F32 = mybir.dt.float32
BF16 = mybir.dt.bfloat16
F16 = mybir.dt.float16
I32 = mybir.dt.int32
U16 = mybir.dt.uint16
AF = mybir.ActivationFunctionType
ALU = mybir.AluOpType
AX = mybir.AxisListType

BF = ml_dtypes.bfloat16

B_, L_, DM = 4, 4096, 2048
NH, DK, DV = 16, 128, 128
SCALE = float(DK) ** -0.5
T = 2048
NBLK = 4
NCH = 16
NTT = 16
FFI = 5632
NPAIR = 44
NIC = 44
NTILES = 2112
TPC = 264
NGRP = NTILES // 8
GPC = TPC // 8

LN_EPS = 1e-5
RMS_EPS_FFN = 1e-6
INV_DM = 1.0 / 2048.0

CONST_VALS = [SCALE, -1.0, 1.0, 2.0, -0.5, INV_DM, LN_EPS, RMS_EPS_FFN]

ENGINES = ("sync", "gpsimd", "tensor", "vector", "scalar")
DMA_QUEUES = ("sync", "gpsimd", "scalar")
NDMASEM = 8
DMA_SEMS = tuple(f"{q}dma{i}" for q in DMA_QUEUES for i in range(NDMASEM))
ALL_SEMS = ENGINES + ("cc",) + DMA_SEMS


class Prog:

    def __init__(self):
        self.items = []
        self.count = {s: 0 for s in ALL_SEMS}
        self.hwm = {e: {s: 0 for s in ALL_SEMS} for e in ENGINES}
        self.last_w = {}
        self.readers = {}
        self.dma_idx = {q: 0 for q in DMA_QUEUES}

    def op(self, eng, fn, reads=(), writes=(), dma=False, deps=(), sem=None):
        dep_evs = []
        for k in reads:
            ev = self.last_w.get(k)
            if ev is not None:
                dep_evs.append(ev)
        for k in writes:
            ev = self.last_w.get(k)
            if ev is not None:
                dep_evs.append(ev)
            dep_evs.extend(self.readers.get(k, ()))
        dep_evs.extend(deps)
        if dma:
            idx = self.dma_idx[eng]
            self.dma_idx[eng] = idx + 1
            slot = idx % NDMASEM
            dmasem = f"{eng}dma{slot}"
            dep_evs.append((dmasem, 16 * (idx // NDMASEM), True))
        wmax = {}
        for (ds, dv, dasync) in dep_evs:
            if ds == eng and eng == "tensor" and not dasync:
                continue
            if self.hwm[eng][ds] >= dv:
                continue
            wmax[ds] = max(wmax.get(ds, 0), dv)
        for ds, dv in wmax.items():
            self.hwm[eng][ds] = dv
        if dma:
            semname = dmasem
        else:
            semname = sem if sem is not None else eng
        inc = 16 if dma else 1
        self.count[semname] += inc
        is_async = dma or (sem is not None)
        ev = (semname, self.count[semname], is_async)
        self.items.append((eng, fn, sorted(wmax.items()), semname, inc))
        for k in reads:
            self.readers.setdefault(k, []).append(ev)
        for k in writes:
            self.last_w[k] = ev
            self.readers[k] = []
        return ev

    def barrier(self):
        snap = {s: self.count[s] for s in ENGINES + DMA_SEMS}
        for e in ENGINES:
            waits = []
            for e2 in ENGINES + DMA_SEMS:
                if e2 == e:
                    continue
                if self.hwm[e][e2] < snap[e2]:
                    waits.append((e2, snap[e2]))
                    self.hwm[e][e2] = snap[e2]
            if waits:
                self.items.append((e, None, waits, e, 0))
        self.last_w = {}
        self.readers = {}

    def flush(self, nc, sems):
        items, self.items = self.items, []
        with nc.Block() as block:
            for eng_name in ENGINES:
                def make(eng_name=eng_name):
                    def body(eng):
                        for (en, fn, waits, semname, inc) in items:
                            if en != eng_name:
                                continue
                            for (se, sv) in waits:
                                eng.wait_ge(sems[se], sv)
                            if fn is not None:
                                fn(eng).then_inc(sems[semname], inc)
                    return body
                getattr(block, eng_name)(make())


def build_nc(debug_outputs=False, phases=("A", "B", "C")):
    nc = bass.Bass("TRN2", target_bir_lowering=False)

    def din(name, shape, dt):
        return nc.dram_tensor(name, shape, dt, kind="ExternalInput").ap()

    x_in = din("x", [T, DM], F32)
    wq_in = din("wq", [DM, DM], F32)
    wk_in = din("wk", [DM, DM], F32)
    wi_in = din("wi", [DM, DM], F32)
    wo_in = din("wo", [DM, DM], F32)
    gwc_in = din("gwc", [128, NH], F32)
    mask8_in = din("mask8", [128, 8], F32)
    trimask_in = din("trimask", [128, 128], BF16)
    ident_in = din("ident", [128, 128], BF16)
    consts_in = din("consts", [128, len(CONST_VALS)], F32)
    packed_in = din("packedsl", [TPC, 128, 16], I32)
    scales_in = din("scalessl", [128, TPC], F16)

    y_out = nc.dram_tensor("y", [T, DM], F32, kind="ExternalOutput").ap()

    dkind = "ExternalOutput" if debug_outputs else "Internal"

    def dscr(name, shape, dt, dbg=False):
        return nc.dram_tensor(name, shape, dt,
                              kind=(dkind if dbg else "Internal")).ap()

    wq16t = dscr("wq16t", [NH, 16, 128, 128], BF16)
    wk16t = dscr("wk16t", [NH, 16, 128, 128], BF16)
    wi16 = dscr("wi16", [DM, DM], BF16)
    wo16 = dscr("wo16", [DM, DM], BF16, dbg=True)
    ln_dram = dscr("ln_dram", [T, DM], BF16, dbg=True)
    ogg_dram = dscr("ogg_dram", [NCH, NH, 128, 128], BF16)
    o_dram = dscr("o_dram", [NCH, NH, 128, 128], BF16, dbg=True)
    x2_dram = dscr("x2_dram", [T, DM], F32, dbg=True)
    dbgf = dscr("dbgf", [128, 4096], F32, dbg=True)
    dbgb = dscr("dbgb", [128, 4096], BF16, dbg=True)
    cc_w_in = nc.dram_tensor("cc_w_in", [GPC * 128, 1024], BF16).ap()
    cc_w_out = nc.dram_tensor("cc_w_out", [NGRP * 128, 1024], BF16,
                              addr_space="Shared").ap()
    cc_s_in = nc.dram_tensor("cc_s_in", [NH * 128, 128], F32).ap()
    cc_s_out = nc.dram_tensor("cc_s_out", [8 * NH * 128, 128], F32,
                              addr_space="Shared").ap()

    P = Prog()

    sems = {}
    sem_ctx = []
    for s in ALL_SEMS:
        cm = nc.semaphore(f"sem_{s}")
        sem_ctx.append(cm)
        sems[s] = cm.__enter__()

    all_ctx = []

    def make_group():
        group = []

        def sb(name, shape, dt):
            cm = nc.sbuf_tensor("sb_" + name, shape, dt)
            group.append(cm)
            all_ctx.append(cm)
            return cm.__enter__()
        return group, sb

    def free_group(group):
        for cm in reversed(group):
            cm.__exit__(None, None, None)
            all_ctx.remove(cm)

    ps_ctx = []

    def psb(name):
        cm = nc.psum_tensor(name, [128, 512], F32)
        ps_ctx.append(cm)
        return cm.__enter__()

    gG, sbG = make_group()
    consts = sbG("consts", [128, len(CONST_VALS)], F32)
    gwc = sbG("gwc", [128, NH], F32)
    mask8 = sbG("mask8", [128, 8], F32)
    trimask = sbG("trimask", [128, 128], BF16)
    ident = sbG("ident", [128, 128], BF16)
    junk = sbG("junk", [128, 2048], BF16)
    junkf = sbG("junkf", [128, 256], F32)
    xln_tok = sbG("xln_tok", [128, 1, 2048], BF16)
    stat = sbG("stat", [128, 16], F32)

    for i, v in enumerate(CONST_VALS):
        nc.const_aps.register(F32, v, consts[:, i:i + 1])

    P0 = psb("P0")
    P1 = psb("P1")
    P2 = psb("P2")
    P3 = psb("P3")
    P4 = psb("P4")
    P5 = psb("P5")
    P6 = psb("P6")
    cm = nc.psum_tensor("P7b", [128, 1024], BF16)
    ps_ctx.append(cm)
    P7b = cm.__enter__()

    P.op("sync", lambda e: e.dma_start(out=consts[:], in_=consts_in),
         writes=["consts"], dma=True)
    P.op("sync", lambda e: e.dma_start(out=gwc[:], in_=gwc_in),
         writes=["gwc"], dma=True)
    P.op("sync", lambda e: e.dma_start(out=mask8[:], in_=mask8_in),
         writes=["mask8"], dma=True)
    P.op("sync", lambda e: e.dma_start(out=trimask[:], in_=trimask_in),
         writes=["trimask"], dma=True)
    P.op("sync", lambda e: e.dma_start(out=ident[:], in_=ident_in),
         writes=["ident"], dma=True)
    P.op("sync", lambda e: e.dma_start(out=scl16[:], in_=scales_in),
         writes=["scl16"], dma=True)
    P.op("vector", lambda e: e.tensor_copy(out=scl[:], in_=scl16[:]),
         reads=["scl16"], writes=["scl"])
    P.op("vector", lambda e: e.memset(S_sb[:], 0.0), writes=["S"])
    P.op("vector", lambda e: e.memset(S_bf[:], 0.0), writes=["Sbf"])
    P.op("vector", lambda e: e.memset(R_run[:], 1.0), writes=["Rrun"])

    wi_r = wi_in.rearrange("(hc p) (dc f) -> dc hc p f", p=128, f=512)
    wi16_r = wi16.rearrange("(hc p) (dc f) -> dc hc p f", p=128, f=512)
    wq_r = wq_in.rearrange("(hc p) (mc f) -> mc hc p f", p=128, f=128)
    wk_r = wk_in.rearrange("(hc p) (mc f) -> mc hc p f", p=128, f=128)

    def cast_wi(dvc):
        P.op("gpsimd", lambda e, dvc=dvc: e.dma_start(
            out=wi16_r[dvc], in_=wi_r[dvc]),
            writes=[f"wi16d{dvc}"], dma=True)

    for dvc in range(4):
        cast_wi(dvc)

    gA, sbA = make_group()
    scl16 = sbA("scl16", [128, TPC], F16)
    scl = sbA("scl", [128, TPC], F32)
    S_sb = sbA("S_sb", [128, NH, 128], F32)
    S_bf = sbA("S_bf", [128, NH, 128], BF16)
    R_run = sbA("R_run", [128, NH], F32)
    xt = sbA("xt", [128, 2, 2048], F32)
    xln_feat = sbA("xln_feat", [128, 16, 512], BF16)
    v_tok = sbA("v_tok", [128, 4, 2048], BF16)
    wi_sb = sbA("wi_sb", [128, 16, 512], BF16)
    wq_sb = sbA("wq_sb", [128, 2, 16, 128], BF16)
    wk_sb = sbA("wk_sb", [128, 2, 16, 128], BF16)
    q_h = sbA("q_h", [128, 4, 512], BF16)
    kraw_h = sbA("kraw_h", [128, 4, 512], F32)
    sp_h = sbA("sp_h", [128, 4, 512], F32)
    gh_h = sbA("gh_h", [128, 4, 512], F32)
    bb_h = sbA("bb_h", [128, 4, 512], F32)
    eg_h = sbA("eg_h", [128, 4, 512], F32)
    qg_h = sbA("qg_h", [128, 4, 512], BF16)
    qb_h = sbA("qb_h", [128, 4, 4, 64], BF16)
    ka_h = sbA("ka_h", [128, 4, 4, 64], BF16)
    kb_h = sbA("kb_h", [128, 4, 4, 64], BF16)
    kba_h = sbA("kba_h", [128, 4, 4, 64], BF16)
    kh_h = sbA("kh_h", [128, 4, 512], BF16)
    kh_tok = sbA("kh_tok", [128, 2, 2, 128], BF16)
    ng_h = sbA("ng_h", [128, 4, 8], F32)
    asb = sbA("asb", [128, 2, 128], BF16)
    zchunk = sbA("zchunk", [128, 128], F32)
    qgg_slab = sbA("qgg_slab", [128, 4, 16, 128], BF16)
    o_slab = sbA("o_slab", [128, 4, 16, 128], BF16)
    dq_pk = sbA("dq_pk", [128, 2, 4, 16], I32)
    dq_nib = sbA("dq_nib", [128, 128, 4], U16)
    dq_bits = sbA("dq_bits", [128, 512], U16)
    dq_sgn = sbA("dq_sgn", [128, 512], U16)
    dq_sgnf = sbA("dq_sgnf", [128, 512], BF16)
    dq_u16 = sbA("dq_u16", [128, 512], U16)
    dq_relu = sbA("dq_relu", [128, 512], BF16)
    dq_min = sbA("dq_min", [128, 512], BF16)
    dq_out = sbA("dq_out", [128, 2, 512], BF16)
    dq_tr = sbA("dq_tr", [128, 2, 128], BF16)

    P.op("vector", lambda e: e.memset(zchunk[:], 0.0), writes=["zchunk"])
    P.op("vector", lambda e: e.memset(P2[64:128, 0:64], 0.0), writes=["PA0"])
    P.op("vector", lambda e: e.memset(P3[64:128, 0:64], 0.0), writes=["PA1"])

    def emit_dequant_dve(bi):
        pk_par = bi % 2
        op_par = bi % 2
        lt0 = 4 * bi
        pk = dq_pk[:, pk_par, :, :]
        P.op("gpsimd",
             lambda e, pk=pk, lt0=lt0: e.dma_start(
                 out=pk, in_=packed_in[lt0:lt0 + 4].rearrange(
                     "t p w -> p t w")),
             writes=[f"dqpk{pk_par}"], dma=True)
        pku = pk.rearrange("p t w -> p (t w)").bitcast(U16)
        for k in range(4):
            P.op("vector",
                 lambda e, k=k, pku=pku: e.tensor_scalar(
                     out=dq_nib[:, :, k], in0=pku, scalar1=4 * k, scalar2=15,
                     op0=ALU.logical_shift_right, op1=ALU.bitwise_and),
                 reads=[f"dqpk{pk_par}"], writes=["dqnib"])
        nib3 = dq_nib[:, :, :]
        P.op("vector", lambda e, nib3=nib3: e.tensor_scalar(
            out=dq_bits[:], in0=nib3, scalar1=7, scalar2=6,
            op0=ALU.bitwise_and, op1=ALU.logical_shift_left),
            reads=["dqnib"], writes=["dqbits"])
        P.op("vector", lambda e: e.tensor_scalar(
            out=dq_u16[:], in0=dq_bits[:], scalar1=0x3F00,
            scalar2=None, op0=ALU.add), reads=["dqbits"], writes=["dqu16"])
        P.op("vector", lambda e, nib3=nib3: e.tensor_scalar(
            out=dq_sgn[:], in0=nib3, scalar1=3, scalar2=1,
            op0=ALU.logical_shift_right, op1=ALU.bitwise_and),
            reads=["dqnib"], writes=["dqsgn"])
        P.op("vector", lambda e: e.tensor_scalar(
            out=dq_sgnf[:], in0=dq_sgn[:], scalar1=-2,
            scalar2=1, op0=ALU.mult, op1=ALU.add),
            reads=["dqsgn"], writes=["dqsgnf"])
        fmv = dq_u16[:].bitcast(BF16)
        P.op("scalar", lambda e, fmv=fmv: e.activation(
            out=dq_relu[:], in_=fmv, func=AF.Relu, bias=-1.0,
            scale=2.0),
            reads=["dqu16", "consts"], writes=["dqrelu"])
        P.op("vector", lambda e, fmv=fmv: e.tensor_tensor(
            out=dq_min[:], in0=dq_relu[:], in1=fmv,
            op=ALU.min),
            reads=["dqrelu", "dqu16"], writes=["dqmin"])
        for j in range(4):
            lt = lt0 + j
            P.op("vector",
                 lambda e, j=j, lt=lt, op_par=op_par: e.scalar_tensor_tensor(
                     out=dq_out[:, op_par, j * 128:(j + 1) * 128],
                     in0=dq_min[:, j * 128:(j + 1) * 128],
                     scalar=scl[:, lt:lt + 1],
                     in1=dq_sgnf[:, j * 128:(j + 1) * 128],
                     op0=ALU.mult, op1=ALU.mult),
                 reads=["dqmin", "dqsgnf", "scl"], writes=[f"dqout{op_par}"])

    def emit_dequant_tr(bi):
        op_par = bi % 2
        lt0 = 4 * bi
        for j in range(4):
            lt = lt0 + j
            tp = j % 2
            P.op("tensor", lambda e, j=j, op_par=op_par: e.transpose(
                P7b[:, 0:128], dq_out[:, op_par, j * 128:(j + 1) * 128],
                ident[:]),
                reads=[f"dqout{op_par}", "ident"], writes=["P7b"])
            P.op("scalar", lambda e, tp=tp: e.copy(
                out=dq_tr[:, tp, :], in_=P7b[:, 0:128]),
                reads=["P7b"], writes=[f"dqtr{tp}"])
            g, cb = lt // 8, lt % 8
            P.op("gpsimd", lambda e, g=g, cb=cb, tp=tp: e.dma_start(
                out=cc_w_in[g * 128:(g + 1) * 128, cb * 128:(cb + 1) * 128],
                in_=dq_tr[:, tp, :]),
                reads=[f"dqtr{tp}"], dma=True)

    dq_iter = iter(range(TPC // 4))

    dq_pending = []

    def emit_some_dequant(n):
        for _ in range(n):
            bi = next(dq_iter, None)
            if bi is None:
                while dq_pending:
                    emit_dequant_tr(dq_pending.pop(0))
                return
            emit_dequant_dve(bi)
            dq_pending.append(bi)
            if len(dq_pending) > 1:
                emit_dequant_tr(dq_pending.pop(0))

    def emit_ln(xin_ap, xkey, out_ap, okey, c0):
        m_c = stat[:, c0:c0 + 1]
        v_c = stat[:, c0 + 1:c0 + 2]
        km, kv = f"stat{c0}", f"stat{c0 + 1}"
        P.op("vector", lambda e: e.reduce_sum(m_c, xin_ap, axis=AX.X),
             reads=[xkey], writes=[km])
        P.op("scalar", lambda e: e.activation(
            out=junk[:, 0:2048], in_=xin_ap, func=AF.Square, accum_out=v_c),
            reads=[xkey], writes=[kv, "junk"])
        P.op("vector", lambda e: e.tensor_scalar(
            out=m_c, in0=m_c, scalar1=INV_DM, scalar2=None, op0=ALU.mult),
            reads=[km], writes=[km])
        P.op("vector", lambda e: e.tensor_scalar(
            out=v_c, in0=v_c, scalar1=INV_DM, scalar2=None, op0=ALU.mult),
            reads=[kv], writes=[kv])
        P.op("vector", lambda e: e.scalar_tensor_tensor(
            out=v_c, in0=m_c, scalar=m_c, in1=v_c,
            op0=ALU.mult, op1=ALU.subtract),
            reads=[km, kv], writes=[kv])
        P.op("vector", lambda e: e.tensor_scalar(
            out=v_c, in0=v_c, scalar1=-1.0, scalar2=None, op0=ALU.mult),
            reads=[kv], writes=[kv])
        P.op("scalar", lambda e: e.activation(
            out=v_c, in_=v_c, func=AF.Ln, bias=LN_EPS),
            reads=[kv, "consts"], writes=[kv])
        P.op("scalar", lambda e: e.activation(
            out=v_c, in_=v_c, func=AF.Exp, scale=-0.5),
            reads=[kv, "consts"], writes=[kv])
        P.op("vector", lambda e: e.tensor_scalar(
            out=out_ap, in0=xin_ap, scalar1=m_c, scalar2=v_c,
            op0=ALU.subtract, op1=ALU.mult),
            reads=[xkey, km, kv], writes=[okey])

    ccw_ev = None
    for tb in range(NBLK):
        if tb == 3:
            P.op("gpsimd", lambda e: e.dma_start(out=wo16, in_=wo_in),
                 writes=["wo16d"], dma=True)
        for tt in range(4):
            gt = tb * 4 + tt
            par = gt % 2
            P.op("sync", lambda e, gt=gt, par=par: e.dma_start(
                out=xt[:, par, :], in_=x_in[gt * 128:(gt + 1) * 128, :]),
                writes=[f"xt{par}"], dma=True)
            emit_ln(xt[:, par, :], f"xt{par}", xln_tok[:, 0, :],
                    "xlntok", 0)
            P.op("sync", lambda e, gt=gt, par=par: e.dma_start(
                out=ln_dram[gt * 128:(gt + 1) * 128, :],
                in_=xln_tok[:, 0, :]),
                reads=["xlntok"], writes=[f"lnd{gt}"], dma=True)
        for hc in range(16):
            P.op("sync", lambda e, hc=hc, tb=tb: e.dma_start(
                out=xln_feat[:, hc, :],
                in_=ln_dram[tb * 512:(tb + 1) * 512,
                            hc * 128:(hc + 1) * 128],
                transpose=True),
                reads=[f"lnd{tb * 4 + i}" for i in range(4)],
                writes=["xlf"], dma=True)

        for dvc in range(4):
            P.op("gpsimd", lambda e, dvc=dvc: e.dma_start(
                out=wi_sb[:, :, :],
                in_=wi16_r[dvc].rearrange("hc p f -> p hc f")),
                reads=[f"wi16d{dvc}"], writes=["wi_sb"], dma=True)
            for c in range(4):
                pp = [P0, P1][c % 2]
                pkey = f"P{c % 2}"
                for hc in range(16):
                    P.op("tensor", lambda e, pp=pp, hc=hc, c=c:
                         e.matmul(pp[:, :],
                                  xln_feat[:, hc, c * 128:(c + 1) * 128],
                                  wi_sb[:, hc, :],
                                  start=(hc == 0), stop=(hc == 15)),
                         reads=["xlf", "wi_sb"], writes=[pkey])
                P.op("scalar", lambda e, pp=pp, c=c, dvc=dvc: e.copy(
                    out=v_tok[:, c, dvc * 512:(dvc + 1) * 512], in_=pp[:, :]),
                    reads=[pkey], writes=[f"vtok{c}"])

        def emit_wload(h):
            wp2 = h % 2
            if tb == 0:
                P.op("gpsimd", lambda e, h=h, wp2=wp2: e.dma_start(
                    out=wq_sb[:, wp2, :, :],
                    in_=wq_r[h].rearrange("hc p f -> p hc f")),
                    writes=[f"wq_sb{wp2}"], dma=True)
                P.op("sync", lambda e, h=h, wp2=wp2: e.dma_start(
                    out=wq16t[h].rearrange("hc p f -> p hc f"),
                    in_=wq_sb[:, wp2, :, :]),
                    reads=[f"wq_sb{wp2}"], writes=[f"wq16d{h}"], dma=True)
                P.op("gpsimd", lambda e, h=h, wp2=wp2: e.dma_start(
                    out=wk_sb[:, wp2, :, :],
                    in_=wk_r[h].rearrange("hc p f -> p hc f")),
                    writes=[f"wk_sb{wp2}"], dma=True)
                P.op("sync", lambda e, h=h, wp2=wp2: e.dma_start(
                    out=wk16t[h].rearrange("hc p f -> p hc f"),
                    in_=wk_sb[:, wp2, :, :]),
                    reads=[f"wk_sb{wp2}"], writes=[f"wk16d{h}"], dma=True)
            else:
                P.op("sync", lambda e, h=h, wp2=wp2: e.dma_start(
                    out=wq_sb[:, wp2, :, :],
                    in_=wq16t[h].rearrange("hc p f -> p hc f")),
                    reads=[f"wq16d{h}"], writes=[f"wq_sb{wp2}"], dma=True)
                P.op("sync", lambda e, h=h, wp2=wp2: e.dma_start(
                    out=wk_sb[:, wp2, :, :],
                    in_=wk16t[h].rearrange("hc p f -> p hc f")),
                    reads=[f"wk16d{h}"], writes=[f"wk_sb{wp2}"], dma=True)

        def emit_proj(h):
            hp = h % 4
            wp2 = h % 2
            for hc in range(16):
                P.op("tensor", lambda e, hc=hc, wp2=wp2: e.matmul(
                    P0[:, :], wq_sb[:, wp2, hc, :], xln_feat[:, hc, :],
                    start=(hc == 0), stop=(hc == 15)),
                    reads=[f"wq_sb{wp2}", "xlf"], writes=["P0"])
            P.op("scalar", lambda e, hp=hp: e.activation(
                out=q_h[:, hp, :], in_=P0[:, :], func=AF.Relu, scale=SCALE),
                reads=["P0", "consts"], writes=[f"qh{hp}"])
            for hc in range(16):
                P.op("tensor", lambda e, hc=hc, wp2=wp2: e.matmul(
                    P1[:, :], wk_sb[:, wp2, hc, :], xln_feat[:, hc, :],
                    start=(hc == 0), stop=(hc == 15)),
                    reads=[f"wk_sb{wp2}", "xlf"], writes=["P1"])

        def emit_prep1(h):
            hp = h % 4
            spv = sp_h[:, hp, :]
            P.op("scalar", lambda e, spv=spv: e.activation(
                out=spv, in_=P1[:, :], func=AF.Abs),
                reads=["P1"], writes=[f"sp{hp}"])
            P.op("scalar", lambda e, hp=hp: e.copy(
                out=kraw_h[:, hp, :], in_=P1[:, :]),
                reads=["P1"], writes=[f"krawh{hp}"])
            P.op("scalar", lambda e, spv=spv: e.activation(
                out=spv, in_=spv, func=AF.Exp, scale=-1.0),
                reads=[f"sp{hp}", "consts"], writes=[f"sp{hp}"])
            P.op("scalar", lambda e, spv=spv: e.activation(
                out=spv, in_=spv, func=AF.Ln, bias=1.0),
                reads=[f"sp{hp}", "consts"], writes=[f"sp{hp}"])

        def emit_prep2(h):
            hp = h % 4
            kr = kraw_h[:, hp, :]
            spv = sp_h[:, hp, :]
            ghv = gh_h[:, hp, :]
            bbv = bb_h[:, hp, :]
            P.op("vector", lambda e, kr=kr, bbv=bbv: e.tensor_scalar(
                out=bbv, in0=kr, scalar1=0.0, scalar2=None, op0=ALU.max),
                reads=[f"krawh{hp}"], writes=[f"bb{hp}"])
            P.op("vector", lambda e, spv=spv, bbv=bbv: e.tensor_add(
                out=spv, in0=spv, in1=bbv),
                reads=[f"sp{hp}", f"bb{hp}"], writes=[f"sp{hp}"])
            for c in range(4):
                P.op("vector", lambda e, c=c, spv=spv, ghv=ghv:
                     e.tensor_tensor_scan(
                         out=ghv[:, c * 128:(c + 1) * 128],
                         data0=spv[:, c * 128:(c + 1) * 128],
                         data1=zchunk[:, :],
                         initial=0.0, op0=ALU.add, op1=ALU.add),
                     reads=[f"sp{hp}", "zchunk"], writes=[f"gh{hp}"])
            P.op("vector", lambda e, kr=kr, spv=spv, bbv=bbv: e.tensor_sub(
                out=bbv, in0=kr, in1=spv),
                reads=[f"krawh{hp}", f"sp{hp}"], writes=[f"bb{hp}"])
            P.op("vector", lambda e, bbv=bbv, ghv=ghv: e.tensor_add(
                out=bbv, in0=bbv, in1=ghv),
                reads=[f"bb{hp}", f"gh{hp}"], writes=[f"bb{hp}"])
            P.op("vector", lambda e, hp=hp, ghv=ghv: e.tensor_scalar(
                out=ng_h[:, hp, 0:8:2],
                in0=ghv[:, 63:512:128],
                scalar1=-1.0, scalar2=None, op0=ALU.mult),
                reads=[f"gh{hp}"], writes=[f"ng{hp}"])
            P.op("vector", lambda e, hp=hp, ghv=ghv: e.tensor_scalar(
                out=ng_h[:, hp, 1:8:2],
                in0=ghv[:, 127:512:128],
                scalar1=-1.0, scalar2=None, op0=ALU.mult),
                reads=[f"gh{hp}"], writes=[f"ng{hp}"])

        def emit_prep3(h):
            hp = h % 4
            ghv = gh_h[:, hp, :]
            bbv = bb_h[:, hp, :]
            egv = eg_h[:, hp, :]
            P.op("scalar", lambda e, ghv=ghv, egv=egv: e.activation(
                out=egv, in_=ghv, func=AF.Exp, scale=-1.0),
                reads=[f"gh{hp}", "consts"], writes=[f"eg{hp}"])
            P.op("vector", lambda e, hp=hp, egv=egv: e.tensor_mul(
                out=qg_h[:, hp, :], in0=q_h[:, hp, :], in1=egv),
                reads=[f"qh{hp}", f"eg{hp}"], writes=[f"qg{hp}"])
            for c in range(4):
                P.op("scalar", lambda e, c=c, hp=hp, bbv=bbv: e.activation(
                    out=ka_h[:, hp, c, :],
                    in_=bbv[:, c * 128:c * 128 + 64],
                    func=AF.Exp),
                    reads=[f"bb{hp}"], writes=[f"ka{hp}"])
            for c in range(4):
                ng63 = ng_h[:, hp, 2 * c:2 * c + 1]
                ng127 = ng_h[:, hp, 2 * c + 1:2 * c + 2]
                g63 = gh_h[:, hp, c * 128 + 63:c * 128 + 64]
                P.op("scalar", lambda e, c=c, hp=hp, bbv=bbv, ng63=ng63:
                     e.activation(
                         out=kb_h[:, hp, c, :],
                         in_=bbv[:, c * 128 + 64:(c + 1) * 128],
                         func=AF.Exp, bias=ng63),
                     reads=[f"bb{hp}", f"ng{hp}"], writes=[f"kb{hp}"])
                P.op("scalar", lambda e, c=c, hp=hp, bbv=bbv, ng63=ng63:
                     e.activation(
                         out=kba_h[:, hp, c, :],
                         in_=bbv[:, c * 128:c * 128 + 64],
                         func=AF.Exp, bias=ng63),
                     reads=[f"bb{hp}", f"ng{hp}"], writes=[f"kba{hp}"])
                P.op("scalar", lambda e, c=c, hp=hp, bbv=bbv, ng127=ng127:
                     e.activation(
                         out=kh_h[:, hp, c * 128:(c + 1) * 128],
                         in_=bbv[:, c * 128:(c + 1) * 128],
                         func=AF.Exp, bias=ng127),
                     reads=[f"bb{hp}", f"ng{hp}"], writes=[f"kh{hp}"])
                P.op("scalar", lambda e, c=c, hp=hp, ghv=ghv, g63=g63:
                     e.activation(
                         out=junkf[:, c * 64:(c + 1) * 64],
                         in_=ghv[:, c * 128 + 64:(c + 1) * 128],
                         func=AF.Exp, scale=-1.0, bias=g63),
                     reads=[f"gh{hp}", "consts"], writes=["jf"])
                P.op("vector", lambda e, c=c, hp=hp: e.tensor_mul(
                    out=qb_h[:, hp, c, :],
                    in0=q_h[:, hp, c * 128 + 64:(c + 1) * 128],
                    in1=junkf[:, c * 64:(c + 1) * 64]),
                    reads=[f"qh{hp}", "jf"], writes=[f"qb{hp}"])
            if tb == 0:
                for c in range(4):
                    P.op("vector", lambda e, c=c, hp=hp, h=h: e.tensor_scalar(
                        out=qgg_slab[:, c, h, :],
                        in0=qg_h[:, hp, c * 128:(c + 1) * 128],
                        scalar1=R_run[:, h:h + 1], scalar2=None,
                        op0=ALU.mult),
                        reads=[f"qg{hp}", "Rrun"], writes=[f"qgg{c}"])
                    P.op("vector", lambda e, c=c, hp=hp, h=h, egv=egv:
                         e.tensor_scalar(
                             out=R_run[:, h:h + 1], in0=R_run[:, h:h + 1],
                             scalar1=egv[:, c * 128 + 127:c * 128 + 128],
                             scalar2=None, op0=ALU.mult),
                         reads=["Rrun", f"eg{hp}"], writes=["Rrun"])

        def emit_apart(h, c):
            hp = h % 4
            cp = c % 2
            PA = [P2, P3][cp]
            pak = f"PA{cp}"
            P.op("tensor", lambda e, c=c, hp=hp, cp=cp: e.transpose(
                P7b[:, 0:128], kh_h[:, hp, c * 128:(c + 1) * 128],
                ident[:]),
                reads=[f"kh{hp}", "ident"], writes=["P7b"])
            P.op("vector", lambda e, hp=hp, cp=cp, h=h: e.tensor_copy(
                out=kh_tok[:, h % 2, cp, :], in_=P7b[:, 0:128]),
                reads=["P7b"], writes=[f"khtok{h % 2}{cp}"])
            P.op("tensor", lambda e, c=c, hp=hp, PA=PA: e.matmul(
                PA[0:64, 0:64], ka_h[:, hp, c, :],
                qg_h[:, hp, c * 128:c * 128 + 64],
                start=True, stop=True),
                reads=[f"ka{hp}", f"qg{hp}"], writes=[pak])
            P.op("tensor", lambda e, c=c, hp=hp, PA=PA: e.matmul(
                PA[0:64, 64:128], kba_h[:, hp, c, :], qb_h[:, hp, c, :],
                start=True, stop=True),
                reads=[f"kba{hp}", f"qb{hp}"], writes=[pak])
            P.op("tensor", lambda e, c=c, hp=hp, PA=PA: e.matmul(
                PA[64:128, 64:128], kb_h[:, hp, c, :], qb_h[:, hp, c, :],
                start=True, stop=True),
                reads=[f"kb{hp}", f"qb{hp}"], writes=[pak])
            P.op("vector", lambda e, cp=cp, PA=PA: e.tensor_tensor(
                out=asb[:, cp, :], in0=PA[:, 0:128],
                in1=trimask[:], op=ALU.mult),
                reads=[pak, "trimask"], writes=[f"asb{cp}"])

        def emit_opart(h, c):
            hp = h % 4
            cp = c % 2
            PO = [P4, P5][cp]
            pok = f"PO{cp}"
            egv = eg_h[:, hp, :]
            P.op("tensor", lambda e, c=c, h=h, cp=cp, PO=PO: e.matmul(
                PO[:, 0:64], v_tok[:, c, h * 128:(h + 1) * 128],
                asb[:, cp, 0:64], start=True, stop=False),
                reads=[f"vtok{c}", f"asb{cp}"], writes=[pok])
            P.op("tensor", lambda e, c=c, h=h, hp=hp, PO=PO: e.matmul(
                PO[:, 0:64], S_bf[:, h, :],
                qg_h[:, hp, c * 128:c * 128 + 64],
                start=False, stop=True),
                reads=["Sbf", f"qg{hp}"], writes=[pok])
            P.op("tensor", lambda e, c=c, h=h, cp=cp, PO=PO: e.matmul(
                PO[:, 64:128], v_tok[:, c, h * 128:(h + 1) * 128],
                asb[:, cp, 64:128], start=True, stop=False),
                reads=[f"vtok{c}", f"asb{cp}"], writes=[pok])
            P.op("tensor", lambda e, c=c, h=h, hp=hp, PO=PO: e.matmul(
                PO[:, 64:128], S_bf[:, h, :],
                qg_h[:, hp, c * 128 + 64:(c + 1) * 128],
                start=False, stop=True),
                reads=["Sbf", f"qg{hp}"], writes=[pok])
            P.op("tensor", lambda e, c=c, h=h, hp=hp: e.matmul(
                P6[:, 0:128], kh_tok[:, h % 2, cp, :],
                v_tok[:, c, h * 128:(h + 1) * 128],
                start=True, stop=True),
                reads=[f"khtok{h % 2}{cp}", f"vtok{c}"], writes=["P6"])
            P.op("scalar", lambda e, c=c, h=h, PO=PO: e.copy(
                out=o_slab[:, c, h, :], in_=PO[:, 0:128]),
                reads=[pok], writes=[f"oslab{c}"])
            P.op("vector", lambda e, c=c, h=h, hp=hp, egv=egv:
                 e.scalar_tensor_tensor(
                     out=S_sb[:, h, :], in0=S_sb[:, h, :],
                     scalar=egv[:, c * 128 + 127:c * 128 + 128],
                     in1=P6[:, 0:128], op0=ALU.mult, op1=ALU.add),
                 reads=["S", f"eg{hp}", "P6"], writes=["S"])
            P.op("vector", lambda e, h=h: e.tensor_copy(
                out=S_bf[:, h, :], in_=S_sb[:, h, :]),
                reads=["S"], writes=["Sbf"])

        def emit_scans(h):
            emit_apart(h, 0)
            for c in range(4):
                if c < 3:
                    emit_apart(h, c + 1)
                emit_opart(h, c)

        for i in range(NH + 3):
            if 0 <= i - 3 < NH:
                emit_scans(i - 3)
            if 0 <= i - 2 < NH:
                emit_prep3(i - 2)
            if 0 <= i - 1 < NH:
                emit_prep1(i - 1)
                emit_prep2(i - 1)
            if i + 1 < NH:
                emit_wload(i + 1)
            if i == 0:
                emit_wload(0)
            if i < NH:
                emit_proj(i)
            if tb < 2 and i < NH:
                emit_some_dequant(3 if i % 8 == 0 else 2)
        if tb == 1:
            emit_some_dequant(999)

        for c in range(4):
            gc = tb * 4 + c
            P.op("sync", lambda e, gc=gc, c=c: e.dma_start(
                out=o_dram[gc].rearrange("h p f -> p h f"),
                in_=o_slab[:, c, :, :]),
                reads=[f"oslab{c}"], writes=[f"od{gc}"], dma=True)
            if tb == 0:
                P.op("sync", lambda e, gc=gc, c=c: e.dma_start(
                    out=ogg_dram[gc].rearrange("h p f -> p h f"),
                    in_=qgg_slab[:, c, :, :]),
                    reads=[f"qgg{c}"], writes=[f"oggd{gc}"], dma=True)

        if tb == 1:
            ccw_ev = P.op(
                "gpsimd", lambda e: e.collective_compute(
                    "AllGather", ALU.bypass,
                    replica_groups=[list(range(8))],
                    ins=[cc_w_in.opt()], outs=[cc_w_out.opt()]),
                deps=[(f"gpsimddma{i}", P.count[f"gpsimddma{i}"], True)
                      for i in range(NDMASEM)],
                sem="cc")

    P.op("sync", lambda e: e.dma_start(
        out=cc_s_in.rearrange("(h p) f -> p h f", p=128), in_=S_sb[:, :, :]),
        reads=["S"], writes=["ccsin"], dma=True)
    ccs_ev = P.op("gpsimd", lambda e: e.collective_compute(
        "AllGather", ALU.bypass,
        replica_groups=[list(range(8))],
        ins=[cc_s_in.opt()], outs=[cc_s_out.opt()]),
        reads=["ccsin"], sem="cc")

    P.barrier()
    P.flush(nc, sems)
    free_group(gA)

    if "B" not in phases:
        for cmx in reversed(ps_ctx):
            cmx.__exit__(None, None, None)
        for cmx in sem_ctx:
            cmx.__exit__(None, None, None)
        return nc

    gB, sbB = make_group()
    wo_sb = sbB("wo_sb", [128, 16, 2048], BF16)
    sgat = sbB("sgat", [128, NH, 128], F32)
    sini = sbB("sini", [128, NH, 128], F32)
    sini_bf = sbB("sini_bf", [128, NH, 128], BF16)
    off_sb = sbB("off_sb", [128, 2, 16, 128], BF16)
    qgg_sb = sbB("qgg_sb", [128, 2, 16, 128], BF16)
    ofull = sbB("ofull", [128, 2, 16, 128], BF16)
    ogw = sbB("ogw", [128, 2, 16, 128], BF16)
    sqsl = sbB("sqsl", [128, 2, 16, 128], BF16)
    sqpart = sbB("sqpart", [128, 2, 128], F32)
    sqpartb = sbB("sqpartb", [128, 2, 128], BF16)
    rstd_sb = sbB("rstd_sb", [128, 2], F32)
    xb_sb = sbB("xb_sb", [128, 2, 2048], F32)
    x2_sb = sbB("x2_sb", [128, 2, 2048], F32)

    P.op("gpsimd", lambda e: e.dma_start(
        out=wo_sb[:, :, :],
        in_=wo16.rearrange("(hc p) f -> p hc f", p=128)),
        reads=["wo16d"], writes=["wo_sb"], dma=True)

    def emit_sini():
        P.op("vector", lambda e: e.memset(sini[:, :, :], 0.0),
             writes=["sini"])
        for r in range(8):
            P.op("sync", lambda e, r=r: e.dma_start(
                out=sgat[:, :, :],
                in_=cc_s_out[r * 2048:(r + 1) * 2048, :]
                .rearrange("(h p) f -> p h f", p=128)),
                deps=[ccs_ev], writes=["sgat"], dma=True)
            P.op("vector", lambda e, r=r: e.scalar_tensor_tensor(
                out=sini[:, :, :], in0=sgat[:, :, :],
                scalar=mask8[:, r:r + 1],
                in1=sini[:, :, :], op0=ALU.mult, op1=ALU.add),
                reads=["sgat", "mask8", "sini"], writes=["sini"])
        P.op("vector", lambda e: e.tensor_copy(out=sini_bf[:, :, :],
                                               in_=sini[:, :, :]),
             reads=["sini"], writes=["sinibf"])

    for gt in list(range(4, NTT)) + list(range(4)):
        par = gt % 2
        if gt == 0:
            emit_sini()
        P.op("sync", lambda e, gt=gt, par=par: e.dma_start(
            out=off_sb[:, par, :, :],
            in_=o_dram[gt].rearrange("h p f -> p h f")),
            writes=[f"off{par}"], dma=True)
        if gt < 4:
            P.op("sync", lambda e, gt=gt, par=par: e.dma_start(
                out=qgg_sb[:, par, :, :],
                in_=ogg_dram[gt].rearrange("h p f -> p h f")),
                writes=[f"qggsb{par}"], dma=True)
        P.op("sync", lambda e, gt=gt, par=par: e.dma_start(
            out=xb_sb[:, par, :], in_=x_in[gt * 128:(gt + 1) * 128, :]),
            writes=[f"xb{par}"], dma=True)
        if gt < 4:
            for h in range(NH):
                pc = h % 2
                PC = [P4, P5][pc]
                pck = f"PB{pc}"
                P.op("tensor", lambda e, h=h, par=par, PC=PC: e.matmul(
                    PC[:, 0:128], sini_bf[:, h, :], qgg_sb[:, par, h, :],
                    start=True, stop=True),
                    reads=["sinibf", f"qggsb{par}"], writes=[pck])
                P.op("vector", lambda e, h=h, par=par, PC=PC: e.tensor_add(
                    out=ofull[:, par, h, :], in0=off_sb[:, par, h, :],
                    in1=PC[:, 0:128]),
                    reads=[f"off{par}", pck], writes=[f"ofull{par}"])
                P.op("vector", lambda e, h=h, par=par: e.tensor_scalar(
                    out=ogw[:, par, h, :], in0=ofull[:, par, h, :],
                    scalar1=gwc[:, h:h + 1], scalar2=None, op0=ALU.mult),
                    reads=[f"ofull{par}", "gwc"], writes=[f"ogw{par}"])
            P.op("scalar", lambda e, par=par: e.activation(
                out=sqsl[:, par, :, :], in_=ofull[:, par, :, :],
                func=AF.Square),
                reads=[f"ofull{par}"], writes=[f"sqsl{par}"])
        else:
            for h in range(NH):
                P.op("vector", lambda e, h=h, par=par: e.tensor_scalar(
                    out=ogw[:, par, h, :], in0=off_sb[:, par, h, :],
                    scalar1=gwc[:, h:h + 1], scalar2=None, op0=ALU.mult),
                    reads=[f"off{par}", "gwc"], writes=[f"ogw{par}"])
            P.op("scalar", lambda e, par=par: e.activation(
                out=sqsl[:, par, :, :], in_=off_sb[:, par, :, :],
                func=AF.Square),
                reads=[f"off{par}"], writes=[f"sqsl{par}"])
        P.op("vector", lambda e, par=par: e.tensor_reduce(
            out=sqpart[:, par, :],
            in_=sqsl[:, par, :, :].rearrange("p h f -> p f h"),
            axis=AX.X, op=ALU.add),
            reads=[f"sqsl{par}"], writes=[f"sqpart{par}"])
        P.op("vector", lambda e, par=par: e.tensor_copy(
            out=sqpartb[:, par, :], in_=sqpart[:, par, :]),
            reads=[f"sqpart{par}"], writes=[f"sqpartb{par}"])
        P.op("tensor", lambda e, par=par: e.matmul(
            P6[:, 0:1], sqpartb[:, par, :], trimask[:, 127:128],
            start=True, stop=True),
            reads=[f"sqpartb{par}", "trimask"], writes=["P6B"])
        rc = rstd_sb[:, par:par + 1]
        P.op("scalar", lambda e, rc=rc: e.activation(
            out=rc, in_=P6[:, 0:1], func=AF.Ln, scale=INV_DM, bias=LN_EPS),
            reads=["P6B", "consts"], writes=[f"rstd{par}"])
        P.op("scalar", lambda e, rc=rc: e.activation(
            out=rc, in_=rc, func=AF.Exp, scale=-0.5),
            reads=[f"rstd{par}", "consts"], writes=[f"rstd{par}"])
        if debug_outputs and gt == 0:
            P.op("sync", lambda e: e.dma_start(
                out=dbgb[:, 2048:4096],
                in_=ofull[:, :, :]),
                reads=["ofull"], writes=["dbgb"], dma=True)
            P.op("sync", lambda e: e.dma_start(
                out=dbgf[:, 3072:3200], in_=sqpart[:, :]),
                reads=["sqpart"], writes=["dbgf"], dma=True)
            P.op("sync", lambda e, rc=rc: e.dma_start(
                out=dbgf[:, 3200:3201], in_=rc,
                allow_slow_non_contiguous=True),
                reads=[f"rstd{par}"], writes=["dbgf"], dma=True)
            P.op("sync", lambda e: e.dma_start(
                out=dbgb[:, 0:2048],
                in_=ogw[:, :, :]),
                reads=["ogw"], writes=["dbgb"], dma=True)
        for mcq in range(4):
            pw = mcq % 2
            PW = [P0, P1][pw]
            pwk = f"PW{pw}"
            for hc in range(16):
                P.op("tensor", lambda e, hc=hc, mcq=mcq, par=par, PW=PW:
                     e.matmul(
                         PW[:, :], ogw[:, par, hc, :],
                         wo_sb[:, hc, mcq * 512:(mcq + 1) * 512],
                         start=(hc == 0), stop=(hc == 15)),
                     reads=[f"ogw{par}", "wo_sb"], writes=[pwk])
            P.op("vector", lambda e, mcq=mcq, par=par, PW=PW, rc=rc:
                 e.scalar_tensor_tensor(
                     out=x2_sb[:, par, mcq * 512:(mcq + 1) * 512],
                     in0=PW[:, :], scalar=rc,
                     in1=xb_sb[:, par, mcq * 512:(mcq + 1) * 512],
                     op0=ALU.mult, op1=ALU.add),
                 reads=[pwk, f"rstd{par}", f"xb{par}"], writes=[f"x2{par}"])
        P.op("sync", lambda e, gt=gt, par=par: e.dma_start(
            out=x2_dram[gt * 128:(gt + 1) * 128, :], in_=x2_sb[:, par, :]),
            reads=[f"x2{par}"], writes=[f"x2d{gt}"], dma=True)
        emit_ln(x2_sb[:, par, :], f"x2{par}", xln_tok[:, 0, :],
                "xlntok", 2)
        P.op("sync", lambda e, gt=gt, par=par: e.dma_start(
            out=ln_dram[gt * 128:(gt + 1) * 128, :],
            in_=xln_tok[:, 0, :]),
            reads=["xlntok"], writes=[f"ln2d{gt}"], dma=True)

    P.barrier()
    P.flush(nc, sems)
    free_group(gB)

    if "C" not in phases:
        for cmx in reversed(ps_ctx):
            cmx.__exit__(None, None, None)
        for cmx in sem_ctx:
            cmx.__exit__(None, None, None)
        return nc

    gC, sbC = make_group()
    xln2_feat = sbC("xln2_feat", [128, 2, 16, 512], BF16)
    hact = sbC("hact", [128, NIC, 512], BF16)
    wgu_sb = sbC("wgu_sb", [128, 2, 4, 1024], BF16)
    wdn_sb = sbC("wdn_sb", [128, 22, 1024], BF16)
    sg_sb = sbC("sg_sb", [128, 2, 512], BF16)
    y_sb = sbC("y_sb", [128, 4, 2048], F32)
    x2e_sb = sbC("x2e_sb", [128, 1, 2048], F32)
    out_sb = sbC("out_sb", [128, 2048], F32)

    for tb in range(4):
        bp = tb % 2
        for hc in range(16):
            P.op("sync", lambda e, hc=hc, tb=tb, bp=bp: e.dma_start(
                out=xln2_feat[:, bp, hc, :],
                in_=ln_dram[tb * 512:(tb + 1) * 512,
                            hc * 128:(hc + 1) * 128],
                transpose=True),
                writes=[f"x2f{bp}"], dma=True)

        for p in range(NPAIR):
            wp = p % 2
            P.op("sync", lambda e, p=p, wp=wp: e.dma_start(
                out=wgu_sb[:, wp, :, :],
                in_=cc_w_out[p * 4 * 128:(p + 1) * 4 * 128, :]
                .rearrange("(g p2) f -> p2 g f", p2=128)),
                deps=[ccw_ev], writes=[f"wgu{wp}"], dma=True)
            pg = p % 2
            PG, PU = [P0, P2][pg], [P1, P3][pg]
            pgk, puk = f"PG{pg}", f"PU{pg}"
            for gu in range(2):
                PP = PG if gu == 0 else PU
                ppk = pgk if gu == 0 else puk
                for hc in range(16):
                    lidx = gu * 16 + hc
                    g, cb = lidx // 8, lidx % 8
                    P.op("tensor",
                         lambda e, hc=hc, g=g, cb=cb, wp=wp, bp=bp, PP=PP:
                         e.matmul(
                             PP[:, :],
                             wgu_sb[:, wp, g, cb * 128:(cb + 1) * 128],
                             xln2_feat[:, bp, hc, :],
                             start=(hc == 0), stop=(hc == 15)),
                         reads=[f"wgu{wp}", f"x2f{bp}"], writes=[ppk])
            P.op("scalar", lambda e, pg=pg, PG=PG: e.activation(
                out=sg_sb[:, pg, :], in_=PG[:, :], func=AF.Silu),
                reads=[pgk], writes=[f"sg{pg}"])
            P.op("vector", lambda e, p=p, pg=pg, PU=PU: e.tensor_tensor(
                out=hact[:, p, :], in0=sg_sb[:, pg, :], in1=PU[:, :],
                op=ALU.mult),
                reads=[f"sg{pg}", puk], writes=["hact"])

        for mcg in range(4):
            for half in range(2):
                g0 = 176 + 22 * mcg + 11 * half
                P.op("sync", lambda e, g0=g0, half=half: e.dma_start(
                    out=wdn_sb[:, half * 11:(half + 1) * 11, :],
                    in_=cc_w_out[g0 * 128:(g0 + 11) * 128, :]
                    .rearrange("(g p2) f -> p2 g f", p2=128)),
                    deps=[ccw_ev], writes=[f"wdn{half}"], dma=True)
            for tt in range(4):
                pd = tt % 2
                PD = [P4, P5][pd]
                pdk = f"PD{pd}"
                for ic in range(NIC):
                    half = 0 if ic < 22 else 1
                    gl = ic // 2
                    co = (ic % 2) * 512
                    P.op("tensor",
                         lambda e, ic=ic, tt=tt, gl=gl, co=co, PD=PD:
                         e.matmul(
                             PD[:, :], hact[:, ic, tt * 128:(tt + 1) * 128],
                             wdn_sb[:, gl, co:co + 512],
                             start=(ic == 0), stop=(ic == NIC - 1)),
                         reads=["hact", f"wdn{half}"], writes=[pdk])
                P.op("scalar", lambda e, tt=tt, mcg=mcg, PD=PD: e.copy(
                    out=y_sb[:, tt, mcg * 512:(mcg + 1) * 512], in_=PD[:, :]),
                    reads=[pdk], writes=[f"y{tt}"])

        for tt in range(4):
            gt = tb * 4 + tt
            pe = tt % 2
            P.op("sync", lambda e, gt=gt: e.dma_start(
                out=x2e_sb[:, 0, :],
                in_=x2_dram[gt * 128:(gt + 1) * 128, :]),
                writes=["x2e0"], dma=True)
            P.op("scalar", lambda e, tt=tt: e.activation(
                out=junk[:, 0:2048], in_=y_sb[:, tt, :], func=AF.Square,
                accum_out=stat[:, 4:5]),
                reads=[f"y{tt}"], writes=["statc", "junk"])
            P.op("scalar", lambda e: e.activation(
                out=stat[:, 4:5], in_=stat[:, 4:5], func=AF.Ln,
                scale=INV_DM, bias=RMS_EPS_FFN),
                reads=["statc", "consts"], writes=["statc"])
            P.op("scalar", lambda e: e.activation(
                out=stat[:, 4:5], in_=stat[:, 4:5], func=AF.Exp, scale=-0.5),
                reads=["statc", "consts"], writes=["statc"])
            P.op("vector", lambda e, tt=tt: e.scalar_tensor_tensor(
                out=out_sb[:, :], in0=y_sb[:, tt, :],
                scalar=stat[:, 4:5], in1=x2e_sb[:, 0, :],
                op0=ALU.mult, op1=ALU.add),
                reads=[f"y{tt}", "statc", "x2e0"], writes=["outsb"])
            P.op("sync", lambda e, gt=gt: e.dma_start(
                out=y_out[gt * 128:(gt + 1) * 128, :], in_=out_sb[:, :]),
                reads=["outsb"], writes=["yd"], dma=True)

    P.barrier()
    P.flush(nc, sems)
    free_group(gC)
    free_group(gG)

    for cmx in reversed(ps_ctx):
        cmx.__exit__(None, None, None)
    for cmx in sem_ctx:
        cmx.__exit__(None, None, None)
    return nc


def _host_inputs(L0_attn_hidden_states, L0_attn_Wq, L0_attn_Wk, L0_attn_Wi,
                 L0_attn_Wo, L0_attn_g_weight, L0_ffn_gate_up_packed,
                 L0_ffn_gate_up_scales, L0_ffn_down_packed,
                 L0_ffn_down_scales):
    x = np.asarray(L0_attn_hidden_states, dtype=np.float32)
    wq = np.ascontiguousarray(np.asarray(L0_attn_Wq, np.float32))
    wk = np.ascontiguousarray(np.asarray(L0_attn_Wk, np.float32))
    wi = np.ascontiguousarray(np.asarray(L0_attn_Wi, np.float32))
    wo = np.ascontiguousarray(np.asarray(L0_attn_Wo, np.float32))
    gw = np.asarray(L0_attn_g_weight, np.float32)
    gup = np.asarray(L0_ffn_gate_up_packed, np.int32)
    gus = np.asarray(L0_ffn_gate_up_scales, np.float16)
    dnp = np.asarray(L0_ffn_down_packed, np.int32)
    dns = np.asarray(L0_ffn_down_scales, np.float16)

    gwc = np.ascontiguousarray(gw.reshape(NH, 128).T)
    trimask = np.zeros((128, 128), dtype=BF)
    su = np.tril(np.ones((64, 64), np.float32)).T.astype(BF)
    trimask[0:64, 0:64] = su
    trimask[0:64, 64:128] = 1.0
    trimask[64:128, 64:128] = su
    ident = np.eye(128, dtype=BF)
    consts = np.broadcast_to(np.array(CONST_VALS, np.float32),
                             (128, len(CONST_VALS))).copy()

    def tile_slices(gidx):
        if gidx < 1408:
            pair, rem = divmod(gidx, 32)
            gu, hc = divmod(rem, 16)
            r0 = (gu * NPAIR + pair) * 128
            pk = gup[r0:r0 + 128, hc * 16:(hc + 1) * 16]
            sc = gus[hc, r0:r0 + 128]
        else:
            g2 = gidx - 1408
            mcg, rem = divmod(g2, 176)
            ic, mi = divmod(rem, 4)
            mt = mcg * 4 + mi
            pk = dnp[mt * 128:(mt + 1) * 128, ic * 16:(ic + 1) * 16]
            sc = dns[ic, mt * 128:(mt + 1) * 128]
        return pk, sc

    per_core = []
    for c in range(8):
        b, half = c // 2, c % 2
        xs = np.ascontiguousarray(x[b, half * T:(half + 1) * T, :])
        pks = np.empty((TPC, 128, 16), np.int32)
        scs = np.empty((TPC, 128), np.float16)
        for i in range(TPC):
            pk, sc = tile_slices(c * TPC + i)
            pks[i] = pk
            scs[i] = sc
        mask8 = np.zeros((128, 8), np.float32)
        if half == 1:
            mask8[:, c - 1] = 1.0
        per_core.append(dict(
            x=xs, wq=wq, wk=wk, wi=wi, wo=wo,
            gwc=gwc, mask8=mask8, trimask=trimask, ident=ident,
            consts=consts, packedsl=pks,
            scalessl=np.ascontiguousarray(scs.T)))
    return per_core


def kernel(**inputs):
    per_core = _host_inputs(**inputs)
    nc = build_nc()
    res = run_bass_kernel_spmd(nc, per_core, core_ids=list(range(8)))
    out = np.empty((B_, L_, DM), np.float32)
    for c in range(8):
        b, half = c // 2, c % 2
        out[b, half * T:(half + 1) * T, :] = res.results[c]["y"]
    return out


if __name__ == "__main__":
    nc = build_nc()
    print("built ok; instructions:", nc.n_instructions())


# version 23
# speedup vs baseline: 1.2522x; 1.0101x over previous
import numpy as np
import ml_dtypes
import concourse.bass as bass
import concourse.mybir as mybir
from concourse.bass_utils import run_bass_kernel_spmd

F32 = mybir.dt.float32
BF16 = mybir.dt.bfloat16
F16 = mybir.dt.float16
I32 = mybir.dt.int32
U16 = mybir.dt.uint16
AF = mybir.ActivationFunctionType
ALU = mybir.AluOpType
AX = mybir.AxisListType

BF = ml_dtypes.bfloat16

B_, L_, DM = 4, 4096, 2048
NH, DK, DV = 16, 128, 128
SCALE = float(DK) ** -0.5
T = 2048
NBLK = 4
NCH = 16
NTT = 16
FFI = 5632
NPAIR = 44
NIC = 44
NTILES = 2112
TPC = 264
NGRP = NTILES // 8
GPC = TPC // 8

LN_EPS = 1e-5
RMS_EPS_FFN = 1e-6
INV_DM = 1.0 / 2048.0

CONST_VALS = [SCALE, -1.0, 1.0, 2.0, -0.5, INV_DM, LN_EPS, RMS_EPS_FFN]

ENGINES = ("sync", "gpsimd", "tensor", "vector", "scalar")
DMA_QUEUES = ("sync", "gpsimd", "scalar")
NDMASEM = 8
DMA_SEMS = tuple(f"{q}dma{i}" for q in DMA_QUEUES for i in range(NDMASEM))
ALL_SEMS = ENGINES + ("cc",) + DMA_SEMS


class Prog:

    def __init__(self):
        self.items = []
        self.count = {s: 0 for s in ALL_SEMS}
        self.hwm = {e: {s: 0 for s in ALL_SEMS} for e in ENGINES}
        self.last_w = {}
        self.readers = {}
        self.dma_idx = {q: 0 for q in DMA_QUEUES}

    def op(self, eng, fn, reads=(), writes=(), dma=False, deps=(), sem=None):
        dep_evs = []
        for k in reads:
            ev = self.last_w.get(k)
            if ev is not None:
                dep_evs.append(ev)
        for k in writes:
            ev = self.last_w.get(k)
            if ev is not None:
                dep_evs.append(ev)
            dep_evs.extend(self.readers.get(k, ()))
        dep_evs.extend(deps)
        if dma:
            idx = self.dma_idx[eng]
            self.dma_idx[eng] = idx + 1
            slot = idx % NDMASEM
            dmasem = f"{eng}dma{slot}"
            dep_evs.append((dmasem, 16 * (idx // NDMASEM), True))
        wmax = {}
        for (ds, dv, dasync) in dep_evs:
            if ds == eng and eng == "tensor" and not dasync:
                continue
            if self.hwm[eng][ds] >= dv:
                continue
            wmax[ds] = max(wmax.get(ds, 0), dv)
        for ds, dv in wmax.items():
            self.hwm[eng][ds] = dv
        if dma:
            semname = dmasem
        else:
            semname = sem if sem is not None else eng
        inc = 16 if dma else 1
        self.count[semname] += inc
        is_async = dma or (sem is not None)
        ev = (semname, self.count[semname], is_async)
        self.items.append((eng, fn, sorted(wmax.items()), semname, inc))
        for k in reads:
            self.readers.setdefault(k, []).append(ev)
        for k in writes:
            self.last_w[k] = ev
            self.readers[k] = []
        return ev

    def barrier(self):
        snap = {s: self.count[s] for s in ENGINES + DMA_SEMS}
        for e in ENGINES:
            waits = []
            for e2 in ENGINES + DMA_SEMS:
                if e2 == e:
                    continue
                if self.hwm[e][e2] < snap[e2]:
                    waits.append((e2, snap[e2]))
                    self.hwm[e][e2] = snap[e2]
            if waits:
                self.items.append((e, None, waits, e, 0))
        self.last_w = {}
        self.readers = {}

    def flush(self, nc, sems):
        items, self.items = self.items, []
        with nc.Block() as block:
            for eng_name in ENGINES:
                def make(eng_name=eng_name):
                    def body(eng):
                        for (en, fn, waits, semname, inc) in items:
                            if en != eng_name:
                                continue
                            for (se, sv) in waits:
                                eng.wait_ge(sems[se], sv)
                            if fn is not None:
                                fn(eng).then_inc(sems[semname], inc)
                    return body
                getattr(block, eng_name)(make())


def build_nc(debug_outputs=False, phases=("A", "B", "C")):
    nc = bass.Bass("TRN2", target_bir_lowering=False)

    def din(name, shape, dt):
        return nc.dram_tensor(name, shape, dt, kind="ExternalInput").ap()

    x_in = din("x", [T, DM], F32)
    wq_in = din("wq", [DM, DM], F32)
    wk_in = din("wk", [DM, DM], F32)
    wi_in = din("wi", [DM, DM], F32)
    wo_in = din("wo", [DM, DM], F32)
    gwc_in = din("gwc", [128, NH], F32)
    mask8_in = din("mask8", [128, 8], F32)
    trimask_in = din("trimask", [128, 128], BF16)
    ident_in = din("ident", [128, 128], BF16)
    consts_in = din("consts", [128, len(CONST_VALS)], F32)
    packed_in = din("packedsl", [TPC, 128, 16], I32)
    scales_in = din("scalessl", [128, TPC], F16)

    y_out = nc.dram_tensor("y", [T, DM], F32, kind="ExternalOutput").ap()

    dkind = "ExternalOutput" if debug_outputs else "Internal"

    def dscr(name, shape, dt, dbg=False):
        return nc.dram_tensor(name, shape, dt,
                              kind=(dkind if dbg else "Internal")).ap()

    wq16t = dscr("wq16t", [NH, 16, 128, 128], BF16)
    wk16t = dscr("wk16t", [NH, 16, 128, 128], BF16)
    wi16 = dscr("wi16", [DM, DM], BF16)
    wo16 = dscr("wo16", [DM, DM], BF16, dbg=True)
    ln_dram = dscr("ln_dram", [T, DM], BF16, dbg=True)
    ogg_dram = dscr("ogg_dram", [NCH, NH, 128, 128], BF16)
    o_dram = dscr("o_dram", [NCH, NH, 128, 128], BF16, dbg=True)
    x2_dram = dscr("x2_dram", [T, DM], F32, dbg=True)
    dbgf = dscr("dbgf", [128, 4096], F32, dbg=True)
    dbgb = dscr("dbgb", [128, 4096], BF16, dbg=True)
    cc_w_in = nc.dram_tensor("cc_w_in", [GPC * 128, 1024], BF16).ap()
    cc_w_out = nc.dram_tensor("cc_w_out", [NGRP * 128, 1024], BF16,
                              addr_space="Shared").ap()
    cc_s_in = nc.dram_tensor("cc_s_in", [NH * 128, 128], F32).ap()
    cc_s_out = nc.dram_tensor("cc_s_out", [8 * NH * 128, 128], F32,
                              addr_space="Shared").ap()

    P = Prog()

    sems = {}
    sem_ctx = []
    for s in ALL_SEMS:
        cm = nc.semaphore(f"sem_{s}")
        sem_ctx.append(cm)
        sems[s] = cm.__enter__()

    all_ctx = []

    def make_group():
        group = []

        def sb(name, shape, dt):
            cm = nc.sbuf_tensor("sb_" + name, shape, dt)
            group.append(cm)
            all_ctx.append(cm)
            return cm.__enter__()
        return group, sb

    def free_group(group):
        for cm in reversed(group):
            cm.__exit__(None, None, None)
            all_ctx.remove(cm)

    ps_ctx = []

    def psb(name):
        cm = nc.psum_tensor(name, [128, 512], F32)
        ps_ctx.append(cm)
        return cm.__enter__()

    gG, sbG = make_group()
    consts = sbG("consts", [128, len(CONST_VALS)], F32)
    gwc = sbG("gwc", [128, NH], F32)
    mask8 = sbG("mask8", [128, 8], F32)
    trimask = sbG("trimask", [128, 128], BF16)
    ident = sbG("ident", [128, 128], BF16)
    junk = sbG("junk", [128, 2048], BF16)
    junkf = sbG("junkf", [128, 256], F32)
    xln_tok = sbG("xln_tok", [128, 1, 2048], BF16)
    stat = sbG("stat", [128, 16], F32)

    for i, v in enumerate(CONST_VALS):
        nc.const_aps.register(F32, v, consts[:, i:i + 1])

    P0 = psb("P0")
    P1 = psb("P1")
    P2 = psb("P2")
    P3 = psb("P3")
    P4 = psb("P4")
    P5 = psb("P5")
    P6 = psb("P6")
    cm = nc.psum_tensor("P7b", [128, 1024], BF16)
    ps_ctx.append(cm)
    P7b = cm.__enter__()

    P.op("sync", lambda e: e.dma_start(out=consts[:], in_=consts_in),
         writes=["consts"], dma=True)
    P.op("sync", lambda e: e.dma_start(out=gwc[:], in_=gwc_in),
         writes=["gwc"], dma=True)
    P.op("sync", lambda e: e.dma_start(out=mask8[:], in_=mask8_in),
         writes=["mask8"], dma=True)
    P.op("sync", lambda e: e.dma_start(out=trimask[:], in_=trimask_in),
         writes=["trimask"], dma=True)
    P.op("sync", lambda e: e.dma_start(out=ident[:], in_=ident_in),
         writes=["ident"], dma=True)
    P.op("sync", lambda e: e.dma_start(out=scl16[:], in_=scales_in),
         writes=["scl16"], dma=True)
    P.op("vector", lambda e: e.tensor_copy(out=scl[:], in_=scl16[:]),
         reads=["scl16"], writes=["scl"])
    P.op("vector", lambda e: e.memset(S_sb[:], 0.0), writes=["S"])
    P.op("vector", lambda e: e.memset(S_bf[:], 0.0), writes=["Sbf"])
    P.op("vector", lambda e: e.memset(R_run[:], 1.0), writes=["Rrun"])

    wi_r = wi_in.rearrange("(hc p) (dc f) -> dc hc p f", p=128, f=512)
    wi16_r = wi16.rearrange("(hc p) (dc f) -> dc hc p f", p=128, f=512)
    wq_r = wq_in.rearrange("(hc p) (mc f) -> mc hc p f", p=128, f=128)
    wk_r = wk_in.rearrange("(hc p) (mc f) -> mc hc p f", p=128, f=128)

    def cast_wi(dvc):
        P.op("gpsimd", lambda e, dvc=dvc: e.dma_start(
            out=wi16_r[dvc], in_=wi_r[dvc]),
            writes=[f"wi16d{dvc}"], dma=True)

    for dvc in range(4):
        cast_wi(dvc)

    gA, sbA = make_group()
    scl16 = sbA("scl16", [128, TPC], F16)
    scl = sbA("scl", [128, TPC], F32)
    S_sb = sbA("S_sb", [128, NH, 128], F32)
    S_bf = sbA("S_bf", [128, NH, 128], BF16)
    R_run = sbA("R_run", [128, NH], F32)
    xt = sbA("xt", [128, 2, 2048], F32)
    xln_feat = sbA("xln_feat", [128, 2, 16, 512], BF16)
    v_tok = sbA("v_tok", [128, 4, 2048], BF16)
    wi_sb = sbA("wi_sb", [128, 16, 512], BF16)
    wq_sb = sbA("wq_sb", [128, 2, 16, 128], BF16)
    wk_sb = sbA("wk_sb", [128, 2, 16, 128], BF16)
    q_h = sbA("q_h", [128, 4, 512], BF16)
    kraw_h = sbA("kraw_h", [128, 2, 512], F32)
    sp_h = sbA("sp_h", [128, 2, 512], F32)
    gh_h = sbA("gh_h", [128, 2, 512], F32)
    bb_h = sbA("bb_h", [128, 2, 512], F32)
    eg_h = sbA("eg_h", [128, 2, 512], F32)
    qg_h = sbA("qg_h", [128, 2, 512], BF16)
    qb_h = sbA("qb_h", [128, 2, 4, 64], BF16)
    ka_h = sbA("ka_h", [128, 2, 4, 64], BF16)
    kb_h = sbA("kb_h", [128, 2, 4, 64], BF16)
    kba_h = sbA("kba_h", [128, 2, 4, 64], BF16)
    kh_h = sbA("kh_h", [128, 2, 512], BF16)
    kh_tok = sbA("kh_tok", [128, 2, 2, 128], BF16)
    ng_h = sbA("ng_h", [128, 2, 8], F32)
    asb = sbA("asb", [128, 2, 128], BF16)
    zchunk = sbA("zchunk", [128, 128], F32)
    qgg_slab = sbA("qgg_slab", [128, 4, 16, 128], BF16)
    o_slab = sbA("o_slab", [128, 4, 16, 128], BF16)
    dq_pk = sbA("dq_pk", [128, 2, 4, 16], I32)
    dq_nib = sbA("dq_nib", [128, 128, 4], U16)
    dq_bits = sbA("dq_bits", [128, 512], U16)
    dq_sgn = sbA("dq_sgn", [128, 512], U16)
    dq_sgnf = sbA("dq_sgnf", [128, 512], BF16)
    dq_u16 = sbA("dq_u16", [128, 512], U16)
    dq_relu = sbA("dq_relu", [128, 512], BF16)
    dq_min = sbA("dq_min", [128, 512], BF16)
    dq_out = sbA("dq_out", [128, 2, 512], BF16)
    dq_tr = sbA("dq_tr", [128, 2, 128], BF16)

    P.op("vector", lambda e: e.memset(zchunk[:], 0.0), writes=["zchunk"])
    P.op("vector", lambda e: e.memset(P2[64:128, 0:64], 0.0), writes=["PA0"])
    P.op("vector", lambda e: e.memset(P3[64:128, 0:64], 0.0), writes=["PA1"])

    def emit_dequant_dve(bi):
        pk_par = bi % 2
        op_par = bi % 2
        lt0 = 4 * bi
        pk = dq_pk[:, pk_par, :, :]
        P.op("gpsimd",
             lambda e, pk=pk, lt0=lt0: e.dma_start(
                 out=pk, in_=packed_in[lt0:lt0 + 4].rearrange(
                     "t p w -> p t w")),
             writes=[f"dqpk{pk_par}"], dma=True)
        pku = pk.rearrange("p t w -> p (t w)").bitcast(U16)
        for k in range(4):
            P.op("vector",
                 lambda e, k=k, pku=pku: e.tensor_scalar(
                     out=dq_nib[:, :, k], in0=pku, scalar1=4 * k, scalar2=15,
                     op0=ALU.logical_shift_right, op1=ALU.bitwise_and),
                 reads=[f"dqpk{pk_par}"], writes=["dqnib"])
        nib3 = dq_nib[:, :, :]
        P.op("vector", lambda e, nib3=nib3: e.tensor_scalar(
            out=dq_bits[:], in0=nib3, scalar1=7, scalar2=6,
            op0=ALU.bitwise_and, op1=ALU.logical_shift_left),
            reads=["dqnib"], writes=["dqbits"])
        P.op("vector", lambda e: e.tensor_scalar(
            out=dq_u16[:], in0=dq_bits[:], scalar1=0x3F00,
            scalar2=None, op0=ALU.add), reads=["dqbits"], writes=["dqu16"])
        P.op("vector", lambda e, nib3=nib3: e.tensor_scalar(
            out=dq_sgn[:], in0=nib3, scalar1=3, scalar2=1,
            op0=ALU.logical_shift_right, op1=ALU.bitwise_and),
            reads=["dqnib"], writes=["dqsgn"])
        P.op("vector", lambda e: e.tensor_scalar(
            out=dq_sgnf[:], in0=dq_sgn[:], scalar1=-2,
            scalar2=1, op0=ALU.mult, op1=ALU.add),
            reads=["dqsgn"], writes=["dqsgnf"])
        fmv = dq_u16[:].bitcast(BF16)
        P.op("scalar", lambda e, fmv=fmv: e.activation(
            out=dq_relu[:], in_=fmv, func=AF.Relu, bias=-1.0,
            scale=2.0),
            reads=["dqu16", "consts"], writes=["dqrelu"])
        P.op("vector", lambda e, fmv=fmv: e.tensor_tensor(
            out=dq_min[:], in0=dq_relu[:], in1=fmv,
            op=ALU.min),
            reads=["dqrelu", "dqu16"], writes=["dqmin"])
        for j in range(4):
            lt = lt0 + j
            P.op("vector",
                 lambda e, j=j, lt=lt, op_par=op_par: e.scalar_tensor_tensor(
                     out=dq_out[:, op_par, j * 128:(j + 1) * 128],
                     in0=dq_min[:, j * 128:(j + 1) * 128],
                     scalar=scl[:, lt:lt + 1],
                     in1=dq_sgnf[:, j * 128:(j + 1) * 128],
                     op0=ALU.mult, op1=ALU.mult),
                 reads=["dqmin", "dqsgnf", "scl"], writes=[f"dqout{op_par}"])

    def emit_dequant_tr(bi):
        op_par = bi % 2
        lt0 = 4 * bi
        for j in range(4):
            lt = lt0 + j
            tp = j % 2
            P.op("tensor", lambda e, j=j, op_par=op_par: e.transpose(
                P7b[:, 0:128], dq_out[:, op_par, j * 128:(j + 1) * 128],
                ident[:]),
                reads=[f"dqout{op_par}", "ident"], writes=["P7b"])
            P.op("scalar", lambda e, tp=tp: e.copy(
                out=dq_tr[:, tp, :], in_=P7b[:, 0:128]),
                reads=["P7b"], writes=[f"dqtr{tp}"])
            g, cb = lt // 8, lt % 8
            P.op("gpsimd", lambda e, g=g, cb=cb, tp=tp: e.dma_start(
                out=cc_w_in[g * 128:(g + 1) * 128, cb * 128:(cb + 1) * 128],
                in_=dq_tr[:, tp, :]),
                reads=[f"dqtr{tp}"], dma=True)

    dq_iter = iter(range(TPC // 4))

    dq_pending = []

    def emit_some_dequant(n):
        for _ in range(n):
            bi = next(dq_iter, None)
            if bi is None:
                while dq_pending:
                    emit_dequant_tr(dq_pending.pop(0))
                return
            emit_dequant_dve(bi)
            dq_pending.append(bi)
            if len(dq_pending) > 1:
                emit_dequant_tr(dq_pending.pop(0))

    def emit_ln(xin_ap, xkey, out_ap, okey, c0):
        m_c = stat[:, c0:c0 + 1]
        v_c = stat[:, c0 + 1:c0 + 2]
        km, kv = f"stat{c0}", f"stat{c0 + 1}"
        P.op("vector", lambda e: e.reduce_sum(m_c, xin_ap, axis=AX.X),
             reads=[xkey], writes=[km])
        P.op("scalar", lambda e: e.activation(
            out=junk[:, 0:2048], in_=xin_ap, func=AF.Square, accum_out=v_c),
            reads=[xkey], writes=[kv, "junk"])
        P.op("vector", lambda e: e.tensor_scalar(
            out=m_c, in0=m_c, scalar1=INV_DM, scalar2=None, op0=ALU.mult),
            reads=[km], writes=[km])
        P.op("vector", lambda e: e.tensor_scalar(
            out=v_c, in0=v_c, scalar1=INV_DM, scalar2=None, op0=ALU.mult),
            reads=[kv], writes=[kv])
        P.op("vector", lambda e: e.scalar_tensor_tensor(
            out=v_c, in0=m_c, scalar=m_c, in1=v_c,
            op0=ALU.mult, op1=ALU.subtract),
            reads=[km, kv], writes=[kv])
        P.op("vector", lambda e: e.tensor_scalar(
            out=v_c, in0=v_c, scalar1=-1.0, scalar2=None, op0=ALU.mult),
            reads=[kv], writes=[kv])
        P.op("scalar", lambda e: e.activation(
            out=v_c, in_=v_c, func=AF.Ln, bias=LN_EPS),
            reads=[kv, "consts"], writes=[kv])
        P.op("scalar", lambda e: e.activation(
            out=v_c, in_=v_c, func=AF.Exp, scale=-0.5),
            reads=[kv, "consts"], writes=[kv])
        P.op("vector", lambda e: e.tensor_scalar(
            out=out_ap, in0=xin_ap, scalar1=m_c, scalar2=v_c,
            op0=ALU.subtract, op1=ALU.mult),
            reads=[xkey, km, kv], writes=[okey])

    ccw_ev = None
    def emit_blockLN(tb):
        for tt in range(4):
            gt = tb * 4 + tt
            par = gt % 2
            P.op("sync", lambda e, gt=gt, par=par: e.dma_start(
                out=xt[:, par, :], in_=x_in[gt * 128:(gt + 1) * 128, :]),
                writes=[f"xt{par}"], dma=True)
            emit_ln(xt[:, par, :], f"xt{par}", xln_tok[:, 0, :],
                    "xlntok", 0)
            P.op("sync", lambda e, gt=gt, par=par: e.dma_start(
                out=ln_dram[gt * 128:(gt + 1) * 128, :],
                in_=xln_tok[:, 0, :]),
                reads=["xlntok"], writes=[f"lnd{gt}"], dma=True)

    emit_blockLN(0)
    for tb in range(NBLK):
        bp = tb % 2
        if tb == 3:
            P.op("gpsimd", lambda e: e.dma_start(out=wo16, in_=wo_in),
                 writes=["wo16d"], dma=True)
        for hc in range(16):
            P.op("sync", lambda e, hc=hc, tb=tb, bp=bp: e.dma_start(
                out=xln_feat[:, bp, hc, :],
                in_=ln_dram[tb * 512:(tb + 1) * 512,
                            hc * 128:(hc + 1) * 128],
                transpose=True),
                reads=[f"lnd{tb * 4 + i}" for i in range(4)],
                writes=[f"xlf{bp}"], dma=True)

        for dvc in range(4):
            P.op("gpsimd", lambda e, dvc=dvc: e.dma_start(
                out=wi_sb[:, :, :],
                in_=wi16_r[dvc].rearrange("hc p f -> p hc f")),
                reads=[f"wi16d{dvc}"], writes=["wi_sb"], dma=True)
            for c in range(4):
                pp = [P0, P1][c % 2]
                pkey = f"P{c % 2}"
                for hc in range(16):
                    P.op("tensor", lambda e, pp=pp, hc=hc, c=c, bp=bp:
                         e.matmul(pp[:, :],
                                  xln_feat[:, bp, hc, c * 128:(c + 1) * 128],
                                  wi_sb[:, hc, :],
                                  start=(hc == 0), stop=(hc == 15)),
                         reads=[f"xlf{bp}", "wi_sb"], writes=[pkey])
                P.op("scalar", lambda e, pp=pp, c=c, dvc=dvc: e.copy(
                    out=v_tok[:, c, dvc * 512:(dvc + 1) * 512], in_=pp[:, :]),
                    reads=[pkey], writes=[f"vtok{c}"])

        def emit_wload(h):
            wp2 = h % 2
            if tb == 0:
                P.op("gpsimd", lambda e, h=h, wp2=wp2: e.dma_start(
                    out=wq_sb[:, wp2, :, :],
                    in_=wq_r[h].rearrange("hc p f -> p hc f")),
                    writes=[f"wq_sb{wp2}"], dma=True)
                P.op("sync", lambda e, h=h, wp2=wp2: e.dma_start(
                    out=wq16t[h].rearrange("hc p f -> p hc f"),
                    in_=wq_sb[:, wp2, :, :]),
                    reads=[f"wq_sb{wp2}"], writes=[f"wq16d{h}"], dma=True)
                P.op("gpsimd", lambda e, h=h, wp2=wp2: e.dma_start(
                    out=wk_sb[:, wp2, :, :],
                    in_=wk_r[h].rearrange("hc p f -> p hc f")),
                    writes=[f"wk_sb{wp2}"], dma=True)
                P.op("sync", lambda e, h=h, wp2=wp2: e.dma_start(
                    out=wk16t[h].rearrange("hc p f -> p hc f"),
                    in_=wk_sb[:, wp2, :, :]),
                    reads=[f"wk_sb{wp2}"], writes=[f"wk16d{h}"], dma=True)
            else:
                P.op("sync", lambda e, h=h, wp2=wp2: e.dma_start(
                    out=wq_sb[:, wp2, :, :],
                    in_=wq16t[h].rearrange("hc p f -> p hc f")),
                    reads=[f"wq16d{h}"], writes=[f"wq_sb{wp2}"], dma=True)
                P.op("sync", lambda e, h=h, wp2=wp2: e.dma_start(
                    out=wk_sb[:, wp2, :, :],
                    in_=wk16t[h].rearrange("hc p f -> p hc f")),
                    reads=[f"wk16d{h}"], writes=[f"wk_sb{wp2}"], dma=True)

        def emit_proj(h):
            hp = h % 4
            wp2 = h % 2
            for hc in range(16):
                P.op("tensor", lambda e, hc=hc, wp2=wp2, bp=bp: e.matmul(
                    P0[:, :], wq_sb[:, wp2, hc, :],
                    xln_feat[:, bp, hc, :],
                    start=(hc == 0), stop=(hc == 15)),
                    reads=[f"wq_sb{wp2}", f"xlf{bp}"], writes=["P0"])
            P.op("scalar", lambda e, hp=hp: e.activation(
                out=q_h[:, hp, :], in_=P0[:, :], func=AF.Relu, scale=SCALE),
                reads=["P0", "consts"], writes=[f"qh{hp}"])
            for hc in range(16):
                P.op("tensor", lambda e, hc=hc, wp2=wp2, bp=bp: e.matmul(
                    P1[:, :], wk_sb[:, wp2, hc, :],
                    xln_feat[:, bp, hc, :],
                    start=(hc == 0), stop=(hc == 15)),
                    reads=[f"wk_sb{wp2}", f"xlf{bp}"], writes=["P1"])

        def emit_prep1(h):
            hp = h % 2
            spv = sp_h[:, hp, :]
            P.op("scalar", lambda e, spv=spv: e.activation(
                out=spv, in_=P1[:, :], func=AF.Abs),
                reads=["P1"], writes=[f"sp{hp}"])
            P.op("scalar", lambda e, hp=hp: e.copy(
                out=kraw_h[:, hp, :], in_=P1[:, :]),
                reads=["P1"], writes=[f"krawh{hp}"])
            P.op("scalar", lambda e, spv=spv: e.activation(
                out=spv, in_=spv, func=AF.Exp, scale=-1.0),
                reads=[f"sp{hp}", "consts"], writes=[f"sp{hp}"])
            P.op("scalar", lambda e, spv=spv: e.activation(
                out=spv, in_=spv, func=AF.Ln, bias=1.0),
                reads=[f"sp{hp}", "consts"], writes=[f"sp{hp}"])

        def emit_prep2(h):
            hp = h % 2
            kr = kraw_h[:, hp, :]
            spv = sp_h[:, hp, :]
            ghv = gh_h[:, hp, :]
            bbv = bb_h[:, hp, :]
            P.op("vector", lambda e, kr=kr, bbv=bbv: e.tensor_scalar(
                out=bbv, in0=kr, scalar1=0.0, scalar2=None, op0=ALU.max),
                reads=[f"krawh{hp}"], writes=[f"bb{hp}"])
            P.op("vector", lambda e, spv=spv, bbv=bbv: e.tensor_add(
                out=spv, in0=spv, in1=bbv),
                reads=[f"sp{hp}", f"bb{hp}"], writes=[f"sp{hp}"])
            for c in range(4):
                P.op("vector", lambda e, c=c, spv=spv, ghv=ghv:
                     e.tensor_tensor_scan(
                         out=ghv[:, c * 128:(c + 1) * 128],
                         data0=spv[:, c * 128:(c + 1) * 128],
                         data1=zchunk[:, :],
                         initial=0.0, op0=ALU.add, op1=ALU.add),
                     reads=[f"sp{hp}", "zchunk"], writes=[f"gh{hp}"])
            P.op("vector", lambda e, kr=kr, spv=spv, bbv=bbv: e.tensor_sub(
                out=bbv, in0=kr, in1=spv),
                reads=[f"krawh{hp}", f"sp{hp}"], writes=[f"bb{hp}"])
            P.op("vector", lambda e, bbv=bbv, ghv=ghv: e.tensor_add(
                out=bbv, in0=bbv, in1=ghv),
                reads=[f"bb{hp}", f"gh{hp}"], writes=[f"bb{hp}"])
            P.op("vector", lambda e, hp=hp, ghv=ghv: e.tensor_scalar(
                out=ng_h[:, hp, 0:8:2],
                in0=ghv[:, 63:512:128],
                scalar1=-1.0, scalar2=None, op0=ALU.mult),
                reads=[f"gh{hp}"], writes=[f"ng{hp}"])
            P.op("vector", lambda e, hp=hp, ghv=ghv: e.tensor_scalar(
                out=ng_h[:, hp, 1:8:2],
                in0=ghv[:, 127:512:128],
                scalar1=-1.0, scalar2=None, op0=ALU.mult),
                reads=[f"gh{hp}"], writes=[f"ng{hp}"])

        def emit_prep3(h):
            hq = h % 4
            hp = h % 2
            ghv = gh_h[:, hp, :]
            bbv = bb_h[:, hp, :]
            egv = eg_h[:, hp, :]
            P.op("scalar", lambda e, ghv=ghv, egv=egv: e.activation(
                out=egv, in_=ghv, func=AF.Exp, scale=-1.0),
                reads=[f"gh{hp}", "consts"], writes=[f"eg{hp}"])
            P.op("vector", lambda e, hp=hp, hq=hq, egv=egv: e.tensor_mul(
                out=qg_h[:, hp, :], in0=q_h[:, hq, :], in1=egv),
                reads=[f"qh{hq}", f"eg{hp}"], writes=[f"qg{hp}"])
            for c in range(4):
                P.op("scalar", lambda e, c=c, hp=hp, bbv=bbv: e.activation(
                    out=ka_h[:, hp, c, :],
                    in_=bbv[:, c * 128:c * 128 + 64],
                    func=AF.Exp),
                    reads=[f"bb{hp}"], writes=[f"ka{hp}"])
            for c in range(4):
                ng63 = ng_h[:, hp, 2 * c:2 * c + 1]
                ng127 = ng_h[:, hp, 2 * c + 1:2 * c + 2]
                g63 = gh_h[:, hp, c * 128 + 63:c * 128 + 64]
                P.op("scalar", lambda e, c=c, hp=hp, bbv=bbv, ng63=ng63:
                     e.activation(
                         out=kb_h[:, hp, c, :],
                         in_=bbv[:, c * 128 + 64:(c + 1) * 128],
                         func=AF.Exp, bias=ng63),
                     reads=[f"bb{hp}", f"ng{hp}"], writes=[f"kb{hp}"])
                P.op("scalar", lambda e, c=c, hp=hp, bbv=bbv, ng63=ng63:
                     e.activation(
                         out=kba_h[:, hp, c, :],
                         in_=bbv[:, c * 128:c * 128 + 64],
                         func=AF.Exp, bias=ng63),
                     reads=[f"bb{hp}", f"ng{hp}"], writes=[f"kba{hp}"])
                P.op("scalar", lambda e, c=c, hp=hp, bbv=bbv, ng127=ng127:
                     e.activation(
                         out=kh_h[:, hp, c * 128:(c + 1) * 128],
                         in_=bbv[:, c * 128:(c + 1) * 128],
                         func=AF.Exp, bias=ng127),
                     reads=[f"bb{hp}", f"ng{hp}"], writes=[f"kh{hp}"])
                P.op("scalar", lambda e, c=c, hp=hp, ghv=ghv, g63=g63:
                     e.activation(
                         out=junkf[:, c * 64:(c + 1) * 64],
                         in_=ghv[:, c * 128 + 64:(c + 1) * 128],
                         func=AF.Exp, scale=-1.0, bias=g63),
                     reads=[f"gh{hp}", "consts"], writes=["jf"])
                P.op("vector", lambda e, c=c, hp=hp, hq=hq: e.tensor_mul(
                    out=qb_h[:, hp, c, :],
                    in0=q_h[:, hq, c * 128 + 64:(c + 1) * 128],
                    in1=junkf[:, c * 64:(c + 1) * 64]),
                    reads=[f"qh{hq}", "jf"], writes=[f"qb{hp}"])
            if tb == 0:
                for c in range(4):
                    P.op("vector", lambda e, c=c, hp=hp, h=h: e.tensor_scalar(
                        out=qgg_slab[:, c, h, :],
                        in0=qg_h[:, hp, c * 128:(c + 1) * 128],
                        scalar1=R_run[:, h:h + 1], scalar2=None,
                        op0=ALU.mult),
                        reads=[f"qg{hp}", "Rrun"], writes=[f"qgg{c}"])
                    P.op("vector", lambda e, c=c, hp=hp, h=h, egv=egv:
                         e.tensor_scalar(
                             out=R_run[:, h:h + 1], in0=R_run[:, h:h + 1],
                             scalar1=egv[:, c * 128 + 127:c * 128 + 128],
                             scalar2=None, op0=ALU.mult),
                         reads=["Rrun", f"eg{hp}"], writes=["Rrun"])

        def emit_apart(h, c):
            hp = h % 2
            cp = c % 2
            PA = [P2, P3][cp]
            pak = f"PA{cp}"
            P.op("tensor", lambda e, c=c, hp=hp, cp=cp: e.transpose(
                P7b[:, 0:128], kh_h[:, hp, c * 128:(c + 1) * 128],
                ident[:]),
                reads=[f"kh{hp}", "ident"], writes=["P7b"])
            P.op("vector", lambda e, hp=hp, cp=cp, h=h: e.tensor_copy(
                out=kh_tok[:, h % 2, cp, :], in_=P7b[:, 0:128]),
                reads=["P7b"], writes=[f"khtok{h % 2}{cp}"])
            P.op("tensor", lambda e, c=c, hp=hp, PA=PA: e.matmul(
                PA[0:64, 0:64], ka_h[:, hp, c, :],
                qg_h[:, hp, c * 128:c * 128 + 64],
                start=True, stop=True),
                reads=[f"ka{hp}", f"qg{hp}"], writes=[pak])
            P.op("tensor", lambda e, c=c, hp=hp, PA=PA: e.matmul(
                PA[0:64, 64:128], kba_h[:, hp, c, :], qb_h[:, hp, c, :],
                start=True, stop=True),
                reads=[f"kba{hp}", f"qb{hp}"], writes=[pak])
            P.op("tensor", lambda e, c=c, hp=hp, PA=PA: e.matmul(
                PA[64:128, 64:128], kb_h[:, hp, c, :], qb_h[:, hp, c, :],
                start=True, stop=True),
                reads=[f"kb{hp}", f"qb{hp}"], writes=[pak])
            P.op("vector", lambda e, cp=cp, PA=PA: e.tensor_tensor(
                out=asb[:, cp, :], in0=PA[:, 0:128],
                in1=trimask[:], op=ALU.mult),
                reads=[pak, "trimask"], writes=[f"asb{cp}"])

        def emit_opart(h, c):
            hp = h % 2
            cp = c % 2
            PO = [P4, P5][cp]
            pok = f"PO{cp}"
            egv = eg_h[:, hp, :]
            P.op("tensor", lambda e, c=c, h=h, cp=cp, PO=PO: e.matmul(
                PO[:, 0:64], v_tok[:, c, h * 128:(h + 1) * 128],
                asb[:, cp, 0:64], start=True, stop=False),
                reads=[f"vtok{c}", f"asb{cp}"], writes=[pok])
            P.op("tensor", lambda e, c=c, h=h, hp=hp, PO=PO: e.matmul(
                PO[:, 0:64], S_bf[:, h, :],
                qg_h[:, hp, c * 128:c * 128 + 64],
                start=False, stop=True),
                reads=["Sbf", f"qg{hp}"], writes=[pok])
            P.op("tensor", lambda e, c=c, h=h, cp=cp, PO=PO: e.matmul(
                PO[:, 64:128], v_tok[:, c, h * 128:(h + 1) * 128],
                asb[:, cp, 64:128], start=True, stop=False),
                reads=[f"vtok{c}", f"asb{cp}"], writes=[pok])
            P.op("tensor", lambda e, c=c, h=h, hp=hp, PO=PO: e.matmul(
                PO[:, 64:128], S_bf[:, h, :],
                qg_h[:, hp, c * 128 + 64:(c + 1) * 128],
                start=False, stop=True),
                reads=["Sbf", f"qg{hp}"], writes=[pok])
            P.op("tensor", lambda e, c=c, h=h, hp=hp: e.matmul(
                P6[:, 0:128], kh_tok[:, h % 2, cp, :],
                v_tok[:, c, h * 128:(h + 1) * 128],
                start=True, stop=True),
                reads=[f"khtok{h % 2}{cp}", f"vtok{c}"], writes=["P6"])
            P.op("scalar", lambda e, c=c, h=h, PO=PO: e.copy(
                out=o_slab[:, c, h, :], in_=PO[:, 0:128]),
                reads=[pok], writes=[f"oslab{c}"])
            P.op("vector", lambda e, c=c, h=h, hp=hp, egv=egv:
                 e.scalar_tensor_tensor(
                     out=S_sb[:, h, :], in0=S_sb[:, h, :],
                     scalar=egv[:, c * 128 + 127:c * 128 + 128],
                     in1=P6[:, 0:128], op0=ALU.mult, op1=ALU.add),
                 reads=["S", f"eg{hp}", "P6"], writes=["S"])
            P.op("vector", lambda e, h=h: e.tensor_copy(
                out=S_bf[:, h, :], in_=S_sb[:, h, :]),
                reads=["S"], writes=["Sbf"])

        def emit_scans(h):
            emit_apart(h, 0)
            for c in range(4):
                if c < 3:
                    emit_apart(h, c + 1)
                emit_opart(h, c)

        for i in range(NH + 3):
            if 0 <= i - 3 < NH:
                emit_scans(i - 3)
            if 0 <= i - 2 < NH:
                emit_prep3(i - 2)
            if 0 <= i - 1 < NH:
                emit_prep1(i - 1)
                emit_prep2(i - 1)
            if i + 1 < NH:
                emit_wload(i + 1)
            if i == 0:
                emit_wload(0)
            if i < NH:
                emit_proj(i)
            if i == 2 and tb + 1 < NBLK:
                emit_blockLN(tb + 1)
            if tb < 2 and i < NH:
                emit_some_dequant(3 if i % 8 == 0 else 2)
        if tb == 1:
            emit_some_dequant(999)

        for c in range(4):
            gc = tb * 4 + c
            P.op("sync", lambda e, gc=gc, c=c: e.dma_start(
                out=o_dram[gc].rearrange("h p f -> p h f"),
                in_=o_slab[:, c, :, :]),
                reads=[f"oslab{c}"], writes=[f"od{gc}"], dma=True)
            if tb == 0:
                P.op("sync", lambda e, gc=gc, c=c: e.dma_start(
                    out=ogg_dram[gc].rearrange("h p f -> p h f"),
                    in_=qgg_slab[:, c, :, :]),
                    reads=[f"qgg{c}"], writes=[f"oggd{gc}"], dma=True)

        if tb == 1:
            ccw_ev = P.op(
                "gpsimd", lambda e: e.collective_compute(
                    "AllGather", ALU.bypass,
                    replica_groups=[list(range(8))],
                    ins=[cc_w_in.opt()], outs=[cc_w_out.opt()]),
                deps=[(f"gpsimddma{i}", P.count[f"gpsimddma{i}"], True)
                      for i in range(NDMASEM)],
                sem="cc")

    P.op("sync", lambda e: e.dma_start(
        out=cc_s_in.rearrange("(h p) f -> p h f", p=128), in_=S_sb[:, :, :]),
        reads=["S"], writes=["ccsin"], dma=True)
    ccs_ev = P.op("gpsimd", lambda e: e.collective_compute(
        "AllGather", ALU.bypass,
        replica_groups=[list(range(8))],
        ins=[cc_s_in.opt()], outs=[cc_s_out.opt()]),
        reads=["ccsin"], sem="cc")

    P.barrier()
    P.flush(nc, sems)
    free_group(gA)

    if "B" not in phases:
        for cmx in reversed(ps_ctx):
            cmx.__exit__(None, None, None)
        for cmx in sem_ctx:
            cmx.__exit__(None, None, None)
        return nc

    gB, sbB = make_group()
    wo_sb = sbB("wo_sb", [128, 16, 2048], BF16)
    sgat = sbB("sgat", [128, NH, 128], F32)
    sini = sbB("sini", [128, NH, 128], F32)
    sini_bf = sbB("sini_bf", [128, NH, 128], BF16)
    off_sb = sbB("off_sb", [128, 2, 16, 128], BF16)
    qgg_sb = sbB("qgg_sb", [128, 2, 16, 128], BF16)
    ofull = sbB("ofull", [128, 2, 16, 128], BF16)
    ogw = sbB("ogw", [128, 2, 16, 128], BF16)
    sqsl = sbB("sqsl", [128, 2, 16, 128], BF16)
    sqpart = sbB("sqpart", [128, 2, 128], F32)
    sqpartb = sbB("sqpartb", [128, 2, 128], BF16)
    rstd_sb = sbB("rstd_sb", [128, 2], F32)
    xb_sb = sbB("xb_sb", [128, 2, 2048], F32)
    x2_sb = sbB("x2_sb", [128, 2, 2048], F32)

    P.op("gpsimd", lambda e: e.dma_start(
        out=wo_sb[:, :, :],
        in_=wo16.rearrange("(hc p) f -> p hc f", p=128)),
        reads=["wo16d"], writes=["wo_sb"], dma=True)

    def emit_sini():
        P.op("vector", lambda e: e.memset(sini[:, :, :], 0.0),
             writes=["sini"])
        for r in range(8):
            P.op("sync", lambda e, r=r: e.dma_start(
                out=sgat[:, :, :],
                in_=cc_s_out[r * 2048:(r + 1) * 2048, :]
                .rearrange("(h p) f -> p h f", p=128)),
                deps=[ccs_ev], writes=["sgat"], dma=True)
            P.op("vector", lambda e, r=r: e.scalar_tensor_tensor(
                out=sini[:, :, :], in0=sgat[:, :, :],
                scalar=mask8[:, r:r + 1],
                in1=sini[:, :, :], op0=ALU.mult, op1=ALU.add),
                reads=["sgat", "mask8", "sini"], writes=["sini"])
        P.op("vector", lambda e: e.tensor_copy(out=sini_bf[:, :, :],
                                               in_=sini[:, :, :]),
             reads=["sini"], writes=["sinibf"])

    for gt in list(range(4, NTT)) + list(range(4)):
        par = gt % 2
        if gt == 0:
            emit_sini()
        P.op("sync", lambda e, gt=gt, par=par: e.dma_start(
            out=off_sb[:, par, :, :],
            in_=o_dram[gt].rearrange("h p f -> p h f")),
            writes=[f"off{par}"], dma=True)
        if gt < 4:
            P.op("sync", lambda e, gt=gt, par=par: e.dma_start(
                out=qgg_sb[:, par, :, :],
                in_=ogg_dram[gt].rearrange("h p f -> p h f")),
                writes=[f"qggsb{par}"], dma=True)
        P.op("sync", lambda e, gt=gt, par=par: e.dma_start(
            out=xb_sb[:, par, :], in_=x_in[gt * 128:(gt + 1) * 128, :]),
            writes=[f"xb{par}"], dma=True)
        if gt < 4:
            for h in range(NH):
                pc = h % 2
                PC = [P4, P5][pc]
                pck = f"PB{pc}"
                P.op("tensor", lambda e, h=h, par=par, PC=PC: e.matmul(
                    PC[:, 0:128], sini_bf[:, h, :], qgg_sb[:, par, h, :],
                    start=True, stop=True),
                    reads=["sinibf", f"qggsb{par}"], writes=[pck])
                P.op("vector", lambda e, h=h, par=par, PC=PC: e.tensor_add(
                    out=ofull[:, par, h, :], in0=off_sb[:, par, h, :],
                    in1=PC[:, 0:128]),
                    reads=[f"off{par}", pck], writes=[f"ofull{par}"])
                P.op("vector", lambda e, h=h, par=par: e.tensor_scalar(
                    out=ogw[:, par, h, :], in0=ofull[:, par, h, :],
                    scalar1=gwc[:, h:h + 1], scalar2=None, op0=ALU.mult),
                    reads=[f"ofull{par}", "gwc"], writes=[f"ogw{par}"])
            P.op("scalar", lambda e, par=par: e.activation(
                out=sqsl[:, par, :, :], in_=ofull[:, par, :, :],
                func=AF.Square),
                reads=[f"ofull{par}"], writes=[f"sqsl{par}"])
        else:
            for h in range(NH):
                P.op("vector", lambda e, h=h, par=par: e.tensor_scalar(
                    out=ogw[:, par, h, :], in0=off_sb[:, par, h, :],
                    scalar1=gwc[:, h:h + 1], scalar2=None, op0=ALU.mult),
                    reads=[f"off{par}", "gwc"], writes=[f"ogw{par}"])
            P.op("scalar", lambda e, par=par: e.activation(
                out=sqsl[:, par, :, :], in_=off_sb[:, par, :, :],
                func=AF.Square),
                reads=[f"off{par}"], writes=[f"sqsl{par}"])
        P.op("vector", lambda e, par=par: e.tensor_reduce(
            out=sqpart[:, par, :],
            in_=sqsl[:, par, :, :].rearrange("p h f -> p f h"),
            axis=AX.X, op=ALU.add),
            reads=[f"sqsl{par}"], writes=[f"sqpart{par}"])
        P.op("vector", lambda e, par=par: e.tensor_copy(
            out=sqpartb[:, par, :], in_=sqpart[:, par, :]),
            reads=[f"sqpart{par}"], writes=[f"sqpartb{par}"])
        P.op("tensor", lambda e, par=par: e.matmul(
            P6[:, 0:1], sqpartb[:, par, :], trimask[:, 127:128],
            start=True, stop=True),
            reads=[f"sqpartb{par}", "trimask"], writes=["P6B"])
        rc = rstd_sb[:, par:par + 1]
        P.op("scalar", lambda e, rc=rc: e.activation(
            out=rc, in_=P6[:, 0:1], func=AF.Ln, scale=INV_DM, bias=LN_EPS),
            reads=["P6B", "consts"], writes=[f"rstd{par}"])
        P.op("scalar", lambda e, rc=rc: e.activation(
            out=rc, in_=rc, func=AF.Exp, scale=-0.5),
            reads=[f"rstd{par}", "consts"], writes=[f"rstd{par}"])
        if debug_outputs and gt == 0:
            P.op("sync", lambda e: e.dma_start(
                out=dbgb[:, 2048:4096],
                in_=ofull[:, :, :]),
                reads=["ofull"], writes=["dbgb"], dma=True)
            P.op("sync", lambda e: e.dma_start(
                out=dbgf[:, 3072:3200], in_=sqpart[:, :]),
                reads=["sqpart"], writes=["dbgf"], dma=True)
            P.op("sync", lambda e, rc=rc: e.dma_start(
                out=dbgf[:, 3200:3201], in_=rc,
                allow_slow_non_contiguous=True),
                reads=[f"rstd{par}"], writes=["dbgf"], dma=True)
            P.op("sync", lambda e: e.dma_start(
                out=dbgb[:, 0:2048],
                in_=ogw[:, :, :]),
                reads=["ogw"], writes=["dbgb"], dma=True)
        for mcq in range(4):
            pw = mcq % 2
            PW = [P0, P1][pw]
            pwk = f"PW{pw}"
            for hc in range(16):
                P.op("tensor", lambda e, hc=hc, mcq=mcq, par=par, PW=PW:
                     e.matmul(
                         PW[:, :], ogw[:, par, hc, :],
                         wo_sb[:, hc, mcq * 512:(mcq + 1) * 512],
                         start=(hc == 0), stop=(hc == 15)),
                     reads=[f"ogw{par}", "wo_sb"], writes=[pwk])
            P.op("vector", lambda e, mcq=mcq, par=par, PW=PW, rc=rc:
                 e.scalar_tensor_tensor(
                     out=x2_sb[:, par, mcq * 512:(mcq + 1) * 512],
                     in0=PW[:, :], scalar=rc,
                     in1=xb_sb[:, par, mcq * 512:(mcq + 1) * 512],
                     op0=ALU.mult, op1=ALU.add),
                 reads=[pwk, f"rstd{par}", f"xb{par}"], writes=[f"x2{par}"])
        P.op("sync", lambda e, gt=gt, par=par: e.dma_start(
            out=x2_dram[gt * 128:(gt + 1) * 128, :], in_=x2_sb[:, par, :]),
            reads=[f"x2{par}"], writes=[f"x2d{gt}"], dma=True)
        emit_ln(x2_sb[:, par, :], f"x2{par}", xln_tok[:, 0, :],
                "xlntok", 2)
        P.op("sync", lambda e, gt=gt, par=par: e.dma_start(
            out=ln_dram[gt * 128:(gt + 1) * 128, :],
            in_=xln_tok[:, 0, :]),
            reads=["xlntok"], writes=[f"ln2d{gt}"], dma=True)

    P.barrier()
    P.flush(nc, sems)
    free_group(gB)

    if "C" not in phases:
        for cmx in reversed(ps_ctx):
            cmx.__exit__(None, None, None)
        for cmx in sem_ctx:
            cmx.__exit__(None, None, None)
        return nc

    gC, sbC = make_group()
    xln2_feat = sbC("xln2_feat", [128, 2, 16, 512], BF16)
    hact = sbC("hact", [128, NIC, 512], BF16)
    wgu_sb = sbC("wgu_sb", [128, 2, 4, 1024], BF16)
    wdn_sb = sbC("wdn_sb", [128, 22, 1024], BF16)
    sg_sb = sbC("sg_sb", [128, 2, 512], BF16)
    y_sb = sbC("y_sb", [128, 4, 2048], F32)
    x2e_sb = sbC("x2e_sb", [128, 1, 2048], F32)
    out_sb = sbC("out_sb", [128, 2048], F32)

    for tb in range(4):
        bp = tb % 2
        for hc in range(16):
            P.op("sync", lambda e, hc=hc, tb=tb, bp=bp: e.dma_start(
                out=xln2_feat[:, bp, hc, :],
                in_=ln_dram[tb * 512:(tb + 1) * 512,
                            hc * 128:(hc + 1) * 128],
                transpose=True),
                writes=[f"x2f{bp}"], dma=True)

        for p in range(NPAIR):
            wp = p % 2
            P.op("sync", lambda e, p=p, wp=wp: e.dma_start(
                out=wgu_sb[:, wp, :, :],
                in_=cc_w_out[p * 4 * 128:(p + 1) * 4 * 128, :]
                .rearrange("(g p2) f -> p2 g f", p2=128)),
                deps=[ccw_ev], writes=[f"wgu{wp}"], dma=True)
            pg = p % 2
            PG, PU = [P0, P2][pg], [P1, P3][pg]
            pgk, puk = f"PG{pg}", f"PU{pg}"
            for gu in range(2):
                PP = PG if gu == 0 else PU
                ppk = pgk if gu == 0 else puk
                for hc in range(16):
                    lidx = gu * 16 + hc
                    g, cb = lidx // 8, lidx % 8
                    P.op("tensor",
                         lambda e, hc=hc, g=g, cb=cb, wp=wp, bp=bp, PP=PP:
                         e.matmul(
                             PP[:, :],
                             wgu_sb[:, wp, g, cb * 128:(cb + 1) * 128],
                             xln2_feat[:, bp, hc, :],
                             start=(hc == 0), stop=(hc == 15)),
                         reads=[f"wgu{wp}", f"x2f{bp}"], writes=[ppk])
            P.op("scalar", lambda e, pg=pg, PG=PG: e.activation(
                out=sg_sb[:, pg, :], in_=PG[:, :], func=AF.Silu),
                reads=[pgk], writes=[f"sg{pg}"])
            P.op("vector", lambda e, p=p, pg=pg, PU=PU: e.tensor_tensor(
                out=hact[:, p, :], in0=sg_sb[:, pg, :], in1=PU[:, :],
                op=ALU.mult),
                reads=[f"sg{pg}", puk], writes=["hact"])

        for mcg in range(4):
            for half in range(2):
                g0 = 176 + 22 * mcg + 11 * half
                P.op("sync", lambda e, g0=g0, half=half: e.dma_start(
                    out=wdn_sb[:, half * 11:(half + 1) * 11, :],
                    in_=cc_w_out[g0 * 128:(g0 + 11) * 128, :]
                    .rearrange("(g p2) f -> p2 g f", p2=128)),
                    deps=[ccw_ev], writes=[f"wdn{half}"], dma=True)
            for tt in range(4):
                pd = tt % 2
                PD = [P4, P5][pd]
                pdk = f"PD{pd}"
                for ic in range(NIC):
                    half = 0 if ic < 22 else 1
                    gl = ic // 2
                    co = (ic % 2) * 512
                    P.op("tensor",
                         lambda e, ic=ic, tt=tt, gl=gl, co=co, PD=PD:
                         e.matmul(
                             PD[:, :], hact[:, ic, tt * 128:(tt + 1) * 128],
                             wdn_sb[:, gl, co:co + 512],
                             start=(ic == 0), stop=(ic == NIC - 1)),
                         reads=["hact", f"wdn{half}"], writes=[pdk])
                P.op("scalar", lambda e, tt=tt, mcg=mcg, PD=PD: e.copy(
                    out=y_sb[:, tt, mcg * 512:(mcg + 1) * 512], in_=PD[:, :]),
                    reads=[pdk], writes=[f"y{tt}"])

        for tt in range(4):
            gt = tb * 4 + tt
            pe = tt % 2
            P.op("sync", lambda e, gt=gt: e.dma_start(
                out=x2e_sb[:, 0, :],
                in_=x2_dram[gt * 128:(gt + 1) * 128, :]),
                writes=["x2e0"], dma=True)
            P.op("scalar", lambda e, tt=tt: e.activation(
                out=junk[:, 0:2048], in_=y_sb[:, tt, :], func=AF.Square,
                accum_out=stat[:, 4:5]),
                reads=[f"y{tt}"], writes=["statc", "junk"])
            P.op("scalar", lambda e: e.activation(
                out=stat[:, 4:5], in_=stat[:, 4:5], func=AF.Ln,
                scale=INV_DM, bias=RMS_EPS_FFN),
                reads=["statc", "consts"], writes=["statc"])
            P.op("scalar", lambda e: e.activation(
                out=stat[:, 4:5], in_=stat[:, 4:5], func=AF.Exp, scale=-0.5),
                reads=["statc", "consts"], writes=["statc"])
            P.op("vector", lambda e, tt=tt: e.scalar_tensor_tensor(
                out=out_sb[:, :], in0=y_sb[:, tt, :],
                scalar=stat[:, 4:5], in1=x2e_sb[:, 0, :],
                op0=ALU.mult, op1=ALU.add),
                reads=[f"y{tt}", "statc", "x2e0"], writes=["outsb"])
            P.op("sync", lambda e, gt=gt: e.dma_start(
                out=y_out[gt * 128:(gt + 1) * 128, :], in_=out_sb[:, :]),
                reads=["outsb"], writes=["yd"], dma=True)

    P.barrier()
    P.flush(nc, sems)
    free_group(gC)
    free_group(gG)

    for cmx in reversed(ps_ctx):
        cmx.__exit__(None, None, None)
    for cmx in sem_ctx:
        cmx.__exit__(None, None, None)
    return nc


def _host_inputs(L0_attn_hidden_states, L0_attn_Wq, L0_attn_Wk, L0_attn_Wi,
                 L0_attn_Wo, L0_attn_g_weight, L0_ffn_gate_up_packed,
                 L0_ffn_gate_up_scales, L0_ffn_down_packed,
                 L0_ffn_down_scales):
    x = np.asarray(L0_attn_hidden_states, dtype=np.float32)
    wq = np.ascontiguousarray(np.asarray(L0_attn_Wq, np.float32))
    wk = np.ascontiguousarray(np.asarray(L0_attn_Wk, np.float32))
    wi = np.ascontiguousarray(np.asarray(L0_attn_Wi, np.float32))
    wo = np.ascontiguousarray(np.asarray(L0_attn_Wo, np.float32))
    gw = np.asarray(L0_attn_g_weight, np.float32)
    gup = np.asarray(L0_ffn_gate_up_packed, np.int32)
    gus = np.asarray(L0_ffn_gate_up_scales, np.float16)
    dnp = np.asarray(L0_ffn_down_packed, np.int32)
    dns = np.asarray(L0_ffn_down_scales, np.float16)

    gwc = np.ascontiguousarray(gw.reshape(NH, 128).T)
    trimask = np.zeros((128, 128), dtype=BF)
    su = np.tril(np.ones((64, 64), np.float32)).T.astype(BF)
    trimask[0:64, 0:64] = su
    trimask[0:64, 64:128] = 1.0
    trimask[64:128, 64:128] = su
    ident = np.eye(128, dtype=BF)
    consts = np.broadcast_to(np.array(CONST_VALS, np.float32),
                             (128, len(CONST_VALS))).copy()

    def tile_slices(gidx):
        if gidx < 1408:
            pair, rem = divmod(gidx, 32)
            gu, hc = divmod(rem, 16)
            r0 = (gu * NPAIR + pair) * 128
            pk = gup[r0:r0 + 128, hc * 16:(hc + 1) * 16]
            sc = gus[hc, r0:r0 + 128]
        else:
            g2 = gidx - 1408
            mcg, rem = divmod(g2, 176)
            ic, mi = divmod(rem, 4)
            mt = mcg * 4 + mi
            pk = dnp[mt * 128:(mt + 1) * 128, ic * 16:(ic + 1) * 16]
            sc = dns[ic, mt * 128:(mt + 1) * 128]
        return pk, sc

    per_core = []
    for c in range(8):
        b, half = c // 2, c % 2
        xs = np.ascontiguousarray(x[b, half * T:(half + 1) * T, :])
        pks = np.empty((TPC, 128, 16), np.int32)
        scs = np.empty((TPC, 128), np.float16)
        for i in range(TPC):
            pk, sc = tile_slices(c * TPC + i)
            pks[i] = pk
            scs[i] = sc
        mask8 = np.zeros((128, 8), np.float32)
        if half == 1:
            mask8[:, c - 1] = 1.0
        per_core.append(dict(
            x=xs, wq=wq, wk=wk, wi=wi, wo=wo,
            gwc=gwc, mask8=mask8, trimask=trimask, ident=ident,
            consts=consts, packedsl=pks,
            scalessl=np.ascontiguousarray(scs.T)))
    return per_core


def kernel(**inputs):
    per_core = _host_inputs(**inputs)
    nc = build_nc()
    res = run_bass_kernel_spmd(nc, per_core, core_ids=list(range(8)))
    out = np.empty((B_, L_, DM), np.float32)
    for c in range(8):
        b, half = c // 2, c % 2
        out[b, half * T:(half + 1) * T, :] = res.results[c]["y"]
    return out


if __name__ == "__main__":
    nc = build_nc()
    print("built ok; instructions:", nc.n_instructions())


# version 24
# speedup vs baseline: 1.2598x; 1.0061x over previous
import numpy as np
import ml_dtypes
import concourse.bass as bass
import concourse.mybir as mybir
from concourse.bass_utils import run_bass_kernel_spmd

F32 = mybir.dt.float32
BF16 = mybir.dt.bfloat16
F16 = mybir.dt.float16
I32 = mybir.dt.int32
U16 = mybir.dt.uint16
AF = mybir.ActivationFunctionType
ALU = mybir.AluOpType
AX = mybir.AxisListType

BF = ml_dtypes.bfloat16

B_, L_, DM = 4, 4096, 2048
NH, DK, DV = 16, 128, 128
SCALE = float(DK) ** -0.5
T = 2048
NBLK = 4
NCH = 16
NTT = 16
FFI = 5632
NPAIR = 44
NIC = 44
NTILES = 2112
TPC = 264
NGRP = NTILES // 8
GPC = TPC // 8

LN_EPS = 1e-5
RMS_EPS_FFN = 1e-6
INV_DM = 1.0 / 2048.0

CONST_VALS = [SCALE, -1.0, 1.0, 2.0, -0.5, INV_DM, LN_EPS, RMS_EPS_FFN]

ENGINES = ("sync", "gpsimd", "tensor", "vector", "scalar")
DMA_QUEUES = ("sync", "gpsimd", "scalar")
NDMASEM = 8
DMA_SEMS = tuple(f"{q}dma{i}" for q in DMA_QUEUES for i in range(NDMASEM))
ALL_SEMS = ENGINES + ("cc",) + DMA_SEMS


class Prog:

    def __init__(self):
        self.items = []
        self.count = {s: 0 for s in ALL_SEMS}
        self.hwm = {e: {s: 0 for s in ALL_SEMS} for e in ENGINES}
        self.last_w = {}
        self.readers = {}
        self.dma_idx = {q: 0 for q in DMA_QUEUES}

    def op(self, eng, fn, reads=(), writes=(), dma=False, deps=(), sem=None):
        dep_evs = []
        for k in reads:
            ev = self.last_w.get(k)
            if ev is not None:
                dep_evs.append(ev)
        for k in writes:
            ev = self.last_w.get(k)
            if ev is not None:
                dep_evs.append(ev)
            dep_evs.extend(self.readers.get(k, ()))
        dep_evs.extend(deps)
        if dma:
            idx = self.dma_idx[eng]
            self.dma_idx[eng] = idx + 1
            slot = idx % NDMASEM
            dmasem = f"{eng}dma{slot}"
            dep_evs.append((dmasem, 16 * (idx // NDMASEM), True))
        wmax = {}
        for (ds, dv, dasync) in dep_evs:
            if ds == eng and eng == "tensor" and not dasync:
                continue
            if self.hwm[eng][ds] >= dv:
                continue
            wmax[ds] = max(wmax.get(ds, 0), dv)
        for ds, dv in wmax.items():
            self.hwm[eng][ds] = dv
        if dma:
            semname = dmasem
        else:
            semname = sem if sem is not None else eng
        inc = 16 if dma else 1
        self.count[semname] += inc
        is_async = dma or (sem is not None)
        ev = (semname, self.count[semname], is_async)
        self.items.append((eng, fn, sorted(wmax.items()), semname, inc))
        for k in reads:
            self.readers.setdefault(k, []).append(ev)
        for k in writes:
            self.last_w[k] = ev
            self.readers[k] = []
        return ev

    def barrier(self):
        snap = {s: self.count[s] for s in ENGINES + DMA_SEMS}
        for e in ENGINES:
            waits = []
            for e2 in ENGINES + DMA_SEMS:
                if e2 == e:
                    continue
                if self.hwm[e][e2] < snap[e2]:
                    waits.append((e2, snap[e2]))
                    self.hwm[e][e2] = snap[e2]
            if waits:
                self.items.append((e, None, waits, e, 0))
        self.last_w = {}
        self.readers = {}

    def flush(self, nc, sems):
        items, self.items = self.items, []
        with nc.Block() as block:
            for eng_name in ENGINES:
                def make(eng_name=eng_name):
                    def body(eng):
                        for (en, fn, waits, semname, inc) in items:
                            if en != eng_name:
                                continue
                            for (se, sv) in waits:
                                eng.wait_ge(sems[se], sv)
                            if fn is not None:
                                fn(eng).then_inc(sems[semname], inc)
                    return body
                getattr(block, eng_name)(make())


def build_nc(debug_outputs=False, phases=("A", "B", "C")):
    nc = bass.Bass("TRN2", target_bir_lowering=False)

    def din(name, shape, dt):
        return nc.dram_tensor(name, shape, dt, kind="ExternalInput").ap()

    x_in = din("x", [T, DM], F32)
    wq_in = din("wq", [DM, DM], F32)
    wk_in = din("wk", [DM, DM], F32)
    wi_in = din("wi", [DM, DM], F32)
    wo_in = din("wo", [DM, DM], F32)
    gwc_in = din("gwc", [128, NH], F32)
    mask8_in = din("mask8", [128, 8], F32)
    trimask_in = din("trimask", [128, 128], BF16)
    ident_in = din("ident", [128, 128], BF16)
    consts_in = din("consts", [128, len(CONST_VALS)], F32)
    packed_in = din("packedsl", [TPC, 128, 16], I32)
    scales_in = din("scalessl", [128, TPC], F16)

    y_out = nc.dram_tensor("y", [T, DM], F32, kind="ExternalOutput").ap()

    dkind = "ExternalOutput" if debug_outputs else "Internal"

    def dscr(name, shape, dt, dbg=False):
        return nc.dram_tensor(name, shape, dt,
                              kind=(dkind if dbg else "Internal")).ap()

    wq16t = dscr("wq16t", [NH, 128, 16 * 128], BF16)
    wk16t = dscr("wk16t", [NH, 128, 16 * 128], BF16)
    wi16 = dscr("wi16", [DM, DM], BF16)
    wo16 = dscr("wo16", [DM, DM], BF16, dbg=True)
    ln_dram = dscr("ln_dram", [T, DM], BF16, dbg=True)
    ogg_dram = dscr("ogg_dram", [NCH, 128, NH * 128], BF16)
    o_dram = dscr("o_dram", [NCH, 128, NH * 128], BF16, dbg=True)
    x2_dram = dscr("x2_dram", [T, DM], F32, dbg=True)
    dbgf = dscr("dbgf", [128, 4096], F32, dbg=True)
    dbgb = dscr("dbgb", [128, 4096], BF16, dbg=True)
    cc_w_in = nc.dram_tensor("cc_w_in", [GPC * 128, 1024], BF16).ap()
    cc_w_out = nc.dram_tensor("cc_w_out", [NGRP * 128, 1024], BF16,
                              addr_space="Shared").ap()
    cc_s_in = nc.dram_tensor("cc_s_in", [128, NH * 128], F32).ap()
    cc_s_out = nc.dram_tensor("cc_s_out", [8 * 128, NH * 128], F32,
                              addr_space="Shared").ap()

    P = Prog()

    sems = {}
    sem_ctx = []
    for s in ALL_SEMS:
        cm = nc.semaphore(f"sem_{s}")
        sem_ctx.append(cm)
        sems[s] = cm.__enter__()

    all_ctx = []

    def make_group():
        group = []

        def sb(name, shape, dt):
            cm = nc.sbuf_tensor("sb_" + name, shape, dt)
            group.append(cm)
            all_ctx.append(cm)
            return cm.__enter__()
        return group, sb

    def free_group(group):
        for cm in reversed(group):
            cm.__exit__(None, None, None)
            all_ctx.remove(cm)

    ps_ctx = []

    def psb(name):
        cm = nc.psum_tensor(name, [128, 512], F32)
        ps_ctx.append(cm)
        return cm.__enter__()

    gG, sbG = make_group()
    consts = sbG("consts", [128, len(CONST_VALS)], F32)
    gwc = sbG("gwc", [128, NH], F32)
    mask8 = sbG("mask8", [128, 8], F32)
    trimask = sbG("trimask", [128, 128], BF16)
    ident = sbG("ident", [128, 128], BF16)
    junk = sbG("junk", [128, 2048], BF16)
    junkf = sbG("junkf", [128, 256], F32)
    xln_tok = sbG("xln_tok", [128, 1, 2048], BF16)
    stat = sbG("stat", [128, 16], F32)

    for i, v in enumerate(CONST_VALS):
        nc.const_aps.register(F32, v, consts[:, i:i + 1])

    P0 = psb("P0")
    P1 = psb("P1")
    P2 = psb("P2")
    P3 = psb("P3")
    P4 = psb("P4")
    P5 = psb("P5")
    P6 = psb("P6")
    cm = nc.psum_tensor("P7b", [128, 1024], BF16)
    ps_ctx.append(cm)
    P7b = cm.__enter__()

    P.op("sync", lambda e: e.dma_start(out=consts[:], in_=consts_in),
         writes=["consts"], dma=True)
    P.op("sync", lambda e: e.dma_start(out=gwc[:], in_=gwc_in),
         writes=["gwc"], dma=True)
    P.op("sync", lambda e: e.dma_start(out=mask8[:], in_=mask8_in),
         writes=["mask8"], dma=True)
    P.op("sync", lambda e: e.dma_start(out=trimask[:], in_=trimask_in),
         writes=["trimask"], dma=True)
    P.op("sync", lambda e: e.dma_start(out=ident[:], in_=ident_in),
         writes=["ident"], dma=True)
    P.op("sync", lambda e: e.dma_start(out=scl16[:], in_=scales_in),
         writes=["scl16"], dma=True)
    P.op("vector", lambda e: e.tensor_copy(out=scl[:], in_=scl16[:]),
         reads=["scl16"], writes=["scl"])
    P.op("vector", lambda e: e.memset(S_sb[:], 0.0), writes=["S"])
    P.op("vector", lambda e: e.memset(S_bf[:], 0.0), writes=["Sbf"])
    P.op("vector", lambda e: e.memset(R_run[:], 1.0), writes=["Rrun"])

    wi_r = wi_in.rearrange("(hc p) (dc f) -> dc hc p f", p=128, f=512)
    wi16_r = wi16.rearrange("(hc p) (dc f) -> dc hc p f", p=128, f=512)
    wq_r = wq_in.rearrange("(hc p) (mc f) -> mc hc p f", p=128, f=128)
    wk_r = wk_in.rearrange("(hc p) (mc f) -> mc hc p f", p=128, f=128)

    def cast_wi(dvc):
        P.op("gpsimd", lambda e, dvc=dvc: e.dma_start(
            out=wi16_r[dvc], in_=wi_r[dvc]),
            writes=[f"wi16d{dvc}"], dma=True)

    for dvc in range(4):
        cast_wi(dvc)

    gA, sbA = make_group()
    scl16 = sbA("scl16", [128, TPC], F16)
    scl = sbA("scl", [128, TPC], F32)
    S_sb = sbA("S_sb", [128, NH, 128], F32)
    S_bf = sbA("S_bf", [128, NH, 128], BF16)
    R_run = sbA("R_run", [128, NH], F32)
    xt = sbA("xt", [128, 2, 2048], F32)
    xln_feat = sbA("xln_feat", [128, 2, 16, 512], BF16)
    v_tok = sbA("v_tok", [128, 4, 2048], BF16)
    wi_sb = sbA("wi_sb", [128, 16, 512], BF16)
    wq_sb = sbA("wq_sb", [128, 2, 16, 128], BF16)
    wk_sb = sbA("wk_sb", [128, 2, 16, 128], BF16)
    q_h = sbA("q_h", [128, 4, 512], BF16)
    kraw_h = sbA("kraw_h", [128, 2, 512], F32)
    sp_h = sbA("sp_h", [128, 2, 512], F32)
    gh_h = sbA("gh_h", [128, 2, 512], F32)
    bb_h = sbA("bb_h", [128, 2, 512], F32)
    eg_h = sbA("eg_h", [128, 2, 512], F32)
    qg_h = sbA("qg_h", [128, 2, 512], BF16)
    qb_h = sbA("qb_h", [128, 2, 4, 64], BF16)
    ka_h = sbA("ka_h", [128, 2, 4, 64], BF16)
    kb_h = sbA("kb_h", [128, 2, 4, 64], BF16)
    kba_h = sbA("kba_h", [128, 2, 4, 64], BF16)
    kh_h = sbA("kh_h", [128, 2, 512], BF16)
    kh_tok = sbA("kh_tok", [128, 2, 2, 128], BF16)
    ng_h = sbA("ng_h", [128, 2, 8], F32)
    asb = sbA("asb", [128, 2, 128], BF16)
    zchunk = sbA("zchunk", [128, 128], F32)
    qgg_slab = sbA("qgg_slab", [128, 4, 16, 128], BF16)
    o_slab = sbA("o_slab", [128, 4, 16, 128], BF16)
    dq_pk = sbA("dq_pk", [128, 2, 4, 16], I32)
    dq_nib = sbA("dq_nib", [128, 128, 4], U16)
    dq_bits = sbA("dq_bits", [128, 512], U16)
    dq_sgn = sbA("dq_sgn", [128, 512], U16)
    dq_sgnf = sbA("dq_sgnf", [128, 512], BF16)
    dq_u16 = sbA("dq_u16", [128, 512], U16)
    dq_relu = sbA("dq_relu", [128, 512], BF16)
    dq_min = sbA("dq_min", [128, 512], BF16)
    dq_out = sbA("dq_out", [128, 2, 512], BF16)
    dq_tr = sbA("dq_tr", [128, 2, 128], BF16)

    P.op("vector", lambda e: e.memset(zchunk[:], 0.0), writes=["zchunk"])
    P.op("vector", lambda e: e.memset(P2[64:128, 0:64], 0.0), writes=["PA0"])
    P.op("vector", lambda e: e.memset(P3[64:128, 0:64], 0.0), writes=["PA1"])

    def emit_dequant_dve(bi):
        pk_par = bi % 2
        op_par = bi % 2
        lt0 = 4 * bi
        pk = dq_pk[:, pk_par, :, :]
        P.op("gpsimd",
             lambda e, pk=pk, lt0=lt0: e.dma_start(
                 out=pk, in_=packed_in[lt0:lt0 + 4].rearrange(
                     "t p w -> p t w")),
             writes=[f"dqpk{pk_par}"], dma=True)
        pku = pk.rearrange("p t w -> p (t w)").bitcast(U16)
        for k in range(4):
            P.op("vector",
                 lambda e, k=k, pku=pku: e.tensor_scalar(
                     out=dq_nib[:, :, k], in0=pku, scalar1=4 * k, scalar2=15,
                     op0=ALU.logical_shift_right, op1=ALU.bitwise_and),
                 reads=[f"dqpk{pk_par}"], writes=["dqnib"])
        nib3 = dq_nib[:, :, :]
        P.op("vector", lambda e, nib3=nib3: e.tensor_scalar(
            out=dq_bits[:], in0=nib3, scalar1=7, scalar2=6,
            op0=ALU.bitwise_and, op1=ALU.logical_shift_left),
            reads=["dqnib"], writes=["dqbits"])
        P.op("vector", lambda e: e.tensor_scalar(
            out=dq_u16[:], in0=dq_bits[:], scalar1=0x3F00,
            scalar2=None, op0=ALU.add), reads=["dqbits"], writes=["dqu16"])
        P.op("vector", lambda e, nib3=nib3: e.tensor_scalar(
            out=dq_sgn[:], in0=nib3, scalar1=3, scalar2=1,
            op0=ALU.logical_shift_right, op1=ALU.bitwise_and),
            reads=["dqnib"], writes=["dqsgn"])
        P.op("vector", lambda e: e.tensor_scalar(
            out=dq_sgnf[:], in0=dq_sgn[:], scalar1=-2,
            scalar2=1, op0=ALU.mult, op1=ALU.add),
            reads=["dqsgn"], writes=["dqsgnf"])
        fmv = dq_u16[:].bitcast(BF16)
        P.op("scalar", lambda e, fmv=fmv: e.activation(
            out=dq_relu[:], in_=fmv, func=AF.Relu, bias=-1.0,
            scale=2.0),
            reads=["dqu16", "consts"], writes=["dqrelu"])
        P.op("vector", lambda e, fmv=fmv: e.tensor_tensor(
            out=dq_min[:], in0=dq_relu[:], in1=fmv,
            op=ALU.min),
            reads=["dqrelu", "dqu16"], writes=["dqmin"])
        for j in range(4):
            lt = lt0 + j
            P.op("vector",
                 lambda e, j=j, lt=lt, op_par=op_par: e.scalar_tensor_tensor(
                     out=dq_out[:, op_par, j * 128:(j + 1) * 128],
                     in0=dq_min[:, j * 128:(j + 1) * 128],
                     scalar=scl[:, lt:lt + 1],
                     in1=dq_sgnf[:, j * 128:(j + 1) * 128],
                     op0=ALU.mult, op1=ALU.mult),
                 reads=["dqmin", "dqsgnf", "scl"], writes=[f"dqout{op_par}"])

    def emit_dequant_tr(bi):
        op_par = bi % 2
        lt0 = 4 * bi
        for j in range(4):
            lt = lt0 + j
            tp = j % 2
            P.op("tensor", lambda e, j=j, op_par=op_par: e.transpose(
                P7b[:, 0:128], dq_out[:, op_par, j * 128:(j + 1) * 128],
                ident[:]),
                reads=[f"dqout{op_par}", "ident"], writes=["P7b"])
            P.op("scalar", lambda e, tp=tp: e.copy(
                out=dq_tr[:, tp, :], in_=P7b[:, 0:128]),
                reads=["P7b"], writes=[f"dqtr{tp}"])
            g, cb = lt // 8, lt % 8
            P.op("gpsimd", lambda e, g=g, cb=cb, tp=tp: e.dma_start(
                out=cc_w_in[g * 128:(g + 1) * 128, cb * 128:(cb + 1) * 128],
                in_=dq_tr[:, tp, :]),
                reads=[f"dqtr{tp}"], dma=True)

    dq_iter = iter(range(TPC // 4))

    dq_pending = []

    def emit_some_dequant(n):
        for _ in range(n):
            bi = next(dq_iter, None)
            if bi is None:
                while dq_pending:
                    emit_dequant_tr(dq_pending.pop(0))
                return
            emit_dequant_dve(bi)
            dq_pending.append(bi)
            if len(dq_pending) > 1:
                emit_dequant_tr(dq_pending.pop(0))

    def emit_ln(xin_ap, xkey, out_ap, okey, c0):
        m_c = stat[:, c0:c0 + 1]
        v_c = stat[:, c0 + 1:c0 + 2]
        km, kv = f"stat{c0}", f"stat{c0 + 1}"
        P.op("vector", lambda e: e.reduce_sum(m_c, xin_ap, axis=AX.X),
             reads=[xkey], writes=[km])
        P.op("scalar", lambda e: e.activation(
            out=junk[:, 0:2048], in_=xin_ap, func=AF.Square, accum_out=v_c),
            reads=[xkey], writes=[kv, "junk"])
        P.op("vector", lambda e: e.tensor_scalar(
            out=m_c, in0=m_c, scalar1=INV_DM, scalar2=None, op0=ALU.mult),
            reads=[km], writes=[km])
        P.op("vector", lambda e: e.tensor_scalar(
            out=v_c, in0=v_c, scalar1=INV_DM, scalar2=None, op0=ALU.mult),
            reads=[kv], writes=[kv])
        P.op("vector", lambda e: e.scalar_tensor_tensor(
            out=v_c, in0=m_c, scalar=m_c, in1=v_c,
            op0=ALU.mult, op1=ALU.subtract),
            reads=[km, kv], writes=[kv])
        P.op("vector", lambda e: e.tensor_scalar(
            out=v_c, in0=v_c, scalar1=-1.0, scalar2=None, op0=ALU.mult),
            reads=[kv], writes=[kv])
        P.op("scalar", lambda e: e.activation(
            out=v_c, in_=v_c, func=AF.Ln, bias=LN_EPS),
            reads=[kv, "consts"], writes=[kv])
        P.op("scalar", lambda e: e.activation(
            out=v_c, in_=v_c, func=AF.Exp, scale=-0.5),
            reads=[kv, "consts"], writes=[kv])
        P.op("vector", lambda e: e.tensor_scalar(
            out=out_ap, in0=xin_ap, scalar1=m_c, scalar2=v_c,
            op0=ALU.subtract, op1=ALU.mult),
            reads=[xkey, km, kv], writes=[okey])

    ccw_ev = None
    def emit_blockLN(tb):
        for tt in range(4):
            gt = tb * 4 + tt
            par = gt % 2
            P.op("sync", lambda e, gt=gt, par=par: e.dma_start(
                out=xt[:, par, :], in_=x_in[gt * 128:(gt + 1) * 128, :]),
                writes=[f"xt{par}"], dma=True)
            emit_ln(xt[:, par, :], f"xt{par}", xln_tok[:, 0, :],
                    "xlntok", 0)
            P.op("sync", lambda e, gt=gt, par=par: e.dma_start(
                out=ln_dram[gt * 128:(gt + 1) * 128, :],
                in_=xln_tok[:, 0, :]),
                reads=["xlntok"], writes=[f"lnd{gt}"], dma=True)

    emit_blockLN(0)
    for tb in range(NBLK):
        bp = tb % 2
        if tb == 3:
            P.op("gpsimd", lambda e: e.dma_start(out=wo16, in_=wo_in),
                 writes=["wo16d"], dma=True)
        for hc in range(16):
            P.op("sync", lambda e, hc=hc, tb=tb, bp=bp: e.dma_start(
                out=xln_feat[:, bp, hc, :],
                in_=ln_dram[tb * 512:(tb + 1) * 512,
                            hc * 128:(hc + 1) * 128],
                transpose=True),
                reads=[f"lnd{tb * 4 + i}" for i in range(4)],
                writes=[f"xlf{bp}"], dma=True)

        for dvc in range(4):
            P.op("gpsimd", lambda e, dvc=dvc: e.dma_start(
                out=wi_sb[:, :, :],
                in_=wi16_r[dvc].rearrange("hc p f -> p hc f")),
                reads=[f"wi16d{dvc}"], writes=["wi_sb"], dma=True)
            for c in range(4):
                pp = [P0, P1][c % 2]
                pkey = f"P{c % 2}"
                for hc in range(16):
                    P.op("tensor", lambda e, pp=pp, hc=hc, c=c, bp=bp:
                         e.matmul(pp[:, :],
                                  xln_feat[:, bp, hc, c * 128:(c + 1) * 128],
                                  wi_sb[:, hc, :],
                                  start=(hc == 0), stop=(hc == 15)),
                         reads=[f"xlf{bp}", "wi_sb"], writes=[pkey])
                P.op("scalar", lambda e, pp=pp, c=c, dvc=dvc: e.copy(
                    out=v_tok[:, c, dvc * 512:(dvc + 1) * 512], in_=pp[:, :]),
                    reads=[pkey], writes=[f"vtok{c}"])

        def emit_wload(h):
            wp2 = h % 2
            if tb == 0:
                P.op("gpsimd", lambda e, h=h, wp2=wp2: e.dma_start(
                    out=wq_sb[:, wp2, :, :],
                    in_=wq_r[h].rearrange("hc p f -> p hc f")),
                    writes=[f"wq_sb{wp2}"], dma=True)
                P.op("sync", lambda e, h=h, wp2=wp2: e.dma_start(
                    out=wq16t[h],
                    in_=wq_sb[:, wp2, :, :].rearrange("p hc f -> p (hc f)")),
                    reads=[f"wq_sb{wp2}"], writes=[f"wq16d{h}"], dma=True)
                P.op("gpsimd", lambda e, h=h, wp2=wp2: e.dma_start(
                    out=wk_sb[:, wp2, :, :],
                    in_=wk_r[h].rearrange("hc p f -> p hc f")),
                    writes=[f"wk_sb{wp2}"], dma=True)
                P.op("sync", lambda e, h=h, wp2=wp2: e.dma_start(
                    out=wk16t[h],
                    in_=wk_sb[:, wp2, :, :].rearrange("p hc f -> p (hc f)")),
                    reads=[f"wk_sb{wp2}"], writes=[f"wk16d{h}"], dma=True)
            else:
                P.op("sync", lambda e, h=h, wp2=wp2: e.dma_start(
                    out=wq_sb[:, wp2, :, :].rearrange("p hc f -> p (hc f)"),
                    in_=wq16t[h]),
                    reads=[f"wq16d{h}"], writes=[f"wq_sb{wp2}"], dma=True)
                P.op("sync", lambda e, h=h, wp2=wp2: e.dma_start(
                    out=wk_sb[:, wp2, :, :].rearrange("p hc f -> p (hc f)"),
                    in_=wk16t[h]),
                    reads=[f"wk16d{h}"], writes=[f"wk_sb{wp2}"], dma=True)

        def emit_proj(h):
            hp = h % 4
            wp2 = h % 2
            for hc in range(16):
                P.op("tensor", lambda e, hc=hc, wp2=wp2, bp=bp: e.matmul(
                    P0[:, :], wq_sb[:, wp2, hc, :],
                    xln_feat[:, bp, hc, :],
                    start=(hc == 0), stop=(hc == 15)),
                    reads=[f"wq_sb{wp2}", f"xlf{bp}"], writes=["P0"])
            P.op("scalar", lambda e, hp=hp: e.activation(
                out=q_h[:, hp, :], in_=P0[:, :], func=AF.Relu, scale=SCALE),
                reads=["P0", "consts"], writes=[f"qh{hp}"])
            for hc in range(16):
                P.op("tensor", lambda e, hc=hc, wp2=wp2, bp=bp: e.matmul(
                    P1[:, :], wk_sb[:, wp2, hc, :],
                    xln_feat[:, bp, hc, :],
                    start=(hc == 0), stop=(hc == 15)),
                    reads=[f"wk_sb{wp2}", f"xlf{bp}"], writes=["P1"])

        def emit_prep1(h):
            hp = h % 2
            spv = sp_h[:, hp, :]
            P.op("scalar", lambda e, spv=spv: e.activation(
                out=spv, in_=P1[:, :], func=AF.Abs),
                reads=["P1"], writes=[f"sp{hp}"])
            P.op("scalar", lambda e, hp=hp: e.copy(
                out=kraw_h[:, hp, :], in_=P1[:, :]),
                reads=["P1"], writes=[f"krawh{hp}"])
            P.op("scalar", lambda e, spv=spv: e.activation(
                out=spv, in_=spv, func=AF.Exp, scale=-1.0),
                reads=[f"sp{hp}", "consts"], writes=[f"sp{hp}"])
            P.op("scalar", lambda e, spv=spv: e.activation(
                out=spv, in_=spv, func=AF.Ln, bias=1.0),
                reads=[f"sp{hp}", "consts"], writes=[f"sp{hp}"])

        def emit_prep2(h):
            hp = h % 2
            kr = kraw_h[:, hp, :]
            spv = sp_h[:, hp, :]
            ghv = gh_h[:, hp, :]
            bbv = bb_h[:, hp, :]
            P.op("vector", lambda e, kr=kr, bbv=bbv: e.tensor_scalar(
                out=bbv, in0=kr, scalar1=0.0, scalar2=None, op0=ALU.max),
                reads=[f"krawh{hp}"], writes=[f"bb{hp}"])
            P.op("vector", lambda e, spv=spv, bbv=bbv: e.tensor_add(
                out=spv, in0=spv, in1=bbv),
                reads=[f"sp{hp}", f"bb{hp}"], writes=[f"sp{hp}"])
            for c in range(4):
                P.op("vector", lambda e, c=c, spv=spv, ghv=ghv:
                     e.tensor_tensor_scan(
                         out=ghv[:, c * 128:(c + 1) * 128],
                         data0=spv[:, c * 128:(c + 1) * 128],
                         data1=zchunk[:, :],
                         initial=0.0, op0=ALU.add, op1=ALU.add),
                     reads=[f"sp{hp}", "zchunk"], writes=[f"gh{hp}"])
            P.op("vector", lambda e, kr=kr, spv=spv, bbv=bbv: e.tensor_sub(
                out=bbv, in0=kr, in1=spv),
                reads=[f"krawh{hp}", f"sp{hp}"], writes=[f"bb{hp}"])
            P.op("vector", lambda e, bbv=bbv, ghv=ghv: e.tensor_add(
                out=bbv, in0=bbv, in1=ghv),
                reads=[f"bb{hp}", f"gh{hp}"], writes=[f"bb{hp}"])
            P.op("vector", lambda e, hp=hp, ghv=ghv: e.tensor_scalar(
                out=ng_h[:, hp, 0:8:2],
                in0=ghv[:, 63:512:128],
                scalar1=-1.0, scalar2=None, op0=ALU.mult),
                reads=[f"gh{hp}"], writes=[f"ng{hp}"])
            P.op("vector", lambda e, hp=hp, ghv=ghv: e.tensor_scalar(
                out=ng_h[:, hp, 1:8:2],
                in0=ghv[:, 127:512:128],
                scalar1=-1.0, scalar2=None, op0=ALU.mult),
                reads=[f"gh{hp}"], writes=[f"ng{hp}"])

        def emit_prep3(h):
            hq = h % 4
            hp = h % 2
            ghv = gh_h[:, hp, :]
            bbv = bb_h[:, hp, :]
            egv = eg_h[:, hp, :]
            P.op("scalar", lambda e, ghv=ghv, egv=egv: e.activation(
                out=egv, in_=ghv, func=AF.Exp, scale=-1.0),
                reads=[f"gh{hp}", "consts"], writes=[f"eg{hp}"])
            P.op("vector", lambda e, hp=hp, hq=hq, egv=egv: e.tensor_mul(
                out=qg_h[:, hp, :], in0=q_h[:, hq, :], in1=egv),
                reads=[f"qh{hq}", f"eg{hp}"], writes=[f"qg{hp}"])
            for c in range(4):
                P.op("scalar", lambda e, c=c, hp=hp, bbv=bbv: e.activation(
                    out=ka_h[:, hp, c, :],
                    in_=bbv[:, c * 128:c * 128 + 64],
                    func=AF.Exp),
                    reads=[f"bb{hp}"], writes=[f"ka{hp}"])
            for c in range(4):
                ng63 = ng_h[:, hp, 2 * c:2 * c + 1]
                ng127 = ng_h[:, hp, 2 * c + 1:2 * c + 2]
                g63 = gh_h[:, hp, c * 128 + 63:c * 128 + 64]
                P.op("scalar", lambda e, c=c, hp=hp, bbv=bbv, ng63=ng63:
                     e.activation(
                         out=kb_h[:, hp, c, :],
                         in_=bbv[:, c * 128 + 64:(c + 1) * 128],
                         func=AF.Exp, bias=ng63),
                     reads=[f"bb{hp}", f"ng{hp}"], writes=[f"kb{hp}"])
                P.op("scalar", lambda e, c=c, hp=hp, bbv=bbv, ng63=ng63:
                     e.activation(
                         out=kba_h[:, hp, c, :],
                         in_=bbv[:, c * 128:c * 128 + 64],
                         func=AF.Exp, bias=ng63),
                     reads=[f"bb{hp}", f"ng{hp}"], writes=[f"kba{hp}"])
                P.op("scalar", lambda e, c=c, hp=hp, bbv=bbv, ng127=ng127:
                     e.activation(
                         out=kh_h[:, hp, c * 128:(c + 1) * 128],
                         in_=bbv[:, c * 128:(c + 1) * 128],
                         func=AF.Exp, bias=ng127),
                     reads=[f"bb{hp}", f"ng{hp}"], writes=[f"kh{hp}"])
                P.op("scalar", lambda e, c=c, hp=hp, ghv=ghv, g63=g63:
                     e.activation(
                         out=junkf[:, c * 64:(c + 1) * 64],
                         in_=ghv[:, c * 128 + 64:(c + 1) * 128],
                         func=AF.Exp, scale=-1.0, bias=g63),
                     reads=[f"gh{hp}", "consts"], writes=["jf"])
                P.op("vector", lambda e, c=c, hp=hp, hq=hq: e.tensor_mul(
                    out=qb_h[:, hp, c, :],
                    in0=q_h[:, hq, c * 128 + 64:(c + 1) * 128],
                    in1=junkf[:, c * 64:(c + 1) * 64]),
                    reads=[f"qh{hq}", "jf"], writes=[f"qb{hp}"])
            if tb == 0:
                for c in range(4):
                    P.op("vector", lambda e, c=c, hp=hp, h=h: e.tensor_scalar(
                        out=qgg_slab[:, c, h, :],
                        in0=qg_h[:, hp, c * 128:(c + 1) * 128],
                        scalar1=R_run[:, h:h + 1], scalar2=None,
                        op0=ALU.mult),
                        reads=[f"qg{hp}", "Rrun"], writes=[f"qgg{c}"])
                    P.op("vector", lambda e, c=c, hp=hp, h=h, egv=egv:
                         e.tensor_scalar(
                             out=R_run[:, h:h + 1], in0=R_run[:, h:h + 1],
                             scalar1=egv[:, c * 128 + 127:c * 128 + 128],
                             scalar2=None, op0=ALU.mult),
                         reads=["Rrun", f"eg{hp}"], writes=["Rrun"])

        def emit_apart(h, c):
            hp = h % 2
            cp = c % 2
            PA = [P2, P3][cp]
            pak = f"PA{cp}"
            P.op("tensor", lambda e, c=c, hp=hp, cp=cp: e.transpose(
                P7b[:, 0:128], kh_h[:, hp, c * 128:(c + 1) * 128],
                ident[:]),
                reads=[f"kh{hp}", "ident"], writes=["P7b"])
            P.op("vector", lambda e, hp=hp, cp=cp, h=h: e.tensor_copy(
                out=kh_tok[:, h % 2, cp, :], in_=P7b[:, 0:128]),
                reads=["P7b"], writes=[f"khtok{h % 2}{cp}"])
            P.op("tensor", lambda e, c=c, hp=hp, PA=PA: e.matmul(
                PA[0:64, 0:64], ka_h[:, hp, c, :],
                qg_h[:, hp, c * 128:c * 128 + 64],
                start=True, stop=True),
                reads=[f"ka{hp}", f"qg{hp}"], writes=[pak])
            P.op("tensor", lambda e, c=c, hp=hp, PA=PA: e.matmul(
                PA[0:64, 64:128], kba_h[:, hp, c, :], qb_h[:, hp, c, :],
                start=True, stop=True),
                reads=[f"kba{hp}", f"qb{hp}"], writes=[pak])
            P.op("tensor", lambda e, c=c, hp=hp, PA=PA: e.matmul(
                PA[64:128, 64:128], kb_h[:, hp, c, :], qb_h[:, hp, c, :],
                start=True, stop=True),
                reads=[f"kb{hp}", f"qb{hp}"], writes=[pak])
            P.op("vector", lambda e, cp=cp, PA=PA: e.tensor_tensor(
                out=asb[:, cp, :], in0=PA[:, 0:128],
                in1=trimask[:], op=ALU.mult),
                reads=[pak, "trimask"], writes=[f"asb{cp}"])

        def emit_opart(h, c):
            hp = h % 2
            cp = c % 2
            PO = [P4, P5][cp]
            pok = f"PO{cp}"
            egv = eg_h[:, hp, :]
            P.op("tensor", lambda e, c=c, h=h, cp=cp, PO=PO: e.matmul(
                PO[:, 0:64], v_tok[:, c, h * 128:(h + 1) * 128],
                asb[:, cp, 0:64], start=True, stop=False),
                reads=[f"vtok{c}", f"asb{cp}"], writes=[pok])
            P.op("tensor", lambda e, c=c, h=h, hp=hp, PO=PO: e.matmul(
                PO[:, 0:64], S_bf[:, h, :],
                qg_h[:, hp, c * 128:c * 128 + 64],
                start=False, stop=True),
                reads=["Sbf", f"qg{hp}"], writes=[pok])
            P.op("tensor", lambda e, c=c, h=h, cp=cp, PO=PO: e.matmul(
                PO[:, 64:128], v_tok[:, c, h * 128:(h + 1) * 128],
                asb[:, cp, 64:128], start=True, stop=False),
                reads=[f"vtok{c}", f"asb{cp}"], writes=[pok])
            P.op("tensor", lambda e, c=c, h=h, hp=hp, PO=PO: e.matmul(
                PO[:, 64:128], S_bf[:, h, :],
                qg_h[:, hp, c * 128 + 64:(c + 1) * 128],
                start=False, stop=True),
                reads=["Sbf", f"qg{hp}"], writes=[pok])
            P.op("tensor", lambda e, c=c, h=h, hp=hp: e.matmul(
                P6[:, 0:128], kh_tok[:, h % 2, cp, :],
                v_tok[:, c, h * 128:(h + 1) * 128],
                start=True, stop=True),
                reads=[f"khtok{h % 2}{cp}", f"vtok{c}"], writes=["P6"])
            P.op("scalar", lambda e, c=c, h=h, PO=PO: e.copy(
                out=o_slab[:, c, h, :], in_=PO[:, 0:128]),
                reads=[pok], writes=[f"oslab{c}"])
            P.op("vector", lambda e, c=c, h=h, hp=hp, egv=egv:
                 e.scalar_tensor_tensor(
                     out=S_sb[:, h, :], in0=S_sb[:, h, :],
                     scalar=egv[:, c * 128 + 127:c * 128 + 128],
                     in1=P6[:, 0:128], op0=ALU.mult, op1=ALU.add),
                 reads=["S", f"eg{hp}", "P6"], writes=["S"])
            P.op("vector", lambda e, h=h: e.tensor_copy(
                out=S_bf[:, h, :], in_=S_sb[:, h, :]),
                reads=["S"], writes=["Sbf"])

        def emit_scans(h):
            emit_apart(h, 0)
            for c in range(4):
                if c < 3:
                    emit_apart(h, c + 1)
                emit_opart(h, c)

        for i in range(NH + 3):
            if 0 <= i - 3 < NH:
                emit_scans(i - 3)
            if 0 <= i - 2 < NH:
                emit_prep3(i - 2)
            if 0 <= i - 1 < NH:
                emit_prep1(i - 1)
                emit_prep2(i - 1)
            if i + 1 < NH:
                emit_wload(i + 1)
            if i == 0:
                emit_wload(0)
            if i < NH:
                emit_proj(i)
            if i == 2 and tb + 1 < NBLK:
                emit_blockLN(tb + 1)
            if tb < 2 and i < NH:
                emit_some_dequant(3 if i % 8 == 0 else 2)
        if tb == 1:
            emit_some_dequant(999)

        for c in range(4):
            gc = tb * 4 + c
            P.op("sync", lambda e, gc=gc, c=c: e.dma_start(
                out=o_dram[gc],
                in_=o_slab[:, c, :, :].rearrange("p h f -> p (h f)")),
                reads=[f"oslab{c}"], writes=[f"od{gc}"], dma=True)
            if tb == 0:
                P.op("sync", lambda e, gc=gc, c=c: e.dma_start(
                    out=ogg_dram[gc],
                    in_=qgg_slab[:, c, :, :].rearrange("p h f -> p (h f)")),
                    reads=[f"qgg{c}"], writes=[f"oggd{gc}"], dma=True)

        if tb == 1:
            ccw_ev = P.op(
                "gpsimd", lambda e: e.collective_compute(
                    "AllGather", ALU.bypass,
                    replica_groups=[list(range(8))],
                    ins=[cc_w_in.opt()], outs=[cc_w_out.opt()]),
                deps=[(f"gpsimddma{i}", P.count[f"gpsimddma{i}"], True)
                      for i in range(NDMASEM)],
                sem="cc")

    P.op("sync", lambda e: e.dma_start(
        out=cc_s_in,
        in_=S_sb[:, :, :].rearrange("p h f -> p (h f)")),
        reads=["S"], writes=["ccsin"], dma=True)
    ccs_ev = P.op("gpsimd", lambda e: e.collective_compute(
        "AllGather", ALU.bypass,
        replica_groups=[list(range(8))],
        ins=[cc_s_in.opt()], outs=[cc_s_out.opt()]),
        reads=["ccsin"], sem="cc")

    P.barrier()
    P.flush(nc, sems)
    free_group(gA)

    if "B" not in phases:
        for cmx in reversed(ps_ctx):
            cmx.__exit__(None, None, None)
        for cmx in sem_ctx:
            cmx.__exit__(None, None, None)
        return nc

    gB, sbB = make_group()
    wo_sb = sbB("wo_sb", [128, 16, 2048], BF16)
    sgat = sbB("sgat", [128, NH, 128], F32)
    sini = sbB("sini", [128, NH, 128], F32)
    sini_bf = sbB("sini_bf", [128, NH, 128], BF16)
    off_sb = sbB("off_sb", [128, 2, 16, 128], BF16)
    qgg_sb = sbB("qgg_sb", [128, 2, 16, 128], BF16)
    ofull = sbB("ofull", [128, 2, 16, 128], BF16)
    ogw = sbB("ogw", [128, 2, 16, 128], BF16)
    sqsl = sbB("sqsl", [128, 2, 16, 128], BF16)
    sqpart = sbB("sqpart", [128, 2, 128], F32)
    sqpartb = sbB("sqpartb", [128, 2, 128], BF16)
    rstd_sb = sbB("rstd_sb", [128, 2], F32)
    xb_sb = sbB("xb_sb", [128, 2, 2048], F32)
    x2_sb = sbB("x2_sb", [128, 2, 2048], F32)

    P.op("gpsimd", lambda e: e.dma_start(
        out=wo_sb[:, :, :],
        in_=wo16.rearrange("(hc p) f -> p hc f", p=128)),
        reads=["wo16d"], writes=["wo_sb"], dma=True)

    def emit_sini():
        P.op("vector", lambda e: e.memset(sini[:, :, :], 0.0),
             writes=["sini"])
        for r in range(8):
            P.op("sync", lambda e, r=r: e.dma_start(
                out=sgat[:, :, :].rearrange("p h f -> p (h f)"),
                in_=cc_s_out[r * 128:(r + 1) * 128, :]),
                deps=[ccs_ev], writes=["sgat"], dma=True)
            P.op("vector", lambda e, r=r: e.scalar_tensor_tensor(
                out=sini[:, :, :], in0=sgat[:, :, :],
                scalar=mask8[:, r:r + 1],
                in1=sini[:, :, :], op0=ALU.mult, op1=ALU.add),
                reads=["sgat", "mask8", "sini"], writes=["sini"])
        P.op("vector", lambda e: e.tensor_copy(out=sini_bf[:, :, :],
                                               in_=sini[:, :, :]),
             reads=["sini"], writes=["sinibf"])

    for gt in list(range(4, NTT)) + list(range(4)):
        par = gt % 2
        if gt == 0:
            emit_sini()
        P.op("sync", lambda e, gt=gt, par=par: e.dma_start(
            out=off_sb[:, par, :, :].rearrange("p h f -> p (h f)"),
            in_=o_dram[gt]),
            writes=[f"off{par}"], dma=True)
        if gt < 4:
            P.op("sync", lambda e, gt=gt, par=par: e.dma_start(
                out=qgg_sb[:, par, :, :].rearrange("p h f -> p (h f)"),
                in_=ogg_dram[gt]),
                writes=[f"qggsb{par}"], dma=True)
        P.op("sync", lambda e, gt=gt, par=par: e.dma_start(
            out=xb_sb[:, par, :], in_=x_in[gt * 128:(gt + 1) * 128, :]),
            writes=[f"xb{par}"], dma=True)
        if gt < 4:
            for h in range(NH):
                pc = h % 2
                PC = [P4, P5][pc]
                pck = f"PB{pc}"
                P.op("tensor", lambda e, h=h, par=par, PC=PC: e.matmul(
                    PC[:, 0:128], sini_bf[:, h, :], qgg_sb[:, par, h, :],
                    start=True, stop=True),
                    reads=["sinibf", f"qggsb{par}"], writes=[pck])
                P.op("vector", lambda e, h=h, par=par, PC=PC: e.tensor_add(
                    out=ofull[:, par, h, :], in0=off_sb[:, par, h, :],
                    in1=PC[:, 0:128]),
                    reads=[f"off{par}", pck], writes=[f"ofull{par}"])
                P.op("vector", lambda e, h=h, par=par: e.tensor_scalar(
                    out=ogw[:, par, h, :], in0=ofull[:, par, h, :],
                    scalar1=gwc[:, h:h + 1], scalar2=None, op0=ALU.mult),
                    reads=[f"ofull{par}", "gwc"], writes=[f"ogw{par}"])
            P.op("scalar", lambda e, par=par: e.activation(
                out=sqsl[:, par, :, :], in_=ofull[:, par, :, :],
                func=AF.Square),
                reads=[f"ofull{par}"], writes=[f"sqsl{par}"])
        else:
            for h in range(NH):
                P.op("vector", lambda e, h=h, par=par: e.tensor_scalar(
                    out=ogw[:, par, h, :], in0=off_sb[:, par, h, :],
                    scalar1=gwc[:, h:h + 1], scalar2=None, op0=ALU.mult),
                    reads=[f"off{par}", "gwc"], writes=[f"ogw{par}"])
            P.op("scalar", lambda e, par=par: e.activation(
                out=sqsl[:, par, :, :], in_=off_sb[:, par, :, :],
                func=AF.Square),
                reads=[f"off{par}"], writes=[f"sqsl{par}"])
        P.op("vector", lambda e, par=par: e.tensor_reduce(
            out=sqpart[:, par, :],
            in_=sqsl[:, par, :, :].rearrange("p h f -> p f h"),
            axis=AX.X, op=ALU.add),
            reads=[f"sqsl{par}"], writes=[f"sqpart{par}"])
        P.op("vector", lambda e, par=par: e.tensor_copy(
            out=sqpartb[:, par, :], in_=sqpart[:, par, :]),
            reads=[f"sqpart{par}"], writes=[f"sqpartb{par}"])
        P.op("tensor", lambda e, par=par: e.matmul(
            P6[:, 0:1], sqpartb[:, par, :], trimask[:, 127:128],
            start=True, stop=True),
            reads=[f"sqpartb{par}", "trimask"], writes=["P6B"])
        rc = rstd_sb[:, par:par + 1]
        P.op("scalar", lambda e, rc=rc: e.activation(
            out=rc, in_=P6[:, 0:1], func=AF.Ln, scale=INV_DM, bias=LN_EPS),
            reads=["P6B", "consts"], writes=[f"rstd{par}"])
        P.op("scalar", lambda e, rc=rc: e.activation(
            out=rc, in_=rc, func=AF.Exp, scale=-0.5),
            reads=[f"rstd{par}", "consts"], writes=[f"rstd{par}"])
        if debug_outputs and gt == 0:
            P.op("sync", lambda e: e.dma_start(
                out=dbgb[:, 2048:4096],
                in_=ofull[:, :, :]),
                reads=["ofull"], writes=["dbgb"], dma=True)
            P.op("sync", lambda e: e.dma_start(
                out=dbgf[:, 3072:3200], in_=sqpart[:, :]),
                reads=["sqpart"], writes=["dbgf"], dma=True)
            P.op("sync", lambda e, rc=rc: e.dma_start(
                out=dbgf[:, 3200:3201], in_=rc,
                allow_slow_non_contiguous=True),
                reads=[f"rstd{par}"], writes=["dbgf"], dma=True)
            P.op("sync", lambda e: e.dma_start(
                out=dbgb[:, 0:2048],
                in_=ogw[:, :, :]),
                reads=["ogw"], writes=["dbgb"], dma=True)
        for mcq in range(4):
            pw = mcq % 2
            PW = [P0, P1][pw]
            pwk = f"PW{pw}"
            for hc in range(16):
                P.op("tensor", lambda e, hc=hc, mcq=mcq, par=par, PW=PW:
                     e.matmul(
                         PW[:, :], ogw[:, par, hc, :],
                         wo_sb[:, hc, mcq * 512:(mcq + 1) * 512],
                         start=(hc == 0), stop=(hc == 15)),
                     reads=[f"ogw{par}", "wo_sb"], writes=[pwk])
            P.op("vector", lambda e, mcq=mcq, par=par, PW=PW, rc=rc:
                 e.scalar_tensor_tensor(
                     out=x2_sb[:, par, mcq * 512:(mcq + 1) * 512],
                     in0=PW[:, :], scalar=rc,
                     in1=xb_sb[:, par, mcq * 512:(mcq + 1) * 512],
                     op0=ALU.mult, op1=ALU.add),
                 reads=[pwk, f"rstd{par}", f"xb{par}"], writes=[f"x2{par}"])
        P.op("sync", lambda e, gt=gt, par=par: e.dma_start(
            out=x2_dram[gt * 128:(gt + 1) * 128, :], in_=x2_sb[:, par, :]),
            reads=[f"x2{par}"], writes=[f"x2d{gt}"], dma=True)
        emit_ln(x2_sb[:, par, :], f"x2{par}", xln_tok[:, 0, :],
                "xlntok", 2)
        P.op("sync", lambda e, gt=gt, par=par: e.dma_start(
            out=ln_dram[gt * 128:(gt + 1) * 128, :],
            in_=xln_tok[:, 0, :]),
            reads=["xlntok"], writes=[f"ln2d{gt}"], dma=True)

    P.barrier()
    P.flush(nc, sems)
    free_group(gB)

    if "C" not in phases:
        for cmx in reversed(ps_ctx):
            cmx.__exit__(None, None, None)
        for cmx in sem_ctx:
            cmx.__exit__(None, None, None)
        return nc

    gC, sbC = make_group()
    xln2_feat = sbC("xln2_feat", [128, 2, 16, 512], BF16)
    hact = sbC("hact", [128, NIC, 512], BF16)
    wgu_sb = sbC("wgu_sb", [128, 2, 4, 1024], BF16)
    wdn_sb = sbC("wdn_sb", [128, 22, 1024], BF16)
    sg_sb = sbC("sg_sb", [128, 2, 512], BF16)
    y_sb = sbC("y_sb", [128, 4, 2048], F32)
    x2e_sb = sbC("x2e_sb", [128, 1, 2048], F32)
    out_sb = sbC("out_sb", [128, 2048], F32)

    for tb in range(4):
        bp = tb % 2
        for hc in range(16):
            P.op("sync", lambda e, hc=hc, tb=tb, bp=bp: e.dma_start(
                out=xln2_feat[:, bp, hc, :],
                in_=ln_dram[tb * 512:(tb + 1) * 512,
                            hc * 128:(hc + 1) * 128],
                transpose=True),
                writes=[f"x2f{bp}"], dma=True)

        for p in range(NPAIR):
            wp = p % 2
            P.op("sync", lambda e, p=p, wp=wp: e.dma_start(
                out=wgu_sb[:, wp, :, :],
                in_=cc_w_out[p * 4 * 128:(p + 1) * 4 * 128, :]
                .rearrange("(g p2) f -> p2 g f", p2=128)),
                deps=[ccw_ev], writes=[f"wgu{wp}"], dma=True)
            pg = p % 2
            PG, PU = [P0, P2][pg], [P1, P3][pg]
            pgk, puk = f"PG{pg}", f"PU{pg}"
            for gu in range(2):
                PP = PG if gu == 0 else PU
                ppk = pgk if gu == 0 else puk
                for hc in range(16):
                    lidx = gu * 16 + hc
                    g, cb = lidx // 8, lidx % 8
                    P.op("tensor",
                         lambda e, hc=hc, g=g, cb=cb, wp=wp, bp=bp, PP=PP:
                         e.matmul(
                             PP[:, :],
                             wgu_sb[:, wp, g, cb * 128:(cb + 1) * 128],
                             xln2_feat[:, bp, hc, :],
                             start=(hc == 0), stop=(hc == 15)),
                         reads=[f"wgu{wp}", f"x2f{bp}"], writes=[ppk])
            P.op("scalar", lambda e, pg=pg, PG=PG: e.activation(
                out=sg_sb[:, pg, :], in_=PG[:, :], func=AF.Silu),
                reads=[pgk], writes=[f"sg{pg}"])
            P.op("vector", lambda e, p=p, pg=pg, PU=PU: e.tensor_tensor(
                out=hact[:, p, :], in0=sg_sb[:, pg, :], in1=PU[:, :],
                op=ALU.mult),
                reads=[f"sg{pg}", puk], writes=["hact"])

        for mcg in range(4):
            for half in range(2):
                g0 = 176 + 22 * mcg + 11 * half
                P.op("sync", lambda e, g0=g0, half=half: e.dma_start(
                    out=wdn_sb[:, half * 11:(half + 1) * 11, :],
                    in_=cc_w_out[g0 * 128:(g0 + 11) * 128, :]
                    .rearrange("(g p2) f -> p2 g f", p2=128)),
                    deps=[ccw_ev], writes=[f"wdn{half}"], dma=True)
            for tt in range(4):
                pd = tt % 2
                PD = [P4, P5][pd]
                pdk = f"PD{pd}"
                for ic in range(NIC):
                    half = 0 if ic < 22 else 1
                    gl = ic // 2
                    co = (ic % 2) * 512
                    P.op("tensor",
                         lambda e, ic=ic, tt=tt, gl=gl, co=co, PD=PD:
                         e.matmul(
                             PD[:, :], hact[:, ic, tt * 128:(tt + 1) * 128],
                             wdn_sb[:, gl, co:co + 512],
                             start=(ic == 0), stop=(ic == NIC - 1)),
                         reads=["hact", f"wdn{half}"], writes=[pdk])
                P.op("scalar", lambda e, tt=tt, mcg=mcg, PD=PD: e.copy(
                    out=y_sb[:, tt, mcg * 512:(mcg + 1) * 512], in_=PD[:, :]),
                    reads=[pdk], writes=[f"y{tt}"])

        for tt in range(4):
            gt = tb * 4 + tt
            pe = tt % 2
            P.op("sync", lambda e, gt=gt: e.dma_start(
                out=x2e_sb[:, 0, :],
                in_=x2_dram[gt * 128:(gt + 1) * 128, :]),
                writes=["x2e0"], dma=True)
            P.op("scalar", lambda e, tt=tt: e.activation(
                out=junk[:, 0:2048], in_=y_sb[:, tt, :], func=AF.Square,
                accum_out=stat[:, 4:5]),
                reads=[f"y{tt}"], writes=["statc", "junk"])
            P.op("scalar", lambda e: e.activation(
                out=stat[:, 4:5], in_=stat[:, 4:5], func=AF.Ln,
                scale=INV_DM, bias=RMS_EPS_FFN),
                reads=["statc", "consts"], writes=["statc"])
            P.op("scalar", lambda e: e.activation(
                out=stat[:, 4:5], in_=stat[:, 4:5], func=AF.Exp, scale=-0.5),
                reads=["statc", "consts"], writes=["statc"])
            P.op("vector", lambda e, tt=tt: e.scalar_tensor_tensor(
                out=out_sb[:, :], in0=y_sb[:, tt, :],
                scalar=stat[:, 4:5], in1=x2e_sb[:, 0, :],
                op0=ALU.mult, op1=ALU.add),
                reads=[f"y{tt}", "statc", "x2e0"], writes=["outsb"])
            P.op("sync", lambda e, gt=gt: e.dma_start(
                out=y_out[gt * 128:(gt + 1) * 128, :], in_=out_sb[:, :]),
                reads=["outsb"], writes=["yd"], dma=True)

    P.barrier()
    P.flush(nc, sems)
    free_group(gC)
    free_group(gG)

    for cmx in reversed(ps_ctx):
        cmx.__exit__(None, None, None)
    for cmx in sem_ctx:
        cmx.__exit__(None, None, None)
    return nc


def _host_inputs(L0_attn_hidden_states, L0_attn_Wq, L0_attn_Wk, L0_attn_Wi,
                 L0_attn_Wo, L0_attn_g_weight, L0_ffn_gate_up_packed,
                 L0_ffn_gate_up_scales, L0_ffn_down_packed,
                 L0_ffn_down_scales):
    x = np.asarray(L0_attn_hidden_states, dtype=np.float32)
    wq = np.ascontiguousarray(np.asarray(L0_attn_Wq, np.float32))
    wk = np.ascontiguousarray(np.asarray(L0_attn_Wk, np.float32))
    wi = np.ascontiguousarray(np.asarray(L0_attn_Wi, np.float32))
    wo = np.ascontiguousarray(np.asarray(L0_attn_Wo, np.float32))
    gw = np.asarray(L0_attn_g_weight, np.float32)
    gup = np.asarray(L0_ffn_gate_up_packed, np.int32)
    gus = np.asarray(L0_ffn_gate_up_scales, np.float16)
    dnp = np.asarray(L0_ffn_down_packed, np.int32)
    dns = np.asarray(L0_ffn_down_scales, np.float16)

    gwc = np.ascontiguousarray(gw.reshape(NH, 128).T)
    trimask = np.zeros((128, 128), dtype=BF)
    su = np.tril(np.ones((64, 64), np.float32)).T.astype(BF)
    trimask[0:64, 0:64] = su
    trimask[0:64, 64:128] = 1.0
    trimask[64:128, 64:128] = su
    ident = np.eye(128, dtype=BF)
    consts = np.broadcast_to(np.array(CONST_VALS, np.float32),
                             (128, len(CONST_VALS))).copy()

    def tile_slices(gidx):
        if gidx < 1408:
            pair, rem = divmod(gidx, 32)
            gu, hc = divmod(rem, 16)
            r0 = (gu * NPAIR + pair) * 128
            pk = gup[r0:r0 + 128, hc * 16:(hc + 1) * 16]
            sc = gus[hc, r0:r0 + 128]
        else:
            g2 = gidx - 1408
            mcg, rem = divmod(g2, 176)
            ic, mi = divmod(rem, 4)
            mt = mcg * 4 + mi
            pk = dnp[mt * 128:(mt + 1) * 128, ic * 16:(ic + 1) * 16]
            sc = dns[ic, mt * 128:(mt + 1) * 128]
        return pk, sc

    per_core = []
    for c in range(8):
        b, half = c // 2, c % 2
        xs = np.ascontiguousarray(x[b, half * T:(half + 1) * T, :])
        pks = np.empty((TPC, 128, 16), np.int32)
        scs = np.empty((TPC, 128), np.float16)
        for i in range(TPC):
            pk, sc = tile_slices(c * TPC + i)
            pks[i] = pk
            scs[i] = sc
        mask8 = np.zeros((128, 8), np.float32)
        if half == 1:
            mask8[:, c - 1] = 1.0
        per_core.append(dict(
            x=xs, wq=wq, wk=wk, wi=wi, wo=wo,
            gwc=gwc, mask8=mask8, trimask=trimask, ident=ident,
            consts=consts, packedsl=pks,
            scalessl=np.ascontiguousarray(scs.T)))
    return per_core


def kernel(**inputs):
    per_core = _host_inputs(**inputs)
    nc = build_nc()
    res = run_bass_kernel_spmd(nc, per_core, core_ids=list(range(8)))
    out = np.empty((B_, L_, DM), np.float32)
    for c in range(8):
        b, half = c // 2, c % 2
        out[b, half * T:(half + 1) * T, :] = res.results[c]["y"]
    return out


if __name__ == "__main__":
    nc = build_nc()
    print("built ok; instructions:", nc.n_instructions())


# version 25
# speedup vs baseline: 1.2763x; 1.0131x over previous
import numpy as np
import ml_dtypes
import concourse.bass as bass
import concourse.mybir as mybir
from concourse.bass_utils import run_bass_kernel_spmd

F32 = mybir.dt.float32
BF16 = mybir.dt.bfloat16
F16 = mybir.dt.float16
I32 = mybir.dt.int32
U16 = mybir.dt.uint16
AF = mybir.ActivationFunctionType
ALU = mybir.AluOpType
AX = mybir.AxisListType

BF = ml_dtypes.bfloat16

B_, L_, DM = 4, 4096, 2048
NH, DK, DV = 16, 128, 128
SCALE = float(DK) ** -0.5
T = 2048
NBLK = 4
NCH = 16
NTT = 16
FFI = 5632
NPAIR = 44
NIC = 44
NTILES = 2112
TPC = 264
NGRP = NTILES // 8
GPC = TPC // 8

LN_EPS = 1e-5
RMS_EPS_FFN = 1e-6
INV_DM = 1.0 / 2048.0

CONST_VALS = [SCALE, -1.0, 1.0, 2.0, -0.5, INV_DM, LN_EPS, RMS_EPS_FFN]

ENGINES = ("sync", "gpsimd", "tensor", "vector", "scalar")
DMA_QUEUES = ("sync", "gpsimd", "scalar")
NDMASEM = 8
DMA_SEMS = tuple(f"{q}dma{i}" for q in DMA_QUEUES for i in range(NDMASEM))
ALL_SEMS = ENGINES + ("cc",) + DMA_SEMS


class Prog:

    def __init__(self):
        self.items = []
        self.count = {s: 0 for s in ALL_SEMS}
        self.hwm = {e: {s: 0 for s in ALL_SEMS} for e in ENGINES}
        self.last_w = {}
        self.readers = {}
        self.dma_idx = {q: 0 for q in DMA_QUEUES}

    def op(self, eng, fn, reads=(), writes=(), dma=False, deps=(), sem=None):
        dep_evs = []
        for k in reads:
            ev = self.last_w.get(k)
            if ev is not None:
                dep_evs.append(ev)
        for k in writes:
            ev = self.last_w.get(k)
            if ev is not None:
                dep_evs.append(ev)
            dep_evs.extend(self.readers.get(k, ()))
        dep_evs.extend(deps)
        if dma:
            idx = self.dma_idx[eng]
            self.dma_idx[eng] = idx + 1
            slot = idx % NDMASEM
            dmasem = f"{eng}dma{slot}"
            dep_evs.append((dmasem, 16 * (idx // NDMASEM), True))
        wmax = {}
        for (ds, dv, dasync) in dep_evs:
            if ds == eng and eng == "tensor" and not dasync:
                continue
            if self.hwm[eng][ds] >= dv:
                continue
            wmax[ds] = max(wmax.get(ds, 0), dv)
        for ds, dv in wmax.items():
            self.hwm[eng][ds] = dv
        if dma:
            semname = dmasem
        else:
            semname = sem if sem is not None else eng
        inc = 16 if dma else 1
        self.count[semname] += inc
        is_async = dma or (sem is not None)
        ev = (semname, self.count[semname], is_async)
        self.items.append((eng, fn, sorted(wmax.items()), semname, inc))
        for k in reads:
            self.readers.setdefault(k, []).append(ev)
        for k in writes:
            self.last_w[k] = ev
            self.readers[k] = []
        return ev

    def barrier(self):
        snap = {s: self.count[s] for s in ENGINES + DMA_SEMS}
        for e in ENGINES:
            waits = []
            for e2 in ENGINES + DMA_SEMS:
                if e2 == e:
                    continue
                if self.hwm[e][e2] < snap[e2]:
                    waits.append((e2, snap[e2]))
                    self.hwm[e][e2] = snap[e2]
            if waits:
                self.items.append((e, None, waits, e, 0))
        self.last_w = {}
        self.readers = {}

    def flush(self, nc, sems):
        items, self.items = self.items, []
        with nc.Block() as block:
            for eng_name in ENGINES:
                def make(eng_name=eng_name):
                    def body(eng):
                        for (en, fn, waits, semname, inc) in items:
                            if en != eng_name:
                                continue
                            for (se, sv) in waits:
                                eng.wait_ge(sems[se], sv)
                            if fn is not None:
                                fn(eng).then_inc(sems[semname], inc)
                    return body
                getattr(block, eng_name)(make())


def build_nc(debug_outputs=False, phases=("A", "B", "C")):
    nc = bass.Bass("TRN2", target_bir_lowering=False)

    def din(name, shape, dt):
        return nc.dram_tensor(name, shape, dt, kind="ExternalInput").ap()

    x_in = din("x", [T, DM], F32)
    wq_in = din("wq", [DM, DM], F32)
    wk_in = din("wk", [DM, DM], F32)
    wi_in = din("wi", [DM, DM], F32)
    wo_in = din("wo", [DM, DM], F32)
    gwc_in = din("gwc", [128, NH], F32)
    mask8_in = din("mask8", [128, 8], F32)
    trimask_in = din("trimask", [128, 128], BF16)
    ident_in = din("ident", [128, 128], BF16)
    consts_in = din("consts", [128, len(CONST_VALS)], F32)
    packed_in = din("packedsl", [TPC, 128, 16], I32)
    scales_in = din("scalessl", [128, TPC], F16)

    y_out = nc.dram_tensor("y", [T, DM], F32, kind="ExternalOutput").ap()

    dkind = "ExternalOutput" if debug_outputs else "Internal"

    def dscr(name, shape, dt, dbg=False):
        return nc.dram_tensor(name, shape, dt,
                              kind=(dkind if dbg else "Internal")).ap()

    wq16t = dscr("wq16t", [NH, 128, 16 * 128], BF16)
    wk16t = dscr("wk16t", [NH, 128, 16 * 128], BF16)
    wi16 = dscr("wi16", [DM, DM], BF16)
    wo16 = dscr("wo16", [DM, DM], BF16, dbg=True)
    ln_dram = dscr("ln_dram", [T, DM], BF16, dbg=True)
    ogg_dram = dscr("ogg_dram", [NCH, 128, NH * 128], BF16)
    o_dram = dscr("o_dram", [NCH, 128, NH * 128], BF16, dbg=True)
    x2_dram = dscr("x2_dram", [T, DM], F32, dbg=True)
    dbgf = dscr("dbgf", [128, 4096], F32, dbg=True)
    dbgb = dscr("dbgb", [128, 4096], BF16, dbg=True)
    cc_w_in = nc.dram_tensor("cc_w_in", [GPC * 128, 1024], BF16).ap()
    cc_w_out = nc.dram_tensor("cc_w_out", [NGRP * 128, 1024], BF16,
                              addr_space="Shared").ap()
    cc_s_in = nc.dram_tensor("cc_s_in", [128, NH * 128], F32).ap()
    cc_s_out = nc.dram_tensor("cc_s_out", [8 * 128, NH * 128], F32,
                              addr_space="Shared").ap()

    P = Prog()

    sems = {}
    sem_ctx = []
    for s in ALL_SEMS:
        cm = nc.semaphore(f"sem_{s}")
        sem_ctx.append(cm)
        sems[s] = cm.__enter__()

    all_ctx = []

    def make_group():
        group = []

        def sb(name, shape, dt):
            cm = nc.sbuf_tensor("sb_" + name, shape, dt)
            group.append(cm)
            all_ctx.append(cm)
            return cm.__enter__()
        return group, sb

    def free_group(group):
        for cm in reversed(group):
            cm.__exit__(None, None, None)
            all_ctx.remove(cm)

    ps_ctx = []

    def psb(name):
        cm = nc.psum_tensor(name, [128, 512], F32)
        ps_ctx.append(cm)
        return cm.__enter__()

    gG, sbG = make_group()
    consts = sbG("consts", [128, len(CONST_VALS)], F32)
    gwc = sbG("gwc", [128, NH], F32)
    mask8 = sbG("mask8", [128, 8], F32)
    trimask = sbG("trimask", [128, 128], BF16)
    ident = sbG("ident", [128, 128], BF16)
    junk = sbG("junk", [128, 2048], BF16)
    junkf = sbG("junkf", [128, 256], F32)
    xln_tok = sbG("xln_tok", [128, 1, 2048], BF16)
    stat = sbG("stat", [128, 16], F32)

    for i, v in enumerate(CONST_VALS):
        nc.const_aps.register(F32, v, consts[:, i:i + 1])

    P0 = psb("P0")
    P1 = psb("P1")
    P2 = psb("P2")
    P3 = psb("P3")
    P4 = psb("P4")
    P5 = psb("P5")
    P6 = psb("P6")
    cm = nc.psum_tensor("P7b", [128, 1024], BF16)
    ps_ctx.append(cm)
    P7b = cm.__enter__()

    P.op("sync", lambda e: e.dma_start(out=consts[:], in_=consts_in),
         writes=["consts"], dma=True)
    P.op("sync", lambda e: e.dma_start(out=gwc[:], in_=gwc_in),
         writes=["gwc"], dma=True)
    P.op("sync", lambda e: e.dma_start(out=mask8[:], in_=mask8_in),
         writes=["mask8"], dma=True)
    P.op("sync", lambda e: e.dma_start(out=trimask[:], in_=trimask_in),
         writes=["trimask"], dma=True)
    P.op("sync", lambda e: e.dma_start(out=ident[:], in_=ident_in),
         writes=["ident"], dma=True)
    P.op("sync", lambda e: e.dma_start(out=scl16[:], in_=scales_in),
         writes=["scl16"], dma=True)
    P.op("vector", lambda e: e.tensor_copy(out=scl[:], in_=scl16[:]),
         reads=["scl16"], writes=["scl"])
    P.op("vector", lambda e: e.memset(S_sb[:], 0.0), writes=["S"])
    P.op("vector", lambda e: e.memset(S_bf[:], 0.0), writes=["Sbf"])
    P.op("vector", lambda e: e.memset(R_run[:], 1.0), writes=["Rrun"])

    wi_r = wi_in.rearrange("(hc p) (dc f) -> dc hc p f", p=128, f=512)
    wi16_r = wi16.rearrange("(hc p) (dc f) -> dc hc p f", p=128, f=512)
    wq_r = wq_in.rearrange("(hc p) (mc f) -> mc hc p f", p=128, f=128)
    wk_r = wk_in.rearrange("(hc p) (mc f) -> mc hc p f", p=128, f=128)

    def cast_wi(dvc):
        P.op("gpsimd", lambda e, dvc=dvc: e.dma_start(
            out=wi16_r[dvc], in_=wi_r[dvc]),
            writes=[f"wi16d{dvc}"], dma=True)

    for dvc in range(4):
        cast_wi(dvc)

    gA, sbA = make_group()
    scl16 = sbA("scl16", [128, TPC], F16)
    scl = sbA("scl", [128, TPC], F32)
    S_sb = sbA("S_sb", [128, NH, 128], F32)
    S_bf = sbA("S_bf", [128, NH, 128], BF16)
    R_run = sbA("R_run", [128, NH], F32)
    xt = sbA("xt", [128, 2, 2048], F32)
    xln_feat = sbA("xln_feat", [128, 2, 16, 512], BF16)
    v_tok = sbA("v_tok", [128, 4, 2048], BF16)
    wi_sb = sbA("wi_sb", [128, 16, 512], BF16)
    wq_sb = sbA("wq_sb", [128, 2, 16, 128], BF16)
    wk_sb = sbA("wk_sb", [128, 2, 16, 128], BF16)
    q_h = sbA("q_h", [128, 4, 512], BF16)
    kraw_h = sbA("kraw_h", [128, 2, 512], F32)
    sp_h = sbA("sp_h", [128, 2, 512], F32)
    gh_h = sbA("gh_h", [128, 2, 512], F32)
    bb_h = sbA("bb_h", [128, 2, 512], F32)
    eg_h = sbA("eg_h", [128, 2, 512], F32)
    qg_h = sbA("qg_h", [128, 2, 512], BF16)
    qb_h = sbA("qb_h", [128, 2, 4, 64], BF16)
    ka_h = sbA("ka_h", [128, 2, 4, 64], BF16)
    kb_h = sbA("kb_h", [128, 2, 4, 64], BF16)
    kba_h = sbA("kba_h", [128, 2, 4, 64], BF16)
    kh_h = sbA("kh_h", [128, 2, 512], BF16)
    kh_tok = sbA("kh_tok", [128, 2, 2, 128], BF16)
    ng_h = sbA("ng_h", [128, 2, 8], F32)
    asb = sbA("asb", [128, 2, 128], BF16)
    zchunk = sbA("zchunk", [128, 128], F32)
    qgg_slab = sbA("qgg_slab", [128, 4, 16, 128], BF16)
    o_slab = sbA("o_slab", [128, 4, 16, 128], BF16)
    dq_pk = sbA("dq_pk", [128, 2, 4, 16], I32)
    dq_nib = sbA("dq_nib", [128, 128, 4], U16)
    dq_bits = sbA("dq_bits", [128, 512], U16)
    dq_sgn = sbA("dq_sgn", [128, 512], U16)
    dq_sgnf = sbA("dq_sgnf", [128, 512], BF16)
    dq_u16 = sbA("dq_u16", [128, 512], U16)
    dq_relu = sbA("dq_relu", [128, 512], BF16)
    dq_min = sbA("dq_min", [128, 512], BF16)
    dq_out = sbA("dq_out", [128, 3, 512], BF16)
    dq_tr = sbA("dq_tr", [128, 2, 128], BF16)

    P.op("vector", lambda e: e.memset(zchunk[:], 0.0), writes=["zchunk"])
    P.op("vector", lambda e: e.memset(P2[64:128, 0:64], 0.0), writes=["PA0"])
    P.op("vector", lambda e: e.memset(P3[64:128, 0:64], 0.0), writes=["PA1"])

    def emit_dequant_dve(bi):
        pk_par = bi % 2
        op_par = bi % 3
        lt0 = 4 * bi
        pk = dq_pk[:, pk_par, :, :]
        P.op("gpsimd",
             lambda e, pk=pk, lt0=lt0: e.dma_start(
                 out=pk, in_=packed_in[lt0:lt0 + 4].rearrange(
                     "t p w -> p t w")),
             writes=[f"dqpk{pk_par}"], dma=True)
        pku = pk.rearrange("p t w -> p (t w)").bitcast(U16)
        for k in range(4):
            P.op("vector",
                 lambda e, k=k, pku=pku: e.tensor_scalar(
                     out=dq_nib[:, :, k], in0=pku, scalar1=4 * k, scalar2=15,
                     op0=ALU.logical_shift_right, op1=ALU.bitwise_and),
                 reads=[f"dqpk{pk_par}"], writes=["dqnib"])
        nib3 = dq_nib[:, :, :]
        P.op("vector", lambda e, nib3=nib3: e.tensor_scalar(
            out=dq_bits[:], in0=nib3, scalar1=7, scalar2=6,
            op0=ALU.bitwise_and, op1=ALU.logical_shift_left),
            reads=["dqnib"], writes=["dqbits"])
        P.op("vector", lambda e: e.tensor_scalar(
            out=dq_u16[:], in0=dq_bits[:], scalar1=0x3F00,
            scalar2=None, op0=ALU.add), reads=["dqbits"], writes=["dqu16"])
        P.op("vector", lambda e, nib3=nib3: e.tensor_scalar(
            out=dq_sgn[:], in0=nib3, scalar1=3, scalar2=1,
            op0=ALU.logical_shift_right, op1=ALU.bitwise_and),
            reads=["dqnib"], writes=["dqsgn"])
        P.op("vector", lambda e: e.tensor_scalar(
            out=dq_sgnf[:], in0=dq_sgn[:], scalar1=-2,
            scalar2=1, op0=ALU.mult, op1=ALU.add),
            reads=["dqsgn"], writes=["dqsgnf"])
        fmv = dq_u16[:].bitcast(BF16)
        P.op("scalar", lambda e, fmv=fmv: e.activation(
            out=dq_relu[:], in_=fmv, func=AF.Relu, bias=-1.0,
            scale=2.0),
            reads=["dqu16", "consts"], writes=["dqrelu"])
        P.op("vector", lambda e, fmv=fmv: e.tensor_tensor(
            out=dq_min[:], in0=dq_relu[:], in1=fmv,
            op=ALU.min),
            reads=["dqrelu", "dqu16"], writes=["dqmin"])
        for j in range(4):
            lt = lt0 + j
            P.op("vector",
                 lambda e, j=j, lt=lt, op_par=op_par: e.scalar_tensor_tensor(
                     out=dq_out[:, op_par, j * 128:(j + 1) * 128],
                     in0=dq_min[:, j * 128:(j + 1) * 128],
                     scalar=scl[:, lt:lt + 1],
                     in1=dq_sgnf[:, j * 128:(j + 1) * 128],
                     op0=ALU.mult, op1=ALU.mult),
                 reads=["dqmin", "dqsgnf", "scl"], writes=[f"dqout{op_par}"])

    def emit_dequant_tr(bi):
        op_par = bi % 3
        lt0 = 4 * bi
        for j in range(4):
            lt = lt0 + j
            tp = j % 2
            P.op("tensor", lambda e, j=j, op_par=op_par: e.transpose(
                P7b[:, 0:128], dq_out[:, op_par, j * 128:(j + 1) * 128],
                ident[:]),
                reads=[f"dqout{op_par}", "ident"], writes=["P7b"])
            P.op("scalar", lambda e, tp=tp: e.copy(
                out=dq_tr[:, tp, :], in_=P7b[:, 0:128]),
                reads=["P7b"], writes=[f"dqtr{tp}"])
            g, cb = lt // 8, lt % 8
            P.op("gpsimd", lambda e, g=g, cb=cb, tp=tp: e.dma_start(
                out=cc_w_in[g * 128:(g + 1) * 128, cb * 128:(cb + 1) * 128],
                in_=dq_tr[:, tp, :]),
                reads=[f"dqtr{tp}"], dma=True)

    dq_iter = iter(range(TPC // 4))

    dq_pending = []

    def emit_some_dequant(n):
        for _ in range(n):
            bi = next(dq_iter, None)
            if bi is None:
                while dq_pending:
                    emit_dequant_tr(dq_pending.pop(0))
                return
            emit_dequant_dve(bi)
            dq_pending.append(bi)
            if len(dq_pending) > 2:
                emit_dequant_tr(dq_pending.pop(0))

    def emit_ln(xin_ap, xkey, out_ap, okey, c0):
        m_c = stat[:, c0:c0 + 1]
        v_c = stat[:, c0 + 1:c0 + 2]
        km, kv = f"stat{c0}", f"stat{c0 + 1}"
        P.op("vector", lambda e: e.reduce_sum(m_c, xin_ap, axis=AX.X),
             reads=[xkey], writes=[km])
        P.op("scalar", lambda e: e.activation(
            out=junk[:, 0:2048], in_=xin_ap, func=AF.Square, accum_out=v_c),
            reads=[xkey], writes=[kv, "junk"])
        P.op("vector", lambda e: e.tensor_scalar(
            out=m_c, in0=m_c, scalar1=INV_DM, scalar2=None, op0=ALU.mult),
            reads=[km], writes=[km])
        P.op("vector", lambda e: e.tensor_scalar(
            out=v_c, in0=v_c, scalar1=INV_DM, scalar2=None, op0=ALU.mult),
            reads=[kv], writes=[kv])
        P.op("vector", lambda e: e.scalar_tensor_tensor(
            out=v_c, in0=m_c, scalar=m_c, in1=v_c,
            op0=ALU.mult, op1=ALU.subtract),
            reads=[km, kv], writes=[kv])
        P.op("vector", lambda e: e.tensor_scalar(
            out=v_c, in0=v_c, scalar1=-1.0, scalar2=None, op0=ALU.mult),
            reads=[kv], writes=[kv])
        P.op("scalar", lambda e: e.activation(
            out=v_c, in_=v_c, func=AF.Ln, bias=LN_EPS),
            reads=[kv, "consts"], writes=[kv])
        P.op("scalar", lambda e: e.activation(
            out=v_c, in_=v_c, func=AF.Exp, scale=-0.5),
            reads=[kv, "consts"], writes=[kv])
        P.op("vector", lambda e: e.tensor_scalar(
            out=out_ap, in0=xin_ap, scalar1=m_c, scalar2=v_c,
            op0=ALU.subtract, op1=ALU.mult),
            reads=[xkey, km, kv], writes=[okey])

    ccw_ev = None
    def emit_blockLN(tb):
        for tt in range(4):
            gt = tb * 4 + tt
            par = gt % 2
            P.op("sync", lambda e, gt=gt, par=par: e.dma_start(
                out=xt[:, par, :], in_=x_in[gt * 128:(gt + 1) * 128, :]),
                writes=[f"xt{par}"], dma=True)
            emit_ln(xt[:, par, :], f"xt{par}", xln_tok[:, 0, :],
                    "xlntok", 0)
            P.op("sync", lambda e, gt=gt, par=par: e.dma_start(
                out=ln_dram[gt * 128:(gt + 1) * 128, :],
                in_=xln_tok[:, 0, :]),
                reads=["xlntok"], writes=[f"lnd{gt}"], dma=True)

    emit_blockLN(0)
    for tb in range(NBLK):
        bp = tb % 2
        if tb == 3:
            P.op("gpsimd", lambda e: e.dma_start(out=wo16, in_=wo_in),
                 writes=["wo16d"], dma=True)
        for hc in range(16):
            P.op("sync", lambda e, hc=hc, tb=tb, bp=bp: e.dma_start(
                out=xln_feat[:, bp, hc, :],
                in_=ln_dram[tb * 512:(tb + 1) * 512,
                            hc * 128:(hc + 1) * 128],
                transpose=True),
                reads=[f"lnd{tb * 4 + i}" for i in range(4)],
                writes=[f"xlf{bp}"], dma=True)

        for dvc in range(4):
            for hf in range(2):
                P.op("gpsimd", lambda e, dvc=dvc, hf=hf: e.dma_start(
                    out=wi_sb[:, hf * 8:(hf + 1) * 8, :],
                    in_=wi16_r[dvc][hf * 8:(hf + 1) * 8].rearrange(
                        "hc p f -> p hc f")),
                    reads=[f"wi16d{dvc}"], writes=[f"wi_sb{hf}"], dma=True)
            for c in range(4):
                pp = [P0, P1][c % 2]
                pkey = f"P{c % 2}"
                for hc in range(16):
                    P.op("tensor", lambda e, pp=pp, hc=hc, c=c, bp=bp:
                         e.matmul(pp[:, :],
                                  xln_feat[:, bp, hc, c * 128:(c + 1) * 128],
                                  wi_sb[:, hc, :],
                                  start=(hc == 0), stop=(hc == 15)),
                         reads=[f"xlf{bp}", f"wi_sb{hc // 8}"],
                         writes=[pkey])
                P.op("scalar", lambda e, pp=pp, c=c, dvc=dvc: e.copy(
                    out=v_tok[:, c, dvc * 512:(dvc + 1) * 512], in_=pp[:, :]),
                    reads=[pkey], writes=[f"vtok{c}"])

        def emit_wload(h):
            wp2 = h % 2
            if tb == 0:
                P.op("gpsimd", lambda e, h=h, wp2=wp2: e.dma_start(
                    out=wq_sb[:, wp2, :, :],
                    in_=wq_r[h].rearrange("hc p f -> p hc f")),
                    writes=[f"wq_sb{wp2}"], dma=True)
                P.op("sync", lambda e, h=h, wp2=wp2: e.dma_start(
                    out=wq16t[h],
                    in_=wq_sb[:, wp2, :, :].rearrange("p hc f -> p (hc f)")),
                    reads=[f"wq_sb{wp2}"], writes=[f"wq16d{h}"], dma=True)
                P.op("gpsimd", lambda e, h=h, wp2=wp2: e.dma_start(
                    out=wk_sb[:, wp2, :, :],
                    in_=wk_r[h].rearrange("hc p f -> p hc f")),
                    writes=[f"wk_sb{wp2}"], dma=True)
                P.op("sync", lambda e, h=h, wp2=wp2: e.dma_start(
                    out=wk16t[h],
                    in_=wk_sb[:, wp2, :, :].rearrange("p hc f -> p (hc f)")),
                    reads=[f"wk_sb{wp2}"], writes=[f"wk16d{h}"], dma=True)
            else:
                P.op("sync", lambda e, h=h, wp2=wp2: e.dma_start(
                    out=wq_sb[:, wp2, :, :].rearrange("p hc f -> p (hc f)"),
                    in_=wq16t[h]),
                    reads=[f"wq16d{h}"], writes=[f"wq_sb{wp2}"], dma=True)
                P.op("sync", lambda e, h=h, wp2=wp2: e.dma_start(
                    out=wk_sb[:, wp2, :, :].rearrange("p hc f -> p (hc f)"),
                    in_=wk16t[h]),
                    reads=[f"wk16d{h}"], writes=[f"wk_sb{wp2}"], dma=True)

        def emit_proj(h):
            hp = h % 4
            wp2 = h % 2
            for hc in range(16):
                P.op("tensor", lambda e, hc=hc, wp2=wp2, bp=bp: e.matmul(
                    P0[:, :], wq_sb[:, wp2, hc, :],
                    xln_feat[:, bp, hc, :],
                    start=(hc == 0), stop=(hc == 15)),
                    reads=[f"wq_sb{wp2}", f"xlf{bp}"], writes=["P0"])
            P.op("scalar", lambda e, hp=hp: e.activation(
                out=q_h[:, hp, :], in_=P0[:, :], func=AF.Relu, scale=SCALE),
                reads=["P0", "consts"], writes=[f"qh{hp}"])
            for hc in range(16):
                P.op("tensor", lambda e, hc=hc, wp2=wp2, bp=bp: e.matmul(
                    P1[:, :], wk_sb[:, wp2, hc, :],
                    xln_feat[:, bp, hc, :],
                    start=(hc == 0), stop=(hc == 15)),
                    reads=[f"wk_sb{wp2}", f"xlf{bp}"], writes=["P1"])

        def emit_prep1(h):
            hp = h % 2
            spv = sp_h[:, hp, :]
            P.op("scalar", lambda e, spv=spv: e.activation(
                out=spv, in_=P1[:, :], func=AF.Abs),
                reads=["P1"], writes=[f"sp{hp}"])
            P.op("scalar", lambda e, hp=hp: e.copy(
                out=kraw_h[:, hp, :], in_=P1[:, :]),
                reads=["P1"], writes=[f"krawh{hp}"])
            P.op("scalar", lambda e, spv=spv: e.activation(
                out=spv, in_=spv, func=AF.Exp, scale=-1.0),
                reads=[f"sp{hp}", "consts"], writes=[f"sp{hp}"])
            P.op("scalar", lambda e, spv=spv: e.activation(
                out=spv, in_=spv, func=AF.Ln, bias=1.0),
                reads=[f"sp{hp}", "consts"], writes=[f"sp{hp}"])

        def emit_prep2(h):
            hp = h % 2
            kr = kraw_h[:, hp, :]
            spv = sp_h[:, hp, :]
            ghv = gh_h[:, hp, :]
            bbv = bb_h[:, hp, :]
            P.op("vector", lambda e, kr=kr, bbv=bbv: e.tensor_scalar(
                out=bbv, in0=kr, scalar1=0.0, scalar2=None, op0=ALU.max),
                reads=[f"krawh{hp}"], writes=[f"bb{hp}"])
            P.op("vector", lambda e, spv=spv, bbv=bbv: e.tensor_add(
                out=spv, in0=spv, in1=bbv),
                reads=[f"sp{hp}", f"bb{hp}"], writes=[f"sp{hp}"])
            for c in range(4):
                P.op("vector", lambda e, c=c, spv=spv, ghv=ghv:
                     e.tensor_tensor_scan(
                         out=ghv[:, c * 128:(c + 1) * 128],
                         data0=spv[:, c * 128:(c + 1) * 128],
                         data1=zchunk[:, :],
                         initial=0.0, op0=ALU.add, op1=ALU.add),
                     reads=[f"sp{hp}", "zchunk"], writes=[f"gh{hp}"])
            P.op("vector", lambda e, kr=kr, spv=spv, bbv=bbv: e.tensor_sub(
                out=bbv, in0=kr, in1=spv),
                reads=[f"krawh{hp}", f"sp{hp}"], writes=[f"bb{hp}"])
            P.op("vector", lambda e, bbv=bbv, ghv=ghv: e.tensor_add(
                out=bbv, in0=bbv, in1=ghv),
                reads=[f"bb{hp}", f"gh{hp}"], writes=[f"bb{hp}"])
            P.op("vector", lambda e, hp=hp, ghv=ghv: e.tensor_scalar(
                out=ng_h[:, hp, 0:8:2],
                in0=ghv[:, 63:512:128],
                scalar1=-1.0, scalar2=None, op0=ALU.mult),
                reads=[f"gh{hp}"], writes=[f"ng{hp}"])
            P.op("vector", lambda e, hp=hp, ghv=ghv: e.tensor_scalar(
                out=ng_h[:, hp, 1:8:2],
                in0=ghv[:, 127:512:128],
                scalar1=-1.0, scalar2=None, op0=ALU.mult),
                reads=[f"gh{hp}"], writes=[f"ng{hp}"])

        def emit_prep3(h):
            hq = h % 4
            hp = h % 2
            ghv = gh_h[:, hp, :]
            bbv = bb_h[:, hp, :]
            egv = eg_h[:, hp, :]
            P.op("scalar", lambda e, ghv=ghv, egv=egv: e.activation(
                out=egv, in_=ghv, func=AF.Exp, scale=-1.0),
                reads=[f"gh{hp}", "consts"], writes=[f"eg{hp}"])
            P.op("vector", lambda e, hp=hp, hq=hq, egv=egv: e.tensor_mul(
                out=qg_h[:, hp, :], in0=q_h[:, hq, :], in1=egv),
                reads=[f"qh{hq}", f"eg{hp}"], writes=[f"qg{hp}"])
            for c in range(4):
                P.op("scalar", lambda e, c=c, hp=hp, bbv=bbv: e.activation(
                    out=ka_h[:, hp, c, :],
                    in_=bbv[:, c * 128:c * 128 + 64],
                    func=AF.Exp),
                    reads=[f"bb{hp}"], writes=[f"ka{hp}"])
            for c in range(4):
                ng63 = ng_h[:, hp, 2 * c:2 * c + 1]
                ng127 = ng_h[:, hp, 2 * c + 1:2 * c + 2]
                g63 = gh_h[:, hp, c * 128 + 63:c * 128 + 64]
                P.op("scalar", lambda e, c=c, hp=hp, bbv=bbv, ng63=ng63:
                     e.activation(
                         out=kb_h[:, hp, c, :],
                         in_=bbv[:, c * 128 + 64:(c + 1) * 128],
                         func=AF.Exp, bias=ng63),
                     reads=[f"bb{hp}", f"ng{hp}"], writes=[f"kb{hp}"])
                P.op("scalar", lambda e, c=c, hp=hp, bbv=bbv, ng63=ng63:
                     e.activation(
                         out=kba_h[:, hp, c, :],
                         in_=bbv[:, c * 128:c * 128 + 64],
                         func=AF.Exp, bias=ng63),
                     reads=[f"bb{hp}", f"ng{hp}"], writes=[f"kba{hp}"])
                P.op("scalar", lambda e, c=c, hp=hp, bbv=bbv, ng127=ng127:
                     e.activation(
                         out=kh_h[:, hp, c * 128:(c + 1) * 128],
                         in_=bbv[:, c * 128:(c + 1) * 128],
                         func=AF.Exp, bias=ng127),
                     reads=[f"bb{hp}", f"ng{hp}"], writes=[f"kh{hp}"])
                P.op("scalar", lambda e, c=c, hp=hp, ghv=ghv, g63=g63:
                     e.activation(
                         out=junkf[:, c * 64:(c + 1) * 64],
                         in_=ghv[:, c * 128 + 64:(c + 1) * 128],
                         func=AF.Exp, scale=-1.0, bias=g63),
                     reads=[f"gh{hp}", "consts"], writes=["jf"])
                P.op("vector", lambda e, c=c, hp=hp, hq=hq: e.tensor_mul(
                    out=qb_h[:, hp, c, :],
                    in0=q_h[:, hq, c * 128 + 64:(c + 1) * 128],
                    in1=junkf[:, c * 64:(c + 1) * 64]),
                    reads=[f"qh{hq}", "jf"], writes=[f"qb{hp}"])
            if tb == 0:
                for c in range(4):
                    P.op("vector", lambda e, c=c, hp=hp, h=h: e.tensor_scalar(
                        out=qgg_slab[:, c, h, :],
                        in0=qg_h[:, hp, c * 128:(c + 1) * 128],
                        scalar1=R_run[:, h:h + 1], scalar2=None,
                        op0=ALU.mult),
                        reads=[f"qg{hp}", "Rrun"], writes=[f"qgg{c}"])
                    P.op("vector", lambda e, c=c, hp=hp, h=h, egv=egv:
                         e.tensor_scalar(
                             out=R_run[:, h:h + 1], in0=R_run[:, h:h + 1],
                             scalar1=egv[:, c * 128 + 127:c * 128 + 128],
                             scalar2=None, op0=ALU.mult),
                         reads=["Rrun", f"eg{hp}"], writes=["Rrun"])

        def emit_apart(h, c):
            hp = h % 2
            cp = c % 2
            PA = [P2, P3][cp]
            pak = f"PA{cp}"
            P.op("tensor", lambda e, c=c, hp=hp, cp=cp: e.transpose(
                P7b[:, 0:128], kh_h[:, hp, c * 128:(c + 1) * 128],
                ident[:]),
                reads=[f"kh{hp}", "ident"], writes=["P7b"])
            P.op("vector", lambda e, hp=hp, cp=cp, h=h: e.tensor_copy(
                out=kh_tok[:, h % 2, cp, :], in_=P7b[:, 0:128]),
                reads=["P7b"], writes=[f"khtok{h % 2}{cp}"])
            P.op("tensor", lambda e, c=c, hp=hp, PA=PA: e.matmul(
                PA[0:64, 0:64], ka_h[:, hp, c, :],
                qg_h[:, hp, c * 128:c * 128 + 64],
                start=True, stop=True),
                reads=[f"ka{hp}", f"qg{hp}"], writes=[pak])
            P.op("tensor", lambda e, c=c, hp=hp, PA=PA: e.matmul(
                PA[0:64, 64:128], kba_h[:, hp, c, :], qb_h[:, hp, c, :],
                start=True, stop=True),
                reads=[f"kba{hp}", f"qb{hp}"], writes=[pak])
            P.op("tensor", lambda e, c=c, hp=hp, PA=PA: e.matmul(
                PA[64:128, 64:128], kb_h[:, hp, c, :], qb_h[:, hp, c, :],
                start=True, stop=True),
                reads=[f"kb{hp}", f"qb{hp}"], writes=[pak])
            P.op("vector", lambda e, cp=cp, PA=PA: e.tensor_tensor(
                out=asb[:, cp, :], in0=PA[:, 0:128],
                in1=trimask[:], op=ALU.mult),
                reads=[pak, "trimask"], writes=[f"asb{cp}"])

        def emit_opart(h, c):
            hp = h % 2
            cp = c % 2
            PO = [P4, P5][cp]
            pok = f"PO{cp}"
            egv = eg_h[:, hp, :]
            P.op("tensor", lambda e, c=c, h=h, cp=cp, PO=PO: e.matmul(
                PO[:, 0:64], v_tok[:, c, h * 128:(h + 1) * 128],
                asb[:, cp, 0:64], start=True, stop=False),
                reads=[f"vtok{c}", f"asb{cp}"], writes=[pok])
            P.op("tensor", lambda e, c=c, h=h, hp=hp, PO=PO: e.matmul(
                PO[:, 0:64], S_bf[:, h, :],
                qg_h[:, hp, c * 128:c * 128 + 64],
                start=False, stop=True),
                reads=["Sbf", f"qg{hp}"], writes=[pok])
            P.op("tensor", lambda e, c=c, h=h, cp=cp, PO=PO: e.matmul(
                PO[:, 64:128], v_tok[:, c, h * 128:(h + 1) * 128],
                asb[:, cp, 64:128], start=True, stop=False),
                reads=[f"vtok{c}", f"asb{cp}"], writes=[pok])
            P.op("tensor", lambda e, c=c, h=h, hp=hp, PO=PO: e.matmul(
                PO[:, 64:128], S_bf[:, h, :],
                qg_h[:, hp, c * 128 + 64:(c + 1) * 128],
                start=False, stop=True),
                reads=["Sbf", f"qg{hp}"], writes=[pok])
            P.op("tensor", lambda e, c=c, h=h, hp=hp: e.matmul(
                P6[:, 0:128], kh_tok[:, h % 2, cp, :],
                v_tok[:, c, h * 128:(h + 1) * 128],
                start=True, stop=True),
                reads=[f"khtok{h % 2}{cp}", f"vtok{c}"], writes=["P6"])
            P.op("scalar", lambda e, c=c, h=h, PO=PO: e.copy(
                out=o_slab[:, c, h, :], in_=PO[:, 0:128]),
                reads=[pok], writes=[f"oslab{c}"])
            P.op("vector", lambda e, c=c, h=h, hp=hp, egv=egv:
                 e.scalar_tensor_tensor(
                     out=S_sb[:, h, :], in0=S_sb[:, h, :],
                     scalar=egv[:, c * 128 + 127:c * 128 + 128],
                     in1=P6[:, 0:128], op0=ALU.mult, op1=ALU.add),
                 reads=["S", f"eg{hp}", "P6"], writes=["S"])
            P.op("vector", lambda e, h=h: e.tensor_copy(
                out=S_bf[:, h, :], in_=S_sb[:, h, :]),
                reads=["S"], writes=["Sbf"])

        def emit_scans(h):
            emit_apart(h, 0)
            for c in range(4):
                if c < 3:
                    emit_apart(h, c + 1)
                emit_opart(h, c)

        for i in range(NH + 3):
            if 0 <= i - 3 < NH:
                emit_scans(i - 3)
            if 0 <= i - 2 < NH:
                emit_prep3(i - 2)
            if 0 <= i - 1 < NH:
                emit_prep1(i - 1)
                emit_prep2(i - 1)
            if i + 1 < NH:
                emit_wload(i + 1)
            if i == 0:
                emit_wload(0)
            if i < NH:
                emit_proj(i)
            if i == 2 and tb + 1 < NBLK:
                emit_blockLN(tb + 1)
            if tb < 2 and i < NH:
                emit_some_dequant(3 if i % 8 == 0 else 2)
        if tb == 1:
            emit_some_dequant(999)

        for c in range(4):
            gc = tb * 4 + c
            P.op("sync", lambda e, gc=gc, c=c: e.dma_start(
                out=o_dram[gc],
                in_=o_slab[:, c, :, :].rearrange("p h f -> p (h f)")),
                reads=[f"oslab{c}"], writes=[f"od{gc}"], dma=True)
            if tb == 0:
                P.op("sync", lambda e, gc=gc, c=c: e.dma_start(
                    out=ogg_dram[gc],
                    in_=qgg_slab[:, c, :, :].rearrange("p h f -> p (h f)")),
                    reads=[f"qgg{c}"], writes=[f"oggd{gc}"], dma=True)

        if tb == 1:
            ccw_ev = P.op(
                "gpsimd", lambda e: e.collective_compute(
                    "AllGather", ALU.bypass,
                    replica_groups=[list(range(8))],
                    ins=[cc_w_in.opt()], outs=[cc_w_out.opt()]),
                deps=[(f"gpsimddma{i}", P.count[f"gpsimddma{i}"], True)
                      for i in range(NDMASEM)],
                sem="cc")

    P.op("sync", lambda e: e.dma_start(
        out=cc_s_in,
        in_=S_sb[:, :, :].rearrange("p h f -> p (h f)")),
        reads=["S"], writes=["ccsin"], dma=True)
    ccs_ev = P.op("gpsimd", lambda e: e.collective_compute(
        "AllGather", ALU.bypass,
        replica_groups=[list(range(8))],
        ins=[cc_s_in.opt()], outs=[cc_s_out.opt()]),
        reads=["ccsin"], sem="cc")

    P.barrier()
    P.flush(nc, sems)
    free_group(gA)

    if "B" not in phases:
        for cmx in reversed(ps_ctx):
            cmx.__exit__(None, None, None)
        for cmx in sem_ctx:
            cmx.__exit__(None, None, None)
        return nc

    gB, sbB = make_group()
    wo_sb = sbB("wo_sb", [128, 16, 2048], BF16)
    sgat = sbB("sgat", [128, NH, 128], F32)
    sini = sbB("sini", [128, NH, 128], F32)
    sini_bf = sbB("sini_bf", [128, NH, 128], BF16)
    off_sb = sbB("off_sb", [128, 2, 16, 128], BF16)
    qgg_sb = sbB("qgg_sb", [128, 2, 16, 128], BF16)
    ofull = sbB("ofull", [128, 2, 16, 128], BF16)
    ogw = sbB("ogw", [128, 2, 16, 128], BF16)
    sqsl = sbB("sqsl", [128, 2, 16, 128], BF16)
    sqpart = sbB("sqpart", [128, 2, 128], F32)
    sqpartb = sbB("sqpartb", [128, 2, 128], BF16)
    rstd_sb = sbB("rstd_sb", [128, 2], F32)
    xb_sb = sbB("xb_sb", [128, 2, 2048], F32)
    x2_sb = sbB("x2_sb", [128, 2, 2048], F32)

    P.op("gpsimd", lambda e: e.dma_start(
        out=wo_sb[:, :, :],
        in_=wo16.rearrange("(hc p) f -> p hc f", p=128)),
        reads=["wo16d"], writes=["wo_sb"], dma=True)

    def emit_sini():
        P.op("vector", lambda e: e.memset(sini[:, :, :], 0.0),
             writes=["sini"])
        for r in range(8):
            P.op("sync", lambda e, r=r: e.dma_start(
                out=sgat[:, :, :].rearrange("p h f -> p (h f)"),
                in_=cc_s_out[r * 128:(r + 1) * 128, :]),
                deps=[ccs_ev], writes=["sgat"], dma=True)
            P.op("vector", lambda e, r=r: e.scalar_tensor_tensor(
                out=sini[:, :, :], in0=sgat[:, :, :],
                scalar=mask8[:, r:r + 1],
                in1=sini[:, :, :], op0=ALU.mult, op1=ALU.add),
                reads=["sgat", "mask8", "sini"], writes=["sini"])
        P.op("vector", lambda e: e.tensor_copy(out=sini_bf[:, :, :],
                                               in_=sini[:, :, :]),
             reads=["sini"], writes=["sinibf"])

    for gt in list(range(4, NTT)) + list(range(4)):
        par = gt % 2
        if gt == 0:
            emit_sini()
        P.op("sync", lambda e, gt=gt, par=par: e.dma_start(
            out=off_sb[:, par, :, :].rearrange("p h f -> p (h f)"),
            in_=o_dram[gt]),
            writes=[f"off{par}"], dma=True)
        if gt < 4:
            P.op("sync", lambda e, gt=gt, par=par: e.dma_start(
                out=qgg_sb[:, par, :, :].rearrange("p h f -> p (h f)"),
                in_=ogg_dram[gt]),
                writes=[f"qggsb{par}"], dma=True)
        P.op("sync", lambda e, gt=gt, par=par: e.dma_start(
            out=xb_sb[:, par, :], in_=x_in[gt * 128:(gt + 1) * 128, :]),
            writes=[f"xb{par}"], dma=True)
        if gt < 4:
            for h in range(NH):
                pc = h % 2
                PC = [P4, P5][pc]
                pck = f"PB{pc}"
                P.op("tensor", lambda e, h=h, par=par, PC=PC: e.matmul(
                    PC[:, 0:128], sini_bf[:, h, :], qgg_sb[:, par, h, :],
                    start=True, stop=True),
                    reads=["sinibf", f"qggsb{par}"], writes=[pck])
                P.op("vector", lambda e, h=h, par=par, PC=PC: e.tensor_add(
                    out=ofull[:, par, h, :], in0=off_sb[:, par, h, :],
                    in1=PC[:, 0:128]),
                    reads=[f"off{par}", pck], writes=[f"ofull{par}"])
                P.op("vector", lambda e, h=h, par=par: e.tensor_scalar(
                    out=ogw[:, par, h, :], in0=ofull[:, par, h, :],
                    scalar1=gwc[:, h:h + 1], scalar2=None, op0=ALU.mult),
                    reads=[f"ofull{par}", "gwc"], writes=[f"ogw{par}"])
            P.op("scalar", lambda e, par=par: e.activation(
                out=sqsl[:, par, :, :], in_=ofull[:, par, :, :],
                func=AF.Square),
                reads=[f"ofull{par}"], writes=[f"sqsl{par}"])
        else:
            for h in range(NH):
                P.op("vector", lambda e, h=h, par=par: e.tensor_scalar(
                    out=ogw[:, par, h, :], in0=off_sb[:, par, h, :],
                    scalar1=gwc[:, h:h + 1], scalar2=None, op0=ALU.mult),
                    reads=[f"off{par}", "gwc"], writes=[f"ogw{par}"])
            P.op("scalar", lambda e, par=par: e.activation(
                out=sqsl[:, par, :, :], in_=off_sb[:, par, :, :],
                func=AF.Square),
                reads=[f"off{par}"], writes=[f"sqsl{par}"])
        P.op("vector", lambda e, par=par: e.tensor_reduce(
            out=sqpart[:, par, :],
            in_=sqsl[:, par, :, :].rearrange("p h f -> p f h"),
            axis=AX.X, op=ALU.add),
            reads=[f"sqsl{par}"], writes=[f"sqpart{par}"])
        P.op("vector", lambda e, par=par: e.tensor_copy(
            out=sqpartb[:, par, :], in_=sqpart[:, par, :]),
            reads=[f"sqpart{par}"], writes=[f"sqpartb{par}"])
        P.op("tensor", lambda e, par=par: e.matmul(
            P6[:, 0:1], sqpartb[:, par, :], trimask[:, 127:128],
            start=True, stop=True),
            reads=[f"sqpartb{par}", "trimask"], writes=["P6B"])
        rc = rstd_sb[:, par:par + 1]
        P.op("scalar", lambda e, rc=rc: e.activation(
            out=rc, in_=P6[:, 0:1], func=AF.Ln, scale=INV_DM, bias=LN_EPS),
            reads=["P6B", "consts"], writes=[f"rstd{par}"])
        P.op("scalar", lambda e, rc=rc: e.activation(
            out=rc, in_=rc, func=AF.Exp, scale=-0.5),
            reads=[f"rstd{par}", "consts"], writes=[f"rstd{par}"])
        if debug_outputs and gt == 0:
            P.op("sync", lambda e: e.dma_start(
                out=dbgb[:, 2048:4096],
                in_=ofull[:, :, :]),
                reads=["ofull"], writes=["dbgb"], dma=True)
            P.op("sync", lambda e: e.dma_start(
                out=dbgf[:, 3072:3200], in_=sqpart[:, :]),
                reads=["sqpart"], writes=["dbgf"], dma=True)
            P.op("sync", lambda e, rc=rc: e.dma_start(
                out=dbgf[:, 3200:3201], in_=rc,
                allow_slow_non_contiguous=True),
                reads=[f"rstd{par}"], writes=["dbgf"], dma=True)
            P.op("sync", lambda e: e.dma_start(
                out=dbgb[:, 0:2048],
                in_=ogw[:, :, :]),
                reads=["ogw"], writes=["dbgb"], dma=True)
        for mcq in range(4):
            pw = mcq % 2
            PW = [P0, P1][pw]
            pwk = f"PW{pw}"
            for hc in range(16):
                P.op("tensor", lambda e, hc=hc, mcq=mcq, par=par, PW=PW:
                     e.matmul(
                         PW[:, :], ogw[:, par, hc, :],
                         wo_sb[:, hc, mcq * 512:(mcq + 1) * 512],
                         start=(hc == 0), stop=(hc == 15)),
                     reads=[f"ogw{par}", "wo_sb"], writes=[pwk])
            P.op("vector", lambda e, mcq=mcq, par=par, PW=PW, rc=rc:
                 e.scalar_tensor_tensor(
                     out=x2_sb[:, par, mcq * 512:(mcq + 1) * 512],
                     in0=PW[:, :], scalar=rc,
                     in1=xb_sb[:, par, mcq * 512:(mcq + 1) * 512],
                     op0=ALU.mult, op1=ALU.add),
                 reads=[pwk, f"rstd{par}", f"xb{par}"], writes=[f"x2{par}"])
        P.op("sync", lambda e, gt=gt, par=par: e.dma_start(
            out=x2_dram[gt * 128:(gt + 1) * 128, :], in_=x2_sb[:, par, :]),
            reads=[f"x2{par}"], writes=[f"x2d{gt}"], dma=True)
        emit_ln(x2_sb[:, par, :], f"x2{par}", xln_tok[:, 0, :],
                "xlntok", 2)
        P.op("sync", lambda e, gt=gt, par=par: e.dma_start(
            out=ln_dram[gt * 128:(gt + 1) * 128, :],
            in_=xln_tok[:, 0, :]),
            reads=["xlntok"], writes=[f"ln2d{gt}"], dma=True)

    P.barrier()
    P.flush(nc, sems)
    free_group(gB)

    if "C" not in phases:
        for cmx in reversed(ps_ctx):
            cmx.__exit__(None, None, None)
        for cmx in sem_ctx:
            cmx.__exit__(None, None, None)
        return nc

    gC, sbC = make_group()
    xln2_feat = sbC("xln2_feat", [128, 2, 16, 512], BF16)
    hact = sbC("hact", [128, NIC, 512], BF16)
    wgu_sb = sbC("wgu_sb", [128, 2, 4, 1024], BF16)
    wdn_sb = sbC("wdn_sb", [128, 22, 1024], BF16)
    sg_sb = sbC("sg_sb", [128, 2, 512], BF16)
    y_sb = sbC("y_sb", [128, 4, 2048], F32)
    x2e_sb = sbC("x2e_sb", [128, 1, 2048], F32)
    out_sb = sbC("out_sb", [128, 2048], F32)

    for tb in range(4):
        bp = tb % 2
        for hc in range(16):
            P.op("sync", lambda e, hc=hc, tb=tb, bp=bp: e.dma_start(
                out=xln2_feat[:, bp, hc, :],
                in_=ln_dram[tb * 512:(tb + 1) * 512,
                            hc * 128:(hc + 1) * 128],
                transpose=True),
                writes=[f"x2f{bp}"], dma=True)

        for p in range(NPAIR):
            wp = p % 2
            P.op("sync", lambda e, p=p, wp=wp: e.dma_start(
                out=wgu_sb[:, wp, :, :],
                in_=cc_w_out[p * 4 * 128:(p + 1) * 4 * 128, :]
                .rearrange("(g p2) f -> p2 g f", p2=128)),
                deps=[ccw_ev], writes=[f"wgu{wp}"], dma=True)
            pg = p % 2
            PG, PU = [P0, P2][pg], [P1, P3][pg]
            pgk, puk = f"PG{pg}", f"PU{pg}"
            for gu in range(2):
                PP = PG if gu == 0 else PU
                ppk = pgk if gu == 0 else puk
                for hc in range(16):
                    lidx = gu * 16 + hc
                    g, cb = lidx // 8, lidx % 8
                    P.op("tensor",
                         lambda e, hc=hc, g=g, cb=cb, wp=wp, bp=bp, PP=PP:
                         e.matmul(
                             PP[:, :],
                             wgu_sb[:, wp, g, cb * 128:(cb + 1) * 128],
                             xln2_feat[:, bp, hc, :],
                             start=(hc == 0), stop=(hc == 15)),
                         reads=[f"wgu{wp}", f"x2f{bp}"], writes=[ppk])
            P.op("scalar", lambda e, pg=pg, PG=PG: e.activation(
                out=sg_sb[:, pg, :], in_=PG[:, :], func=AF.Silu),
                reads=[pgk], writes=[f"sg{pg}"])
            P.op("vector", lambda e, p=p, pg=pg, PU=PU: e.tensor_tensor(
                out=hact[:, p, :], in0=sg_sb[:, pg, :], in1=PU[:, :],
                op=ALU.mult),
                reads=[f"sg{pg}", puk], writes=["hact"])

        for mcg in range(4):
            for half in range(2):
                g0 = 176 + 22 * mcg + 11 * half
                P.op("sync", lambda e, g0=g0, half=half: e.dma_start(
                    out=wdn_sb[:, half * 11:(half + 1) * 11, :],
                    in_=cc_w_out[g0 * 128:(g0 + 11) * 128, :]
                    .rearrange("(g p2) f -> p2 g f", p2=128)),
                    deps=[ccw_ev], writes=[f"wdn{half}"], dma=True)
            for tt in range(4):
                pd = tt % 2
                PD = [P4, P5][pd]
                pdk = f"PD{pd}"
                for ic in range(NIC):
                    half = 0 if ic < 22 else 1
                    gl = ic // 2
                    co = (ic % 2) * 512
                    P.op("tensor",
                         lambda e, ic=ic, tt=tt, gl=gl, co=co, PD=PD:
                         e.matmul(
                             PD[:, :], hact[:, ic, tt * 128:(tt + 1) * 128],
                             wdn_sb[:, gl, co:co + 512],
                             start=(ic == 0), stop=(ic == NIC - 1)),
                         reads=["hact", f"wdn{half}"], writes=[pdk])
                P.op("scalar", lambda e, tt=tt, mcg=mcg, PD=PD: e.copy(
                    out=y_sb[:, tt, mcg * 512:(mcg + 1) * 512], in_=PD[:, :]),
                    reads=[pdk], writes=[f"y{tt}"])

        for tt in range(4):
            gt = tb * 4 + tt
            pe = tt % 2
            P.op("sync", lambda e, gt=gt: e.dma_start(
                out=x2e_sb[:, 0, :],
                in_=x2_dram[gt * 128:(gt + 1) * 128, :]),
                writes=["x2e0"], dma=True)
            P.op("scalar", lambda e, tt=tt: e.activation(
                out=junk[:, 0:2048], in_=y_sb[:, tt, :], func=AF.Square,
                accum_out=stat[:, 4:5]),
                reads=[f"y{tt}"], writes=["statc", "junk"])
            P.op("scalar", lambda e: e.activation(
                out=stat[:, 4:5], in_=stat[:, 4:5], func=AF.Ln,
                scale=INV_DM, bias=RMS_EPS_FFN),
                reads=["statc", "consts"], writes=["statc"])
            P.op("scalar", lambda e: e.activation(
                out=stat[:, 4:5], in_=stat[:, 4:5], func=AF.Exp, scale=-0.5),
                reads=["statc", "consts"], writes=["statc"])
            P.op("vector", lambda e, tt=tt: e.scalar_tensor_tensor(
                out=out_sb[:, :], in0=y_sb[:, tt, :],
                scalar=stat[:, 4:5], in1=x2e_sb[:, 0, :],
                op0=ALU.mult, op1=ALU.add),
                reads=[f"y{tt}", "statc", "x2e0"], writes=["outsb"])
            P.op("sync", lambda e, gt=gt: e.dma_start(
                out=y_out[gt * 128:(gt + 1) * 128, :], in_=out_sb[:, :]),
                reads=["outsb"], writes=["yd"], dma=True)

    P.barrier()
    P.flush(nc, sems)
    free_group(gC)
    free_group(gG)

    for cmx in reversed(ps_ctx):
        cmx.__exit__(None, None, None)
    for cmx in sem_ctx:
        cmx.__exit__(None, None, None)
    return nc


def _host_inputs(L0_attn_hidden_states, L0_attn_Wq, L0_attn_Wk, L0_attn_Wi,
                 L0_attn_Wo, L0_attn_g_weight, L0_ffn_gate_up_packed,
                 L0_ffn_gate_up_scales, L0_ffn_down_packed,
                 L0_ffn_down_scales):
    x = np.asarray(L0_attn_hidden_states, dtype=np.float32)
    wq = np.ascontiguousarray(np.asarray(L0_attn_Wq, np.float32))
    wk = np.ascontiguousarray(np.asarray(L0_attn_Wk, np.float32))
    wi = np.ascontiguousarray(np.asarray(L0_attn_Wi, np.float32))
    wo = np.ascontiguousarray(np.asarray(L0_attn_Wo, np.float32))
    gw = np.asarray(L0_attn_g_weight, np.float32)
    gup = np.asarray(L0_ffn_gate_up_packed, np.int32)
    gus = np.asarray(L0_ffn_gate_up_scales, np.float16)
    dnp = np.asarray(L0_ffn_down_packed, np.int32)
    dns = np.asarray(L0_ffn_down_scales, np.float16)

    gwc = np.ascontiguousarray(gw.reshape(NH, 128).T)
    trimask = np.zeros((128, 128), dtype=BF)
    su = np.tril(np.ones((64, 64), np.float32)).T.astype(BF)
    trimask[0:64, 0:64] = su
    trimask[0:64, 64:128] = 1.0
    trimask[64:128, 64:128] = su
    ident = np.eye(128, dtype=BF)
    consts = np.broadcast_to(np.array(CONST_VALS, np.float32),
                             (128, len(CONST_VALS))).copy()

    def tile_slices(gidx):
        if gidx < 1408:
            pair, rem = divmod(gidx, 32)
            gu, hc = divmod(rem, 16)
            r0 = (gu * NPAIR + pair) * 128
            pk = gup[r0:r0 + 128, hc * 16:(hc + 1) * 16]
            sc = gus[hc, r0:r0 + 128]
        else:
            g2 = gidx - 1408
            mcg, rem = divmod(g2, 176)
            ic, mi = divmod(rem, 4)
            mt = mcg * 4 + mi
            pk = dnp[mt * 128:(mt + 1) * 128, ic * 16:(ic + 1) * 16]
            sc = dns[ic, mt * 128:(mt + 1) * 128]
        return pk, sc

    per_core = []
    for c in range(8):
        b, half = c // 2, c % 2
        xs = np.ascontiguousarray(x[b, half * T:(half + 1) * T, :])
        pks = np.empty((TPC, 128, 16), np.int32)
        scs = np.empty((TPC, 128), np.float16)
        for i in range(TPC):
            pk, sc = tile_slices(c * TPC + i)
            pks[i] = pk
            scs[i] = sc
        mask8 = np.zeros((128, 8), np.float32)
        if half == 1:
            mask8[:, c - 1] = 1.0
        per_core.append(dict(
            x=xs, wq=wq, wk=wk, wi=wi, wo=wo,
            gwc=gwc, mask8=mask8, trimask=trimask, ident=ident,
            consts=consts, packedsl=pks,
            scalessl=np.ascontiguousarray(scs.T)))
    return per_core


def kernel(**inputs):
    per_core = _host_inputs(**inputs)
    nc = build_nc()
    res = run_bass_kernel_spmd(nc, per_core, core_ids=list(range(8)))
    out = np.empty((B_, L_, DM), np.float32)
    for c in range(8):
        b, half = c // 2, c % 2
        out[b, half * T:(half + 1) * T, :] = res.results[c]["y"]
    return out


if __name__ == "__main__":
    nc = build_nc()
    print("built ok; instructions:", nc.n_instructions())
